# version 02
# baseline (speedup 1.0000x reference)
import numpy as np
import ml_dtypes
from contextlib import ExitStack

import concourse.bass as bass
import concourse.mybir as mybir
from concourse.bass_utils import run_bass_kernel_spmd

F32 = mybir.dt.float32
BF16 = mybir.dt.bfloat16
I32 = mybir.dt.int32
AF = mybir.ActivationFunctionType
ALU = mybir.AluOpType
AX = mybir.AxisListType

D = 1024
SEQ = 4096
NH = 16
HD = 64
CH = 128
ND = D // 128
TS = 512
EPS_TOP = 1e-5
EPS_L1LN = 1e-12
EPS_ATTN = 1e-6
Q127 = 1.0 / 127.0

ENGINES = ("sync", "scalar", "tensor", "vector", "gpsimd")
ESEM = {"sync": "Ssy", "scalar": "Sac", "tensor": "Spe", "vector": "Sdv", "gpsimd": "Sgp"}



class Prog:
    def __init__(self, nc):
        self.nc = nc
        self.ops = {e: [] for e in ENGINES}
        self.cnt = {s: 0 for s in ESEM.values()}
        self.sem_handles = {}
        self.last_wait = {e: {} for e in ENGINES}

    def _wait(self, engine, s, v):
        lw = self.last_wait[engine]
        if lw.get(s, -1) >= v:
            return
        lw[s] = v

        def w(eng, s=s, v=v):
            eng.wait_ge(self.sem_handles[s], v)
        self.ops[engine].append(w)

    def op(self, engine, fn, deps=(), inc=1, strict=False):
        best = {}
        for t in deps:
            if t is None:
                continue
            s, v = t
            if best.get(s, -1) < v:
                best[s] = v
        own = ESEM[engine]
        for s, v in sorted(best.items()):
            if s != own or strict:
                self._wait(engine, s, v)
        if inc:
            s = own
            self.cnt[s] += inc
            tok = (s, self.cnt[s])

            def wrapped(eng, fn=fn, s=s, inc=inc):
                fn(eng).then_inc(self.sem_handles[s], inc)
            self.ops[engine].append(wrapped)
            return tok
        self.ops[engine].append(lambda eng, fn=fn: fn(eng))
        return None

    def sop(self, engine, fn, deps=()):
        tok = self.op(engine, fn, deps=deps, inc=1)
        self._wait(engine, tok[0], tok[1])
        return tok

    def dma(self, engine, out, in_, deps=(), strict=False):
        return self.op(engine, lambda eng, o=out, i=in_: eng.dma_start(out=o, in_=i),
                       deps=deps, inc=16, strict=strict)

    def now(self):
        return [(s, self.cnt[s]) for s in self.cnt if self.cnt[s] > 0]

    def join(self, engines=ENGINES):
        toks = self.now()
        for e in engines:
            for s, v in toks:
                if s != ESEM[e]:
                    self._wait(e, s, v)

    def build(self):
        st = ExitStack()
        for s in ESEM.values():
            self.sem_handles[s] = st.enter_context(self.nc.semaphore(s))
        block = st.enter_context(self.nc.Block())
        for e in ENGINES:
            lst = self.ops[e]
            if not lst:
                continue

            def run(eng, lst=lst):
                for fn in lst:
                    fn(eng)
            getattr(block, e)(run)
        st.close()


def merge(*toks):
    out = []
    for t in toks:
        if t is None:
            continue
        if isinstance(t, (list, tuple)) and (len(t) == 0 or not isinstance(t[0], str)):
            out.extend(x for x in t if x is not None)
        else:
            out.append(t)
    return out


class CBuf:
    def __init__(self, nc, name, n, w, dtype):
        self.h = nc.alloc_sbuf_tensor(name, [128, n * w], dtype)
        self.n = n
        self.w = w

    def sl(self, c, a, b):
        return self.h[:, c * self.w + a: c * self.w + b]

    def chunk(self, c):
        return self.h[:, c * self.w:(c + 1) * self.w]


class Ctx:
    def __init__(self, nc, p, T):
        self.nc = nc
        self.p = p
        self.T = T
        self.NS = T // TS
        self.ps = nc.alloc_psum_tensor("ps", [128, 4096], F32)
        self.bank_ready = [None] * 8
        self.bank_free = [None] * 8
        self.ring_i = 0

    def bank(self, b, p0=0, p1=128, w=TS):
        return self.ps[p0:p1, b * TS: b * TS + w]

    def ring(self):
        banks = getattr(self, "ring_banks", None) or list(range(8))
        b = banks[self.ring_i % len(banks)]
        self.ring_i += 1
        return b


def acc_group(ctx, bank, mms, deps=(), p0=0, p1=128, w=TS, col0=0):
    p = ctx.p
    out_ap = ctx.ps[p0:p1, bank * TS + col0: bank * TS + col0 + w]
    n = len(mms)
    tok = None
    for i, (lhsT, rhs) in enumerate(mms):
        d = list(deps) if i == 0 else []
        if i == 0 and ctx.bank_free[bank] is not None:
            d.append(ctx.bank_free[bank])

        def fn(eng, lhsT=lhsT, rhs=rhs, i=i, out_ap=out_ap):
            return eng.matmul(out_ap, lhsT, rhs, start=(i == 0), stop=(i == n - 1))
        tok = p.op("tensor", fn, deps=d, inc=(1 if i == n - 1 else 0))
    ctx.bank_ready[bank] = tok
    ctx.bank_free[bank] = None
    return tok


def bank_read(ctx, bank, tok):
    ctx.bank_free[bank] = tok


class WStream:

    def __init__(self, ctx, name, nbuf=4):
        self.ctx = ctx
        self.nbuf = nbuf
        self.bufs = [CBuf(ctx.nc, f"{name}{i}", 8, 512, BF16) for i in range(nbuf)]
        self.busy = [None] * nbuf
        self.i = 0

    def load(self, dram, r0, nk, c0, w):
        assert nk * w <= 4096
        p = self.ctx.p
        idx = self.i % self.nbuf
        self.i += 1
        buf = self.bufs[idx]
        dep = [self.busy[idx]] if self.busy[idx] is not None else []
        tok = None
        for c in range(nk):
            tok = p.dma("gpsimd",
                        buf.h[:, c * w:(c + 1) * w],
                        dram[r0 + c * 128: r0 + (c + 1) * 128, c0:c0 + w],
                        deps=dep if c == 0 else ())

        def sl(c, a, b, buf=buf, w=w):
            return buf.h[:, c * w + a: c * w + b]
        return idx, sl, tok

    def done(self, idx, tok):
        prev = self.busy[idx]
        if prev is None or tok[1] > prev[1]:
            self.busy[idx] = tok



def _col_pack(v):
    v = np.asarray(v, dtype=np.float32)
    n = v.shape[0] // 128
    return np.ascontiguousarray(v.reshape(n, 128).T)


def _t(w):
    return np.ascontiguousarray(np.asarray(w).T)


def _row(v):
    return np.ascontiguousarray(np.asarray(v, np.float32).reshape(1, -1))


def host_pack(inputs, core):
    m = {}
    x = np.asarray(inputs["L0_attn_x"][core])
    m["xT"] = _t(x)
    for i in range(2):
        m[f"pw{i}"] = np.ascontiguousarray(np.asarray(inputs["L0_attn_proj_w"][i]))
        m[f"pb{i}_col"] = _col_pack(inputs["L0_attn_proj_b"][i])
        m[f"pb{i}_row"] = np.ascontiguousarray(
            np.asarray(inputs["L0_attn_proj_b"][i], np.float32)[1024:3072].reshape(1, -1))
        m[f"rw{i}"] = np.ascontiguousarray(np.asarray(inputs["L0_attn_reto_w"][i]))
        m[f"rb{i}"] = _col_pack(inputs["L0_attn_reto_b"][i])
        m[f"f1w{i}"] = np.ascontiguousarray(np.asarray(inputs["L0_attn_ffn1_w"][i]))
        m[f"f1b{i}"] = _col_pack(inputs["L0_attn_ffn1_b"][i])
        m[f"f2w{i}"] = np.ascontiguousarray(np.asarray(inputs["L0_attn_ffn2_w"][i]))
        m[f"f2b{i}"] = _col_pack(inputs["L0_attn_ffn2_b"][i])
        m[f"ln1g{i}"] = _col_pack(inputs["L0_attn_ln1_g"][i])
        m[f"ln1b{i}"] = _col_pack(inputs["L0_attn_ln1_b"][i])
        m[f"ln2g{i}"] = _col_pack(inputs["L0_attn_ln2_g"][i])
        m[f"ln2b{i}"] = _col_pack(inputs["L0_attn_ln2_b"][i])
    m["lnfg"] = _col_pack(inputs["L0_attn_lnf_g"])
    m["lnfb"] = _col_pack(inputs["L0_attn_lnf_b"])
    m["wgT"] = _t(inputs["L0_ffn_w_gate"])
    m["wuT"] = _t(inputs["L0_ffn_w_up"])
    m["wdT"] = _t(inputs["L0_ffn_w_down"])
    for nm, key in (("l1_wq", "L1_attn_Wq"), ("l1_wk", "L1_attn_Wk"),
                    ("l1_wv", "L1_attn_Wv"), ("l1_wd", "L1_attn_Wd")):
        m[nm] = _t(inputs[key])
    m["bq_col"] = _col_pack(inputs["L1_attn_bq"])
    m["bk_row"] = _row(inputs["L1_attn_bk"])
    m["bv_row"] = _row(inputs["L1_attn_bv"])
    m["bd_col"] = _col_pack(inputs["L1_attn_bd"])
    m["l1g"] = _col_pack(inputs["L1_attn_gamma"])
    m["l1b"] = _col_pack(inputs["L1_attn_beta"])
    m["sgT"] = _t(inputs["L1_ffn_sg"])
    m["suT"] = _t(inputs["L1_ffn_su"])
    m["sdT"] = _t(inputs["L1_ffn_sd"])
    m["rgT"] = np.ascontiguousarray(np.asarray(inputs["L1_ffn_rg"]).transpose(0, 2, 1).reshape(7 * D, 2816))
    m["ruT"] = np.ascontiguousarray(np.asarray(inputs["L1_ffn_ru"]).transpose(0, 2, 1).reshape(7 * D, 2816))
    m["rdT"] = np.ascontiguousarray(np.asarray(inputs["L1_ffn_rd"]).transpose(0, 2, 1).reshape(7 * 2816, D))
    m["wrT"] = _t(inputs["L1_ffn_wr"])
    m["rb_row"] = _row(inputs["L1_ffn_rb"])
    return m


def host_consts():
    c = {}
    gamma = (1.0 - 2.0 ** (-5.0 - np.arange(NH, dtype=np.float64)))
    j = np.arange(CH, dtype=np.float64)
    dts = np.zeros((NH, CH, CH), np.float32)
    for h in range(NH):
        mask = (j[None, :] - j[:, None]) >= 0
        vals = (gamma[h] ** (-(j + 1.0)))[:, None]
        dts[h] = np.where(mask, vals, 0.0).astype(np.float32)
    c["DT"] = dts.reshape(NH * CH, CH)
    qd = np.zeros((ND, 128, CH), np.float32)
    for cc in range(ND):
        for half in range(2):
            h = 2 * cc + half
            qd[cc, half * 64:(half + 1) * 64, :] = (gamma[h] ** (j + 1.0)).astype(np.float32)[None, :]
    c["QD"] = qd.reshape(ND * 128, CH)
    kd = np.zeros((2, CH, 512), np.float32)
    for s in range(2):
        for hh in range(8):
            h = s * 8 + hh
            colv = (gamma[h] ** (CH - 1.0 - j)).astype(np.float32)
            kd[s, :, hh * 64:(hh + 1) * 64] = colv[:, None]
    c["KD"] = kd.reshape(2 * CH, 512)
    gc = np.zeros((128, 512), np.float32)
    for h in range(NH):
        gc[(h % 2) * 64:(h % 2) * 64 + 64, (h // 2) * 64:(h // 2) * 64 + 64] = np.float32(gamma[h] ** CH)
    c["GC"] = gc
    c["ones1"] = np.ones((1, 128), np.float32)
    c["onesc"] = np.ones((128, 1), np.float32)
    sel7 = np.zeros((7, 7 * 128), np.float32)
    for e in range(7):
        sel7[e, e * 128:(e + 1) * 128] = 1.0
    c["SEL7"] = sel7
    sel16 = np.zeros((16, 8 * 128), np.float32)
    for cc in range(8):
        sel16[2 * cc, cc * 128: cc * 128 + 64] = 1.0
        sel16[2 * cc + 1, cc * 128 + 64: cc * 128 + 128] = 1.0
    c["SEL16"] = sel16
    c["SEL2"] = np.zeros((2, 128), np.float32)
    c["SEL2"][0, 0:64] = 1.0
    c["SEL2"][1, 64:128] = 1.0
    c["ident"] = np.eye(128, dtype=np.float32)
    return c



class SB:
    def __init__(self, ctx, T, consts, need=(), moe=False):
        nc, p = ctx.nc, ctx.p
        self.T = T
        self.x = CBuf(nc, "x", ND, T, F32)
        self.xb = CBuf(nc, "xb", ND, T, BF16)
        self.h = CBuf(nc, "h", ND, T, BF16)
        self.rows = nc.alloc_sbuf_tensor("rows", [1, 2 * T], F32)
        self.t1 = [nc.alloc_sbuf_tensor(f"t1_{i}", [128, TS], F32) for i in range(2)]
        self.t1_busy = [None, None]
        self.const = {}
        for name in set(("ones1", "ident") + tuple(need)):
            arr = consts[name]
            dr = nc.inline_tensor(np.ascontiguousarray(arr), "ic_" + name)
            hh = nc.alloc_sbuf_tensor("cst_" + name, list(arr.shape), F32)
            p.dma("sync", hh[:, :], dr.ap())
            self.const[name] = hh
        onesb = nc.inline_tensor(np.ones((128, 1), np.float32), "ic_ones_b")
        self.ones_bf = nc.alloc_sbuf_tensor("cst_ones_bf", [128, 1], BF16)
        p.dma("gpsimd", self.ones_bf[:, :], onesb.ap())
        onesr = nc.inline_tensor(np.ones((1, 128), np.float32), "ic_ones_r")
        self.ones_row_bf = nc.alloc_sbuf_tensor("cst_ones_r_bf", [1, 128], BF16)
        p.dma("gpsimd", self.ones_row_bf[:, :], onesr.ap())
        self.vec = {}
        self.vecrow = {}
        if moe:
            self.rtr = nc.alloc_sbuf_tensor("rtr", [128, 64], F32)
            self.rtr_busy = [None, None]
            self.w_rows = None
            self.w_rows_tok = None
            self.wbb = None
            self.wr_b = None
            self.wr_tok = None
            self.rb_row_bf = None

    def h_f32(self, i):
        return self.h.h[:, 2 * i * self.T:(2 * i + 2) * self.T].bitcast(F32)


def load_vec_inputs(ctx, sb, names_shapes, bf_rows=(), bf_only_rows=()):
    nc, p = ctx.nc, ctx.p
    for name, shape in names_shapes.items():
        d = nc.dram_tensor(name, list(shape), F32, kind="ExternalInput").ap()
        if name not in bf_only_rows:
            h = nc.alloc_sbuf_tensor("v_" + name, list(shape), F32)
            p.dma("sync", h[:, :], d[:, :])
            sb.vec[name] = h
        if name in bf_rows or name in bf_only_rows:
            hb = nc.alloc_sbuf_tensor("vb_" + name, list(shape), BF16)
            p.dma("gpsimd", hb[:, :], d[:, :])
            sb.vecrow[name] = hb



def emit_ln(ctx, sb, *, g=None, b=None, eps, out_bf=None, add_f32=None,
            bf_only=False, name=""):
    p = ctx.p
    T = ctx.T
    NSl = ctx.NS
    x = sb.x
    scr = [(sb.h.chunk(0), sb.h.chunk(1)), (sb.h.chunk(2), sb.h.chunk(3))]
    onesb = sb.ones_bf
    onesf = sb.const["ones1"]

    prev_mm = {}
    for c in range(ND):
        xbs, xsq = scr[c % 2]
        dep = merge([prev_mm.get(c - 2)])
        p.op("scalar", lambda eng, xbs=xbs, c=c: eng.activation(xbs, x.chunk(c), AF.Copy),
             deps=dep, inc=0)
        ct = p.op("scalar", lambda eng, xsq=xsq, xbs=xbs: eng.activation(xsq, xbs, AF.Square))
        last = None
        for t in range(NSl):
            for kind in range(2):
                bank = kind * NSl + t
                src = (xbs if kind == 0 else xsq)[:, t * TS:(t + 1) * TS]
                d = [ct]
                if c == 0 and ctx.bank_free[bank] is not None:
                    d.append(ctx.bank_free[bank])
                last = p.op("tensor", lambda eng, bank=bank, src=src, c=c:
                            eng.matmul(ctx.bank(bank, 0, 1), onesb[:, 0:1], src,
                                       start=(c == 0), stop=(c == ND - 1)),
                            deps=d, inc=(1 if (kind == 1 and t == NSl - 1) else 0))
        prev_mm[c] = last
    mm_done = prev_mm[ND - 1]

    mu = sb.rows[0:1, 0:T]
    rstd = sb.rows[0:1, T:2 * T]
    for t in range(NSl):
        p.op("vector", lambda eng, t=t: eng.tensor_scalar_mul(
            mu[:, t * TS:(t + 1) * TS], ctx.bank(t, 0, 1), 1.0 / D),
            deps=[mm_done] if t == 0 else (), inc=0)
        p.op("vector", lambda eng, t=t: eng.tensor_scalar_mul(
            rstd[:, t * TS:(t + 1) * TS], ctx.bank(NSl + t, 0, 1), 1.0 / D), inc=0)
    for t in range(NSl):
        p.op("vector", lambda eng, t=t: eng.tensor_tensor(
            ctx.bank(t, 0, 1), mu[:, t * TS:(t + 1) * TS],
            mu[:, t * TS:(t + 1) * TS], ALU.mult), inc=0)
        p.op("vector", lambda eng, t=t: eng.tensor_tensor(
            rstd[:, t * TS:(t + 1) * TS], rstd[:, t * TS:(t + 1) * TS],
            ctx.bank(t, 0, 1), ALU.subtract), inc=0)
    var_tok = p.op("vector", lambda eng: eng.tensor_scalar_add(rstd, rstd, float(eps)))
    for i in range(2 * NSl):
        bank_read(ctx, i, var_tok)
    sq_tok = p.op("scalar", lambda eng: eng.activation(rstd, rstd, AF.Sqrt),
                  deps=[var_tok])
    rstd_tok = p.op("vector", lambda eng: eng.reciprocal(rstd, rstd), deps=[sq_tok])

    for t in range(NSl):
        bmu, brs = 0, 1
        p.op("tensor", lambda eng, t=t: eng.matmul(
            ctx.bank(bmu), onesf[0:1, :], mu[:, t * TS:(t + 1) * TS], start=True, stop=True),
            deps=merge([rstd_tok, ctx.bank_free[bmu]]), inc=0)
        bc_tok = p.op("tensor", lambda eng, t=t: eng.matmul(
            ctx.bank(brs), onesf[0:1, :], rstd[:, t * TS:(t + 1) * TS], start=True, stop=True),
            deps=merge([ctx.bank_free[brs]]))
        last = bc_tok
        for c in range(ND):
            tt = sb.t1[c % 2]
            p.op("vector", lambda eng, c=c, t=t, tt=tt: eng.tensor_tensor(
                tt[:], x.sl(c, t * TS, (t + 1) * TS), ctx.bank(bmu), ALU.subtract),
                deps=merge([bc_tok, sb.t1_busy[c % 2]]), inc=0)
            dve_tok = p.op("vector", lambda eng, tt=tt: eng.tensor_tensor(
                tt[:], tt[:], ctx.bank(brs), ALU.mult))
            dst = x.sl(c, t * TS, (t + 1) * TS)
            if bf_only:
                a_tok = p.op("vector", lambda eng, c=c, tt=tt: eng.tensor_scalar(
                    tt[:], tt[:], g[:, c:c + 1], b[:, c:c + 1], ALU.mult, ALU.add),
                    deps=[dve_tok])
                w_tok = p.op("scalar", lambda eng, c=c, t=t, tt=tt: eng.activation(
                    out_bf.sl(c, t * TS, (t + 1) * TS), tt[:], AF.Copy), deps=[a_tok])
                sb.t1_busy[c % 2] = w_tok
                last = dve_tok
                continue
            if g is not None:
                if add_f32 is not None:
                    add_ap, add_tok = add_f32(c, t)
                    a_tok = p.op("vector", lambda eng, c=c, tt=tt: eng.tensor_scalar(
                        tt[:], tt[:], g[:, c:c + 1], b[:, c:c + 1], ALU.mult, ALU.add),
                        deps=[dve_tok])
                    w_tok = p.op("vector", lambda eng, tt=tt, dst=dst, a=add_ap:
                                 eng.tensor_tensor(dst, tt[:], a, ALU.add),
                                 deps=merge([a_tok, add_tok]))
                else:
                    w_tok = p.op("vector", lambda eng, c=c, tt=tt, dst=dst: eng.tensor_scalar(
                        dst, tt[:], g[:, c:c + 1], b[:, c:c + 1], ALU.mult, ALU.add),
                        deps=[dve_tok])
            else:
                assert add_f32 is None
                w_tok = p.op("scalar", lambda eng, tt=tt, dst=dst: eng.activation(
                    dst, tt[:], AF.Copy), deps=[dve_tok])
            sb.t1_busy[c % 2] = w_tok
            if out_bf is not None:
                p.op("scalar", lambda eng, c=c, t=t, dst=dst: eng.activation(
                    out_bf.sl(c, t * TS, (t + 1) * TS), dst, AF.Copy), deps=[w_tok])
            last = dve_tok
        bank_read(ctx, bmu, last)
        bank_read(ctx, brs, last)
    p.join()


def emit_cast(ctx, src_cb, dst_cb, scale=None):
    p = ctx.p
    for c in range(ND):
        if scale is None:
            p.op("scalar", lambda eng, c=c: eng.activation(
                dst_cb.chunk(c), src_cb.chunk(c), AF.Copy))
        else:
            p.op("scalar", lambda eng, c=c: eng.activation(
                dst_cb.chunk(c), src_cb.chunk(c), AF.Copy, scale=scale))



def emit_gated_ffn(ctx, sb, *, xb, wg, wu, wd, HID, wst, act=AF.Silu,
                   up_scale=None, wb_ap=None, first_bias=None):
    p = ctx.p
    T = ctx.T
    NSl = ctx.NS
    NHC = HID // 128
    h = sb.h
    upscr = [ctx.nc, None]
    for g0 in range(0, NHC, 8):
        ng = min(8, NHC - g0)
        gl = {}
        ul = {}
        for gi in range(ng):
            hc = g0 + gi
            half = gi // 4
            if half not in gl:
                nw = min(4, ng - half * 4) * 128
                gl[half] = wst.load(wg, 0, ND, (g0 + half * 4) * 128, nw)
                ul[half] = wst.load(wu, 0, ND, (g0 + half * 4) * 128, nw)
            slot, wsl, wtok = gl[half]
            uslot, usl, utok = ul[half]
            gcol = (gi % 4) * 128
            bg = 4 + (gi % 2)
            bu = 6 + (gi % 2)

            u_toks = []
            for t in range(NSl):
                gt = acc_group(ctx, bg, [(wsl(c, gcol, gcol + 128),
                                          xb.sl(c, t * TS, (t + 1) * TS)) for c in range(ND)],
                               deps=[wtok], col0=0)
                e1 = p.op("scalar", lambda eng, gi=gi, t=t, bg=bg: eng.activation(
                    h.sl(gi, t * TS, (t + 1) * TS), ctx.bank(bg), act), deps=[gt])
                bank_read(ctx, bg, e1)
                wst.done(slot, gt)
                ut = acc_group(ctx, bu, [(usl(c, gcol, gcol + 128),
                                          xb.sl(c, t * TS, (t + 1) * TS)) for c in range(ND)],
                               deps=[utok], col0=0)
                wst.done(uslot, ut)
                tt = sb.t1[t % 2]
                if wb_ap is not None:
                    e2 = p.op("vector", lambda eng, t=t, bu=bu, tt=tt: eng.tensor_tensor(
                        tt[:], ctx.bank(bu), wb_ap[:, t * TS:(t + 1) * TS], ALU.mult),
                        deps=merge([ut, sb.t1_busy[t % 2]]))
                else:
                    sc = {} if up_scale is None else {"scale": up_scale}
                    e2 = p.op("scalar", lambda eng, t=t, bu=bu, tt=tt, sc=sc: eng.activation(
                        tt[:], ctx.bank(bu), AF.Copy, **sc),
                        deps=merge([ut, sb.t1_busy[t % 2]]))
                bank_read(ctx, bu, e2)
                m_tok = p.op("vector", lambda eng, gi=gi, t=t, tt=tt: eng.tensor_tensor(
                    h.sl(gi, t * TS, (t + 1) * TS), h.sl(gi, t * TS, (t + 1) * TS),
                    tt[:], ALU.mult), deps=merge([e1, e2]))
                sb.t1_busy[t % 2] = m_tok
        if getattr(ctx, "dbg_h", None) is not None and g0 == 0 and ctx.dbg_h_armed:
            p.join()
            for gi in range(ng):
                p.dma("sync", ctx.dbg_h[gi * 128:(gi + 1) * 128, 0:T], h.chunk(gi))
            p.join()
            ctx.dbg_h_armed = False
        dl = []
        for half in range(0, ng, 4):
            nked = min(4, ng - half)
            dl.append(wst.load(wd, (g0 + half) * 128, nked, 0, 1024))
        for f2 in range(ND):
            for t in range(NSl):
                bank = t if (f2 % 2 == 0) else 4 + t
                mms = []
                dep0 = [dl[-1][2], m_tok]
                for gi in range(ng):
                    dslot, dsl, dtok = dl[gi // 4]
                    mms.append((dsl(gi % 4, f2 * 128, (f2 + 1) * 128),
                                h.sl(gi, t * TS, (t + 1) * TS)))
                gt = acc_group(ctx, bank, mms, deps=dep0)
                for dslot, dsl, dtok in dl:
                    wst.done(dslot, gt)
                if first_bias is not None and g0 == 0:
                    ev = p.op("vector", lambda eng, f2=f2, t=t, bank=bank:
                              eng.scalar_tensor_tensor(
                                  sb.x.sl(f2, t * TS, (t + 1) * TS), ctx.bank(bank),
                                  first_bias[:, f2:f2 + 1],
                                  sb.x.sl(f2, t * TS, (t + 1) * TS), ALU.add, ALU.add),
                              deps=[gt])
                else:
                    ev = p.op("vector", lambda eng, f2=f2, t=t, bank=bank:
                              eng.tensor_tensor(
                                  sb.x.sl(f2, t * TS, (t + 1) * TS),
                                  sb.x.sl(f2, t * TS, (t + 1) * TS),
                                  ctx.bank(bank), ALU.add),
                              deps=[gt])
                bank_read(ctx, bank, ev)
    p.join()



def proj_fm(ctx, *, wdram, wr0, nk, xb, fch_list, wst, evac, extra=None,
            t_list=None, ldeps=()):
    p = ctx.p
    NSl = ctx.NS if t_list is None else None
    ts = range(ctx.NS) if t_list is None else t_list
    i = 0
    for fb0 in range(0, len(fch_list), 4):
        fgrp = fch_list[fb0:fb0 + 4]
        slot, wsl, wtok = wst.load(wdram, wr0, nk, fgrp[0] * 128, len(fgrp) * 128)
        for mi, f in enumerate(fgrp):
            for t in ts:
                bank = ctx.ring()
                mms = [(wsl(c, mi * 128, (mi + 1) * 128),
                        xb.sl(c, t * TS, (t + 1) * TS)) for c in range(nk)]
                if extra is not None:
                    mms = mms + extra(f, t)
                gt = acc_group(ctx, bank, mms, deps=merge([wtok], list(ldeps)))
                wst.done(slot, gt)
                evac(f, t, bank, gt)
                i += 1



def emit_l0ffn(ctx, sb, dram, wst):
    p = ctx.p
    emit_cast(ctx, sb.x, sb.xb, scale=Q127)
    p.join()
    emit_gated_ffn(ctx, sb, xb=sb.xb, wg=dram["wgT"], wu=dram["wuT"],
                   wd=dram["wdT"], HID=3072, wst=wst, act=AF.Silu,
                   up_scale=Q127)





def emit_router(ctx, sb, *, xb, wr_b, rb_row_bf, ones_row_bf, w_rows):
    p = ctx.p
    T = ctx.T
    NTC = T // 128
    nc = ctx.nc
    rtr = sb.rtr
    rbusy = sb.rtr_busy
    identf = sb.const["ident"]
    for tc in range(NTC):
        sl = (tc % 2) * 32
        bank = ctx.ring()
        mms = [(xb.sl(c, tc * 128, (tc + 1) * 128), wr_b[:, c * 8:c * 8 + 7])
               for c in range(ND)]
        mms.append((ones_row_bf[0:1, :], rb_row_bf[0:1, 0:7]))
        gt = acc_group(ctx, bank, mms, deps=merge([sb.wr_tok, rbusy[tc % 2]]), w=7)
        pb = ctx.ps[0:128, bank * TS: bank * TS + 7]
        pm = rtr[:, sl + 24:sl + 25]
        pmn = rtr[:, sl + 25:sl + 26]
        ssum = rtr[:, sl + 26:sl + 27]
        rec = rtr[:, sl + 27:sl + 28]
        m1 = rtr[:, sl + 28:sl + 29]
        m2 = rtr[:, sl + 29:sl + 30]
        s12 = rtr[:, sl + 30:sl + 31]
        rec2 = rtr[:, sl + 31:sl + 32]
        pe_ = rtr[:, sl + 0:sl + 7]
        msk = rtr[:, sl + 8:sl + 15]
        tmp7 = rtr[:, sl + 16:sl + 23]
        p.sop("vector", lambda eng, pb=pb, pm=pm: eng.reduce_max(pm, pb, AX.X),
              deps=[gt])
        v1 = p.sop("vector", lambda eng, pm=pm, pmn=pmn: eng.tensor_scalar_mul(pmn, pm, -1.0))
        et = p.op("scalar", lambda eng, pb=pb, pe_=pe_, pmn=pmn: eng.activation(
            pe_, pb, AF.Exp, bias=pmn), deps=[v1])
        bank_read(ctx, bank, et)
        p.sop("vector", lambda eng, pe_=pe_, ssum=ssum: eng.reduce_sum(ssum, pe_, AX.X),
              deps=[et])
        p.sop("vector", lambda eng, ssum=ssum, rec=rec: eng.reciprocal(rec, ssum))
        p.sop("vector", lambda eng, pe_=pe_, rec=rec: eng.tensor_scalar_mul(pe_, pe_, rec))
        p.sop("vector", lambda eng, pe_=pe_, m1=m1: eng.reduce_max(m1, pe_, AX.X))
        p.sop("vector", lambda eng, pe_=pe_, m1=m1, tmp7=tmp7: eng.tensor_single_scalar(
            tmp7, pe_, m1, ALU.is_ge))
        p.sop("vector", lambda eng, tmp7=tmp7: eng.tensor_scalar_mul(tmp7, tmp7, 2.0))
        p.sop("vector", lambda eng, pe_=pe_, tmp7=tmp7, msk=msk: eng.tensor_tensor(
            msk, pe_, tmp7, ALU.subtract))
        p.sop("vector", lambda eng, msk=msk, m2=m2: eng.reduce_max(m2, msk, AX.X))
        p.sop("vector", lambda eng, m1=m1, m2=m2, s12=s12: eng.tensor_tensor(
            s12, m1, m2, ALU.add))
        p.sop("vector", lambda eng, s12=s12, rec2=rec2: eng.reciprocal(rec2, s12))
        p.sop("vector", lambda eng, pe_=pe_, m2=m2, tmp7=tmp7: eng.tensor_single_scalar(
            tmp7, pe_, m2, ALU.is_ge))
        p.sop("vector", lambda eng, pe_=pe_, tmp7=tmp7: eng.tensor_tensor(
            pe_, pe_, tmp7, ALU.mult))
        wt = p.sop("vector", lambda eng, pe_=pe_, rec2=rec2: eng.tensor_scalar_mul(
            pe_, pe_, rec2))
        bank2 = ctx.ring()
        pt = p.op("tensor", lambda eng, pe_=pe_, bank2=bank2: eng.transpose(
            ctx.ps[0:7, bank2 * TS: bank2 * TS + 128], pe_, identf[:, :]),
            deps=merge([wt, ctx.bank_free[bank2]]))
        ctx.bank_ready[bank2] = pt
        ctx.bank_free[bank2] = None
        cb = p.op("vector", lambda eng, tc=tc, bank2=bank2: eng.tensor_copy(
            w_rows[0:7, tc * 128:(tc + 1) * 128],
            ctx.ps[0:7, bank2 * TS: bank2 * TS + 128]), deps=[pt])
        bank_read(ctx, bank2, cb)
        rbusy[tc % 2] = cb
    p.join()


def emit_moe_experts(ctx, sb, dram, wst, n_routed=7):
    p = ctx.p
    emit_gated_ffn(ctx, sb, xb=sb.xb, wg=dram["sgT"], wu=dram["suT"],
                   wd=dram["sdT"], HID=2816, wst=wst, act=AF.Silu)
    sel7 = sb.const["SEL7"]
    for e in range(n_routed):
        for t in range(ctx.NS):
            bank = ctx.ring()
            bt = acc_group(ctx, bank, [(sel7[0:7, e * 128:(e + 1) * 128],
                                        sb.w_rows[0:7, t * TS:(t + 1) * TS])],
                           deps=[sb.w_rows_tok] if sb.w_rows_tok else ())
            ct = p.op("vector", lambda eng, t=t, bank=bank: eng.tensor_copy(
                sb.wbb[:, t * TS:(t + 1) * TS], ctx.bank(bank)), deps=[bt])
            bank_read(ctx, bank, ct)
        p.join()
        emit_gated_ffn(ctx, sb, xb=sb.xb,
                       wg=dram["rgT"][e * D:(e + 1) * D, :],
                       wu=dram["ruT"][e * D:(e + 1) * D, :],
                       wd=dram["rdT"][e * 2816:(e + 1) * 2816, :],
                       HID=2816, wst=wst, act=AF.Silu, wb_ap=sb.wbb[:, :])
    p.join()




def emit_phi_evac(ctx, sb, dst_ap, bank, gtok, bias_ap, neg_bias_ap):
    p = ctx.p
    i = sb.phi_i % 2
    sb.phi_i += 1
    sA = sb.phi_s[3 * i]
    sB = sb.phi_s[3 * i + 1]
    sC = sb.phi_s[3 * i + 2]
    kw1 = {} if bias_ap is None else {"bias": bias_ap}
    kw2 = {} if neg_bias_ap is None else {"bias": neg_bias_ap}
    r1 = p.op("scalar", lambda eng, kw1=kw1: eng.activation(sA, ctx.bank(bank), AF.Relu, **kw1),
              deps=merge([gtok, sb.phi_busy[i]]))
    r2 = p.op("scalar", lambda eng, kw2=kw2: eng.activation(sB, ctx.bank(bank), AF.Relu,
                                                            scale=-1.0, **kw2))
    e2 = p.op("scalar", lambda eng: eng.activation(sC, sB, AF.Exp, scale=-1.0),
              deps=[r2], strict=True)
    w = p.op("vector", lambda eng: eng.tensor_tensor(sB, sA, sC, ALU.add),
             deps=[e2])
    f1 = p.op("scalar", lambda eng, dst=dst_ap: eng.activation(dst, sB, AF.Copy),
              deps=[w])
    sb.phi_busy[i] = f1
    return f1


def emit_l1_phase_a(ctx, sb, dram, wst, pq_dram, tile):
    p = ctx.p
    nc = ctx.nc
    T = ctx.T
    t0 = tile * T
    NSl = ctx.NS
    emit_cast(ctx, sb.x, sb.xb)
    p.join()
    bq = sb.vec["bq_col"]
    nbq = sb.vec_neg["bq_col"]
    for fb in range(0, ND, 4):
        slot, wsl, wtok = wst.load(dram["wqT"], 0, ND, fb * 128, 512)
        for mf in range(4):
            f = fb + mf
            toks = []
            for t in range(NSl):
                bank = ctx.ring()
                gt = acc_group(ctx, bank, [(wsl(c, mf * 128, (mf + 1) * 128),
                                            sb.xb.sl(c, t * TS, (t + 1) * TS))
                                           for c in range(ND)], deps=[wtok])
                wst.done(slot, gt)
                if getattr(ctx, "dbg_relu_only", False):
                    w = p.op("scalar", lambda eng, f=f, t=t, bank=bank: eng.activation(
                        sb.h.sl(f, t * TS, (t + 1) * TS), ctx.bank(bank), AF.Relu,
                        bias=bq[:, f:f + 1]), deps=[gt])
                else:
                    w = emit_phi_evac(ctx, sb, sb.h.sl(f, t * TS, (t + 1) * TS), bank, gt,
                                      bq[:, f:f + 1], nbq[:, f:f + 1])
                bank_read(ctx, bank, w)
                toks.append(w)
            p.dma("sync", pq_dram[f * 128:(f + 1) * 128, t0:t0 + T], sb.h.chunk(f),
                  deps=toks)
    NTC = T // 128
    bkb = sb.vecrow["bk_row"]
    bvb = sb.vecrow["bv_row"]
    onesr = sb.ones_row_bf
    onesc = sb.ones_bf
    for half in range(2):
        kslot, ksl, ktok = wst.load(dram["wkT"], 0, ND, half * 512, 512)
        vslot, vsl, vtok = wst.load(dram["wvT"], 0, ND, half * 512, 512)
        for tc in range(NTC):
            ktm = sb.ktm[tc % 2]
            vtm = sb.vtm[tc % 2]
            kv_toks = []
            bank = ctx.ring()
            mms = [(sb.xb.sl(c, tc * 128, (tc + 1) * 128),
                    ksl(c, 0, 512)) for c in range(ND)]
            mms.append((onesr[0:1, :], bkb[0:1, half * 512:(half + 1) * 512]))
            gt = acc_group(ctx, bank, mms, deps=[ktok, sb.ktm_busy[tc % 2]])
            wst.done(kslot, gt)
            w = emit_phi_evac_tm(ctx, sb, ktm[:, 0:512], bank, gt)
            bank_read(ctx, bank, w)
            kv_toks.append(w)
            bank = ctx.ring()
            mms = [(sb.xb.sl(c, tc * 128, (tc + 1) * 128),
                    vsl(c, 0, 512)) for c in range(ND)]
            mms.append((onesr[0:1, :], bvb[0:1, half * 512:(half + 1) * 512]))
            gt = acc_group(ctx, bank, mms, deps=[vtok, sb.vtm_busy[tc % 2]])
            wst.done(vslot, gt)
            w = p.op("scalar", lambda eng, vtm=vtm, bank=bank: eng.activation(
                vtm[:, 0:512], ctx.bank(bank), AF.Copy), deps=[gt])
            bank_read(ctx, bank, w)
            kv_toks.append(w)
            bank = ctx.ring()
            fdep = merge(kv_toks, [ctx.bank_free[bank]])
            last = None
            for hh in range(8):
                h = half * 8 + hh
                p0 = (h % 2) * 64
                c0 = (h // 2) * 64
                last = p.op("tensor",
                            lambda eng, hh=hh, p0=p0, c0=c0, bank=bank, ktm=ktm, vtm=vtm:
                            eng.matmul(ctx.ps[p0:p0 + 64, bank * TS + c0: bank * TS + c0 + 64],
                                       ktm[:, hh * 64:(hh + 1) * 64],
                                       vtm[:, hh * 64:(hh + 1) * 64],
                                       start=True, stop=True),
                            deps=fdep if hh == 0 else (), inc=(1 if hh == 7 else 0))
            ctx.bank_ready[bank] = last
            ctx.bank_free[bank] = None
            cola = (half * 8 // 2) * 64
            at = p.op("vector", lambda eng, bank=bank, cola=cola: eng.tensor_tensor(
                sb.kvt[:, cola:cola + 256], sb.kvt[:, cola:cola + 256],
                ctx.ps[0:128, bank * TS + cola: bank * TS + cola + 256], ALU.add),
                deps=[last])
            bank_read(ctx, bank, at)
            bank = ctx.ring()
            fdep = merge(kv_toks, [ctx.bank_free[bank]])
            last = None
            for db in range(4):
                last = p.op("tensor", lambda eng, db=db, bank=bank, ktm=ktm:
                            eng.matmul(ctx.ps[0:128, bank * TS + db: bank * TS + db + 1],
                                       ktm[:, db * 128:(db + 1) * 128], onesc[:, 0:1],
                                       start=True, stop=True),
                            deps=fdep if db == 0 else (), inc=(1 if db == 3 else 0))
            ctx.bank_ready[bank] = last
            ctx.bank_free[bank] = None
            at2 = p.sop("vector", lambda eng, bank=bank, half=half: eng.tensor_tensor(
                sb.ksum_col[:, half * 4:(half + 1) * 4],
                sb.ksum_col[:, half * 4:(half + 1) * 4],
                ctx.ps[0:128, bank * TS: bank * TS + 4], ALU.add), deps=[at, last])
            bank_read(ctx, bank, at2)
            sb.ktm_busy[tc % 2] = last
            sb.vtm_busy[tc % 2] = last
    p.join()


def emit_phi_evac_tm(ctx, sb, dst_ap, bank, gtok):
    return emit_phi_evac(ctx, sb, dst_ap, bank, gtok, None, None)


def emit_l1_phase_b(ctx, sb, dram, wst, pq_dram, xin_dram, tile, out_bf=None):
    p = ctx.p
    nc = ctx.nc
    T = ctx.T
    t0 = tile * T
    NSl = ctx.NS
    qdep = [("Ssy", p.cnt["Ssy"])] if p.cnt["Ssy"] else []
    for c in range(ND):
        p.dma("sync", sb.x.chunk(c), xin_dram[c * 128:(c + 1) * 128, t0:t0 + T],
              deps=qdep if c == 0 else (), strict=True)
        p.dma("sync", sb.h.chunk(c), pq_dram[c * 128:(c + 1) * 128, t0:t0 + T])
    p.join()
    for t in range(NSl):
        bank = ctx.ring()
        gt = acc_group(ctx, bank, [(sb.ks_cb.chunk(c), sb.h.sl(c, t * TS, (t + 1) * TS))
                                   for c in range(ND)], deps=[sb.ks_tok], p0=0, p1=16)
        zt = p.op("vector", lambda eng, t=t, bank=bank: eng.reciprocal(
            sb.zrow[0:16, t * TS:(t + 1) * TS], ctx.bank(bank, 0, 16)), deps=[gt])
        bank_read(ctx, bank, zt)
    p.join()
    sel16 = sb.const["SEL16"]
    pq_toks = {}
    for c in range(ND):
        for t in range(NSl):
            bank = ctx.ring()
            bt = acc_group(ctx, bank, [(sel16[0:16, c * 128:(c + 1) * 128],
                                        sb.zrow[0:16, t * TS:(t + 1) * TS])])
            w = p.op("vector", lambda eng, c=c, t=t, bank=bank: eng.tensor_tensor(
                sb.xb.sl(c, t * TS, (t + 1) * TS), sb.h.sl(c, t * TS, (t + 1) * TS),
                ctx.bank(bank), ALU.mult), deps=[bt])
            bank_read(ctx, bank, w)
            pq_toks[(c, t)] = w
    kb = p.op("scalar", lambda eng: eng.activation(sb.kvt_bf[:, :], sb.kvt[:, :], AF.Copy))
    for fc in range(ND):
        for t in range(NSl):
            bank = ctx.ring()
            dep0 = merge([kb, pq_toks[(fc, t)], ctx.bank_free[bank]])
            last = None
            for half in range(2):
                h = 2 * fc + half
                p0 = half * 64
                c0 = (h // 2) * 64
                last = p.op("tensor", lambda eng, p0=p0, c0=c0, bank=bank, fc=fc, t=t:
                            eng.matmul(ctx.ps[p0:p0 + 64, bank * TS: bank * TS + TS],
                                       sb.kvt_bf[p0:p0 + 64, c0:c0 + 64],
                                       sb.xb.h[p0:p0 + 64,
                                               fc * T + t * TS: fc * T + (t + 1) * TS],
                                       start=True, stop=True),
                            deps=dep0 if half == 0 else (), inc=(1 if half == 1 else 0))
            ctx.bank_ready[bank] = last
            ctx.bank_free[bank] = None
            ev = p.op("scalar", lambda eng, fc=fc, t=t, bank=bank: eng.activation(
                sb.h.sl(fc, t * TS, (t + 1) * TS), ctx.bank(bank), AF.Copy), deps=[last])
            bank_read(ctx, bank, ev)
    p.join()
    bd = sb.vec["bd_col"]
    for fb in range(0, ND, 4):
        slot, wsl, wtok = wst.load(dram["wdT"], 0, ND, fb * 128, 512)
        for mf in range(4):
            f = fb + mf
            for t in range(NSl):
                bank = ctx.ring()
                gt = acc_group(ctx, bank, [(wsl(c, mf * 128, (mf + 1) * 128),
                                            sb.h.sl(c, t * TS, (t + 1) * TS))
                                           for c in range(ND)], deps=[wtok])
                wst.done(slot, gt)
                ev = p.op("vector", lambda eng, f=f, t=t, bank=bank:
                          eng.scalar_tensor_tensor(
                              sb.x.sl(f, t * TS, (t + 1) * TS), ctx.bank(bank),
                              bd[:, f:f + 1], sb.x.sl(f, t * TS, (t + 1) * TS),
                              ALU.add, ALU.add), deps=[gt])
                bank_read(ctx, bank, ev)
    p.join()
    xs2 = sb.xs2

    def add_f32(c, t, xin=xin_dram, t0=t0):
        dep = [("Sdv", ctx.p.cnt["Sdv"])] if ctx.p.cnt["Sdv"] else []
        tok = ctx.p.dma("sync", xs2[:, c * TS:(c + 1) * TS],
                        xin[c * 128:(c + 1) * 128, t0 + t * TS: t0 + (t + 1) * TS],
                        deps=dep, strict=True)
        return xs2[:, c * TS:(c + 1) * TS], tok
    emit_ln(ctx, sb, g=sb.vec["l1g"], b=sb.vec["l1b"], eps=EPS_L1LN, add_f32=add_f32)
    emit_ln(ctx, sb, eps=EPS_TOP, out_bf=out_bf)




def setup_l1(ctx, sb, dram):
    nc, p = ctx.nc, ctx.p
    T = ctx.T
    sb.ktm = [nc.alloc_sbuf_tensor(f"ktm{i}", [128, 512], BF16) for i in range(2)]
    sb.vtm = [nc.alloc_sbuf_tensor(f"vtm{i}", [128, 512], BF16) for i in range(2)]
    sb.ktm_busy = [None, None]
    sb.vtm_busy = [None, None]
    phi_h = nc.alloc_sbuf_tensor("phi_h", [128, 6 * TS], F32)
    sb.phi_s = [phi_h[:, i * TS:(i + 1) * TS] for i in range(6)]
    sb.phi_busy = [None, None]
    sb.phi_i = 0
    sb.kvt = nc.alloc_sbuf_tensor("kvt", [128, 512], F32)
    sb.kvt_bf = sb.ktm[0][:, :]
    sb.ksum_col = nc.alloc_sbuf_tensor("ksum_col", [128, 8], F32)
    sb.zrow = phi_h[0:16, 0:T]
    sb.ks_cb = CBuf(nc, "ks_cb", ND, 16, BF16)
    sb.ks_tok = None
    sb.xs2 = sb.h.h[:, 4 * T:8 * T].bitcast(F32)
    p.op("gpsimd", lambda eng: eng.memset(sb.kvt[:, :], 0.0), inc=0)
    mt = p.op("gpsimd", lambda eng: eng.memset(sb.ksum_col[:, :], 0.0))
    sb.kv_zero_tok = mt
    p.join()
    sb.vec_neg = {}
    for nm in ("bq_col",):
        h = nc.alloc_sbuf_tensor("vn_" + nm, [128, 8], F32)
        p.op("vector", lambda eng, nm=nm, h=h: eng.tensor_scalar_mul(
            h[:, :], sb.vec[nm][:, :], -1.0))
        sb.vec_neg[nm] = h
    p.join()


def emit_ks_build(ctx, sb):
    p = ctx.p
    nc = ctx.nc
    mz = p.op("gpsimd", lambda eng: eng.memset(sb.ks_cb.h[:, :], 0.0))
    ke = p.op("vector", lambda eng: eng.tensor_scalar_add(
        sb.ksum_col[:, :], sb.ksum_col[:, :], float(EPS_ATTN)))
    last = None
    for c in range(ND):
        p.op("scalar", lambda eng, c=c: eng.copy(
            sb.ks_cb.h[0:64, c * 16 + 2 * c: c * 16 + 2 * c + 1],
            sb.ksum_col[0:64, c:c + 1]), deps=[ke, mz], inc=0)
        last = p.op("scalar", lambda eng, c=c: eng.copy(
            sb.ks_cb.h[64:128, c * 16 + 2 * c + 1: c * 16 + 2 * c + 2],
            sb.ksum_col[64:128, c:c + 1]))
    sb.ks_tok = last
    p.join()


def build_l1_program(T, debug=None):
    nc = bass.Bass("TRN2", target_bir_lowering=False, debug=False)
    p = Prog(nc)
    ctx = Ctx(nc, p, T)
    consts = host_consts()
    sb = SB(ctx, T, consts, need=("SEL16",))
    if debug == "relu":
        ctx.dbg_relu_only = True
        debug = "a"
    xin = nc.dram_tensor("s_in", [D, SEQ], F32, kind="ExternalInput").ap()
    xout = nc.dram_tensor("s_out", [D, SEQ], F32, kind="ExternalOutput").ap()
    pq = nc.dram_tensor("pq_scr", [D, SEQ], BF16,
                        kind=("ExternalOutput" if debug == "a" else "Internal")).ap()
    if debug == "a":
        dbg_kvt = nc.dram_tensor("dbg_kvt", [128, 512], F32, kind="ExternalOutput").ap()
        dbg_ks = nc.dram_tensor("dbg_ks", [128, 8], F32, kind="ExternalOutput").ap()
    dram = {}
    for nm, dn in (("wqT", "l1_wq"), ("wkT", "l1_wk"), ("wvT", "l1_wv"), ("wdT", "l1_wd")):
        dram[nm] = nc.dram_tensor(dn, [D, D], F32, kind="ExternalInput").ap()
    load_vec_inputs(ctx, sb, {"bq_col": (128, 8), "bd_col": (128, 8),
                              "l1g": (128, 8), "l1b": (128, 8),
                              "bk_row": (1, 1024), "bv_row": (1, 1024)},
                    bf_rows=("bk_row", "bv_row"))
    setup_l1(ctx, sb, dram)
    wst = WStream(ctx, "w", nbuf=2)
    for tile in range(SEQ // T):
        t0 = tile * T
        for c in range(ND):
            p.dma("sync", sb.x.chunk(c), xin[c * 128:(c + 1) * 128, t0:t0 + T],
                  strict=True)
        p.join()
        emit_l1_phase_a(ctx, sb, dram, wst, pq, tile)
    if debug == "a":
        p.dma("sync", dbg_kvt[:, :], sb.kvt[:, :])
        p.dma("sync", dbg_ks[:, :], sb.ksum_col[:, :])
        p.dma("sync", xout[0:128, 0:T], sb.x.chunk(0))
        p._wait("gpsimd", "Ssy", p.cnt["Ssy"])
        p.build()
        return nc
    emit_ks_build(ctx, sb)
    prev_store = None
    for tile in range(SEQ // T):
        t0 = tile * T
        emit_l1_phase_b(ctx, sb, dram, wst, pq, xin, tile)
        for c in range(ND):
            prev_store = p.dma("sync", xout[c * 128:(c + 1) * 128, t0:t0 + T],
                               sb.x.chunk(c))
        p.join()
    p._wait("gpsimd", "Ssy", p.cnt["Ssy"])
    p.build()
    return nc





def setup_l0(ctx, sb, consts):
    nc, p = ctx.nc, ctx.p
    sb.dt_b = nc.alloc_sbuf_tensor("dt_b", [128, NH * CH], BF16)
    sb.qd_b = nc.alloc_sbuf_tensor("qd_b", [128, ND * CH], BF16)
    sb.kd_b = nc.alloc_sbuf_tensor("kd_b", [128, 2 * 512], BF16)
    sb.gc_f = nc.alloc_sbuf_tensor("gc_f", [128, 512], F32)
    for name, dst, w in (("DT", sb.dt_b, CH), ("QD", sb.qd_b, CH), ("KD", sb.kd_b, 512)):
        arr = consts[name]
        dr = nc.inline_tensor(np.ascontiguousarray(arr), "ic_" + name)
        n = arr.shape[0] // 128
        for i in range(n):
            p.dma("gpsimd", dst[:, i * w:(i + 1) * w], dr.ap()[i * 128:(i + 1) * 128, :])
    drg = nc.inline_tensor(np.ascontiguousarray(consts["GC"]), "ic_GC")
    p.dma("sync", sb.gc_f[:, :], drg.ap())
    sb.S = [nc.alloc_sbuf_tensor(f"S{i}", [128, 512], F32) for i in range(2)]
    sb.S_bf = [nc.alloc_sbuf_tensor(f"Sbf{i}", [128, 512], BF16) for i in range(2)]
    sb.S_tok = [None, None]
    sb.at_bf = [nc.alloc_sbuf_tensor(f"atbf{i}", [128, CH], BF16) for i in range(2)]
    sb.at_busy = [None, None]
    sb.rr = nc.alloc_sbuf_tensor("rr", [128, 4 * 4096], BF16)
    sb.pbrow = nc.alloc_sbuf_tensor("pbrow", [1, 2048], BF16)
    sb.krt = [nc.alloc_sbuf_tensor(f"krt{i}", [128, 1024], BF16) for i in range(4)]
    sb.vrt = [nc.alloc_sbuf_tensor(f"vrt{i}", [128, 1024], BF16) for i in range(4)]
    sb.krt_busy = [None] * 4
    sb.vrt_busy = [None] * 4
    for i in range(2):
        p.op("gpsimd", lambda eng, i=i: eng.memset(sb.S[i][:, :], 0.0), inc=0)
        p.op("gpsimd", lambda eng, i=i: eng.memset(sb.S_bf[i][:, :], 0.0))
    p.join()


def emit_retention_mixer(ctx, sb, dram, wst, layer):
    p = ctx.p
    T = ctx.T
    pw = dram[f"pw{layer}"]
    rw = dram[f"rw{layer}"]
    pbc = sb.vec[f"pb{layer}_col"]
    pbr = sb.pbrow
    p.dma("gpsimd", pbr[0:1, :], dram[f"pbrow{layer}"][0:1, :], deps=p.now())
    rbc = sb.vec[f"rb{layer}"]
    onesr = sb.ones_row_bf
    S = sb.S[layer]
    S_bf = sb.S_bf[layer]
    h = sb.h
    rr = sb.rr

    def reg(base, fc, a, b):
        r = base // 2
        return rr[:, r * 4096 + fc * 512 + a: r * 4096 + fc * 512 + b]

    NSC = T // 512
    for sc in range(NSC):
        tok0 = sc * 512
        for colbase, kind in ((0, "q"), (1024, "k"), (3072, "xo")):
            for fb in (0, 4):
                slot, wsl, wtok = wst.load(pw, 0, ND, colbase + fb * 128, 512)
                for mf in range(4):
                    f = fb + mf
                    fg = (colbase // 128) + f
                    bank = ctx.ring()
                    gt = acc_group(ctx, bank,
                                   [(wsl(c, mf * 128, (mf + 1) * 128),
                                     sb.xb.sl(c, tok0, tok0 + 512)) for c in range(ND)],
                                   deps=[wtok])
                    wst.done(slot, gt)
                    pb_ap = pbc[:, fg:fg + 1]
                    pk = ctx.bank(bank)
                    if kind == "q":
                        qd_ap = sb.qd_b[:, f * CH:(f + 1) * CH]
                        for tci4 in range(4):
                            dst = reg(0, f, tci4 * CH, (tci4 + 1) * CH)
                            pk4 = ctx.ps[0:128, bank * TS + tci4 * CH: bank * TS + (tci4 + 1) * CH]
                            ev = p.op("vector",
                                      lambda eng, dst=dst, pk4=pk4, pb_ap=pb_ap, qd_ap=qd_ap:
                                      eng.scalar_tensor_tensor(dst, pk4, pb_ap, qd_ap,
                                                               ALU.add, ALU.mult),
                                      deps=[gt] if tci4 == 0 else ())
                    elif kind == "k":
                        dst = reg(2, f, 0, 512)
                        ev = p.op("vector", lambda eng, dst=dst, pk=pk, pb_ap=pb_ap:
                                  eng.tensor_scalar_add(dst, pk, pb_ap), deps=[gt])
                    else:
                        dst = reg(4, f, 0, 512)
                        ev = p.op("scalar", lambda eng, dst=dst, pk=pk, pb_ap=pb_ap:
                                  eng.activation(dst, pk, AF.Silu, bias=pb_ap), deps=[gt])
                    bank_read(ctx, bank, ev)
        for half in range(2):
            kslot, ksl, ktok = wst.load(pw, 0, ND, 1024 + half * 512, 512)
            vslot, vsl, vtok = wst.load(pw, 0, ND, 2048 + half * 512, 512)
            for tci in range(4):
                tc = sc * 4 + tci
                krt = sb.krt[tci]
                vrt = sb.vrt[tci]
                bank = ctx.ring()
                mms = [(sb.xb.sl(c, tc * 128, (tc + 1) * 128), ksl(c, 0, 512))
                       for c in range(ND)]
                mms.append((onesr[0:1, :], pbr[0:1, half * 512:(half + 1) * 512]))
                gt = acc_group(ctx, bank, mms, deps=[ktok, sb.krt_busy[tci]])
                wst.done(kslot, gt)
                dst = krt[:, half * 512:(half + 1) * 512]
                kd_ap = sb.kd_b[:, half * 512:(half + 1) * 512]
                pk = ctx.bank(bank)
                ev = p.op("vector", lambda eng, dst=dst, pk=pk, kd_ap=kd_ap:
                          eng.tensor_tensor(dst, pk, kd_ap, ALU.mult), deps=[gt])
                bank_read(ctx, bank, ev)
                bank = ctx.ring()
                mms = [(sb.xb.sl(c, tc * 128, (tc + 1) * 128), vsl(c, 0, 512))
                       for c in range(ND)]
                mms.append((onesr[0:1, :], pbr[0:1, 1024 + half * 512:1024 + (half + 1) * 512]))
                gt = acc_group(ctx, bank, mms, deps=[vtok, sb.vrt_busy[tci]])
                wst.done(vslot, gt)
                dst = vrt[:, half * 512:(half + 1) * 512]
                pk = ctx.bank(bank)
                ev = p.op("scalar", lambda eng, dst=dst, pk=pk:
                          eng.activation(dst, pk, AF.Copy), deps=[gt])
                bank_read(ctx, bank, ev)
        p.join()
        ctx.ring_banks = [0, 1, 2, 3, 4]
        ret2_evs = []
        for tci in range(4):
            tc = sc * 4 + tci
            cofs = tci * 128
            krt = sb.krt[tci]
            vrt = sb.vrt[tci]
            ret_banks = [6, 7]
            ret_last = {}
            for hh in range(NH):
                hp = (hh % 2) * 64
                fc = hh // 2
                k_ap = rr[hp:hp + 64, 1 * 4096 + fc * 512 + cofs: 1 * 4096 + fc * 512 + cofs + 128]
                q_ap = rr[hp:hp + 64, 0 * 4096 + fc * 512 + cofs: 0 * 4096 + fc * 512 + cofs + 128]
                bank = ctx.ring()
                at_out = ctx.bank(bank, 0, 128, CH)
                d0 = merge([ctx.bank_free[bank], sb.at_busy[hh % 2]])
                at_mm = p.op("tensor", lambda eng, at_out=at_out, k_ap=k_ap, q_ap=q_ap:
                             eng.matmul(at_out, k_ap, q_ap, start=True, stop=True),
                             deps=d0)
                ctx.bank_ready[bank] = at_mm
                ctx.bank_free[bank] = None
                atb = sb.at_bf[hh % 2]
                dt_ap = sb.dt_b[:, hh * CH:(hh + 1) * CH]
                at_ev = p.op("vector", lambda eng, atb=atb, at_out=at_out, dt_ap=dt_ap:
                             eng.tensor_tensor(atb[:, :], at_out, dt_ap, ALU.mult),
                             deps=[at_mm])
                bank_read(ctx, bank, at_ev)
                sb.at_busy[hh % 2] = at_ev
                rbank = ret_banks[hh // 8]
                cblk = ((hh // 2) % 4) * 128
                r_out = ctx.ps[hp:hp + 64, rbank * TS + cblk: rbank * TS + cblk + 128]
                v_ap = vrt[:, hh * 64:(hh + 1) * 64]
                s_ap = S_bf[hp:hp + 64, fc * 64:fc * 64 + 64]
                rdep = merge([at_ev, sb.S_tok[layer]] +
                             ([ctx.bank_free[rbank]] if hh % 8 == 0 else []))
                p.op("tensor", lambda eng, r_out=r_out, v_ap=v_ap, atb=atb:
                     eng.matmul(r_out, v_ap, atb[:, :], start=True, stop=False),
                     deps=rdep, inc=0)
                rt = p.op("tensor", lambda eng, r_out=r_out, s_ap=s_ap, q_ap=q_ap:
                          eng.matmul(r_out, s_ap, q_ap, start=False, stop=True))
                ret_last[hh] = rt
                if hh % 8 == 7:
                    ctx.bank_ready[rbank] = rt
                    ctx.bank_free[rbank] = None
            for fc in range(ND):
                rbank = ret_banks[fc // 4]
                cblk = (fc % 4) * 128
                dst = reg(6, fc, cofs, cofs + 128)
                ps_ap = ctx.ps[0:128, rbank * TS + cblk: rbank * TS + cblk + 128]
                xo_ap = rr[:, 2 * 4096 + fc * 512 + cofs: 2 * 4096 + fc * 512 + cofs + 128]
                ev = p.op("vector", lambda eng, dst=dst, ps_ap=ps_ap, xo_ap=xo_ap:
                          eng.tensor_tensor(dst, ps_ap, xo_ap, ALU.mult),
                          deps=[ret_last[fc * 2 + 1]])
                bank_read(ctx, rbank, ev)
                ret2_evs.append(ev)
            ibank = 5
            ilast = None
            for hh in range(NH):
                hp = (hh % 2) * 64
                c0 = (hh // 2) * 64
                i_out = ctx.ps[hp:hp + 64, ibank * TS + c0: ibank * TS + c0 + 64]
                kk = krt[:, hh * 64:(hh + 1) * 64]
                vv = vrt[:, hh * 64:(hh + 1) * 64]
                d0 = merge([ctx.bank_free[ibank]]) if hh == 0 else []
                ilast = p.op("tensor", lambda eng, i_out=i_out, kk=kk, vv=vv:
                             eng.matmul(i_out, kk, vv, start=True, stop=True),
                             deps=d0, inc=(1 if hh == NH - 1 else 0))
            ctx.bank_ready[ibank] = ilast
            ctx.bank_free[ibank] = None
            sd = merge([ilast, sb.S_tok[layer]] + [ret_last[hh] for hh in range(NH)])
            p.op("vector", lambda eng: eng.tensor_tensor(S[:, :], S[:, :], sb.gc_f[:, :],
                                                         ALU.mult), deps=sd, inc=0)
            s2 = p.op("vector", lambda eng, ibank=ibank: eng.tensor_tensor(
                S[:, :], S[:, :], ctx.bank(ibank), ALU.add))
            bank_read(ctx, ibank, s2)
            sbf = p.op("scalar", lambda eng: eng.activation(S_bf[:, :], S[:, :], AF.Copy),
                       deps=[s2])
            sb.S_tok[layer] = sbf
            sb.krt_busy[tci] = ilast
            sb.vrt_busy[tci] = ilast
        ctx.ring_banks = None
        for fb in (0, 4):
            slot, wsl, wtok = wst.load(rw, 0, ND, fb * 128, 512)
            for mf in range(4):
                f = fb + mf
                bank = ctx.ring()
                gt = acc_group(ctx, bank,
                               [(wsl(c, mf * 128, (mf + 1) * 128),
                                 reg(6, c, 0, 512)) for c in range(ND)],
                               deps=merge([wtok], ret2_evs))
                wst.done(slot, gt)
                x_ap = sb.x.sl(f, tok0, tok0 + 512)
                pk = ctx.bank(bank)
                rb_ap = rbc[:, f:f + 1]
                ev = p.op("vector", lambda eng, x_ap=x_ap, pk=pk, rb_ap=rb_ap:
                          eng.scalar_tensor_tensor(x_ap, pk, rb_ap, x_ap,
                                                   ALU.add, ALU.add), deps=[gt])
                bank_read(ctx, bank, ev)
    p.join()


def emit_plain_ffn(ctx, sb, *, xb, w1, w2, b1_col, b2_col, HID, wst, act=AF.Gelu):
    p = ctx.p
    T = ctx.T
    NSl = ctx.NS
    NHC = HID // 128
    h = sb.h
    for g0 in range(0, NHC, 8):
        ng = min(8, NHC - g0)
        gl = {}
        m_tok = None
        for gi in range(ng):
            half = gi // 4
            if half not in gl:
                nw = min(4, ng - half * 4) * 128
                gl[half] = wst.load(w1, 0, ND, (g0 + half * 4) * 128, nw)
            slot, wsl, wtok = gl[half]
            gcol = (gi % 4) * 128
            bg = 4 + (gi % 2)
            for t in range(NSl):
                gt = acc_group(ctx, bg, [(wsl(c, gcol, gcol + 128),
                                          xb.sl(c, t * TS, (t + 1) * TS)) for c in range(ND)],
                               deps=[wtok])
                wst.done(slot, gt)
                m_tok = p.op("scalar", lambda eng, gi=gi, t=t, bg=bg, g0=g0: eng.activation(
                    h.sl(gi, t * TS, (t + 1) * TS), ctx.bank(bg), act,
                    bias=b1_col[:, g0 + gi:g0 + gi + 1]), deps=[gt])
                bank_read(ctx, bg, m_tok)
        dl = []
        for half in range(0, ng, 4):
            nked = min(4, ng - half)
            dl.append(wst.load(w2, (g0 + half) * 128, nked, 0, 1024))
        for f2 in range(ND):
            for t in range(NSl):
                bank = t if (f2 % 2 == 0) else 4 + t
                mms = []
                dep0 = [dl[-1][2], m_tok]
                for gi in range(ng):
                    dslot, dsl, dtok = dl[gi // 4]
                    mms.append((dsl(gi % 4, f2 * 128, (f2 + 1) * 128),
                                h.sl(gi, t * TS, (t + 1) * TS)))
                gt = acc_group(ctx, bank, mms, deps=dep0)
                for dslot, dsl, dtok in dl:
                    wst.done(dslot, gt)
                if g0 == 0:
                    ev = p.op("vector", lambda eng, f2=f2, t=t, bank=bank:
                              eng.scalar_tensor_tensor(
                                  sb.x.sl(f2, t * TS, (t + 1) * TS), ctx.bank(bank),
                                  b2_col[:, f2:f2 + 1],
                                  sb.x.sl(f2, t * TS, (t + 1) * TS), ALU.add, ALU.add),
                              deps=[gt])
                else:
                    ev = p.op("vector", lambda eng, f2=f2, t=t, bank=bank:
                              eng.tensor_tensor(
                                  sb.x.sl(f2, t * TS, (t + 1) * TS),
                                  sb.x.sl(f2, t * TS, (t + 1) * TS),
                                  ctx.bank(bank), ALU.add), deps=[gt])
                bank_read(ctx, bank, ev)
    p.join()


def emit_l0attn(ctx, sb, dram, wst, xin_dram, tile):
    p = ctx.p
    T = ctx.T
    t0 = tile * T
    dbg = getattr(ctx, "l0_dbg", None)
    for layer in range(2):
        emit_ln(ctx, sb, g=sb.vec[f"ln1g{layer}"], b=sb.vec[f"ln1b{layer}"],
                eps=EPS_TOP, out_bf=sb.xb, bf_only=True)
        if dbg is not None and layer == 0 and tile == 0 and "lnx" in dbg:
            p.join()
            for c in range(ND):
                p.dma("sync", dbg["lnx"][c * 128:(c + 1) * 128, 0:T], sb.xb.chunk(c))
            p.join()
        emit_retention_mixer(ctx, sb, dram, wst, layer)
        if dbg is not None and layer == 0 and tile == 0 and "mix" in dbg:
            p.join()
            for c in range(ND):
                p.dma("sync", dbg["mix"][c * 128:(c + 1) * 128, 0:T], sb.x.chunk(c))
            p.join()
        emit_ln(ctx, sb, g=sb.vec[f"ln2g{layer}"], b=sb.vec[f"ln2b{layer}"],
                eps=EPS_TOP, out_bf=sb.xb, bf_only=True)
        emit_plain_ffn(ctx, sb, xb=sb.xb, w1=dram[f"f1w{layer}"], w2=dram[f"f2w{layer}"],
                       b1_col=sb.vec[f"f1b{layer}"], b2_col=sb.vec[f"f2b{layer}"],
                       HID=4096, wst=wst, act=AF.Gelu)
        if dbg is not None and layer == 0 and tile == 0 and "ffn" in dbg:
            p.join()
            for c in range(ND):
                p.dma("sync", dbg["ffn"][c * 128:(c + 1) * 128, 0:T], sb.x.chunk(c))
            p.join()
    def add_f32(c, t, xin=xin_dram, t0=t0):
        dep = [("Sdv", p.cnt["Sdv"])] if p.cnt["Sdv"] else []
        tok = p.dma("sync", sb.xs2[:, c * TS:(c + 1) * TS],
                    xin[c * 128:(c + 1) * 128, t0 + t * TS: t0 + (t + 1) * TS],
                    deps=dep, strict=True)
        return sb.xs2[:, c * TS:(c + 1) * TS], tok
    emit_ln(ctx, sb, g=sb.vec["lnfg"], b=sb.vec["lnfb"], eps=EPS_TOP, add_f32=add_f32)
    emit_ln(ctx, sb, eps=EPS_TOP)


def build_l0attn_program(T, debug=()):
    nc = bass.Bass("TRN2", target_bir_lowering=False, debug=False)
    p = Prog(nc)
    ctx = Ctx(nc, p, T)
    consts = host_consts()
    sb = SB(ctx, T, consts)
    xin = nc.dram_tensor("s_in", [D, SEQ], F32, kind="ExternalInput").ap()
    xout = nc.dram_tensor("s_out", [D, SEQ], F32, kind="ExternalOutput").ap()
    if debug:
        ctx.l0_dbg = {}
        for nm in debug:
            dt_ = BF16 if nm == "lnx" else F32
            ctx.l0_dbg[nm] = nc.dram_tensor("dbg_" + nm, [D, T], dt_, kind="ExternalOutput").ap()
    dram = {}
    for i in range(2):
        dram[f"pw{i}"] = nc.dram_tensor(f"pw{i}", [D, 4 * D], F32, kind="ExternalInput").ap()
        dram[f"rw{i}"] = nc.dram_tensor(f"rw{i}", [D, D], F32, kind="ExternalInput").ap()
        dram[f"f1w{i}"] = nc.dram_tensor(f"f1w{i}", [D, 4 * D], F32, kind="ExternalInput").ap()
        dram[f"f2w{i}"] = nc.dram_tensor(f"f2w{i}", [4 * D, D], F32, kind="ExternalInput").ap()
    vecs = {"lnfg": (128, 8), "lnfb": (128, 8)}
    rows = []
    for i in range(2):
        for nm in (f"rb{i}", f"f2b{i}", f"ln1g{i}", f"ln1b{i}",
                   f"ln2g{i}", f"ln2b{i}"):
            vecs[nm] = (128, 8)
        vecs[f"pb{i}_col"] = (128, 32)
        vecs[f"f1b{i}"] = (128, 32)
        dram[f"pbrow{i}"] = nc.dram_tensor(f"pb{i}_row", [1, 2048], F32,
                                           kind="ExternalInput").ap()
    load_vec_inputs(ctx, sb, vecs)
    setup_l0(ctx, sb, consts)
    sb.xs2 = sb.rr[:, 0:2 * ND * TS].bitcast(F32)
    wst = WStream(ctx, "w", nbuf=2)
    prev_store = None
    for tile in range(SEQ // T):
        t0 = tile * T
        for c in range(ND):
            p.dma("sync", sb.x.chunk(c), xin[c * 128:(c + 1) * 128, t0:t0 + T],
                  deps=[prev_store] if (prev_store and c == 0) else (), strict=True)
        p.join()
        emit_l0attn(ctx, sb, dram, wst, xin, tile)
        for c in range(ND):
            prev_store = p.dma("sync", xout[c * 128:(c + 1) * 128, t0:t0 + T], sb.x.chunk(c))
    p._wait("gpsimd", "Ssy", p.cnt["Ssy"])
    p.build()
    return nc



def build_l0ffn_program(T):
    nc = bass.Bass("TRN2", target_bir_lowering=False, debug=False)
    p = Prog(nc)
    ctx = Ctx(nc, p, T)
    consts = host_consts()
    sb = SB(ctx, T, consts)
    xin = nc.dram_tensor("s_in", [D, SEQ], F32, kind="ExternalInput").ap()
    xout = nc.dram_tensor("s_out", [D, SEQ], F32, kind="ExternalOutput").ap()
    dram = {
        "wgT": nc.dram_tensor("wgT", [D, 3072], I32, kind="ExternalInput").ap(),
        "wuT": nc.dram_tensor("wuT", [D, 3072], I32, kind="ExternalInput").ap(),
        "wdT": nc.dram_tensor("wdT", [3072, D], I32, kind="ExternalInput").ap(),
    }
    wst = WStream(ctx, "w", nbuf=4)
    prev_store = None
    for tile in range(SEQ // T):
        t0 = tile * T
        for c in range(ND):
            p.dma("sync", sb.x.chunk(c), xin[c * 128:(c + 1) * 128, t0:t0 + T],
                  deps=[prev_store] if (prev_store and c == 0) else (), strict=True)
        p.join()
        emit_l0ffn(ctx, sb, dram, wst)
        emit_ln(ctx, sb, eps=EPS_TOP)
        for c in range(ND):
            prev_store = p.dma("sync", xout[c * 128:(c + 1) * 128, t0:t0 + T], sb.x.chunk(c))
    p._wait("gpsimd", "Ssy", p.cnt["Ssy"])
    p.build()
    return nc




def _load_router_weights(ctx, sb, wrT_dram, rb_dram):
    nc, p = ctx.nc, ctx.p
    sb.wr_b = nc.alloc_sbuf_tensor("wr_b", [128, 64], BF16)
    tok = None
    for c in range(ND):
        tok = p.dma("gpsimd", sb.wr_b[:, c * 8:c * 8 + 7],
                    wrT_dram[c * 128:(c + 1) * 128, 0:7])
    sb.rb_row_bf = nc.alloc_sbuf_tensor("rb_row_bf", [1, 8], BF16)
    tok = p.dma("gpsimd", sb.rb_row_bf[0:1, 0:7], rb_dram[0:1, 0:7])
    sb.wr_tok = tok


def build_moe_program(T, debug_w=False, n_routed=7):
    nc = bass.Bass("TRN2", target_bir_lowering=False, debug=False)
    p = Prog(nc)
    ctx = Ctx(nc, p, T)
    consts = host_consts()
    sb = SB(ctx, T, consts, need=("SEL7",), moe=True)
    sb.w_rows = nc.alloc_sbuf_tensor("w_rows", [7, T], F32)
    sb.wbb = nc.alloc_sbuf_tensor("wbb", [128, T], F32)
    xin = nc.dram_tensor("s_in", [D, SEQ], F32, kind="ExternalInput").ap()
    xout = nc.dram_tensor("s_out", [D, SEQ], F32, kind="ExternalOutput").ap()
    dram = {}
    for nm, shape in (("sgT", [D, 2816]), ("suT", [D, 2816]), ("sdT", [2816, D]),
                      ("rgT", [7 * D, 2816]), ("ruT", [7 * D, 2816]),
                      ("rdT", [7 * 2816, D]), ("wrT", [D, 7]), ("rb_row", [1, 7])):
        dram[nm] = nc.dram_tensor(nm, shape, F32, kind="ExternalInput").ap()
    _load_router_weights(ctx, sb, dram["wrT"], dram["rb_row"])
    dbg_w = nc.dram_tensor("dbg_w", [7, SEQ], F32, kind="ExternalOutput").ap() if debug_w else None
    dbg_z = nc.dram_tensor("dbg_z", [D, SEQ], F32, kind="ExternalOutput").ap() if debug_w else None
    if debug_w:
        ctx.dbg_h = nc.dram_tensor("dbg_h", [1024, T], BF16, kind="ExternalOutput").ap()
        ctx.dbg_h_armed = True
    else:
        ctx.dbg_h = None
    wst = WStream(ctx, "w", nbuf=3)
    prev_store = None
    for tile in range(SEQ // T):
        t0 = tile * T
        for c in range(ND):
            p.dma("sync", sb.x.chunk(c), xin[c * 128:(c + 1) * 128, t0:t0 + T],
                  deps=[prev_store] if (prev_store and c == 0) else (), strict=True)
        p.join()
        emit_cast(ctx, sb.x, sb.xb)
        p.join()
        emit_router(ctx, sb, xb=sb.xb, wr_b=sb.wr_b, rb_row_bf=sb.rb_row_bf,
                    ones_row_bf=sb.ones_row_bf, w_rows=sb.w_rows)
        if debug_w:
            p.dma("sync", dbg_w[0:7, t0:t0 + T], sb.w_rows[0:7, :])
        emit_moe_experts(ctx, sb, dram, wst, n_routed=n_routed)
        if debug_w:
            for c in range(ND):
                p.dma("sync", dbg_z[c * 128:(c + 1) * 128, t0:t0 + T], sb.x.chunk(c))
        emit_ln(ctx, sb, eps=EPS_TOP)
        for c in range(ND):
            prev_store = p.dma("sync", xout[c * 128:(c + 1) * 128, t0:t0 + T], sb.x.chunk(c))
    p._wait("gpsimd", "Ssy", p.cnt["Ssy"])
    p.build()
    return nc





def build_fused_program(T=1024):
    nc = bass.Bass("TRN2", target_bir_lowering=False, debug=False)
    p = Prog(nc)
    ctx = Ctx(nc, p, T)
    consts = host_consts()
    sb = SB(ctx, T, consts, need=("SEL7", "SEL16"), moe=True)
    xin = nc.dram_tensor("xT", [D, SEQ], F32, kind="ExternalInput").ap()
    xout = nc.dram_tensor("s_out", [D, SEQ], F32, kind="ExternalOutput").ap()
    s2 = nc.dram_tensor("s2_scr", [D, SEQ], F32, kind="Internal").ap()
    pq = nc.dram_tensor("pq_scr", [D, SEQ], BF16, kind="Internal").ap()

    dram = {}
    for i in range(2):
        dram[f"pw{i}"] = nc.dram_tensor(f"pw{i}", [D, 4 * D], F32, kind="ExternalInput").ap()
        dram[f"rw{i}"] = nc.dram_tensor(f"rw{i}", [D, D], F32, kind="ExternalInput").ap()
        dram[f"f1w{i}"] = nc.dram_tensor(f"f1w{i}", [D, 4 * D], F32, kind="ExternalInput").ap()
        dram[f"f2w{i}"] = nc.dram_tensor(f"f2w{i}", [4 * D, D], F32, kind="ExternalInput").ap()
    dram["wgT"] = nc.dram_tensor("wgT", [D, 3072], I32, kind="ExternalInput").ap()
    dram["wuT"] = nc.dram_tensor("wuT", [D, 3072], I32, kind="ExternalInput").ap()
    dram["wdT"] = nc.dram_tensor("wdT", [3072, D], I32, kind="ExternalInput").ap()
    for nm, dn in (("wqT", "l1_wq"), ("wkT", "l1_wk"), ("wvT", "l1_wv"), ("wdT_l1", "l1_wd")):
        dram[nm] = nc.dram_tensor(dn, [D, D], F32, kind="ExternalInput").ap()
    for nm, shape in (("sgT", [D, 2816]), ("suT", [D, 2816]), ("sdT", [2816, D]),
                      ("rgT", [7 * D, 2816]), ("ruT", [7 * D, 2816]),
                      ("rdT", [7 * 2816, D]), ("wrT", [D, 7]), ("rb_row", [1, 7])):
        dram[nm] = nc.dram_tensor(nm, shape, F32, kind="ExternalInput").ap()

    vecs = {"lnfg": (128, 8), "lnfb": (128, 8),
            "bq_col": (128, 8), "bd_col": (128, 8),
            "l1g": (128, 8), "l1b": (128, 8),
            "bk_row": (1, 1024), "bv_row": (1, 1024)}
    rows_only = []
    for i in range(2):
        for nm in (f"rb{i}", f"f2b{i}", f"ln1g{i}", f"ln1b{i}", f"ln2g{i}", f"ln2b{i}"):
            vecs[nm] = (128, 8)
        vecs[f"pb{i}_col"] = (128, 32)
        vecs[f"f1b{i}"] = (128, 32)
        dram[f"pbrow{i}"] = nc.dram_tensor(f"pb{i}_row", [1, 2048], F32,
                                           kind="ExternalInput").ap()
    load_vec_inputs(ctx, sb, vecs, bf_rows=("bk_row", "bv_row"))
    setup_l0(ctx, sb, consts)
    setup_l1(ctx, sb, {"wkT": dram["wkT"], "wvT": dram["wvT"]})
    _load_router_weights(ctx, sb, dram["wrT"], dram["rb_row"])
    sb.xs2 = sb.rr[:, 0:2 * ND * TS].bitcast(F32)
    sb.wbb = sb.rr[:, 8192:8192 + 2 * T].bitcast(F32)
    sb.w_rows = sb.rr[0:7, 8192 + 2 * T:8192 + 4 * T].bitcast(F32)
    wst = WStream(ctx, "w", nbuf=2)
    l1dram = {"wqT": dram["wqT"], "wkT": dram["wkT"], "wvT": dram["wvT"],
              "wdT": dram["wdT_l1"]}

    NT = SEQ // T
    for tile in range(NT):
        t0 = tile * T
        qdep = [("Ssy", p.cnt["Ssy"])] if p.cnt["Ssy"] else []
        for c in range(ND):
            p.dma("sync", sb.x.chunk(c), xin[c * 128:(c + 1) * 128, t0:t0 + T],
                  deps=qdep if c == 0 else (), strict=True)
        p.join()
        emit_l0attn(ctx, sb, dram, wst, xin, tile)
        emit_l0ffn(ctx, sb, dram, wst)
        emit_ln(ctx, sb, eps=EPS_TOP)
        for c in range(ND):
            p.dma("sync", s2[c * 128:(c + 1) * 128, t0:t0 + T], sb.x.chunk(c))
        emit_l1_phase_a(ctx, sb, l1dram, wst, pq, tile)
    emit_ks_build(ctx, sb)
    for tile in range(NT):
        t0 = tile * T
        emit_l1_phase_b(ctx, sb, l1dram, wst, pq, s2, tile)
        emit_cast(ctx, sb.x, sb.xb)
        p.join()
        emit_router(ctx, sb, xb=sb.xb, wr_b=sb.wr_b, rb_row_bf=sb.rb_row_bf,
                    ones_row_bf=sb.ones_row_bf, w_rows=sb.w_rows)
        emit_moe_experts(ctx, sb, dram, wst)
        emit_ln(ctx, sb, eps=EPS_TOP)
        for c in range(ND):
            p.dma("sync", xout[c * 128:(c + 1) * 128, t0:t0 + T], sb.x.chunk(c))
        p.join()
    p._wait("gpsimd", "Ssy", p.cnt["Ssy"])
    p.build()
    return nc


def kernel(**inputs):
    if "f" not in _PROGS:
        _PROGS["f"] = build_fused_program(1024)
    nc = _PROGS["f"]
    n = 8
    packs = [host_pack(inputs, core) for core in range(n)]
    names = ["pw0", "pw1", "rw0", "rw1", "f1w0", "f1w1", "f2w0", "f2w1",
             "lnfg", "lnfb", "wgT", "wuT", "wdT",
             "l1_wq", "l1_wk", "l1_wv", "l1_wd",
             "bq_col", "bd_col", "l1g", "l1b", "bk_row", "bv_row",
             "sgT", "suT", "sdT", "rgT", "ruT", "rdT", "wrT", "rb_row"]
    for i in range(2):
        names += [f"pb{i}_col", f"pb{i}_row", f"rb{i}", f"f1b{i}", f"f2b{i}",
                  f"ln1g{i}", f"ln1b{i}", f"ln2g{i}", f"ln2b{i}"]
    in_maps = [dict(_sub(p_, names), xT=p_["xT"]) for p_ in packs]
    res = run_bass_kernel_spmd(nc, in_maps, core_ids=list(range(n)))
    outs = [np.ascontiguousarray(np.asarray(r["s_out"]).T) for r in res.results]
    return np.stack(outs, axis=0)



_PROGS = {}


def _get_progs():
    if "a" not in _PROGS:
        _PROGS["a"] = build_l0attn_program(2048)
        _PROGS["b"] = build_l0ffn_program(2048)
        _PROGS["c"] = build_l1_program(2048)
        _PROGS["d"] = build_moe_program(2048)
    return _PROGS


def _sub(m, names):
    return {k: m[k] for k in names}


def kernel_unfused(**inputs):
    progs = _get_progs()
    n = 8
    packs = [host_pack(inputs, core) for core in range(n)]
    cores = list(range(n))

    a_names = ["pw0", "pw1", "rw0", "rw1", "f1w0", "f1w1", "f2w0", "f2w1",
               "lnfg", "lnfb"]
    for i in range(2):
        a_names += [f"pb{i}_col", f"pb{i}_row", f"rb{i}", f"f1b{i}", f"f2b{i}",
                    f"ln1g{i}", f"ln1b{i}", f"ln2g{i}", f"ln2b{i}"]
    in_maps = [dict(_sub(p_, a_names), s_in=p_["xT"]) for p_ in packs]
    res = run_bass_kernel_spmd(progs["a"], in_maps, core_ids=cores)
    s1 = [np.asarray(r["s_out"]) for r in res.results]

    b_names = ["wgT", "wuT", "wdT"]
    in_maps = [dict(_sub(p_, b_names), s_in=s) for p_, s in zip(packs, s1)]
    res = run_bass_kernel_spmd(progs["b"], in_maps, core_ids=cores)
    s2 = [np.asarray(r["s_out"]) for r in res.results]

    c_names = ["l1_wq", "l1_wk", "l1_wv", "l1_wd", "bq_col", "bd_col",
               "l1g", "l1b", "bk_row", "bv_row"]
    in_maps = [dict(_sub(p_, c_names), s_in=s) for p_, s in zip(packs, s2)]
    res = run_bass_kernel_spmd(progs["c"], in_maps, core_ids=cores)
    s3 = [np.asarray(r["s_out"]) for r in res.results]

    d_names = ["sgT", "suT", "sdT", "rgT", "ruT", "rdT", "wrT", "rb_row"]
    in_maps = [dict(_sub(p_, d_names), s_in=s) for p_, s in zip(packs, s3)]
    res = run_bass_kernel_spmd(progs["d"], in_maps, core_ids=cores)
    outs = [np.ascontiguousarray(np.asarray(r["s_out"]).T) for r in res.results]
    return np.stack(outs, axis=0)


# version 03
# speedup vs baseline: 1.1272x; 1.1272x over previous
import numpy as np
import ml_dtypes
from contextlib import ExitStack

import concourse.bass as bass
import concourse.mybir as mybir
from concourse.bass_utils import run_bass_kernel_spmd

F32 = mybir.dt.float32
BF16 = mybir.dt.bfloat16
I32 = mybir.dt.int32
AF = mybir.ActivationFunctionType
ALU = mybir.AluOpType
AX = mybir.AxisListType

D = 1024
SEQ = 4096
NH = 16
HD = 64
CH = 128
ND = D // 128
TS = 512
EPS_TOP = 1e-5
EPS_L1LN = 1e-12
EPS_ATTN = 1e-6
Q127 = 1.0 / 127.0

ENGINES = ("sync", "scalar", "tensor", "vector", "gpsimd")
ESEM = {"sync": "Ssy", "scalar": "Sac", "tensor": "Spe", "vector": "Sdv", "gpsimd": "Sgp"}



class Prog:
    def __init__(self, nc):
        self.nc = nc
        self.ops = {e: [] for e in ENGINES}
        self.cnt = {s: 0 for s in ESEM.values()}
        self.sem_handles = {}
        self.last_wait = {e: {} for e in ENGINES}
        self.tag = ""

    def _wait(self, engine, s, v):
        lw = self.last_wait[engine]
        if lw.get(s, -1) >= v:
            return
        lw[s] = v

        def w(eng, s=s, v=v):
            eng.wait_ge(self.sem_handles[s], v)
        self.ops[engine].append(w)

    def op(self, engine, fn, deps=(), inc=1, strict=False):
        best = {}
        for t in deps:
            if t is None:
                continue
            s, v = t
            if best.get(s, -1) < v:
                best[s] = v
        own = ESEM[engine]
        for s, v in sorted(best.items()):
            if s != own or strict:
                self._wait(engine, s, v)
        tag = self.tag
        if inc:
            s = own
            self.cnt[s] += inc
            tok = (s, self.cnt[s])

            def wrapped(eng, fn=fn, s=s, inc=inc, tag=tag):
                inst = fn(eng)
                if tag:
                    inst.annotate(tag)
                inst.then_inc(self.sem_handles[s], inc)
            self.ops[engine].append(wrapped)
            return tok

        def plain(eng, fn=fn, tag=tag):
            inst = fn(eng)
            if tag:
                inst.annotate(tag)
        self.ops[engine].append(plain)
        return None

    def sop(self, engine, fn, deps=()):
        tok = self.op(engine, fn, deps=deps, inc=1)
        self._wait(engine, tok[0], tok[1])
        return tok

    def dma(self, engine, out, in_, deps=(), strict=False):
        return self.op(engine, lambda eng, o=out, i=in_: eng.dma_start(out=o, in_=i),
                       deps=deps, inc=16, strict=strict)

    def now(self):
        return [(s, self.cnt[s]) for s in self.cnt if self.cnt[s] > 0]

    def join(self, engines=ENGINES):
        toks = self.now()
        for e in engines:
            for s, v in toks:
                if s != ESEM[e]:
                    self._wait(e, s, v)

    def build(self):
        st = ExitStack()
        for s in ESEM.values():
            self.sem_handles[s] = st.enter_context(self.nc.semaphore(s))
        block = st.enter_context(self.nc.Block())
        for e in ENGINES:
            lst = self.ops[e]
            if not lst:
                continue

            def run(eng, lst=lst):
                for fn in lst:
                    fn(eng)
            getattr(block, e)(run)
        st.close()


def merge(*toks):
    out = []
    for t in toks:
        if t is None:
            continue
        if isinstance(t, (list, tuple)) and (len(t) == 0 or not isinstance(t[0], str)):
            out.extend(x for x in t if x is not None)
        else:
            out.append(t)
    return out


class CBuf:
    def __init__(self, nc, name, n, w, dtype):
        self.h = nc.alloc_sbuf_tensor(name, [128, n * w], dtype)
        self.n = n
        self.w = w

    def sl(self, c, a, b):
        return self.h[:, c * self.w + a: c * self.w + b]

    def chunk(self, c):
        return self.h[:, c * self.w:(c + 1) * self.w]


class Ctx:
    def __init__(self, nc, p, T):
        self.nc = nc
        self.p = p
        self.T = T
        self.NS = T // TS
        self.ps = nc.alloc_psum_tensor("ps", [128, 4096], F32)
        self.bank_ready = [None] * 8
        self.bank_free = [None] * 8
        self.ring_i = 0

    def bank(self, b, p0=0, p1=128, w=TS):
        return self.ps[p0:p1, b * TS: b * TS + w]

    def ring(self):
        banks = getattr(self, "ring_banks", None) or list(range(8))
        b = banks[self.ring_i % len(banks)]
        self.ring_i += 1
        return b


def acc_group(ctx, bank, mms, deps=(), p0=0, p1=128, w=TS, col0=0):
    p = ctx.p
    out_ap = ctx.ps[p0:p1, bank * TS + col0: bank * TS + col0 + w]
    n = len(mms)
    tok = None
    for i, (lhsT, rhs) in enumerate(mms):
        d = list(deps) if i == 0 else []
        if i == 0 and ctx.bank_free[bank] is not None:
            d.append(ctx.bank_free[bank])

        def fn(eng, lhsT=lhsT, rhs=rhs, i=i, out_ap=out_ap):
            return eng.matmul(out_ap, lhsT, rhs, start=(i == 0), stop=(i == n - 1))
        tok = p.op("tensor", fn, deps=d, inc=(1 if i == n - 1 else 0))
    ctx.bank_ready[bank] = tok
    ctx.bank_free[bank] = None
    return tok


def bank_read(ctx, bank, tok):
    ctx.bank_free[bank] = tok


class WStream:

    def __init__(self, ctx, name, nbuf=4):
        self.ctx = ctx
        self.nbuf = nbuf
        self.bufs = [CBuf(ctx.nc, f"{name}{i}", 8, 512, BF16) for i in range(nbuf)]
        self.busy = [None] * nbuf
        self.i = 0

    def load(self, dram, r0, nk, c0, w):
        assert nk * w <= 4096
        p = self.ctx.p
        idx = self.i % self.nbuf
        self.i += 1
        buf = self.bufs[idx]
        dep = [self.busy[idx]] if self.busy[idx] is not None else []
        tok = None
        for c in range(nk):
            tok = p.dma("gpsimd",
                        buf.h[:, c * w:(c + 1) * w],
                        dram[r0 + c * 128: r0 + (c + 1) * 128, c0:c0 + w],
                        deps=dep if c == 0 else ())

        def sl(c, a, b, buf=buf, w=w):
            return buf.h[:, c * w + a: c * w + b]
        return idx, sl, tok

    def done(self, idx, tok):
        prev = self.busy[idx]
        if prev is None or tok[1] > prev[1]:
            self.busy[idx] = tok



def _col_pack(v):
    v = np.asarray(v, dtype=np.float32)
    n = v.shape[0] // 128
    return np.ascontiguousarray(v.reshape(n, 128).T)


def _t(w):
    return np.ascontiguousarray(np.asarray(w).T)


def _row(v):
    return np.ascontiguousarray(np.asarray(v, np.float32).reshape(1, -1))


def host_pack(inputs, core):
    m = {}
    x = np.asarray(inputs["L0_attn_x"][core])
    m["xT"] = _t(x)
    for i in range(2):
        m[f"pw{i}"] = np.ascontiguousarray(np.asarray(inputs["L0_attn_proj_w"][i]))
        m[f"pb{i}_col"] = _col_pack(inputs["L0_attn_proj_b"][i])
        m[f"pb{i}_row"] = np.ascontiguousarray(
            np.asarray(inputs["L0_attn_proj_b"][i], np.float32)[1024:3072].reshape(1, -1))
        m[f"rw{i}"] = np.ascontiguousarray(np.asarray(inputs["L0_attn_reto_w"][i]))
        m[f"rb{i}"] = _col_pack(inputs["L0_attn_reto_b"][i])
        m[f"f1w{i}"] = np.ascontiguousarray(np.asarray(inputs["L0_attn_ffn1_w"][i]))
        m[f"f1b{i}"] = _col_pack(inputs["L0_attn_ffn1_b"][i])
        m[f"f2w{i}"] = np.ascontiguousarray(np.asarray(inputs["L0_attn_ffn2_w"][i]))
        m[f"f2b{i}"] = _col_pack(inputs["L0_attn_ffn2_b"][i])
        m[f"ln1g{i}"] = _col_pack(inputs["L0_attn_ln1_g"][i])
        m[f"ln1b{i}"] = _col_pack(inputs["L0_attn_ln1_b"][i])
        m[f"ln2g{i}"] = _col_pack(inputs["L0_attn_ln2_g"][i])
        m[f"ln2b{i}"] = _col_pack(inputs["L0_attn_ln2_b"][i])
    m["lnfg"] = _col_pack(inputs["L0_attn_lnf_g"])
    m["lnfb"] = _col_pack(inputs["L0_attn_lnf_b"])
    m["wgT"] = _t(inputs["L0_ffn_w_gate"])
    m["wuT"] = _t(inputs["L0_ffn_w_up"])
    m["wdT"] = _t(inputs["L0_ffn_w_down"])
    for nm, key in (("l1_wq", "L1_attn_Wq"), ("l1_wk", "L1_attn_Wk"),
                    ("l1_wv", "L1_attn_Wv"), ("l1_wd", "L1_attn_Wd")):
        m[nm] = _t(inputs[key])
    m["bq_col"] = _col_pack(inputs["L1_attn_bq"])
    m["bk_row"] = _row(inputs["L1_attn_bk"])
    m["bv_row"] = _row(inputs["L1_attn_bv"])
    m["bd_col"] = _col_pack(inputs["L1_attn_bd"])
    m["l1g"] = _col_pack(inputs["L1_attn_gamma"])
    m["l1b"] = _col_pack(inputs["L1_attn_beta"])
    m["sgT"] = _t(inputs["L1_ffn_sg"])
    m["suT"] = _t(inputs["L1_ffn_su"])
    m["sdT"] = _t(inputs["L1_ffn_sd"])
    m["rgT"] = np.ascontiguousarray(np.asarray(inputs["L1_ffn_rg"]).transpose(0, 2, 1).reshape(7 * D, 2816))
    m["ruT"] = np.ascontiguousarray(np.asarray(inputs["L1_ffn_ru"]).transpose(0, 2, 1).reshape(7 * D, 2816))
    m["rdT"] = np.ascontiguousarray(np.asarray(inputs["L1_ffn_rd"]).transpose(0, 2, 1).reshape(7 * 2816, D))
    m["wrT"] = _t(inputs["L1_ffn_wr"])
    m["rb_row"] = _row(inputs["L1_ffn_rb"])
    return m


def host_consts():
    c = {}
    gamma = (1.0 - 2.0 ** (-5.0 - np.arange(NH, dtype=np.float64)))
    j = np.arange(CH, dtype=np.float64)
    dts = np.zeros((NH, CH, CH), np.float32)
    for h in range(NH):
        mask = (j[None, :] - j[:, None]) >= 0
        vals = (gamma[h] ** (-(j + 1.0)))[:, None]
        dts[h] = np.where(mask, vals, 0.0).astype(np.float32)
    c["DT"] = dts.reshape(NH * CH, CH)
    qd = np.zeros((ND, 128, CH), np.float32)
    for cc in range(ND):
        for half in range(2):
            h = 2 * cc + half
            qd[cc, half * 64:(half + 1) * 64, :] = (gamma[h] ** (j + 1.0)).astype(np.float32)[None, :]
    c["QD"] = qd.reshape(ND * 128, CH)
    kd = np.zeros((2, CH, 512), np.float32)
    for s in range(2):
        for hh in range(8):
            h = s * 8 + hh
            colv = (gamma[h] ** (CH - 1.0 - j)).astype(np.float32)
            kd[s, :, hh * 64:(hh + 1) * 64] = colv[:, None]
    c["KD"] = kd.reshape(2 * CH, 512)
    gc = np.zeros((128, 512), np.float32)
    for h in range(NH):
        gc[(h % 2) * 64:(h % 2) * 64 + 64, (h // 2) * 64:(h // 2) * 64 + 64] = np.float32(gamma[h] ** CH)
    c["GC"] = gc
    c["ones1"] = np.ones((1, 128), np.float32)
    c["onesc"] = np.ones((128, 1), np.float32)
    sel7 = np.zeros((7, 7 * 128), np.float32)
    for e in range(7):
        sel7[e, e * 128:(e + 1) * 128] = 1.0
    c["SEL7"] = sel7
    sel16 = np.zeros((16, 8 * 128), np.float32)
    for cc in range(8):
        sel16[2 * cc, cc * 128: cc * 128 + 64] = 1.0
        sel16[2 * cc + 1, cc * 128 + 64: cc * 128 + 128] = 1.0
    c["SEL16"] = sel16
    c["SEL2"] = np.zeros((2, 128), np.float32)
    c["SEL2"][0, 0:64] = 1.0
    c["SEL2"][1, 64:128] = 1.0
    c["ident"] = np.eye(128, dtype=np.float32)
    return c



class SB:
    def __init__(self, ctx, T, consts, need=(), moe=False):
        nc, p = ctx.nc, ctx.p
        self.T = T
        self.x = CBuf(nc, "x", ND, T, F32)
        self.xb = CBuf(nc, "xb", ND, T, BF16)
        self.h = CBuf(nc, "h", ND, T, BF16)
        self.rows = nc.alloc_sbuf_tensor("rows", [1, 2 * T], F32)
        self.t1 = [nc.alloc_sbuf_tensor(f"t1_{i}", [128, TS], F32) for i in range(2)]
        self.t1_busy = [None, None]
        self.const = {}
        bf_consts = getattr(self, "_bf_consts", ("SEL7",))
        for name in set(("ones1", "ident") + tuple(need)):
            arr = consts[name]
            dr = nc.inline_tensor(np.ascontiguousarray(arr), "ic_" + name)
            if name in bf_consts:
                hh = nc.alloc_sbuf_tensor("cst_" + name, list(arr.shape), BF16)
                p.dma("gpsimd", hh[:, :], dr.ap())
            else:
                hh = nc.alloc_sbuf_tensor("cst_" + name, list(arr.shape), F32)
                p.dma("sync", hh[:, :], dr.ap())
            self.const[name] = hh
        onesb = nc.inline_tensor(np.ones((128, 1), np.float32), "ic_ones_b")
        self.ones_bf = nc.alloc_sbuf_tensor("cst_ones_bf", [128, 1], BF16)
        p.dma("gpsimd", self.ones_bf[:, :], onesb.ap())
        onesr = nc.inline_tensor(np.ones((1, 128), np.float32), "ic_ones_r")
        self.ones_row_bf = nc.alloc_sbuf_tensor("cst_ones_r_bf", [1, 128], BF16)
        p.dma("gpsimd", self.ones_row_bf[:, :], onesr.ap())
        self.vec = {}
        self.vecrow = {}
        if moe:
            self.rtr = nc.alloc_sbuf_tensor("rtr", [128, 64], F32)
            self.rtr_busy = [None, None]
            self.w_rows = None
            self.w_rows_tok = None
            self.wbb = None
            self.wr_b = None
            self.wr_tok = None
            self.rb_row_bf = None

    def h_f32(self, i):
        return self.h.h[:, 2 * i * self.T:(2 * i + 2) * self.T].bitcast(F32)


def load_vec_inputs(ctx, sb, names_shapes, bf_rows=(), bf_only_rows=()):
    nc, p = ctx.nc, ctx.p
    for name, shape in names_shapes.items():
        d = nc.dram_tensor(name, list(shape), F32, kind="ExternalInput").ap()
        if name not in bf_only_rows:
            h = nc.alloc_sbuf_tensor("v_" + name, list(shape), F32)
            p.dma("sync", h[:, :], d[:, :])
            sb.vec[name] = h
        if name in bf_rows or name in bf_only_rows:
            hb = nc.alloc_sbuf_tensor("vb_" + name, list(shape), BF16)
            p.dma("gpsimd", hb[:, :], d[:, :])
            sb.vecrow[name] = hb



def emit_ln(ctx, sb, *, g=None, b=None, eps, out_bf=None, add_f32=None,
            bf_only=False, name=""):
    p = ctx.p
    T = ctx.T
    NSl = ctx.NS
    x = sb.x
    scr = [(sb.h.chunk(0), sb.h.chunk(1)), (sb.h.chunk(2), sb.h.chunk(3))]
    onesb = sb.ones_bf
    onesf = sb.const["ones1"]

    prev_mm = {}
    for c in range(ND):
        xbs, xsq = scr[c % 2]
        dep = merge([prev_mm.get(c - 2)])
        p.op("scalar", lambda eng, xbs=xbs, c=c: eng.activation(xbs, x.chunk(c), AF.Copy),
             deps=dep, inc=0)
        ct = p.op("scalar", lambda eng, xsq=xsq, xbs=xbs: eng.activation(xsq, xbs, AF.Square))
        last = None
        for t in range(NSl):
            for kind in range(2):
                bank = kind * NSl + t
                src = (xbs if kind == 0 else xsq)[:, t * TS:(t + 1) * TS]
                d = [ct]
                if c == 0 and ctx.bank_free[bank] is not None:
                    d.append(ctx.bank_free[bank])
                last = p.op("tensor", lambda eng, bank=bank, src=src, c=c:
                            eng.matmul(ctx.bank(bank, 0, 1), onesb[:, 0:1], src,
                                       start=(c == 0), stop=(c == ND - 1)),
                            deps=d, inc=(1 if (kind == 1 and t == NSl - 1) else 0))
        prev_mm[c] = last
    mm_done = prev_mm[ND - 1]

    mu = sb.rows[0:1, 0:T]
    rstd = sb.rows[0:1, T:2 * T]
    for t in range(NSl):
        p.op("vector", lambda eng, t=t: eng.tensor_scalar_mul(
            mu[:, t * TS:(t + 1) * TS], ctx.bank(t, 0, 1), 1.0 / D),
            deps=[mm_done] if t == 0 else (), inc=0)
        p.op("vector", lambda eng, t=t: eng.tensor_scalar_mul(
            rstd[:, t * TS:(t + 1) * TS], ctx.bank(NSl + t, 0, 1), 1.0 / D), inc=0)
    for t in range(NSl):
        p.op("vector", lambda eng, t=t: eng.tensor_tensor(
            ctx.bank(t, 0, 1), mu[:, t * TS:(t + 1) * TS],
            mu[:, t * TS:(t + 1) * TS], ALU.mult), inc=0)
        p.op("vector", lambda eng, t=t: eng.tensor_tensor(
            rstd[:, t * TS:(t + 1) * TS], rstd[:, t * TS:(t + 1) * TS],
            ctx.bank(t, 0, 1), ALU.subtract), inc=0)
    var_tok = p.op("vector", lambda eng: eng.tensor_scalar_add(rstd, rstd, float(eps)))
    for i in range(2 * NSl):
        bank_read(ctx, i, var_tok)
    sq_tok = p.op("scalar", lambda eng: eng.activation(rstd, rstd, AF.Sqrt),
                  deps=[var_tok])
    rstd_tok = p.op("vector", lambda eng: eng.reciprocal(rstd, rstd), deps=[sq_tok])

    for t in range(NSl):
        bmu, brs = 0, 1
        p.op("tensor", lambda eng, t=t: eng.matmul(
            ctx.bank(bmu), onesf[0:1, :], mu[:, t * TS:(t + 1) * TS], start=True, stop=True),
            deps=merge([rstd_tok, ctx.bank_free[bmu]]), inc=0)
        bc_tok = p.op("tensor", lambda eng, t=t: eng.matmul(
            ctx.bank(brs), onesf[0:1, :], rstd[:, t * TS:(t + 1) * TS], start=True, stop=True),
            deps=merge([ctx.bank_free[brs]]))
        last = bc_tok
        for c in range(ND):
            tt = sb.t1[c % 2]
            p.op("vector", lambda eng, c=c, t=t, tt=tt: eng.tensor_tensor(
                tt[:], x.sl(c, t * TS, (t + 1) * TS), ctx.bank(bmu), ALU.subtract),
                deps=merge([bc_tok, sb.t1_busy[c % 2]]), inc=0)
            dve_tok = p.op("vector", lambda eng, tt=tt: eng.tensor_tensor(
                tt[:], tt[:], ctx.bank(brs), ALU.mult))
            dst = x.sl(c, t * TS, (t + 1) * TS)
            if bf_only:
                a_tok = p.op("vector", lambda eng, c=c, tt=tt: eng.tensor_scalar(
                    tt[:], tt[:], g[:, c:c + 1], b[:, c:c + 1], ALU.mult, ALU.add),
                    deps=[dve_tok])
                w_tok = p.op("scalar", lambda eng, c=c, t=t, tt=tt: eng.activation(
                    out_bf.sl(c, t * TS, (t + 1) * TS), tt[:], AF.Copy), deps=[a_tok])
                sb.t1_busy[c % 2] = w_tok
                last = dve_tok
                continue
            if g is not None:
                if add_f32 is not None:
                    add_ap, add_tok = add_f32(c, t)
                    a_tok = p.op("vector", lambda eng, c=c, tt=tt: eng.tensor_scalar(
                        tt[:], tt[:], g[:, c:c + 1], b[:, c:c + 1], ALU.mult, ALU.add),
                        deps=[dve_tok])
                    w_tok = p.op("vector", lambda eng, tt=tt, dst=dst, a=add_ap:
                                 eng.tensor_tensor(dst, tt[:], a, ALU.add),
                                 deps=merge([a_tok, add_tok]))
                else:
                    w_tok = p.op("vector", lambda eng, c=c, tt=tt, dst=dst: eng.tensor_scalar(
                        dst, tt[:], g[:, c:c + 1], b[:, c:c + 1], ALU.mult, ALU.add),
                        deps=[dve_tok])
            else:
                assert add_f32 is None
                w_tok = p.op("scalar", lambda eng, tt=tt, dst=dst: eng.activation(
                    dst, tt[:], AF.Copy), deps=[dve_tok])
            sb.t1_busy[c % 2] = w_tok
            if out_bf is not None:
                p.op("scalar", lambda eng, c=c, t=t, dst=dst: eng.activation(
                    out_bf.sl(c, t * TS, (t + 1) * TS), dst, AF.Copy), deps=[w_tok])
            last = dve_tok
        bank_read(ctx, bmu, last)
        bank_read(ctx, brs, last)
    p.join()


def emit_cast(ctx, src_cb, dst_cb, scale=None):
    p = ctx.p
    for c in range(ND):
        if scale is None:
            p.op("scalar", lambda eng, c=c: eng.activation(
                dst_cb.chunk(c), src_cb.chunk(c), AF.Copy))
        else:
            p.op("scalar", lambda eng, c=c: eng.activation(
                dst_cb.chunk(c), src_cb.chunk(c), AF.Copy, scale=scale))



def emit_gated_ffn(ctx, sb, *, xb, wg, wu, wd, HID, wst, act=AF.Silu,
                   up_scale=None, wb_ap=None, first_bias=None):
    p = ctx.p
    T = ctx.T
    NSl = ctx.NS
    NHC = HID // 128
    h = sb.h
    upscr = [ctx.nc, None]
    for g0 in range(0, NHC, 8):
        ng = min(8, NHC - g0)
        gl = {}
        ul = {}
        for gi in range(ng):
            hc = g0 + gi
            half = gi // 4
            if half not in gl:
                nw = min(4, ng - half * 4) * 128
                gl[half] = wst.load(wg, 0, ND, (g0 + half * 4) * 128, nw)
                ul[half] = wst.load(wu, 0, ND, (g0 + half * 4) * 128, nw)
            slot, wsl, wtok = gl[half]
            uslot, usl, utok = ul[half]
            gcol = (gi % 4) * 128
            bg = 4 + (gi % 2)
            bu = 6 + (gi % 2)

            u_toks = []
            for t in range(NSl):
                gt = acc_group(ctx, bg, [(wsl(c, gcol, gcol + 128),
                                          xb.sl(c, t * TS, (t + 1) * TS)) for c in range(ND)],
                               deps=[wtok], col0=0)
                e1 = p.op("scalar", lambda eng, gi=gi, t=t, bg=bg: eng.activation(
                    h.sl(gi, t * TS, (t + 1) * TS), ctx.bank(bg), act), deps=[gt])
                bank_read(ctx, bg, e1)
                wst.done(slot, gt)
                ut = acc_group(ctx, bu, [(usl(c, gcol, gcol + 128),
                                          xb.sl(c, t * TS, (t + 1) * TS)) for c in range(ND)],
                               deps=[utok], col0=0)
                wst.done(uslot, ut)
                tt = sb.t1[t % 2]
                if wb_ap is not None:
                    e2 = p.op("vector", lambda eng, t=t, bu=bu, tt=tt: eng.tensor_tensor(
                        tt[:], ctx.bank(bu), wb_ap[:, t * TS:(t + 1) * TS], ALU.mult),
                        deps=merge([ut, sb.t1_busy[t % 2]]))
                else:
                    sc = {} if up_scale is None else {"scale": up_scale}
                    e2 = p.op("scalar", lambda eng, t=t, bu=bu, tt=tt, sc=sc: eng.activation(
                        tt[:], ctx.bank(bu), AF.Copy, **sc),
                        deps=merge([ut, sb.t1_busy[t % 2]]))
                bank_read(ctx, bu, e2)
                m_tok = p.op("vector", lambda eng, gi=gi, t=t, tt=tt: eng.tensor_tensor(
                    h.sl(gi, t * TS, (t + 1) * TS), h.sl(gi, t * TS, (t + 1) * TS),
                    tt[:], ALU.mult), deps=merge([e1, e2]))
                sb.t1_busy[t % 2] = m_tok
        if getattr(ctx, "dbg_h", None) is not None and g0 == 0 and ctx.dbg_h_armed:
            p.join()
            for gi in range(ng):
                p.dma("sync", ctx.dbg_h[gi * 128:(gi + 1) * 128, 0:T], h.chunk(gi))
            p.join()
            ctx.dbg_h_armed = False
        dl = []
        for half in range(0, ng, 4):
            nked = min(4, ng - half)
            dl.append(wst.load(wd, (g0 + half) * 128, nked, 0, 1024))
        for f2 in range(ND):
            for t in range(NSl):
                bank = t if (f2 % 2 == 0) else 4 + t
                mms = []
                dep0 = [dl[-1][2], m_tok]
                for gi in range(ng):
                    dslot, dsl, dtok = dl[gi // 4]
                    mms.append((dsl(gi % 4, f2 * 128, (f2 + 1) * 128),
                                h.sl(gi, t * TS, (t + 1) * TS)))
                gt = acc_group(ctx, bank, mms, deps=dep0)
                for dslot, dsl, dtok in dl:
                    wst.done(dslot, gt)
                if first_bias is not None and g0 == 0:
                    ev = p.op("vector", lambda eng, f2=f2, t=t, bank=bank:
                              eng.scalar_tensor_tensor(
                                  sb.x.sl(f2, t * TS, (t + 1) * TS), ctx.bank(bank),
                                  first_bias[:, f2:f2 + 1],
                                  sb.x.sl(f2, t * TS, (t + 1) * TS), ALU.add, ALU.add),
                              deps=[gt])
                else:
                    ev = p.op("vector", lambda eng, f2=f2, t=t, bank=bank:
                              eng.tensor_tensor(
                                  sb.x.sl(f2, t * TS, (t + 1) * TS),
                                  sb.x.sl(f2, t * TS, (t + 1) * TS),
                                  ctx.bank(bank), ALU.add),
                              deps=[gt])
                bank_read(ctx, bank, ev)
    p.join()



def proj_fm(ctx, *, wdram, wr0, nk, xb, fch_list, wst, evac, extra=None,
            t_list=None, ldeps=()):
    p = ctx.p
    NSl = ctx.NS if t_list is None else None
    ts = range(ctx.NS) if t_list is None else t_list
    i = 0
    for fb0 in range(0, len(fch_list), 4):
        fgrp = fch_list[fb0:fb0 + 4]
        slot, wsl, wtok = wst.load(wdram, wr0, nk, fgrp[0] * 128, len(fgrp) * 128)
        for mi, f in enumerate(fgrp):
            for t in ts:
                bank = ctx.ring()
                mms = [(wsl(c, mi * 128, (mi + 1) * 128),
                        xb.sl(c, t * TS, (t + 1) * TS)) for c in range(nk)]
                if extra is not None:
                    mms = mms + extra(f, t)
                gt = acc_group(ctx, bank, mms, deps=merge([wtok], list(ldeps)))
                wst.done(slot, gt)
                evac(f, t, bank, gt)
                i += 1



def emit_l0ffn(ctx, sb, dram, wst):
    p = ctx.p
    emit_cast(ctx, sb.x, sb.xb, scale=Q127)
    p.join()
    emit_gated_ffn(ctx, sb, xb=sb.xb, wg=dram["wgT"], wu=dram["wuT"],
                   wd=dram["wdT"], HID=3072, wst=wst, act=AF.Silu,
                   up_scale=Q127)





def emit_router(ctx, sb, *, xb, wr_b, rb_row_bf, ones_row_bf, w_rows):
    p = ctx.p
    T = ctx.T
    NTC = T // 128
    nc = ctx.nc
    rtr = sb.rtr
    rbusy = sb.rtr_busy
    identf = sb.const["ident"]
    for tc in range(NTC):
        sl = (tc % 2) * 32
        bank = ctx.ring()
        mms = [(xb.sl(c, tc * 128, (tc + 1) * 128), wr_b[:, c * 8:c * 8 + 7])
               for c in range(ND)]
        mms.append((ones_row_bf[0:1, :], rb_row_bf[0:1, 0:7]))
        gt = acc_group(ctx, bank, mms, deps=merge([sb.wr_tok, rbusy[tc % 2]]), w=7)
        pb = ctx.ps[0:128, bank * TS: bank * TS + 7]
        pm = rtr[:, sl + 24:sl + 25]
        pmn = rtr[:, sl + 25:sl + 26]
        ssum = rtr[:, sl + 26:sl + 27]
        rec = rtr[:, sl + 27:sl + 28]
        m1 = rtr[:, sl + 28:sl + 29]
        m2 = rtr[:, sl + 29:sl + 30]
        s12 = rtr[:, sl + 30:sl + 31]
        rec2 = rtr[:, sl + 31:sl + 32]
        pe_ = rtr[:, sl + 0:sl + 7]
        msk = rtr[:, sl + 8:sl + 15]
        tmp7 = rtr[:, sl + 16:sl + 23]
        p.sop("vector", lambda eng, pb=pb, pm=pm: eng.reduce_max(pm, pb, AX.X),
              deps=[gt])
        v1 = p.sop("vector", lambda eng, pm=pm, pmn=pmn: eng.tensor_scalar_mul(pmn, pm, -1.0))
        et = p.op("scalar", lambda eng, pb=pb, pe_=pe_, pmn=pmn: eng.activation(
            pe_, pb, AF.Exp, bias=pmn), deps=[v1])
        bank_read(ctx, bank, et)
        p.sop("vector", lambda eng, pe_=pe_, ssum=ssum: eng.reduce_sum(ssum, pe_, AX.X),
              deps=[et])
        p.sop("vector", lambda eng, ssum=ssum, rec=rec: eng.reciprocal(rec, ssum))
        p.sop("vector", lambda eng, pe_=pe_, rec=rec: eng.tensor_scalar_mul(pe_, pe_, rec))
        p.sop("vector", lambda eng, pe_=pe_, m1=m1: eng.reduce_max(m1, pe_, AX.X))
        p.sop("vector", lambda eng, pe_=pe_, m1=m1, tmp7=tmp7: eng.tensor_single_scalar(
            tmp7, pe_, m1, ALU.is_ge))
        p.sop("vector", lambda eng, tmp7=tmp7: eng.tensor_scalar_mul(tmp7, tmp7, 2.0))
        p.sop("vector", lambda eng, pe_=pe_, tmp7=tmp7, msk=msk: eng.tensor_tensor(
            msk, pe_, tmp7, ALU.subtract))
        p.sop("vector", lambda eng, msk=msk, m2=m2: eng.reduce_max(m2, msk, AX.X))
        p.sop("vector", lambda eng, m1=m1, m2=m2, s12=s12: eng.tensor_tensor(
            s12, m1, m2, ALU.add))
        p.sop("vector", lambda eng, s12=s12, rec2=rec2: eng.reciprocal(rec2, s12))
        p.sop("vector", lambda eng, pe_=pe_, m2=m2, tmp7=tmp7: eng.tensor_single_scalar(
            tmp7, pe_, m2, ALU.is_ge))
        p.sop("vector", lambda eng, pe_=pe_, tmp7=tmp7: eng.tensor_tensor(
            pe_, pe_, tmp7, ALU.mult))
        wt = p.sop("vector", lambda eng, pe_=pe_, rec2=rec2: eng.tensor_scalar_mul(
            pe_, pe_, rec2))
        bank2 = ctx.ring()
        pt = p.op("tensor", lambda eng, pe_=pe_, bank2=bank2: eng.transpose(
            ctx.ps[0:7, bank2 * TS: bank2 * TS + 128], pe_, identf[:, :]),
            deps=merge([wt, ctx.bank_free[bank2]]))
        ctx.bank_ready[bank2] = pt
        ctx.bank_free[bank2] = None
        cb = p.op("vector", lambda eng, tc=tc, bank2=bank2: eng.tensor_copy(
            w_rows[0:7, tc * 128:(tc + 1) * 128],
            ctx.ps[0:7, bank2 * TS: bank2 * TS + 128]), deps=[pt])
        bank_read(ctx, bank2, cb)
        rbusy[tc % 2] = cb
    p.join()


def emit_moe_experts(ctx, sb, dram, wst, n_routed=7):
    p = ctx.p
    emit_gated_ffn(ctx, sb, xb=sb.xb, wg=dram["sgT"], wu=dram["suT"],
                   wd=dram["sdT"], HID=2816, wst=wst, act=AF.Silu)
    sel7 = sb.const["SEL7"]
    for e in range(n_routed):
        for t in range(ctx.NS):
            bank = ctx.ring()
            bt = acc_group(ctx, bank, [(sel7[0:7, e * 128:(e + 1) * 128],
                                        sb.w_rows[0:7, t * TS:(t + 1) * TS])],
                           deps=[sb.w_rows_tok] if sb.w_rows_tok else ())
            ct = p.op("vector", lambda eng, t=t, bank=bank: eng.tensor_copy(
                sb.wbb[:, t * TS:(t + 1) * TS], ctx.bank(bank)), deps=[bt])
            bank_read(ctx, bank, ct)
        p.join()
        emit_gated_ffn(ctx, sb, xb=sb.xb,
                       wg=dram["rgT"][e * D:(e + 1) * D, :],
                       wu=dram["ruT"][e * D:(e + 1) * D, :],
                       wd=dram["rdT"][e * 2816:(e + 1) * 2816, :],
                       HID=2816, wst=wst, act=AF.Silu, wb_ap=sb.wbb[:, :])
    p.join()




def emit_phi_evac(ctx, sb, dst_ap, bank, gtok, bias_ap, neg_bias_ap):
    p = ctx.p
    i = sb.phi_i % 2
    sb.phi_i += 1
    sA = sb.phi_s[3 * i]
    sB = sb.phi_s[3 * i + 1]
    sC = sb.phi_s[3 * i + 2]
    kw1 = {} if bias_ap is None else {"bias": bias_ap}
    kw2 = {} if neg_bias_ap is None else {"bias": neg_bias_ap}
    r1 = p.op("scalar", lambda eng, kw1=kw1: eng.activation(sA, ctx.bank(bank), AF.Relu, **kw1),
              deps=merge([gtok, sb.phi_busy[i]]))
    r2 = p.op("scalar", lambda eng, kw2=kw2: eng.activation(sB, ctx.bank(bank), AF.Relu,
                                                            scale=-1.0, **kw2))
    e2 = p.op("scalar", lambda eng: eng.activation(sC, sB, AF.Exp, scale=-1.0),
              deps=[r2], strict=True)
    w = p.op("vector", lambda eng: eng.tensor_tensor(sB, sA, sC, ALU.add),
             deps=[e2])
    f1 = p.op("scalar", lambda eng, dst=dst_ap: eng.activation(dst, sB, AF.Copy),
              deps=[w])
    sb.phi_busy[i] = f1
    return f1


def emit_l1_phase_a(ctx, sb, dram, wst, pq_dram, tile):
    p = ctx.p
    nc = ctx.nc
    T = ctx.T
    t0 = tile * T
    NSl = ctx.NS
    emit_cast(ctx, sb.x, sb.xb)
    p.join()
    bq = sb.vec["bq_col"]
    nbq = sb.vec_neg["bq_col"]
    for fb in range(0, ND, 4):
        slot, wsl, wtok = wst.load(dram["wqT"], 0, ND, fb * 128, 512)
        for mf in range(4):
            f = fb + mf
            toks = []
            for t in range(NSl):
                bank = ctx.ring()
                gt = acc_group(ctx, bank, [(wsl(c, mf * 128, (mf + 1) * 128),
                                            sb.xb.sl(c, t * TS, (t + 1) * TS))
                                           for c in range(ND)], deps=[wtok])
                wst.done(slot, gt)
                if getattr(ctx, "dbg_relu_only", False):
                    w = p.op("scalar", lambda eng, f=f, t=t, bank=bank: eng.activation(
                        sb.h.sl(f, t * TS, (t + 1) * TS), ctx.bank(bank), AF.Relu,
                        bias=bq[:, f:f + 1]), deps=[gt])
                else:
                    w = emit_phi_evac(ctx, sb, sb.h.sl(f, t * TS, (t + 1) * TS), bank, gt,
                                      bq[:, f:f + 1], nbq[:, f:f + 1])
                bank_read(ctx, bank, w)
                toks.append(w)
            p.dma("sync", pq_dram[f * 128:(f + 1) * 128, t0:t0 + T], sb.h.chunk(f),
                  deps=toks)
    NTC = T // 128
    bkb = sb.vecrow["bk_row"]
    bvb = sb.vecrow["bv_row"]
    onesr = sb.ones_row_bf
    onesc = sb.ones_bf
    for half in range(2):
        kslot, ksl, ktok = wst.load(dram["wkT"], 0, ND, half * 512, 512)
        vslot, vsl, vtok = wst.load(dram["wvT"], 0, ND, half * 512, 512)
        for tc in range(NTC):
            ktm = sb.ktm[tc % 2]
            vtm = sb.vtm[tc % 2]
            kv_toks = []
            bank = ctx.ring()
            mms = [(sb.xb.sl(c, tc * 128, (tc + 1) * 128),
                    ksl(c, 0, 512)) for c in range(ND)]
            mms.append((onesr[0:1, :], bkb[0:1, half * 512:(half + 1) * 512]))
            gt = acc_group(ctx, bank, mms, deps=[ktok, sb.ktm_busy[tc % 2]])
            wst.done(kslot, gt)
            w = emit_phi_evac_tm(ctx, sb, ktm[:, 0:512], bank, gt)
            bank_read(ctx, bank, w)
            kv_toks.append(w)
            bank = ctx.ring()
            mms = [(sb.xb.sl(c, tc * 128, (tc + 1) * 128),
                    vsl(c, 0, 512)) for c in range(ND)]
            mms.append((onesr[0:1, :], bvb[0:1, half * 512:(half + 1) * 512]))
            gt = acc_group(ctx, bank, mms, deps=[vtok, sb.vtm_busy[tc % 2]])
            wst.done(vslot, gt)
            w = p.op("scalar", lambda eng, vtm=vtm, bank=bank: eng.activation(
                vtm[:, 0:512], ctx.bank(bank), AF.Copy), deps=[gt])
            bank_read(ctx, bank, w)
            kv_toks.append(w)
            bank = ctx.ring()
            fdep = merge(kv_toks, [ctx.bank_free[bank]])
            last = None
            for hh in range(8):
                h = half * 8 + hh
                p0 = (h % 2) * 64
                c0 = (h // 2) * 64
                last = p.op("tensor",
                            lambda eng, hh=hh, p0=p0, c0=c0, bank=bank, ktm=ktm, vtm=vtm:
                            eng.matmul(ctx.ps[p0:p0 + 64, bank * TS + c0: bank * TS + c0 + 64],
                                       ktm[:, hh * 64:(hh + 1) * 64],
                                       vtm[:, hh * 64:(hh + 1) * 64],
                                       start=True, stop=True),
                            deps=fdep if hh == 0 else (), inc=(1 if hh == 7 else 0))
            ctx.bank_ready[bank] = last
            ctx.bank_free[bank] = None
            cola = (half * 8 // 2) * 64
            at = p.op("vector", lambda eng, bank=bank, cola=cola: eng.tensor_tensor(
                sb.kvt[:, cola:cola + 256], sb.kvt[:, cola:cola + 256],
                ctx.ps[0:128, bank * TS + cola: bank * TS + cola + 256], ALU.add),
                deps=[last])
            bank_read(ctx, bank, at)
            bank = ctx.ring()
            fdep = merge(kv_toks, [ctx.bank_free[bank]])
            last = None
            for db in range(4):
                last = p.op("tensor", lambda eng, db=db, bank=bank, ktm=ktm:
                            eng.matmul(ctx.ps[0:128, bank * TS + db: bank * TS + db + 1],
                                       ktm[:, db * 128:(db + 1) * 128], onesc[:, 0:1],
                                       start=True, stop=True),
                            deps=fdep if db == 0 else (), inc=(1 if db == 3 else 0))
            ctx.bank_ready[bank] = last
            ctx.bank_free[bank] = None
            at2 = p.sop("vector", lambda eng, bank=bank, half=half: eng.tensor_tensor(
                sb.ksum_col[:, half * 4:(half + 1) * 4],
                sb.ksum_col[:, half * 4:(half + 1) * 4],
                ctx.ps[0:128, bank * TS: bank * TS + 4], ALU.add), deps=[at, last])
            bank_read(ctx, bank, at2)
            sb.ktm_busy[tc % 2] = last
            sb.vtm_busy[tc % 2] = last
    p.join()


def emit_phi_evac_tm(ctx, sb, dst_ap, bank, gtok):
    return emit_phi_evac(ctx, sb, dst_ap, bank, gtok, None, None)


def emit_l1_phase_b(ctx, sb, dram, wst, pq_dram, xin_dram, tile, out_bf=None):
    p = ctx.p
    nc = ctx.nc
    T = ctx.T
    t0 = tile * T
    NSl = ctx.NS
    qdep = [("Ssy", p.cnt["Ssy"])] if p.cnt["Ssy"] else []
    for c in range(ND):
        p.dma("sync", sb.x.chunk(c), xin_dram[c * 128:(c + 1) * 128, t0:t0 + T],
              deps=qdep if c == 0 else (), strict=True)
        p.dma("sync", sb.h.chunk(c), pq_dram[c * 128:(c + 1) * 128, t0:t0 + T])
    p.join()
    for t in range(NSl):
        bank = ctx.ring()
        gt = acc_group(ctx, bank, [(sb.ks_cb.chunk(c), sb.h.sl(c, t * TS, (t + 1) * TS))
                                   for c in range(ND)], deps=[sb.ks_tok], p0=0, p1=16)
        zt = p.op("vector", lambda eng, t=t, bank=bank: eng.reciprocal(
            sb.zrow[0:16, t * TS:(t + 1) * TS], ctx.bank(bank, 0, 16)), deps=[gt])
        bank_read(ctx, bank, zt)
    p.join()
    sel16 = sb.const["SEL16"]
    pq_toks = {}
    for c in range(ND):
        for t in range(NSl):
            bank = ctx.ring()
            bt = acc_group(ctx, bank, [(sel16[0:16, c * 128:(c + 1) * 128],
                                        sb.zrow[0:16, t * TS:(t + 1) * TS])])
            w = p.op("vector", lambda eng, c=c, t=t, bank=bank: eng.tensor_tensor(
                sb.xb.sl(c, t * TS, (t + 1) * TS), sb.h.sl(c, t * TS, (t + 1) * TS),
                ctx.bank(bank), ALU.mult), deps=[bt])
            bank_read(ctx, bank, w)
            pq_toks[(c, t)] = w
    kb = p.op("scalar", lambda eng: eng.activation(sb.kvt_bf[:, :], sb.kvt[:, :], AF.Copy))
    for fc in range(ND):
        for t in range(NSl):
            bank = ctx.ring()
            dep0 = merge([kb, pq_toks[(fc, t)], ctx.bank_free[bank]])
            last = None
            for half in range(2):
                h = 2 * fc + half
                p0 = half * 64
                c0 = (h // 2) * 64
                last = p.op("tensor", lambda eng, p0=p0, c0=c0, bank=bank, fc=fc, t=t:
                            eng.matmul(ctx.ps[p0:p0 + 64, bank * TS: bank * TS + TS],
                                       sb.kvt_bf[p0:p0 + 64, c0:c0 + 64],
                                       sb.xb.h[p0:p0 + 64,
                                               fc * T + t * TS: fc * T + (t + 1) * TS],
                                       start=True, stop=True),
                            deps=dep0 if half == 0 else (), inc=(1 if half == 1 else 0))
            ctx.bank_ready[bank] = last
            ctx.bank_free[bank] = None
            ev = p.op("scalar", lambda eng, fc=fc, t=t, bank=bank: eng.activation(
                sb.h.sl(fc, t * TS, (t + 1) * TS), ctx.bank(bank), AF.Copy), deps=[last])
            bank_read(ctx, bank, ev)
    p.join()
    bd = sb.vec["bd_col"]
    for fb in range(0, ND, 4):
        slot, wsl, wtok = wst.load(dram["wdT"], 0, ND, fb * 128, 512)
        for mf in range(4):
            f = fb + mf
            for t in range(NSl):
                bank = ctx.ring()
                gt = acc_group(ctx, bank, [(wsl(c, mf * 128, (mf + 1) * 128),
                                            sb.h.sl(c, t * TS, (t + 1) * TS))
                                           for c in range(ND)], deps=[wtok])
                wst.done(slot, gt)
                ev = p.op("vector", lambda eng, f=f, t=t, bank=bank:
                          eng.scalar_tensor_tensor(
                              sb.x.sl(f, t * TS, (t + 1) * TS), ctx.bank(bank),
                              bd[:, f:f + 1], sb.x.sl(f, t * TS, (t + 1) * TS),
                              ALU.add, ALU.add), deps=[gt])
                bank_read(ctx, bank, ev)
    p.join()
    xs2 = sb.xs2

    def add_f32(c, t, xin=xin_dram, t0=t0):
        dep = [("Sdv", ctx.p.cnt["Sdv"])] if ctx.p.cnt["Sdv"] else []
        tok = ctx.p.dma("sync", xs2[:, c * TS:(c + 1) * TS],
                        xin[c * 128:(c + 1) * 128, t0 + t * TS: t0 + (t + 1) * TS],
                        deps=dep, strict=True)
        return xs2[:, c * TS:(c + 1) * TS], tok
    emit_ln(ctx, sb, g=sb.vec["l1g"], b=sb.vec["l1b"], eps=EPS_L1LN, add_f32=add_f32)
    emit_ln(ctx, sb, eps=EPS_TOP, out_bf=out_bf)




def setup_l1(ctx, sb, dram):
    nc, p = ctx.nc, ctx.p
    T = ctx.T
    sb.ktm = [nc.alloc_sbuf_tensor(f"ktm{i}", [128, 512], BF16) for i in range(2)]
    sb.vtm = [nc.alloc_sbuf_tensor(f"vtm{i}", [128, 512], BF16) for i in range(2)]
    sb.ktm_busy = [None, None]
    sb.vtm_busy = [None, None]
    phi_h = nc.alloc_sbuf_tensor("phi_h", [128, 6 * TS], F32)
    sb.phi_s = [phi_h[:, i * TS:(i + 1) * TS] for i in range(6)]
    sb.phi_busy = [None, None]
    sb.phi_i = 0
    sb.kvt = nc.alloc_sbuf_tensor("kvt", [128, 512], F32)
    sb.kvt_bf = sb.ktm[0][:, :]
    sb.ksum_col = nc.alloc_sbuf_tensor("ksum_col", [128, 8], F32)
    sb.zrow = phi_h[0:16, 0:T]
    sb.ks_cb = CBuf(nc, "ks_cb", ND, 16, BF16)
    sb.ks_tok = None
    sb.xs2 = sb.h.h[:, 4 * T:8 * T].bitcast(F32)
    p.op("gpsimd", lambda eng: eng.memset(sb.kvt[:, :], 0.0), inc=0)
    mt = p.op("gpsimd", lambda eng: eng.memset(sb.ksum_col[:, :], 0.0))
    sb.kv_zero_tok = mt
    p.join()
    sb.vec_neg = {}
    for nm in ("bq_col",):
        h = nc.alloc_sbuf_tensor("vn_" + nm, [128, 8], F32)
        p.op("vector", lambda eng, nm=nm, h=h: eng.tensor_scalar_mul(
            h[:, :], sb.vec[nm][:, :], -1.0))
        sb.vec_neg[nm] = h
    p.join()


def emit_ks_build(ctx, sb):
    p = ctx.p
    nc = ctx.nc
    mz = p.op("gpsimd", lambda eng: eng.memset(sb.ks_cb.h[:, :], 0.0))
    ke = p.op("vector", lambda eng: eng.tensor_scalar_add(
        sb.ksum_col[:, :], sb.ksum_col[:, :], float(EPS_ATTN)))
    last = None
    for c in range(ND):
        p.op("scalar", lambda eng, c=c: eng.copy(
            sb.ks_cb.h[0:64, c * 16 + 2 * c: c * 16 + 2 * c + 1],
            sb.ksum_col[0:64, c:c + 1]), deps=[ke, mz], inc=0)
        last = p.op("scalar", lambda eng, c=c: eng.copy(
            sb.ks_cb.h[64:128, c * 16 + 2 * c + 1: c * 16 + 2 * c + 2],
            sb.ksum_col[64:128, c:c + 1]))
    sb.ks_tok = last
    p.join()


def build_l1_program(T, debug=None):
    nc = bass.Bass("TRN2", target_bir_lowering=False, debug=False)
    p = Prog(nc)
    ctx = Ctx(nc, p, T)
    consts = host_consts()
    sb = SB(ctx, T, consts, need=("SEL16",))
    if debug == "relu":
        ctx.dbg_relu_only = True
        debug = "a"
    xin = nc.dram_tensor("s_in", [D, SEQ], F32, kind="ExternalInput").ap()
    xout = nc.dram_tensor("s_out", [D, SEQ], F32, kind="ExternalOutput").ap()
    pq = nc.dram_tensor("pq_scr", [D, SEQ], BF16,
                        kind=("ExternalOutput" if debug == "a" else "Internal")).ap()
    if debug == "a":
        dbg_kvt = nc.dram_tensor("dbg_kvt", [128, 512], F32, kind="ExternalOutput").ap()
        dbg_ks = nc.dram_tensor("dbg_ks", [128, 8], F32, kind="ExternalOutput").ap()
    dram = {}
    for nm, dn in (("wqT", "l1_wq"), ("wkT", "l1_wk"), ("wvT", "l1_wv"), ("wdT", "l1_wd")):
        dram[nm] = nc.dram_tensor(dn, [D, D], F32, kind="ExternalInput").ap()
    load_vec_inputs(ctx, sb, {"bq_col": (128, 8), "bd_col": (128, 8),
                              "l1g": (128, 8), "l1b": (128, 8),
                              "bk_row": (1, 1024), "bv_row": (1, 1024)},
                    bf_rows=("bk_row", "bv_row"))
    setup_l1(ctx, sb, dram)
    wst = WStream(ctx, "w", nbuf=2)
    for tile in range(SEQ // T):
        t0 = tile * T
        for c in range(ND):
            p.dma("sync", sb.x.chunk(c), xin[c * 128:(c + 1) * 128, t0:t0 + T],
                  strict=True)
        p.join()
        emit_l1_phase_a(ctx, sb, dram, wst, pq, tile)
    if debug == "a":
        p.dma("sync", dbg_kvt[:, :], sb.kvt[:, :])
        p.dma("sync", dbg_ks[:, :], sb.ksum_col[:, :])
        p.dma("sync", xout[0:128, 0:T], sb.x.chunk(0))
        p._wait("gpsimd", "Ssy", p.cnt["Ssy"])
        p.build()
        return nc
    emit_ks_build(ctx, sb)
    prev_store = None
    for tile in range(SEQ // T):
        t0 = tile * T
        emit_l1_phase_b(ctx, sb, dram, wst, pq, xin, tile)
        for c in range(ND):
            prev_store = p.dma("sync", xout[c * 128:(c + 1) * 128, t0:t0 + T],
                               sb.x.chunk(c))
        p.join()
    p._wait("gpsimd", "Ssy", p.cnt["Ssy"])
    p.build()
    return nc





def setup_l0(ctx, sb, consts):
    nc, p = ctx.nc, ctx.p
    sb.dt_b = nc.alloc_sbuf_tensor("dt_b", [128, NH * CH], BF16)
    sb.qd_b = nc.alloc_sbuf_tensor("qd_b", [128, ND * CH], BF16)
    sb.kd_b = nc.alloc_sbuf_tensor("kd_b", [128, 2 * 512], BF16)
    sb.gc_f = nc.alloc_sbuf_tensor("gc_f", [128, 512], F32)
    for name, dst, w in (("DT", sb.dt_b, CH), ("QD", sb.qd_b, CH), ("KD", sb.kd_b, 512)):
        arr = consts[name]
        dr = nc.inline_tensor(np.ascontiguousarray(arr), "ic_" + name)
        n = arr.shape[0] // 128
        for i in range(n):
            p.dma("gpsimd", dst[:, i * w:(i + 1) * w], dr.ap()[i * 128:(i + 1) * 128, :])
    drg = nc.inline_tensor(np.ascontiguousarray(consts["GC"]), "ic_GC")
    p.dma("sync", sb.gc_f[:, :], drg.ap())
    sb.S = [nc.alloc_sbuf_tensor(f"S{i}", [128, 512], F32) for i in range(2)]
    sb.S_bf = [nc.alloc_sbuf_tensor(f"Sbf{i}", [128, 512], BF16) for i in range(2)]
    sb.S_tok = [None, None]
    sb.at_bf = [nc.alloc_sbuf_tensor(f"atbf{i}", [128, CH], BF16) for i in range(2)]
    sb.at_busy = [None, None]
    sb.rr = nc.alloc_sbuf_tensor("rr", [128, 4 * 4096], BF16)
    sb.pbrow = nc.alloc_sbuf_tensor("pbrow", [1, 2048], BF16)
    sb.krt = [nc.alloc_sbuf_tensor(f"krt{i}", [128, 1024], BF16) for i in range(4)]
    sb.vrt = [nc.alloc_sbuf_tensor(f"vrt{i}", [128, 1024], BF16) for i in range(4)]
    sb.krt_busy = [None] * 4
    sb.vrt_busy = [None] * 4
    for i in range(2):
        p.op("gpsimd", lambda eng, i=i: eng.memset(sb.S[i][:, :], 0.0), inc=0)
        p.op("gpsimd", lambda eng, i=i: eng.memset(sb.S_bf[i][:, :], 0.0))
    p.join()


def emit_retention_mixer(ctx, sb, dram, wst, layer):
    p = ctx.p
    T = ctx.T
    pw = dram[f"pw{layer}"]
    rw = dram[f"rw{layer}"]
    pbc = sb.vec[f"pb{layer}_col"]
    pbr = sb.pbrow
    p.dma("gpsimd", pbr[0:1, :], dram[f"pbrow{layer}"][0:1, :], deps=p.now())
    rbc = sb.vec[f"rb{layer}"]
    onesr = sb.ones_row_bf
    S = sb.S[layer]
    S_bf = sb.S_bf[layer]
    h = sb.h
    rr = sb.rr

    def reg(base, fc, a, b):
        r = base // 2
        return rr[:, r * 4096 + fc * 512 + a: r * 4096 + fc * 512 + b]

    NSC = T // 512
    for sc in range(NSC):
        tok0 = sc * 512
        for colbase, kind in ((0, "q"), (1024, "k"), (3072, "xo")):
            for fb in (0, 4):
                slot, wsl, wtok = wst.load(pw, 0, ND, colbase + fb * 128, 512)
                for mf in range(4):
                    f = fb + mf
                    fg = (colbase // 128) + f
                    bank = ctx.ring()
                    gt = acc_group(ctx, bank,
                                   [(wsl(c, mf * 128, (mf + 1) * 128),
                                     sb.xb.sl(c, tok0, tok0 + 512)) for c in range(ND)],
                                   deps=[wtok])
                    wst.done(slot, gt)
                    pb_ap = pbc[:, fg:fg + 1]
                    pk = ctx.bank(bank)
                    if kind == "q":
                        qd_ap = sb.qd_b[:, f * CH:(f + 1) * CH]
                        for tci4 in range(4):
                            dst = reg(0, f, tci4 * CH, (tci4 + 1) * CH)
                            pk4 = ctx.ps[0:128, bank * TS + tci4 * CH: bank * TS + (tci4 + 1) * CH]
                            ev = p.op("vector",
                                      lambda eng, dst=dst, pk4=pk4, pb_ap=pb_ap, qd_ap=qd_ap:
                                      eng.scalar_tensor_tensor(dst, pk4, pb_ap, qd_ap,
                                                               ALU.add, ALU.mult),
                                      deps=[gt] if tci4 == 0 else ())
                    elif kind == "k":
                        dst = reg(2, f, 0, 512)
                        ev = p.op("vector", lambda eng, dst=dst, pk=pk, pb_ap=pb_ap:
                                  eng.tensor_scalar_add(dst, pk, pb_ap), deps=[gt])
                    else:
                        dst = reg(4, f, 0, 512)
                        ev = p.op("scalar", lambda eng, dst=dst, pk=pk, pb_ap=pb_ap:
                                  eng.activation(dst, pk, AF.Silu, bias=pb_ap), deps=[gt])
                    bank_read(ctx, bank, ev)
        for half in range(2):
            kslot, ksl, ktok = wst.load(pw, 0, ND, 1024 + half * 512, 512)
            vslot, vsl, vtok = wst.load(pw, 0, ND, 2048 + half * 512, 512)
            for tci in range(4):
                tc = sc * 4 + tci
                krt = sb.krt[tci]
                vrt = sb.vrt[tci]
                bank = ctx.ring()
                mms = [(sb.xb.sl(c, tc * 128, (tc + 1) * 128), ksl(c, 0, 512))
                       for c in range(ND)]
                mms.append((onesr[0:1, :], pbr[0:1, half * 512:(half + 1) * 512]))
                gt = acc_group(ctx, bank, mms, deps=[ktok, sb.krt_busy[tci]])
                wst.done(kslot, gt)
                dst = krt[:, half * 512:(half + 1) * 512]
                kd_ap = sb.kd_b[:, half * 512:(half + 1) * 512]
                pk = ctx.bank(bank)
                ev = p.op("vector", lambda eng, dst=dst, pk=pk, kd_ap=kd_ap:
                          eng.tensor_tensor(dst, pk, kd_ap, ALU.mult), deps=[gt])
                bank_read(ctx, bank, ev)
                bank = ctx.ring()
                mms = [(sb.xb.sl(c, tc * 128, (tc + 1) * 128), vsl(c, 0, 512))
                       for c in range(ND)]
                mms.append((onesr[0:1, :], pbr[0:1, 1024 + half * 512:1024 + (half + 1) * 512]))
                gt = acc_group(ctx, bank, mms, deps=[vtok, sb.vrt_busy[tci]])
                wst.done(vslot, gt)
                dst = vrt[:, half * 512:(half + 1) * 512]
                pk = ctx.bank(bank)
                ev = p.op("scalar", lambda eng, dst=dst, pk=pk:
                          eng.activation(dst, pk, AF.Copy), deps=[gt])
                bank_read(ctx, bank, ev)
        p.join()
        ctx.ring_banks = [0, 1, 2, 3, 4]
        ret2_evs = []
        for tci in range(4):
            tc = sc * 4 + tci
            cofs = tci * 128
            krt = sb.krt[tci]
            vrt = sb.vrt[tci]
            ret_banks = [6, 7]
            ret_last = {}
            for hh in range(NH):
                hp = (hh % 2) * 64
                fc = hh // 2
                k_ap = rr[hp:hp + 64, 1 * 4096 + fc * 512 + cofs: 1 * 4096 + fc * 512 + cofs + 128]
                q_ap = rr[hp:hp + 64, 0 * 4096 + fc * 512 + cofs: 0 * 4096 + fc * 512 + cofs + 128]
                bank = ctx.ring()
                at_out = ctx.bank(bank, 0, 128, CH)
                d0 = merge([ctx.bank_free[bank], sb.at_busy[hh % 2]])
                at_mm = p.op("tensor", lambda eng, at_out=at_out, k_ap=k_ap, q_ap=q_ap:
                             eng.matmul(at_out, k_ap, q_ap, start=True, stop=True),
                             deps=d0)
                ctx.bank_ready[bank] = at_mm
                ctx.bank_free[bank] = None
                atb = sb.at_bf[hh % 2]
                dt_ap = sb.dt_b[:, hh * CH:(hh + 1) * CH]
                at_ev = p.op("vector", lambda eng, atb=atb, at_out=at_out, dt_ap=dt_ap:
                             eng.tensor_tensor(atb[:, :], at_out, dt_ap, ALU.mult),
                             deps=[at_mm])
                bank_read(ctx, bank, at_ev)
                sb.at_busy[hh % 2] = at_ev
                rbank = ret_banks[hh // 8]
                cblk = ((hh // 2) % 4) * 128
                r_out = ctx.ps[hp:hp + 64, rbank * TS + cblk: rbank * TS + cblk + 128]
                v_ap = vrt[:, hh * 64:(hh + 1) * 64]
                s_ap = S_bf[hp:hp + 64, fc * 64:fc * 64 + 64]
                rdep = merge([at_ev, sb.S_tok[layer]] +
                             ([ctx.bank_free[rbank]] if hh % 8 == 0 else []))
                p.op("tensor", lambda eng, r_out=r_out, v_ap=v_ap, atb=atb:
                     eng.matmul(r_out, v_ap, atb[:, :], start=True, stop=False),
                     deps=rdep, inc=0)
                rt = p.op("tensor", lambda eng, r_out=r_out, s_ap=s_ap, q_ap=q_ap:
                          eng.matmul(r_out, s_ap, q_ap, start=False, stop=True))
                ret_last[hh] = rt
                if hh % 8 == 7:
                    ctx.bank_ready[rbank] = rt
                    ctx.bank_free[rbank] = None
            for fc in range(ND):
                rbank = ret_banks[fc // 4]
                cblk = (fc % 4) * 128
                dst = reg(6, fc, cofs, cofs + 128)
                ps_ap = ctx.ps[0:128, rbank * TS + cblk: rbank * TS + cblk + 128]
                xo_ap = rr[:, 2 * 4096 + fc * 512 + cofs: 2 * 4096 + fc * 512 + cofs + 128]
                ev = p.op("vector", lambda eng, dst=dst, ps_ap=ps_ap, xo_ap=xo_ap:
                          eng.tensor_tensor(dst, ps_ap, xo_ap, ALU.mult),
                          deps=[ret_last[fc * 2 + 1]])
                bank_read(ctx, rbank, ev)
                ret2_evs.append(ev)
            ibank = 5
            ilast = None
            for hh in range(NH):
                hp = (hh % 2) * 64
                c0 = (hh // 2) * 64
                i_out = ctx.ps[hp:hp + 64, ibank * TS + c0: ibank * TS + c0 + 64]
                kk = krt[:, hh * 64:(hh + 1) * 64]
                vv = vrt[:, hh * 64:(hh + 1) * 64]
                d0 = merge([ctx.bank_free[ibank]]) if hh == 0 else []
                ilast = p.op("tensor", lambda eng, i_out=i_out, kk=kk, vv=vv:
                             eng.matmul(i_out, kk, vv, start=True, stop=True),
                             deps=d0, inc=(1 if hh == NH - 1 else 0))
            ctx.bank_ready[ibank] = ilast
            ctx.bank_free[ibank] = None
            sd = merge([ilast, sb.S_tok[layer]] + [ret_last[hh] for hh in range(NH)])
            p.op("vector", lambda eng: eng.tensor_tensor(S[:, :], S[:, :], sb.gc_f[:, :],
                                                         ALU.mult), deps=sd, inc=0)
            s2 = p.op("vector", lambda eng, ibank=ibank: eng.tensor_tensor(
                S[:, :], S[:, :], ctx.bank(ibank), ALU.add))
            bank_read(ctx, ibank, s2)
            sbf = p.op("scalar", lambda eng: eng.activation(S_bf[:, :], S[:, :], AF.Copy),
                       deps=[s2])
            sb.S_tok[layer] = sbf
            sb.krt_busy[tci] = ilast
            sb.vrt_busy[tci] = ilast
        ctx.ring_banks = None
        for fb in (0, 4):
            slot, wsl, wtok = wst.load(rw, 0, ND, fb * 128, 512)
            for mf in range(4):
                f = fb + mf
                bank = ctx.ring()
                gt = acc_group(ctx, bank,
                               [(wsl(c, mf * 128, (mf + 1) * 128),
                                 reg(6, c, 0, 512)) for c in range(ND)],
                               deps=merge([wtok], ret2_evs))
                wst.done(slot, gt)
                x_ap = sb.x.sl(f, tok0, tok0 + 512)
                pk = ctx.bank(bank)
                rb_ap = rbc[:, f:f + 1]
                ev = p.op("vector", lambda eng, x_ap=x_ap, pk=pk, rb_ap=rb_ap:
                          eng.scalar_tensor_tensor(x_ap, pk, rb_ap, x_ap,
                                                   ALU.add, ALU.add), deps=[gt])
                bank_read(ctx, bank, ev)
    p.join()


def emit_plain_ffn(ctx, sb, *, xb, w1, w2, b1_col, b2_col, HID, wst, act=AF.Gelu):
    p = ctx.p
    T = ctx.T
    NSl = ctx.NS
    NHC = HID // 128
    h = sb.h
    for g0 in range(0, NHC, 8):
        ng = min(8, NHC - g0)
        gl = {}
        m_tok = None
        for gi in range(ng):
            half = gi // 4
            if half not in gl:
                nw = min(4, ng - half * 4) * 128
                gl[half] = wst.load(w1, 0, ND, (g0 + half * 4) * 128, nw)
            slot, wsl, wtok = gl[half]
            gcol = (gi % 4) * 128
            bg = 4 + (gi % 2)
            for t in range(NSl):
                gt = acc_group(ctx, bg, [(wsl(c, gcol, gcol + 128),
                                          xb.sl(c, t * TS, (t + 1) * TS)) for c in range(ND)],
                               deps=[wtok])
                wst.done(slot, gt)
                m_tok = p.op("scalar", lambda eng, gi=gi, t=t, bg=bg, g0=g0: eng.activation(
                    h.sl(gi, t * TS, (t + 1) * TS), ctx.bank(bg), act,
                    bias=b1_col[:, g0 + gi:g0 + gi + 1]), deps=[gt])
                bank_read(ctx, bg, m_tok)
        dl = []
        for half in range(0, ng, 4):
            nked = min(4, ng - half)
            dl.append(wst.load(w2, (g0 + half) * 128, nked, 0, 1024))
        for f2 in range(ND):
            for t in range(NSl):
                bank = t if (f2 % 2 == 0) else 4 + t
                mms = []
                dep0 = [dl[-1][2], m_tok]
                for gi in range(ng):
                    dslot, dsl, dtok = dl[gi // 4]
                    mms.append((dsl(gi % 4, f2 * 128, (f2 + 1) * 128),
                                h.sl(gi, t * TS, (t + 1) * TS)))
                gt = acc_group(ctx, bank, mms, deps=dep0)
                for dslot, dsl, dtok in dl:
                    wst.done(dslot, gt)
                if g0 == 0:
                    ev = p.op("vector", lambda eng, f2=f2, t=t, bank=bank:
                              eng.scalar_tensor_tensor(
                                  sb.x.sl(f2, t * TS, (t + 1) * TS), ctx.bank(bank),
                                  b2_col[:, f2:f2 + 1],
                                  sb.x.sl(f2, t * TS, (t + 1) * TS), ALU.add, ALU.add),
                              deps=[gt])
                else:
                    ev = p.op("vector", lambda eng, f2=f2, t=t, bank=bank:
                              eng.tensor_tensor(
                                  sb.x.sl(f2, t * TS, (t + 1) * TS),
                                  sb.x.sl(f2, t * TS, (t + 1) * TS),
                                  ctx.bank(bank), ALU.add), deps=[gt])
                bank_read(ctx, bank, ev)
    p.join()


def emit_l0attn(ctx, sb, dram, wst, xin_dram, tile):
    p = ctx.p
    T = ctx.T
    t0 = tile * T
    dbg = getattr(ctx, "l0_dbg", None)
    for layer in range(2):
        emit_ln(ctx, sb, g=sb.vec[f"ln1g{layer}"], b=sb.vec[f"ln1b{layer}"],
                eps=EPS_TOP, out_bf=sb.xb, bf_only=True)
        if dbg is not None and layer == 0 and tile == 0 and "lnx" in dbg:
            p.join()
            for c in range(ND):
                p.dma("sync", dbg["lnx"][c * 128:(c + 1) * 128, 0:T], sb.xb.chunk(c))
            p.join()
        emit_retention_mixer(ctx, sb, dram, wst, layer)
        if dbg is not None and layer == 0 and tile == 0 and "mix" in dbg:
            p.join()
            for c in range(ND):
                p.dma("sync", dbg["mix"][c * 128:(c + 1) * 128, 0:T], sb.x.chunk(c))
            p.join()
        emit_ln(ctx, sb, g=sb.vec[f"ln2g{layer}"], b=sb.vec[f"ln2b{layer}"],
                eps=EPS_TOP, out_bf=sb.xb, bf_only=True)
        emit_plain_ffn(ctx, sb, xb=sb.xb, w1=dram[f"f1w{layer}"], w2=dram[f"f2w{layer}"],
                       b1_col=sb.vec[f"f1b{layer}"], b2_col=sb.vec[f"f2b{layer}"],
                       HID=4096, wst=wst, act=AF.Gelu)
        if dbg is not None and layer == 0 and tile == 0 and "ffn" in dbg:
            p.join()
            for c in range(ND):
                p.dma("sync", dbg["ffn"][c * 128:(c + 1) * 128, 0:T], sb.x.chunk(c))
            p.join()
    def add_f32(c, t, xin=xin_dram, t0=t0):
        dep = [("Sdv", p.cnt["Sdv"])] if p.cnt["Sdv"] else []
        tok = p.dma("sync", sb.xs2[:, c * TS:(c + 1) * TS],
                    xin[c * 128:(c + 1) * 128, t0 + t * TS: t0 + (t + 1) * TS],
                    deps=dep, strict=True)
        return sb.xs2[:, c * TS:(c + 1) * TS], tok
    emit_ln(ctx, sb, g=sb.vec["lnfg"], b=sb.vec["lnfb"], eps=EPS_TOP, add_f32=add_f32)
    emit_ln(ctx, sb, eps=EPS_TOP)


def build_l0attn_program(T, debug=()):
    nc = bass.Bass("TRN2", target_bir_lowering=False, debug=False)
    p = Prog(nc)
    ctx = Ctx(nc, p, T)
    consts = host_consts()
    sb = SB(ctx, T, consts)
    xin = nc.dram_tensor("s_in", [D, SEQ], F32, kind="ExternalInput").ap()
    xout = nc.dram_tensor("s_out", [D, SEQ], F32, kind="ExternalOutput").ap()
    if debug:
        ctx.l0_dbg = {}
        for nm in debug:
            dt_ = BF16 if nm == "lnx" else F32
            ctx.l0_dbg[nm] = nc.dram_tensor("dbg_" + nm, [D, T], dt_, kind="ExternalOutput").ap()
    dram = {}
    for i in range(2):
        dram[f"pw{i}"] = nc.dram_tensor(f"pw{i}", [D, 4 * D], F32, kind="ExternalInput").ap()
        dram[f"rw{i}"] = nc.dram_tensor(f"rw{i}", [D, D], F32, kind="ExternalInput").ap()
        dram[f"f1w{i}"] = nc.dram_tensor(f"f1w{i}", [D, 4 * D], F32, kind="ExternalInput").ap()
        dram[f"f2w{i}"] = nc.dram_tensor(f"f2w{i}", [4 * D, D], F32, kind="ExternalInput").ap()
    vecs = {"lnfg": (128, 8), "lnfb": (128, 8)}
    rows = []
    for i in range(2):
        for nm in (f"rb{i}", f"f2b{i}", f"ln1g{i}", f"ln1b{i}",
                   f"ln2g{i}", f"ln2b{i}"):
            vecs[nm] = (128, 8)
        vecs[f"pb{i}_col"] = (128, 32)
        vecs[f"f1b{i}"] = (128, 32)
        dram[f"pbrow{i}"] = nc.dram_tensor(f"pb{i}_row", [1, 2048], F32,
                                           kind="ExternalInput").ap()
    load_vec_inputs(ctx, sb, vecs)
    setup_l0(ctx, sb, consts)
    sb.xs2 = sb.rr[:, 0:2 * ND * TS].bitcast(F32)
    wst = WStream(ctx, "w", nbuf=2)
    prev_store = None
    for tile in range(SEQ // T):
        t0 = tile * T
        for c in range(ND):
            p.dma("sync", sb.x.chunk(c), xin[c * 128:(c + 1) * 128, t0:t0 + T],
                  deps=[prev_store] if (prev_store and c == 0) else (), strict=True)
        p.join()
        emit_l0attn(ctx, sb, dram, wst, xin, tile)
        for c in range(ND):
            prev_store = p.dma("sync", xout[c * 128:(c + 1) * 128, t0:t0 + T], sb.x.chunk(c))
    p._wait("gpsimd", "Ssy", p.cnt["Ssy"])
    p.build()
    return nc



def build_l0ffn_program(T):
    nc = bass.Bass("TRN2", target_bir_lowering=False, debug=False)
    p = Prog(nc)
    ctx = Ctx(nc, p, T)
    consts = host_consts()
    sb = SB(ctx, T, consts)
    xin = nc.dram_tensor("s_in", [D, SEQ], F32, kind="ExternalInput").ap()
    xout = nc.dram_tensor("s_out", [D, SEQ], F32, kind="ExternalOutput").ap()
    dram = {
        "wgT": nc.dram_tensor("wgT", [D, 3072], I32, kind="ExternalInput").ap(),
        "wuT": nc.dram_tensor("wuT", [D, 3072], I32, kind="ExternalInput").ap(),
        "wdT": nc.dram_tensor("wdT", [3072, D], I32, kind="ExternalInput").ap(),
    }
    wst = WStream(ctx, "w", nbuf=4)
    prev_store = None
    for tile in range(SEQ // T):
        t0 = tile * T
        for c in range(ND):
            p.dma("sync", sb.x.chunk(c), xin[c * 128:(c + 1) * 128, t0:t0 + T],
                  deps=[prev_store] if (prev_store and c == 0) else (), strict=True)
        p.join()
        emit_l0ffn(ctx, sb, dram, wst)
        emit_ln(ctx, sb, eps=EPS_TOP)
        for c in range(ND):
            prev_store = p.dma("sync", xout[c * 128:(c + 1) * 128, t0:t0 + T], sb.x.chunk(c))
    p._wait("gpsimd", "Ssy", p.cnt["Ssy"])
    p.build()
    return nc




def _load_router_weights(ctx, sb, wrT_dram, rb_dram):
    nc, p = ctx.nc, ctx.p
    sb.wr_b = nc.alloc_sbuf_tensor("wr_b", [128, 64], BF16)
    tok = None
    for c in range(ND):
        tok = p.dma("gpsimd", sb.wr_b[:, c * 8:c * 8 + 7],
                    wrT_dram[c * 128:(c + 1) * 128, 0:7])
    sb.rb_row_bf = nc.alloc_sbuf_tensor("rb_row_bf", [1, 8], BF16)
    tok = p.dma("gpsimd", sb.rb_row_bf[0:1, 0:7], rb_dram[0:1, 0:7])
    sb.wr_tok = tok


def build_moe_program(T, debug_w=False, n_routed=7):
    nc = bass.Bass("TRN2", target_bir_lowering=False, debug=False)
    p = Prog(nc)
    ctx = Ctx(nc, p, T)
    consts = host_consts()
    sb = SB(ctx, T, consts, need=("SEL7",), moe=True)
    sb.w_rows = nc.alloc_sbuf_tensor("w_rows", [7, T], BF16)
    sb.wbb = nc.alloc_sbuf_tensor("wbb", [128, T], F32)
    xin = nc.dram_tensor("s_in", [D, SEQ], F32, kind="ExternalInput").ap()
    xout = nc.dram_tensor("s_out", [D, SEQ], F32, kind="ExternalOutput").ap()
    dram = {}
    for nm, shape in (("sgT", [D, 2816]), ("suT", [D, 2816]), ("sdT", [2816, D]),
                      ("rgT", [7 * D, 2816]), ("ruT", [7 * D, 2816]),
                      ("rdT", [7 * 2816, D]), ("wrT", [D, 7]), ("rb_row", [1, 7])):
        dram[nm] = nc.dram_tensor(nm, shape, F32, kind="ExternalInput").ap()
    _load_router_weights(ctx, sb, dram["wrT"], dram["rb_row"])
    dbg_w = nc.dram_tensor("dbg_w", [7, SEQ], F32, kind="ExternalOutput").ap() if debug_w else None
    dbg_z = nc.dram_tensor("dbg_z", [D, SEQ], F32, kind="ExternalOutput").ap() if debug_w else None
    if debug_w:
        ctx.dbg_h = nc.dram_tensor("dbg_h", [1024, T], BF16, kind="ExternalOutput").ap()
        ctx.dbg_h_armed = True
    else:
        ctx.dbg_h = None
    wst = WStream(ctx, "w", nbuf=3)
    prev_store = None
    for tile in range(SEQ // T):
        t0 = tile * T
        for c in range(ND):
            p.dma("sync", sb.x.chunk(c), xin[c * 128:(c + 1) * 128, t0:t0 + T],
                  deps=[prev_store] if (prev_store and c == 0) else (), strict=True)
        p.join()
        emit_cast(ctx, sb.x, sb.xb)
        p.join()
        emit_router(ctx, sb, xb=sb.xb, wr_b=sb.wr_b, rb_row_bf=sb.rb_row_bf,
                    ones_row_bf=sb.ones_row_bf, w_rows=sb.w_rows)
        if debug_w:
            p.dma("sync", dbg_w[0:7, t0:t0 + T], sb.w_rows[0:7, :])
        emit_moe_experts(ctx, sb, dram, wst, n_routed=n_routed)
        if debug_w:
            for c in range(ND):
                p.dma("sync", dbg_z[c * 128:(c + 1) * 128, t0:t0 + T], sb.x.chunk(c))
        emit_ln(ctx, sb, eps=EPS_TOP)
        for c in range(ND):
            prev_store = p.dma("sync", xout[c * 128:(c + 1) * 128, t0:t0 + T], sb.x.chunk(c))
    p._wait("gpsimd", "Ssy", p.cnt["Ssy"])
    p.build()
    return nc





def build_fused_program(T=1024):
    nc = bass.Bass("TRN2", target_bir_lowering=False, debug=False)
    p = Prog(nc)
    ctx = Ctx(nc, p, T)
    consts = host_consts()
    sb = SB(ctx, T, consts, need=("SEL7", "SEL16"), moe=True)
    xin = nc.dram_tensor("xT", [D, SEQ], F32, kind="ExternalInput").ap()
    xout = nc.dram_tensor("s_out", [D, SEQ], F32, kind="ExternalOutput").ap()
    s2 = nc.dram_tensor("s2_scr", [D, SEQ], F32, kind="Internal").ap()
    pq = nc.dram_tensor("pq_scr", [D, SEQ], BF16, kind="Internal").ap()

    dram = {}
    for i in range(2):
        dram[f"pw{i}"] = nc.dram_tensor(f"pw{i}", [D, 4 * D], F32, kind="ExternalInput").ap()
        dram[f"rw{i}"] = nc.dram_tensor(f"rw{i}", [D, D], F32, kind="ExternalInput").ap()
        dram[f"f1w{i}"] = nc.dram_tensor(f"f1w{i}", [D, 4 * D], F32, kind="ExternalInput").ap()
        dram[f"f2w{i}"] = nc.dram_tensor(f"f2w{i}", [4 * D, D], F32, kind="ExternalInput").ap()
    dram["wgT"] = nc.dram_tensor("wgT", [D, 3072], I32, kind="ExternalInput").ap()
    dram["wuT"] = nc.dram_tensor("wuT", [D, 3072], I32, kind="ExternalInput").ap()
    dram["wdT"] = nc.dram_tensor("wdT", [3072, D], I32, kind="ExternalInput").ap()
    for nm, dn in (("wqT", "l1_wq"), ("wkT", "l1_wk"), ("wvT", "l1_wv"), ("wdT_l1", "l1_wd")):
        dram[nm] = nc.dram_tensor(dn, [D, D], F32, kind="ExternalInput").ap()
    for nm, shape in (("sgT", [D, 2816]), ("suT", [D, 2816]), ("sdT", [2816, D]),
                      ("rgT", [7 * D, 2816]), ("ruT", [7 * D, 2816]),
                      ("rdT", [7 * 2816, D]), ("wrT", [D, 7]), ("rb_row", [1, 7])):
        dram[nm] = nc.dram_tensor(nm, shape, F32, kind="ExternalInput").ap()

    vecs = {"lnfg": (128, 8), "lnfb": (128, 8),
            "bq_col": (128, 8), "bd_col": (128, 8),
            "l1g": (128, 8), "l1b": (128, 8),
            "bk_row": (1, 1024), "bv_row": (1, 1024)}
    rows_only = []
    for i in range(2):
        for nm in (f"rb{i}", f"f2b{i}", f"ln1g{i}", f"ln1b{i}", f"ln2g{i}", f"ln2b{i}"):
            vecs[nm] = (128, 8)
        vecs[f"pb{i}_col"] = (128, 32)
        vecs[f"f1b{i}"] = (128, 32)
        dram[f"pbrow{i}"] = nc.dram_tensor(f"pb{i}_row", [1, 2048], F32,
                                           kind="ExternalInput").ap()
    load_vec_inputs(ctx, sb, vecs, bf_rows=("bk_row", "bv_row"))
    setup_l0(ctx, sb, consts)
    setup_l1(ctx, sb, {"wkT": dram["wkT"], "wvT": dram["wvT"]})
    _load_router_weights(ctx, sb, dram["wrT"], dram["rb_row"])
    sb.xs2 = sb.rr[:, 0:2 * ND * TS].bitcast(F32)
    sb.wbb = sb.rr[:, 8192:8192 + 2 * T].bitcast(F32)
    sb.w_rows = sb.rr[0:7, 8192 + 2 * T:8192 + 3 * T]
    wst = WStream(ctx, "w", nbuf=3)
    l1dram = {"wqT": dram["wqT"], "wkT": dram["wkT"], "wvT": dram["wvT"],
              "wdT": dram["wdT_l1"]}

    NT = SEQ // T
    for tile in range(NT):
        t0 = tile * T
        qdep = [("Ssy", p.cnt["Ssy"])] if p.cnt["Ssy"] else []
        for c in range(ND):
            p.dma("sync", sb.x.chunk(c), xin[c * 128:(c + 1) * 128, t0:t0 + T],
                  deps=qdep if c == 0 else (), strict=True)
        p.join()
        p.tag = "l0attn"
        emit_l0attn(ctx, sb, dram, wst, xin, tile)
        p.tag = "l0ffn"
        emit_l0ffn(ctx, sb, dram, wst)
        p.tag = "ln2top"
        emit_ln(ctx, sb, eps=EPS_TOP)
        for c in range(ND):
            p.dma("sync", s2[c * 128:(c + 1) * 128, t0:t0 + T], sb.x.chunk(c))
        p.tag = "l1a"
        emit_l1_phase_a(ctx, sb, l1dram, wst, pq, tile)
    p.tag = "ks"
    emit_ks_build(ctx, sb)
    for tile in range(NT):
        t0 = tile * T
        p.tag = "l1b"
        emit_l1_phase_b(ctx, sb, l1dram, wst, pq, s2, tile)
        p.tag = "moe"
        emit_cast(ctx, sb.x, sb.xb)
        p.join()
        emit_router(ctx, sb, xb=sb.xb, rb_row_bf=sb.rb_row_bf, wr_b=sb.wr_b,
                    ones_row_bf=sb.ones_row_bf, w_rows=sb.w_rows)
        emit_moe_experts(ctx, sb, dram, wst)
        p.tag = "ln4top"
        emit_ln(ctx, sb, eps=EPS_TOP)
        for c in range(ND):
            p.dma("sync", xout[c * 128:(c + 1) * 128, t0:t0 + T], sb.x.chunk(c))
        p.join()
    p._wait("gpsimd", "Ssy", p.cnt["Ssy"])
    p.build()
    return nc


def kernel(**inputs):
    if "f" not in _PROGS:
        _PROGS["f"] = build_fused_program(1024)
    nc = _PROGS["f"]
    n = 8
    packs = [host_pack(inputs, core) for core in range(n)]
    names = ["pw0", "pw1", "rw0", "rw1", "f1w0", "f1w1", "f2w0", "f2w1",
             "lnfg", "lnfb", "wgT", "wuT", "wdT",
             "l1_wq", "l1_wk", "l1_wv", "l1_wd",
             "bq_col", "bd_col", "l1g", "l1b", "bk_row", "bv_row",
             "sgT", "suT", "sdT", "rgT", "ruT", "rdT", "wrT", "rb_row"]
    for i in range(2):
        names += [f"pb{i}_col", f"pb{i}_row", f"rb{i}", f"f1b{i}", f"f2b{i}",
                  f"ln1g{i}", f"ln1b{i}", f"ln2g{i}", f"ln2b{i}"]
    in_maps = [dict(_sub(p_, names), xT=p_["xT"]) for p_ in packs]
    res = run_bass_kernel_spmd(nc, in_maps, core_ids=list(range(n)))
    outs = [np.ascontiguousarray(np.asarray(r["s_out"]).T) for r in res.results]
    return np.stack(outs, axis=0)



_PROGS = {}


def _get_progs():
    if "a" not in _PROGS:
        _PROGS["a"] = build_l0attn_program(2048)
        _PROGS["b"] = build_l0ffn_program(2048)
        _PROGS["c"] = build_l1_program(2048)
        _PROGS["d"] = build_moe_program(2048)
    return _PROGS


def _sub(m, names):
    return {k: m[k] for k in names}


def kernel_unfused(**inputs):
    progs = _get_progs()
    n = 8
    packs = [host_pack(inputs, core) for core in range(n)]
    cores = list(range(n))

    a_names = ["pw0", "pw1", "rw0", "rw1", "f1w0", "f1w1", "f2w0", "f2w1",
               "lnfg", "lnfb"]
    for i in range(2):
        a_names += [f"pb{i}_col", f"pb{i}_row", f"rb{i}", f"f1b{i}", f"f2b{i}",
                    f"ln1g{i}", f"ln1b{i}", f"ln2g{i}", f"ln2b{i}"]
    in_maps = [dict(_sub(p_, a_names), s_in=p_["xT"]) for p_ in packs]
    res = run_bass_kernel_spmd(progs["a"], in_maps, core_ids=cores)
    s1 = [np.asarray(r["s_out"]) for r in res.results]

    b_names = ["wgT", "wuT", "wdT"]
    in_maps = [dict(_sub(p_, b_names), s_in=s) for p_, s in zip(packs, s1)]
    res = run_bass_kernel_spmd(progs["b"], in_maps, core_ids=cores)
    s2 = [np.asarray(r["s_out"]) for r in res.results]

    c_names = ["l1_wq", "l1_wk", "l1_wv", "l1_wd", "bq_col", "bd_col",
               "l1g", "l1b", "bk_row", "bv_row"]
    in_maps = [dict(_sub(p_, c_names), s_in=s) for p_, s in zip(packs, s2)]
    res = run_bass_kernel_spmd(progs["c"], in_maps, core_ids=cores)
    s3 = [np.asarray(r["s_out"]) for r in res.results]

    d_names = ["sgT", "suT", "sdT", "rgT", "ruT", "rdT", "wrT", "rb_row"]
    in_maps = [dict(_sub(p_, d_names), s_in=s) for p_, s in zip(packs, s3)]
    res = run_bass_kernel_spmd(progs["d"], in_maps, core_ids=cores)
    outs = [np.ascontiguousarray(np.asarray(r["s_out"]).T) for r in res.results]
    return np.stack(outs, axis=0)


# version 04
# speedup vs baseline: 1.1768x; 1.0440x over previous
import numpy as np
import ml_dtypes
from contextlib import ExitStack

import concourse.bass as bass
import concourse.mybir as mybir
from concourse.bass_utils import run_bass_kernel_spmd

F32 = mybir.dt.float32
BF16 = mybir.dt.bfloat16
I32 = mybir.dt.int32
AF = mybir.ActivationFunctionType
ALU = mybir.AluOpType
AX = mybir.AxisListType

D = 1024
SEQ = 4096
NH = 16
HD = 64
CH = 128
ND = D // 128
TS = 512
EPS_TOP = 1e-5
EPS_L1LN = 1e-12
EPS_ATTN = 1e-6
Q127 = 1.0 / 127.0

ENGINES = ("sync", "scalar", "tensor", "vector", "gpsimd")
ESEM = {"sync": "Ssy", "scalar": "Sac", "tensor": "Spe", "vector": "Sdv", "gpsimd": "Sgp"}



class Prog:
    def __init__(self, nc):
        self.nc = nc
        self.ops = {e: [] for e in ENGINES}
        self.cnt = {s: 0 for s in ESEM.values()}
        self.sem_handles = {}
        self.last_wait = {e: {} for e in ENGINES}
        self.tag = ""

    def _wait(self, engine, s, v):
        lw = self.last_wait[engine]
        if lw.get(s, -1) >= v:
            return
        lw[s] = v

        def w(eng, s=s, v=v):
            eng.wait_ge(self.sem_handles[s], v)
        self.ops[engine].append(w)

    def op(self, engine, fn, deps=(), inc=1, strict=False):
        best = {}
        for t in deps:
            if t is None:
                continue
            s, v = t
            if best.get(s, -1) < v:
                best[s] = v
        own = ESEM[engine]
        for s, v in sorted(best.items()):
            if s != own or strict:
                self._wait(engine, s, v)
        tag = self.tag
        if inc:
            s = own
            self.cnt[s] += inc
            tok = (s, self.cnt[s])

            def wrapped(eng, fn=fn, s=s, inc=inc, tag=tag):
                inst = fn(eng)
                if tag:
                    inst.annotate(tag)
                inst.then_inc(self.sem_handles[s], inc)
            self.ops[engine].append(wrapped)
            return tok

        def plain(eng, fn=fn, tag=tag):
            inst = fn(eng)
            if tag:
                inst.annotate(tag)
        self.ops[engine].append(plain)
        return None

    def sop(self, engine, fn, deps=()):
        tok = self.op(engine, fn, deps=deps, inc=1)
        self._wait(engine, tok[0], tok[1])
        return tok

    def dma(self, engine, out, in_, deps=(), strict=False):
        return self.op(engine, lambda eng, o=out, i=in_: eng.dma_start(out=o, in_=i),
                       deps=deps, inc=16, strict=strict)

    def now(self):
        return [(s, self.cnt[s]) for s in self.cnt if self.cnt[s] > 0]

    def join(self, engines=ENGINES):
        toks = self.now()
        for e in engines:
            for s, v in toks:
                if s != ESEM[e]:
                    self._wait(e, s, v)

    def build(self):
        st = ExitStack()
        for s in ESEM.values():
            self.sem_handles[s] = st.enter_context(self.nc.semaphore(s))
        block = st.enter_context(self.nc.Block())
        for e in ENGINES:
            lst = self.ops[e]
            if not lst:
                continue

            def run(eng, lst=lst):
                for fn in lst:
                    fn(eng)
            getattr(block, e)(run)
        st.close()


def merge(*toks):
    out = []
    for t in toks:
        if t is None:
            continue
        if isinstance(t, (list, tuple)) and (len(t) == 0 or not isinstance(t[0], str)):
            out.extend(x for x in t if x is not None)
        else:
            out.append(t)
    return out


class CBuf:
    def __init__(self, nc, name, n, w, dtype):
        self.h = nc.alloc_sbuf_tensor(name, [128, n * w], dtype)
        self.n = n
        self.w = w

    def sl(self, c, a, b):
        return self.h[:, c * self.w + a: c * self.w + b]

    def chunk(self, c):
        return self.h[:, c * self.w:(c + 1) * self.w]


class Ctx:
    def __init__(self, nc, p, T):
        self.nc = nc
        self.p = p
        self.T = T
        self.NS = T // TS
        self.ps = nc.alloc_psum_tensor("ps", [128, 4096], F32)
        self.bank_ready = [None] * 8
        self.bank_free = [None] * 8
        self.ring_i = 0

    def bank(self, b, p0=0, p1=128, w=TS):
        return self.ps[p0:p1, b * TS: b * TS + w]

    def ring(self):
        banks = getattr(self, "ring_banks", None) or list(range(8))
        b = banks[self.ring_i % len(banks)]
        self.ring_i += 1
        return b


def acc_group(ctx, bank, mms, deps=(), p0=0, p1=128, w=TS, col0=0):
    p = ctx.p
    out_ap = ctx.ps[p0:p1, bank * TS + col0: bank * TS + col0 + w]
    n = len(mms)
    tok = None
    for i, (lhsT, rhs) in enumerate(mms):
        d = list(deps) if i == 0 else []
        if i == 0 and ctx.bank_free[bank] is not None:
            d.append(ctx.bank_free[bank])

        def fn(eng, lhsT=lhsT, rhs=rhs, i=i, out_ap=out_ap):
            return eng.matmul(out_ap, lhsT, rhs, start=(i == 0), stop=(i == n - 1))
        tok = p.op("tensor", fn, deps=d, inc=(1 if i == n - 1 else 0))
    ctx.bank_ready[bank] = tok
    ctx.bank_free[bank] = None
    return tok


def bank_read(ctx, bank, tok):
    ctx.bank_free[bank] = tok


class WStream:

    def __init__(self, ctx, name, nbuf=4):
        self.ctx = ctx
        self.nbuf = nbuf
        self.bufs = [CBuf(ctx.nc, f"{name}{i}", 8, 512, BF16) for i in range(nbuf)]
        self.busy = [None] * nbuf
        self.i = 0

    def load(self, dram, r0, nk, c0, w):
        assert nk * w <= 4096
        p = self.ctx.p
        idx = self.i % self.nbuf
        self.i += 1
        buf = self.bufs[idx]
        dep = [self.busy[idx]] if self.busy[idx] is not None else []
        src3 = dram[r0:r0 + nk * 128, c0:c0 + w].rearrange("(c p) w -> p c w", p=128)
        dst3 = buf.h[:, 0:nk * w].rearrange("p (c w) -> p c w", c=nk)
        tok = p.dma("gpsimd", dst3, src3, deps=dep)

        def sl(c, a, b, buf=buf, w=w):
            return buf.h[:, c * w + a: c * w + b]
        return idx, sl, tok

    def done(self, idx, tok):
        prev = self.busy[idx]
        if prev is None or tok[1] > prev[1]:
            self.busy[idx] = tok



def _col_pack(v):
    v = np.asarray(v, dtype=np.float32)
    n = v.shape[0] // 128
    return np.ascontiguousarray(v.reshape(n, 128).T)


def _t(w):
    return np.ascontiguousarray(np.asarray(w).T)


def _row(v):
    return np.ascontiguousarray(np.asarray(v, np.float32).reshape(1, -1))


def host_pack(inputs, core):
    m = {}
    x = np.asarray(inputs["L0_attn_x"][core])
    m["xT"] = _t(x)
    for i in range(2):
        m[f"pw{i}"] = np.ascontiguousarray(np.asarray(inputs["L0_attn_proj_w"][i]))
        m[f"pb{i}_col"] = _col_pack(inputs["L0_attn_proj_b"][i])
        m[f"pb{i}_row"] = np.ascontiguousarray(
            np.asarray(inputs["L0_attn_proj_b"][i], np.float32)[1024:3072].reshape(1, -1))
        m[f"rw{i}"] = np.ascontiguousarray(np.asarray(inputs["L0_attn_reto_w"][i]))
        m[f"rb{i}"] = _col_pack(inputs["L0_attn_reto_b"][i])
        m[f"f1w{i}"] = np.ascontiguousarray(np.asarray(inputs["L0_attn_ffn1_w"][i]))
        m[f"f1b{i}"] = _col_pack(inputs["L0_attn_ffn1_b"][i])
        m[f"f2w{i}"] = np.ascontiguousarray(np.asarray(inputs["L0_attn_ffn2_w"][i]))
        m[f"f2b{i}"] = _col_pack(inputs["L0_attn_ffn2_b"][i])
        m[f"ln1g{i}"] = _col_pack(inputs["L0_attn_ln1_g"][i])
        m[f"ln1b{i}"] = _col_pack(inputs["L0_attn_ln1_b"][i])
        m[f"ln2g{i}"] = _col_pack(inputs["L0_attn_ln2_g"][i])
        m[f"ln2b{i}"] = _col_pack(inputs["L0_attn_ln2_b"][i])
    m["lnfg"] = _col_pack(inputs["L0_attn_lnf_g"])
    m["lnfb"] = _col_pack(inputs["L0_attn_lnf_b"])
    m["wgT"] = _t(inputs["L0_ffn_w_gate"])
    m["wuT"] = _t(inputs["L0_ffn_w_up"])
    m["wdT"] = _t(inputs["L0_ffn_w_down"])
    for nm, key in (("l1_wq", "L1_attn_Wq"), ("l1_wk", "L1_attn_Wk"),
                    ("l1_wv", "L1_attn_Wv"), ("l1_wd", "L1_attn_Wd")):
        m[nm] = _t(inputs[key])
    m["bq_col"] = _col_pack(inputs["L1_attn_bq"])
    m["bk_row"] = _row(inputs["L1_attn_bk"])
    m["bv_row"] = _row(inputs["L1_attn_bv"])
    m["bd_col"] = _col_pack(inputs["L1_attn_bd"])
    m["l1g"] = _col_pack(inputs["L1_attn_gamma"])
    m["l1b"] = _col_pack(inputs["L1_attn_beta"])
    m["sgT"] = _t(inputs["L1_ffn_sg"])
    m["suT"] = _t(inputs["L1_ffn_su"])
    m["sdT"] = _t(inputs["L1_ffn_sd"])
    m["rgT"] = np.ascontiguousarray(np.asarray(inputs["L1_ffn_rg"]).transpose(0, 2, 1).reshape(7 * D, 2816))
    m["ruT"] = np.ascontiguousarray(np.asarray(inputs["L1_ffn_ru"]).transpose(0, 2, 1).reshape(7 * D, 2816))
    m["rdT"] = np.ascontiguousarray(np.asarray(inputs["L1_ffn_rd"]).transpose(0, 2, 1).reshape(7 * 2816, D))
    m["wrT"] = _t(inputs["L1_ffn_wr"])
    m["rb_row"] = _row(inputs["L1_ffn_rb"])
    return m


def host_consts():
    c = {}
    gamma = (1.0 - 2.0 ** (-5.0 - np.arange(NH, dtype=np.float64)))
    j = np.arange(CH, dtype=np.float64)
    dts = np.zeros((NH, CH, CH), np.float32)
    for h in range(NH):
        mask = (j[None, :] - j[:, None]) >= 0
        vals = (gamma[h] ** (-(j + 1.0)))[:, None]
        dts[h] = np.where(mask, vals, 0.0).astype(np.float32)
    c["DT"] = dts.reshape(NH * CH, CH)
    qd = np.zeros((ND, 128, CH), np.float32)
    for cc in range(ND):
        for half in range(2):
            h = 2 * cc + half
            qd[cc, half * 64:(half + 1) * 64, :] = (gamma[h] ** (j + 1.0)).astype(np.float32)[None, :]
    c["QD"] = qd.reshape(ND * 128, CH)
    kd = np.zeros((2, CH, 512), np.float32)
    for s in range(2):
        for hh in range(8):
            h = s * 8 + hh
            colv = (gamma[h] ** (CH - 1.0 - j)).astype(np.float32)
            kd[s, :, hh * 64:(hh + 1) * 64] = colv[:, None]
    c["KD"] = kd.reshape(2 * CH, 512)
    gc = np.zeros((128, 512), np.float32)
    for h in range(NH):
        gc[(h % 2) * 64:(h % 2) * 64 + 64, (h // 2) * 64:(h // 2) * 64 + 64] = np.float32(gamma[h] ** CH)
    c["GC"] = gc
    c["ones1"] = np.ones((1, 128), np.float32)
    c["onesc"] = np.ones((128, 1), np.float32)
    c["ones_d"] = np.full((1, 128), 1.0 / D, np.float32)
    c["ones_sd"] = np.full((1, 128), float(np.sqrt(D)), np.float32)
    sel7 = np.zeros((7, 7 * 128), np.float32)
    for e in range(7):
        sel7[e, e * 128:(e + 1) * 128] = 1.0
    c["SEL7"] = sel7
    sel16 = np.zeros((16, 8 * 128), np.float32)
    for cc in range(8):
        sel16[2 * cc, cc * 128: cc * 128 + 64] = 1.0
        sel16[2 * cc + 1, cc * 128 + 64: cc * 128 + 128] = 1.0
    c["SEL16"] = sel16
    c["SEL2"] = np.zeros((2, 128), np.float32)
    c["SEL2"][0, 0:64] = 1.0
    c["SEL2"][1, 64:128] = 1.0
    c["ident"] = np.eye(128, dtype=np.float32)
    return c



class SB:
    def __init__(self, ctx, T, consts, need=(), moe=False):
        nc, p = ctx.nc, ctx.p
        self.T = T
        self.x = CBuf(nc, "x", ND, T, F32)
        self.xb = CBuf(nc, "xb", ND, T, BF16)
        self.h = CBuf(nc, "h", ND, T, BF16)
        self.rows = nc.alloc_sbuf_tensor("rows", [1, 2 * T], F32)
        self.t1 = [nc.alloc_sbuf_tensor(f"t1_{i}", [128, TS], F32) for i in range(2)]
        self.t1_busy = [None, None]
        self.const = {}
        bf_consts = getattr(self, "_bf_consts", ("SEL7",))
        for name in set(("ones_d", "ones_sd") + tuple(need)):
            arr = consts[name]
            dr = nc.inline_tensor(np.ascontiguousarray(arr), "ic_" + name)
            if name in bf_consts:
                hh = nc.alloc_sbuf_tensor("cst_" + name, list(arr.shape), BF16)
                p.dma("gpsimd", hh[:, :], dr.ap())
            else:
                hh = nc.alloc_sbuf_tensor("cst_" + name, list(arr.shape), F32)
                p.dma("sync", hh[:, :], dr.ap())
            self.const[name] = hh
        onesb = nc.inline_tensor(np.ones((128, 1), np.float32), "ic_ones_b")
        self.ones_bf = nc.alloc_sbuf_tensor("cst_ones_bf", [128, 1], BF16)
        p.dma("gpsimd", self.ones_bf[:, :], onesb.ap())
        onesr = nc.inline_tensor(np.ones((1, 128), np.float32), "ic_ones_r")
        self.ones_row_bf = nc.alloc_sbuf_tensor("cst_ones_r_bf", [1, 128], BF16)
        p.dma("gpsimd", self.ones_row_bf[:, :], onesr.ap())
        self.vec = {}
        self.vecrow = {}
        if moe:
            self.rtr = None
            self.rtr_busy = [None, None]
            self.w_rows = None
            self.w_rows_tok = None
            self.wbb = None
            self.wr_b = None
            self.wr_tok = None
            self.rb_row_bf = None

    def h_f32(self, i):
        return self.h.h[:, 2 * i * self.T:(2 * i + 2) * self.T].bitcast(F32)


def load_vec_inputs(ctx, sb, names_shapes, bf_rows=(), bf_only_rows=()):
    nc, p = ctx.nc, ctx.p
    for name, shape in names_shapes.items():
        d = nc.dram_tensor(name, list(shape), F32, kind="ExternalInput").ap()
        if name not in bf_only_rows:
            h = nc.alloc_sbuf_tensor("v_" + name, list(shape), F32)
            p.dma("sync", h[:, :], d[:, :])
            sb.vec[name] = h
        if name in bf_rows or name in bf_only_rows:
            hb = nc.alloc_sbuf_tensor("vb_" + name, list(shape), BF16)
            p.dma("gpsimd", hb[:, :], d[:, :])
            sb.vecrow[name] = hb



def emit_ln(ctx, sb, *, g=None, b=None, eps, out_bf=None, add_f32=None,
            bf_only=False, name=""):
    p = ctx.p
    T = ctx.T
    NSl = ctx.NS
    x = sb.x
    scr = [(sb.h.chunk(2 * i), sb.h.chunk(2 * i + 1)) for i in range(4)]
    onesb = sb.ones_bf

    prev_mm = {}
    for c in range(ND):
        xbs, xsq = scr[c % 4]
        dep = merge([prev_mm.get(c - 4)])
        p.op("scalar", lambda eng, xbs=xbs, c=c: eng.activation(xbs, x.chunk(c), AF.Copy),
             deps=dep, inc=0)
        ct = p.op("scalar", lambda eng, xsq=xsq, xbs=xbs: eng.activation(xsq, xbs, AF.Square))
        last = None
        for t in range(NSl):
            for kind in range(2):
                bank = kind * NSl + t
                src = (xbs if kind == 0 else xsq)[:, t * TS:(t + 1) * TS]
                d = [ct]
                if c == 0 and ctx.bank_free[bank] is not None:
                    d.append(ctx.bank_free[bank])
                last = p.op("tensor", lambda eng, bank=bank, src=src, c=c:
                            eng.matmul(ctx.bank(bank, 0, 1), onesb[:, 0:1], src,
                                       start=(c == 0), stop=(c == ND - 1)),
                            deps=d, inc=(1 if (kind == 1 and t == NSl - 1) else 0))
        prev_mm[c] = last
    mm_done = prev_mm[ND - 1]

    mu = sb.rows[0:1, 0:T]
    rstd = sb.rows[0:1, T:2 * T]
    ps_sum = ctx.ps[0:1, 0:NSl * TS]
    ps_sq = ctx.ps[0:1, NSl * TS:2 * NSl * TS]
    p.op("vector", lambda eng: eng.tensor_copy(mu, ps_sum), deps=[mm_done], inc=0)
    p.op("vector", lambda eng: eng.tensor_copy(rstd, ps_sq), inc=0)
    p.op("vector", lambda eng: eng.tensor_tensor(ps_sum, mu, mu, ALU.mult), inc=0)
    var_tok = p.op("vector", lambda eng: eng.scalar_tensor_tensor(
        rstd, ps_sum, -1.0 / D, rstd, ALU.mult, ALU.add))
    p.op("vector", lambda eng: eng.tensor_scalar_add(rstd, rstd, float(D * eps)), inc=0)
    sq_tok = p.op("scalar", lambda eng: eng.activation(rstd, rstd, AF.Sqrt),
                  deps=[var_tok])
    rstd_tok = p.op("vector", lambda eng: eng.reciprocal(rstd, rstd), deps=[sq_tok])
    for i in range(2 * NSl):
        bank_read(ctx, i, rstd_tok)

    ones_d = sb.const["ones_d"]
    ones_sd = sb.const["ones_sd"]
    for t in range(NSl):
        bmu, brs = 0, 1
        p.op("tensor", lambda eng, t=t: eng.matmul(
            ctx.bank(bmu), ones_d[0:1, :], mu[:, t * TS:(t + 1) * TS], start=True, stop=True),
            deps=merge([rstd_tok, ctx.bank_free[bmu]]), inc=0)
        bc_tok = p.op("tensor", lambda eng, t=t: eng.matmul(
            ctx.bank(brs), ones_sd[0:1, :], rstd[:, t * TS:(t + 1) * TS], start=True, stop=True),
            deps=merge([ctx.bank_free[brs]]))
        last = bc_tok
        for c in range(ND):
            tt = sb.t1[c % 2]
            p.op("vector", lambda eng, c=c, t=t, tt=tt: eng.tensor_tensor(
                tt[:], x.sl(c, t * TS, (t + 1) * TS), ctx.bank(bmu), ALU.subtract),
                deps=merge([bc_tok, sb.t1_busy[c % 2]]), inc=0)
            dve_tok = p.op("vector", lambda eng, tt=tt: eng.tensor_tensor(
                tt[:], tt[:], ctx.bank(brs), ALU.mult))
            dst = x.sl(c, t * TS, (t + 1) * TS)
            if bf_only:
                a_tok = p.op("vector", lambda eng, c=c, tt=tt: eng.tensor_scalar(
                    tt[:], tt[:], g[:, c:c + 1], b[:, c:c + 1], ALU.mult, ALU.add),
                    deps=[dve_tok])
                w_tok = p.op("scalar", lambda eng, c=c, t=t, tt=tt: eng.activation(
                    out_bf.sl(c, t * TS, (t + 1) * TS), tt[:], AF.Copy), deps=[a_tok])
                sb.t1_busy[c % 2] = w_tok
                last = dve_tok
                continue
            if g is not None:
                if add_f32 is not None:
                    add_ap, add_tok = add_f32(c, t)
                    a_tok = p.op("vector", lambda eng, c=c, tt=tt: eng.tensor_scalar(
                        tt[:], tt[:], g[:, c:c + 1], b[:, c:c + 1], ALU.mult, ALU.add),
                        deps=[dve_tok])
                    w_tok = p.op("vector", lambda eng, tt=tt, dst=dst, a=add_ap:
                                 eng.tensor_tensor(dst, tt[:], a, ALU.add),
                                 deps=merge([a_tok, add_tok]))
                else:
                    w_tok = p.op("vector", lambda eng, c=c, tt=tt, dst=dst: eng.tensor_scalar(
                        dst, tt[:], g[:, c:c + 1], b[:, c:c + 1], ALU.mult, ALU.add),
                        deps=[dve_tok])
            else:
                assert add_f32 is None
                w_tok = p.op("scalar", lambda eng, tt=tt, dst=dst: eng.activation(
                    dst, tt[:], AF.Copy), deps=[dve_tok])
            sb.t1_busy[c % 2] = w_tok
            if out_bf is not None:
                p.op("scalar", lambda eng, c=c, t=t, dst=dst: eng.activation(
                    out_bf.sl(c, t * TS, (t + 1) * TS), dst, AF.Copy), deps=[w_tok])
            last = dve_tok
        bank_read(ctx, bmu, last)
        bank_read(ctx, brs, last)
    p.join()


def emit_cast(ctx, src_cb, dst_cb, scale=None):
    p = ctx.p
    for c in range(ND):
        if scale is None:
            p.op("scalar", lambda eng, c=c: eng.activation(
                dst_cb.chunk(c), src_cb.chunk(c), AF.Copy))
        else:
            p.op("scalar", lambda eng, c=c: eng.activation(
                dst_cb.chunk(c), src_cb.chunk(c), AF.Copy, scale=scale))



def emit_gated_ffn(ctx, sb, *, xb, wg, wu, wd, HID, wst, act=AF.Silu,
                   up_scale=None, wb_ap=None, first_bias=None):
    p = ctx.p
    T = ctx.T
    NSl = ctx.NS
    NHC = HID // 128
    h = sb.h
    upscr = [ctx.nc, None]
    for g0 in range(0, NHC, 8):
        ng = min(8, NHC - g0)
        gl = {}
        ul = {}
        for gi in range(ng):
            hc = g0 + gi
            half = gi // 4
            if half not in gl:
                nw = min(4, ng - half * 4) * 128
                gl[half] = wst.load(wg, 0, ND, (g0 + half * 4) * 128, nw)
                ul[half] = wst.load(wu, 0, ND, (g0 + half * 4) * 128, nw)
            slot, wsl, wtok = gl[half]
            uslot, usl, utok = ul[half]
            gcol = (gi % 4) * 128
            bg = 4 + (gi % 2)
            bu = 6 + (gi % 2)

            u_toks = []
            for t in range(NSl):
                gt = acc_group(ctx, bg, [(wsl(c, gcol, gcol + 128),
                                          xb.sl(c, t * TS, (t + 1) * TS)) for c in range(ND)],
                               deps=[wtok], col0=0)
                e1 = p.op("scalar", lambda eng, gi=gi, t=t, bg=bg: eng.activation(
                    h.sl(gi, t * TS, (t + 1) * TS), ctx.bank(bg), act), deps=[gt])
                bank_read(ctx, bg, e1)
                wst.done(slot, gt)
                ut = acc_group(ctx, bu, [(usl(c, gcol, gcol + 128),
                                          xb.sl(c, t * TS, (t + 1) * TS)) for c in range(ND)],
                               deps=[utok], col0=0)
                wst.done(uslot, ut)
                tt = sb.t1[t % 2]
                if wb_ap is not None:
                    e2 = p.op("vector", lambda eng, t=t, bu=bu, tt=tt: eng.tensor_tensor(
                        tt[:], ctx.bank(bu), wb_ap[:, t * TS:(t + 1) * TS], ALU.mult),
                        deps=merge([ut, sb.t1_busy[t % 2]]))
                else:
                    sc = {} if up_scale is None else {"scale": up_scale}
                    e2 = p.op("scalar", lambda eng, t=t, bu=bu, tt=tt, sc=sc: eng.activation(
                        tt[:], ctx.bank(bu), AF.Copy, **sc),
                        deps=merge([ut, sb.t1_busy[t % 2]]))
                bank_read(ctx, bu, e2)
                m_tok = p.op("vector", lambda eng, gi=gi, t=t, tt=tt: eng.tensor_tensor(
                    h.sl(gi, t * TS, (t + 1) * TS), h.sl(gi, t * TS, (t + 1) * TS),
                    tt[:], ALU.mult), deps=merge([e1, e2]))
                sb.t1_busy[t % 2] = m_tok
        if getattr(ctx, "dbg_h", None) is not None and g0 == 0 and ctx.dbg_h_armed:
            p.join()
            for gi in range(ng):
                p.dma("sync", ctx.dbg_h[gi * 128:(gi + 1) * 128, 0:T], h.chunk(gi))
            p.join()
            ctx.dbg_h_armed = False
        dl = []
        for half in range(0, ng, 4):
            nked = min(4, ng - half)
            dl.append(wst.load(wd, (g0 + half) * 128, nked, 0, 1024))
        for f2 in range(ND):
            for t in range(NSl):
                bank = t if (f2 % 2 == 0) else 4 + t
                mms = []
                dep0 = [dl[-1][2], m_tok]
                for gi in range(ng):
                    dslot, dsl, dtok = dl[gi // 4]
                    mms.append((dsl(gi % 4, f2 * 128, (f2 + 1) * 128),
                                h.sl(gi, t * TS, (t + 1) * TS)))
                gt = acc_group(ctx, bank, mms, deps=dep0)
                for dslot, dsl, dtok in dl:
                    wst.done(dslot, gt)
                if first_bias is not None and g0 == 0:
                    ev = p.op("vector", lambda eng, f2=f2, t=t, bank=bank:
                              eng.scalar_tensor_tensor(
                                  sb.x.sl(f2, t * TS, (t + 1) * TS), ctx.bank(bank),
                                  first_bias[:, f2:f2 + 1],
                                  sb.x.sl(f2, t * TS, (t + 1) * TS), ALU.add, ALU.add),
                              deps=[gt])
                else:
                    ev = p.op("vector", lambda eng, f2=f2, t=t, bank=bank:
                              eng.tensor_tensor(
                                  sb.x.sl(f2, t * TS, (t + 1) * TS),
                                  sb.x.sl(f2, t * TS, (t + 1) * TS),
                                  ctx.bank(bank), ALU.add),
                              deps=[gt])
                bank_read(ctx, bank, ev)
    p.join()



def proj_fm(ctx, *, wdram, wr0, nk, xb, fch_list, wst, evac, extra=None,
            t_list=None, ldeps=()):
    p = ctx.p
    NSl = ctx.NS if t_list is None else None
    ts = range(ctx.NS) if t_list is None else t_list
    i = 0
    for fb0 in range(0, len(fch_list), 4):
        fgrp = fch_list[fb0:fb0 + 4]
        slot, wsl, wtok = wst.load(wdram, wr0, nk, fgrp[0] * 128, len(fgrp) * 128)
        for mi, f in enumerate(fgrp):
            for t in ts:
                bank = ctx.ring()
                mms = [(wsl(c, mi * 128, (mi + 1) * 128),
                        xb.sl(c, t * TS, (t + 1) * TS)) for c in range(nk)]
                if extra is not None:
                    mms = mms + extra(f, t)
                gt = acc_group(ctx, bank, mms, deps=merge([wtok], list(ldeps)))
                wst.done(slot, gt)
                evac(f, t, bank, gt)
                i += 1



def emit_l0ffn(ctx, sb, dram, wst):
    p = ctx.p
    emit_cast(ctx, sb.x, sb.xb, scale=Q127)
    p.join()
    emit_gated_ffn(ctx, sb, xb=sb.xb, wg=dram["wgT"], wu=dram["wuT"],
                   wd=dram["wdT"], HID=3072, wst=wst, act=AF.Silu,
                   up_scale=Q127)





def emit_router(ctx, sb, *, xb, wr_b, rb_row_bf, ones_row_bf, w_rows):
    p = ctx.p
    T = ctx.T
    NTC = T // 128
    nc = ctx.nc
    rtr = sb.rtr
    rbusy = sb.rtr_busy
    identf = sb.const["ident"]
    for tc in range(NTC):
        sl = (tc % 2) * 32
        bank = ctx.ring()
        mms = [(xb.sl(c, tc * 128, (tc + 1) * 128), wr_b[:, c * 8:c * 8 + 7])
               for c in range(ND)]
        mms.append((ones_row_bf[0:1, :], rb_row_bf[0:1, 0:7]))
        gt = acc_group(ctx, bank, mms, deps=merge([sb.wr_tok, rbusy[tc % 2]]), w=7)
        pb = ctx.ps[0:128, bank * TS: bank * TS + 7]
        pm = rtr[:, sl + 24:sl + 25]
        pmn = rtr[:, sl + 25:sl + 26]
        ssum = rtr[:, sl + 26:sl + 27]
        rec = rtr[:, sl + 27:sl + 28]
        m1 = rtr[:, sl + 28:sl + 29]
        m2 = rtr[:, sl + 29:sl + 30]
        s12 = rtr[:, sl + 30:sl + 31]
        rec2 = rtr[:, sl + 31:sl + 32]
        pe_ = rtr[:, sl + 0:sl + 7]
        msk = rtr[:, sl + 8:sl + 15]
        tmp7 = rtr[:, sl + 16:sl + 23]
        p.sop("vector", lambda eng, pb=pb, pm=pm: eng.reduce_max(pm, pb, AX.X),
              deps=[gt])
        v1 = p.sop("vector", lambda eng, pm=pm, pmn=pmn: eng.tensor_scalar_mul(pmn, pm, -1.0))
        et = p.op("scalar", lambda eng, pb=pb, pe_=pe_, pmn=pmn: eng.activation(
            pe_, pb, AF.Exp, bias=pmn), deps=[v1])
        bank_read(ctx, bank, et)
        p.sop("vector", lambda eng, pe_=pe_, ssum=ssum: eng.reduce_sum(ssum, pe_, AX.X),
              deps=[et])
        p.sop("vector", lambda eng, ssum=ssum, rec=rec: eng.reciprocal(rec, ssum))
        p.sop("vector", lambda eng, pe_=pe_, rec=rec: eng.tensor_scalar_mul(pe_, pe_, rec))
        p.sop("vector", lambda eng, pe_=pe_, m1=m1: eng.reduce_max(m1, pe_, AX.X))
        p.sop("vector", lambda eng, pe_=pe_, m1=m1, tmp7=tmp7: eng.tensor_single_scalar(
            tmp7, pe_, m1, ALU.is_ge))
        p.sop("vector", lambda eng, tmp7=tmp7: eng.tensor_scalar_mul(tmp7, tmp7, 2.0))
        p.sop("vector", lambda eng, pe_=pe_, tmp7=tmp7, msk=msk: eng.tensor_tensor(
            msk, pe_, tmp7, ALU.subtract))
        p.sop("vector", lambda eng, msk=msk, m2=m2: eng.reduce_max(m2, msk, AX.X))
        p.sop("vector", lambda eng, m1=m1, m2=m2, s12=s12: eng.tensor_tensor(
            s12, m1, m2, ALU.add))
        p.sop("vector", lambda eng, s12=s12, rec2=rec2: eng.reciprocal(rec2, s12))
        p.sop("vector", lambda eng, pe_=pe_, m2=m2, tmp7=tmp7: eng.tensor_single_scalar(
            tmp7, pe_, m2, ALU.is_ge))
        p.sop("vector", lambda eng, pe_=pe_, tmp7=tmp7: eng.tensor_tensor(
            pe_, pe_, tmp7, ALU.mult))
        wt = p.sop("vector", lambda eng, pe_=pe_, rec2=rec2: eng.tensor_scalar_mul(
            pe_, pe_, rec2))
        bank2 = ctx.ring()
        pt = p.op("tensor", lambda eng, pe_=pe_, bank2=bank2: eng.transpose(
            ctx.ps[0:7, bank2 * TS: bank2 * TS + 128], pe_, identf[:, :]),
            deps=merge([wt, ctx.bank_free[bank2]]))
        ctx.bank_ready[bank2] = pt
        ctx.bank_free[bank2] = None
        cb = p.op("vector", lambda eng, tc=tc, bank2=bank2: eng.tensor_copy(
            w_rows[0:7, tc * 128:(tc + 1) * 128],
            ctx.ps[0:7, bank2 * TS: bank2 * TS + 128]), deps=[pt])
        bank_read(ctx, bank2, cb)
        rbusy[tc % 2] = cb
    p.join()


def emit_moe_experts(ctx, sb, dram, wst, n_routed=7):
    p = ctx.p
    emit_gated_ffn(ctx, sb, xb=sb.xb, wg=dram["sgT"], wu=dram["suT"],
                   wd=dram["sdT"], HID=2816, wst=wst, act=AF.Silu)
    sel7 = sb.const["SEL7"]
    for e in range(n_routed):
        for t in range(ctx.NS):
            bank = ctx.ring()
            bt = acc_group(ctx, bank, [(sel7[0:7, e * 128:(e + 1) * 128],
                                        sb.w_rows[0:7, t * TS:(t + 1) * TS])],
                           deps=[sb.w_rows_tok] if sb.w_rows_tok else ())
            ct = p.op("vector", lambda eng, t=t, bank=bank: eng.tensor_copy(
                sb.wbb[:, t * TS:(t + 1) * TS], ctx.bank(bank)), deps=[bt])
            bank_read(ctx, bank, ct)
        p.join()
        emit_gated_ffn(ctx, sb, xb=sb.xb,
                       wg=dram["rgT"][e * D:(e + 1) * D, :],
                       wu=dram["ruT"][e * D:(e + 1) * D, :],
                       wd=dram["rdT"][e * 2816:(e + 1) * 2816, :],
                       HID=2816, wst=wst, act=AF.Silu, wb_ap=sb.wbb[:, :])
    p.join()




def emit_phi_evac(ctx, sb, dst_ap, bank, gtok, bias_ap, neg_bias_ap):
    p = ctx.p
    i = sb.phi_i % 2
    sb.phi_i += 1
    sA = sb.phi_s[3 * i]
    sB = sb.phi_s[3 * i + 1]
    sC = sb.phi_s[3 * i + 2]
    kw1 = {} if bias_ap is None else {"bias": bias_ap}
    kw2 = {} if neg_bias_ap is None else {"bias": neg_bias_ap}
    r1 = p.op("scalar", lambda eng, kw1=kw1: eng.activation(sA, ctx.bank(bank), AF.Relu, **kw1),
              deps=merge([gtok, sb.phi_busy[i]]))
    r2 = p.op("scalar", lambda eng, kw2=kw2: eng.activation(sB, ctx.bank(bank), AF.Relu,
                                                            scale=-1.0, **kw2))
    e2 = p.op("scalar", lambda eng: eng.activation(sC, sB, AF.Exp, scale=-1.0),
              deps=[r2], strict=True)
    w = p.op("vector", lambda eng: eng.tensor_tensor(sB, sA, sC, ALU.add),
             deps=[e2])
    f1 = p.op("scalar", lambda eng, dst=dst_ap: eng.activation(dst, sB, AF.Copy),
              deps=[w])
    sb.phi_busy[i] = f1
    return f1


def emit_l1_phase_a(ctx, sb, dram, wst, pq_dram, tile):
    p = ctx.p
    nc = ctx.nc
    T = ctx.T
    t0 = tile * T
    NSl = ctx.NS
    emit_cast(ctx, sb.x, sb.xb)
    p.join()
    bq = sb.vec["bq_col"]
    nbq = sb.vec_neg["bq_col"]
    for fb in range(0, ND, 4):
        slot, wsl, wtok = wst.load(dram["wqT"], 0, ND, fb * 128, 512)
        for mf in range(4):
            f = fb + mf
            toks = []
            for t in range(NSl):
                bank = ctx.ring()
                gt = acc_group(ctx, bank, [(wsl(c, mf * 128, (mf + 1) * 128),
                                            sb.xb.sl(c, t * TS, (t + 1) * TS))
                                           for c in range(ND)], deps=[wtok])
                wst.done(slot, gt)
                if getattr(ctx, "dbg_relu_only", False):
                    w = p.op("scalar", lambda eng, f=f, t=t, bank=bank: eng.activation(
                        sb.h.sl(f, t * TS, (t + 1) * TS), ctx.bank(bank), AF.Relu,
                        bias=bq[:, f:f + 1]), deps=[gt])
                else:
                    w = emit_phi_evac(ctx, sb, sb.h.sl(f, t * TS, (t + 1) * TS), bank, gt,
                                      bq[:, f:f + 1], nbq[:, f:f + 1])
                bank_read(ctx, bank, w)
                toks.append(w)
            p.dma("sync", pq_dram[f * 128:(f + 1) * 128, t0:t0 + T], sb.h.chunk(f),
                  deps=toks)
    NTC = T // 128
    bkb = sb.vecrow["bk_row"]
    bvb = sb.vecrow["bv_row"]
    onesr = sb.ones_row_bf
    onesc = sb.ones_bf
    for half in range(2):
        kslot, ksl, ktok = wst.load(dram["wkT"], 0, ND, half * 512, 512)
        vslot, vsl, vtok = wst.load(dram["wvT"], 0, ND, half * 512, 512)
        for tc in range(NTC):
            ktm = sb.ktm[tc % 2]
            vtm = sb.vtm[tc % 2]
            kv_toks = []
            bank = ctx.ring()
            mms = [(sb.xb.sl(c, tc * 128, (tc + 1) * 128),
                    ksl(c, 0, 512)) for c in range(ND)]
            mms.append((onesr[0:1, :], bkb[0:1, half * 512:(half + 1) * 512]))
            gt = acc_group(ctx, bank, mms, deps=[ktok, sb.ktm_busy[tc % 2]])
            wst.done(kslot, gt)
            w = emit_phi_evac_tm(ctx, sb, ktm[:, 0:512], bank, gt)
            bank_read(ctx, bank, w)
            kv_toks.append(w)
            bank = ctx.ring()
            mms = [(sb.xb.sl(c, tc * 128, (tc + 1) * 128),
                    vsl(c, 0, 512)) for c in range(ND)]
            mms.append((onesr[0:1, :], bvb[0:1, half * 512:(half + 1) * 512]))
            gt = acc_group(ctx, bank, mms, deps=[vtok, sb.vtm_busy[tc % 2]])
            wst.done(vslot, gt)
            w = p.op("scalar", lambda eng, vtm=vtm, bank=bank: eng.activation(
                vtm[:, 0:512], ctx.bank(bank), AF.Copy), deps=[gt])
            bank_read(ctx, bank, w)
            kv_toks.append(w)
            bank = ctx.ring()
            fdep = merge(kv_toks, [ctx.bank_free[bank]])
            last = None
            for hh in range(8):
                h = half * 8 + hh
                p0 = (h % 2) * 64
                c0 = (h // 2) * 64
                last = p.op("tensor",
                            lambda eng, hh=hh, p0=p0, c0=c0, bank=bank, ktm=ktm, vtm=vtm:
                            eng.matmul(ctx.ps[p0:p0 + 64, bank * TS + c0: bank * TS + c0 + 64],
                                       ktm[:, hh * 64:(hh + 1) * 64],
                                       vtm[:, hh * 64:(hh + 1) * 64],
                                       start=True, stop=True),
                            deps=fdep if hh == 0 else (), inc=(1 if hh == 7 else 0))
            ctx.bank_ready[bank] = last
            ctx.bank_free[bank] = None
            cola = (half * 8 // 2) * 64
            at = p.op("vector", lambda eng, bank=bank, cola=cola: eng.tensor_tensor(
                sb.kvt[:, cola:cola + 256], sb.kvt[:, cola:cola + 256],
                ctx.ps[0:128, bank * TS + cola: bank * TS + cola + 256], ALU.add),
                deps=[last])
            bank_read(ctx, bank, at)
            bank = ctx.ring()
            fdep = merge(kv_toks, [ctx.bank_free[bank]])
            last = None
            for db in range(4):
                last = p.op("tensor", lambda eng, db=db, bank=bank, ktm=ktm:
                            eng.matmul(ctx.ps[0:128, bank * TS + db: bank * TS + db + 1],
                                       ktm[:, db * 128:(db + 1) * 128], onesc[:, 0:1],
                                       start=True, stop=True),
                            deps=fdep if db == 0 else (), inc=(1 if db == 3 else 0))
            ctx.bank_ready[bank] = last
            ctx.bank_free[bank] = None
            at2 = p.sop("vector", lambda eng, bank=bank, half=half: eng.tensor_tensor(
                sb.ksum_col[:, half * 4:(half + 1) * 4],
                sb.ksum_col[:, half * 4:(half + 1) * 4],
                ctx.ps[0:128, bank * TS: bank * TS + 4], ALU.add), deps=[at, last])
            bank_read(ctx, bank, at2)
            sb.ktm_busy[tc % 2] = last
            sb.vtm_busy[tc % 2] = last
    p.join()


def emit_phi_evac_tm(ctx, sb, dst_ap, bank, gtok):
    return emit_phi_evac(ctx, sb, dst_ap, bank, gtok, None, None)


def emit_l1_phase_b(ctx, sb, dram, wst, pq_dram, xin_dram, tile, out_bf=None):
    p = ctx.p
    nc = ctx.nc
    T = ctx.T
    t0 = tile * T
    NSl = ctx.NS
    qdep = [("Ssy", p.cnt["Ssy"])] if p.cnt["Ssy"] else []
    for c in range(ND):
        p.dma("sync", sb.x.chunk(c), xin_dram[c * 128:(c + 1) * 128, t0:t0 + T],
              deps=qdep if c == 0 else (), strict=True)
        p.dma("sync", sb.h.chunk(c), pq_dram[c * 128:(c + 1) * 128, t0:t0 + T])
    p.join()
    for t in range(NSl):
        bank = ctx.ring()
        gt = acc_group(ctx, bank, [(sb.ks_cb.chunk(c), sb.h.sl(c, t * TS, (t + 1) * TS))
                                   for c in range(ND)], deps=[sb.ks_tok], p0=0, p1=16)
        zt = p.op("vector", lambda eng, t=t, bank=bank: eng.reciprocal(
            sb.zrow[0:16, t * TS:(t + 1) * TS], ctx.bank(bank, 0, 16)), deps=[gt])
        bank_read(ctx, bank, zt)
    p.join()
    sel16 = sb.const["SEL16"]
    pq_toks = {}
    for c in range(ND):
        for t in range(NSl):
            bank = ctx.ring()
            bt = acc_group(ctx, bank, [(sel16[0:16, c * 128:(c + 1) * 128],
                                        sb.zrow[0:16, t * TS:(t + 1) * TS])])
            w = p.op("vector", lambda eng, c=c, t=t, bank=bank: eng.tensor_tensor(
                sb.xb.sl(c, t * TS, (t + 1) * TS), sb.h.sl(c, t * TS, (t + 1) * TS),
                ctx.bank(bank), ALU.mult), deps=[bt])
            bank_read(ctx, bank, w)
            pq_toks[(c, t)] = w
    kb = p.op("scalar", lambda eng: eng.activation(sb.kvt_bf[:, :], sb.kvt[:, :], AF.Copy))
    for fc in range(ND):
        for t in range(NSl):
            bank = ctx.ring()
            dep0 = merge([kb, pq_toks[(fc, t)], ctx.bank_free[bank]])
            last = None
            for half in range(2):
                h = 2 * fc + half
                p0 = half * 64
                c0 = (h // 2) * 64
                last = p.op("tensor", lambda eng, p0=p0, c0=c0, bank=bank, fc=fc, t=t:
                            eng.matmul(ctx.ps[p0:p0 + 64, bank * TS: bank * TS + TS],
                                       sb.kvt_bf[p0:p0 + 64, c0:c0 + 64],
                                       sb.xb.h[p0:p0 + 64,
                                               fc * T + t * TS: fc * T + (t + 1) * TS],
                                       start=True, stop=True),
                            deps=dep0 if half == 0 else (), inc=(1 if half == 1 else 0))
            ctx.bank_ready[bank] = last
            ctx.bank_free[bank] = None
            ev = p.op("scalar", lambda eng, fc=fc, t=t, bank=bank: eng.activation(
                sb.h.sl(fc, t * TS, (t + 1) * TS), ctx.bank(bank), AF.Copy), deps=[last])
            bank_read(ctx, bank, ev)
    p.join()
    bd = sb.vec["bd_col"]
    for fb in range(0, ND, 4):
        slot, wsl, wtok = wst.load(dram["wdT"], 0, ND, fb * 128, 512)
        for mf in range(4):
            f = fb + mf
            for t in range(NSl):
                bank = ctx.ring()
                gt = acc_group(ctx, bank, [(wsl(c, mf * 128, (mf + 1) * 128),
                                            sb.h.sl(c, t * TS, (t + 1) * TS))
                                           for c in range(ND)], deps=[wtok])
                wst.done(slot, gt)
                ev = p.op("vector", lambda eng, f=f, t=t, bank=bank:
                          eng.scalar_tensor_tensor(
                              sb.x.sl(f, t * TS, (t + 1) * TS), ctx.bank(bank),
                              bd[:, f:f + 1], sb.x.sl(f, t * TS, (t + 1) * TS),
                              ALU.add, ALU.add), deps=[gt])
                bank_read(ctx, bank, ev)
    p.join()
    xs2 = sb.xs2

    def add_f32(c, t, xin=xin_dram, t0=t0):
        dep = [("Sdv", ctx.p.cnt["Sdv"])] if ctx.p.cnt["Sdv"] else []
        tok = ctx.p.dma("sync", xs2[:, c * TS:(c + 1) * TS],
                        xin[c * 128:(c + 1) * 128, t0 + t * TS: t0 + (t + 1) * TS],
                        deps=dep, strict=True)
        return xs2[:, c * TS:(c + 1) * TS], tok
    emit_ln(ctx, sb, g=sb.vec["l1g"], b=sb.vec["l1b"], eps=EPS_L1LN, add_f32=add_f32)
    emit_ln(ctx, sb, eps=EPS_TOP, out_bf=out_bf)




def setup_l1(ctx, sb, dram):
    nc, p = ctx.nc, ctx.p
    T = ctx.T
    sb.ktm = [nc.alloc_sbuf_tensor(f"ktm{i}", [128, 512], BF16) for i in range(2)]
    sb.vtm = [nc.alloc_sbuf_tensor(f"vtm{i}", [128, 512], BF16) for i in range(2)]
    sb.ktm_busy = [None, None]
    sb.vtm_busy = [None, None]
    phi_h = nc.alloc_sbuf_tensor("phi_h", [128, 6 * TS], F32)
    sb.phi_s = [phi_h[:, i * TS:(i + 1) * TS] for i in range(6)]
    sb.phi_busy = [None, None]
    sb.phi_i = 0
    sb.kvt = nc.alloc_sbuf_tensor("kvt", [128, 512], F32)
    sb.kvt_bf = sb.ktm[0][:, :]
    sb.ksum_col = nc.alloc_sbuf_tensor("ksum_col", [128, 8], F32)
    sb.zrow = phi_h[0:16, 0:T]
    if getattr(sb, "ks_cb", None) is None:
        sb.ks_cb = CBuf(nc, "ks_cb", ND, 16, BF16)
    sb.ks_tok = None
    sb.xs2 = sb.h.h[:, 4 * T:8 * T].bitcast(F32)
    p.op("gpsimd", lambda eng: eng.memset(sb.kvt[:, :], 0.0), inc=0)
    mt = p.op("gpsimd", lambda eng: eng.memset(sb.ksum_col[:, :], 0.0))
    sb.kv_zero_tok = mt
    p.join()
    sb.vec_neg = {}
    for nm in ("bq_col",):
        h = nc.alloc_sbuf_tensor("vn_" + nm, [128, 8], F32)
        p.op("vector", lambda eng, nm=nm, h=h: eng.tensor_scalar_mul(
            h[:, :], sb.vec[nm][:, :], -1.0))
        sb.vec_neg[nm] = h
    p.join()


def emit_ks_build(ctx, sb):
    p = ctx.p
    nc = ctx.nc
    mz = p.op("gpsimd", lambda eng: eng.memset(sb.ks_cb.h[:, :], 0.0))
    ke = p.op("vector", lambda eng: eng.tensor_scalar_add(
        sb.ksum_col[:, :], sb.ksum_col[:, :], float(EPS_ATTN)))
    last = None
    for c in range(ND):
        p.op("scalar", lambda eng, c=c: eng.copy(
            sb.ks_cb.h[0:64, c * 16 + 2 * c: c * 16 + 2 * c + 1],
            sb.ksum_col[0:64, c:c + 1]), deps=[ke, mz], inc=0)
        last = p.op("scalar", lambda eng, c=c: eng.copy(
            sb.ks_cb.h[64:128, c * 16 + 2 * c + 1: c * 16 + 2 * c + 2],
            sb.ksum_col[64:128, c:c + 1]))
    sb.ks_tok = last
    p.join()


def build_l1_program(T, debug=None):
    nc = bass.Bass("TRN2", target_bir_lowering=False, debug=False)
    p = Prog(nc)
    ctx = Ctx(nc, p, T)
    consts = host_consts()
    sb = SB(ctx, T, consts, need=("SEL16",))
    if debug == "relu":
        ctx.dbg_relu_only = True
        debug = "a"
    xin = nc.dram_tensor("s_in", [D, SEQ], F32, kind="ExternalInput").ap()
    xout = nc.dram_tensor("s_out", [D, SEQ], F32, kind="ExternalOutput").ap()
    pq = nc.dram_tensor("pq_scr", [D, SEQ], BF16,
                        kind=("ExternalOutput" if debug == "a" else "Internal")).ap()
    if debug == "a":
        dbg_kvt = nc.dram_tensor("dbg_kvt", [128, 512], F32, kind="ExternalOutput").ap()
        dbg_ks = nc.dram_tensor("dbg_ks", [128, 8], F32, kind="ExternalOutput").ap()
    dram = {}
    for nm, dn in (("wqT", "l1_wq"), ("wkT", "l1_wk"), ("wvT", "l1_wv"), ("wdT", "l1_wd")):
        dram[nm] = nc.dram_tensor(dn, [D, D], F32, kind="ExternalInput").ap()
    load_vec_inputs(ctx, sb, {"bq_col": (128, 8), "bd_col": (128, 8),
                              "l1g": (128, 8), "l1b": (128, 8),
                              "bk_row": (1, 1024), "bv_row": (1, 1024)},
                    bf_rows=("bk_row", "bv_row"))
    setup_l1(ctx, sb, dram)
    wst = WStream(ctx, "w", nbuf=2)
    for tile in range(SEQ // T):
        t0 = tile * T
        for c in range(ND):
            p.dma("sync", sb.x.chunk(c), xin[c * 128:(c + 1) * 128, t0:t0 + T],
                  strict=True)
        p.join()
        emit_l1_phase_a(ctx, sb, dram, wst, pq, tile)
    if debug == "a":
        p.dma("sync", dbg_kvt[:, :], sb.kvt[:, :])
        p.dma("sync", dbg_ks[:, :], sb.ksum_col[:, :])
        p.dma("sync", xout[0:128, 0:T], sb.x.chunk(0))
        p._wait("gpsimd", "Ssy", p.cnt["Ssy"])
        p.build()
        return nc
    emit_ks_build(ctx, sb)
    prev_store = None
    for tile in range(SEQ // T):
        t0 = tile * T
        emit_l1_phase_b(ctx, sb, dram, wst, pq, xin, tile)
        for c in range(ND):
            prev_store = p.dma("sync", xout[c * 128:(c + 1) * 128, t0:t0 + T],
                               sb.x.chunk(c))
        p.join()
    p._wait("gpsimd", "Ssy", p.cnt["Ssy"])
    p.build()
    return nc





def setup_l0(ctx, sb, consts):
    nc, p = ctx.nc, ctx.p
    sb.dt_b = nc.alloc_sbuf_tensor("dt_b", [128, NH * CH], BF16)
    sb.qd_b = nc.alloc_sbuf_tensor("qd_b", [128, ND * CH], BF16)
    sb.kd_b = nc.alloc_sbuf_tensor("kd_b", [128, 2 * 512], BF16)
    sb.gc_f = nc.alloc_sbuf_tensor("gc_f", [128, 512], F32)
    for name, dst, w in (("DT", sb.dt_b, CH), ("QD", sb.qd_b, CH), ("KD", sb.kd_b, 512)):
        arr = consts[name]
        dr = nc.inline_tensor(np.ascontiguousarray(arr), "ic_" + name)
        n = arr.shape[0] // 128
        for i in range(n):
            p.dma("gpsimd", dst[:, i * w:(i + 1) * w], dr.ap()[i * 128:(i + 1) * 128, :])
    drg = nc.inline_tensor(np.ascontiguousarray(consts["GC"]), "ic_GC")
    p.dma("sync", sb.gc_f[:, :], drg.ap())
    sb.S = [nc.alloc_sbuf_tensor(f"S{i}", [128, 512], F32) for i in range(2)]
    sb.S_bf = [nc.alloc_sbuf_tensor(f"Sbf{i}", [128, 512], BF16) for i in range(2)]
    sb.S_tok = [None, None]
    sb.at_bf = [nc.alloc_sbuf_tensor(f"atbf{i}", [128, CH], BF16) for i in range(4)]
    sb.at_busy = [None] * 4
    sb.rr = nc.alloc_sbuf_tensor("rr", [128, 4 * 4096], BF16)
    sb.pbrow = nc.alloc_sbuf_tensor("pbrow", [1, 2048], BF16)
    sb.krt = [nc.alloc_sbuf_tensor(f"krt{i}", [128, 1024], BF16) for i in range(4)]
    sb.vrt = [nc.alloc_sbuf_tensor(f"vrt{i}", [128, 1024], BF16) for i in range(4)]
    sb.krt_busy = [None] * 4
    sb.vrt_busy = [None] * 4
    for i in range(2):
        p.op("gpsimd", lambda eng, i=i: eng.memset(sb.S[i][:, :], 0.0), inc=0)
        p.op("gpsimd", lambda eng, i=i: eng.memset(sb.S_bf[i][:, :], 0.0))
    p.join()


def emit_retention_mixer(ctx, sb, dram, wst, layer):
    p = ctx.p
    T = ctx.T
    pw = dram[f"pw{layer}"]
    rw = dram[f"rw{layer}"]
    pbc = sb.vec[f"pb{layer}_col"]
    pbr = sb.pbrow
    p.dma("gpsimd", pbr[0:1, :], dram[f"pbrow{layer}"][0:1, :], deps=p.now())
    rbc = sb.vec[f"rb{layer}"]
    onesr = sb.ones_row_bf
    S = sb.S[layer]
    S_bf = sb.S_bf[layer]
    h = sb.h
    rr = sb.rr

    def reg(base, fc, a, b):
        r = base // 2
        return rr[:, r * 4096 + fc * 512 + a: r * 4096 + fc * 512 + b]

    NSC = T // 512
    for sc in range(NSC):
        tok0 = sc * 512
        for colbase, kind in ((0, "q"), (1024, "k"), (3072, "xo")):
            for fb in (0, 4):
                slot, wsl, wtok = wst.load(pw, 0, ND, colbase + fb * 128, 512)
                for mf in range(4):
                    f = fb + mf
                    fg = (colbase // 128) + f
                    bank = ctx.ring()
                    gt = acc_group(ctx, bank,
                                   [(wsl(c, mf * 128, (mf + 1) * 128),
                                     sb.xb.sl(c, tok0, tok0 + 512)) for c in range(ND)],
                                   deps=[wtok])
                    wst.done(slot, gt)
                    pb_ap = pbc[:, fg:fg + 1]
                    pk = ctx.bank(bank)
                    if kind == "q":
                        qd_ap = sb.qd_b[:, f * CH:(f + 1) * CH]
                        for tci4 in range(4):
                            dst = reg(0, f, tci4 * CH, (tci4 + 1) * CH)
                            pk4 = ctx.ps[0:128, bank * TS + tci4 * CH: bank * TS + (tci4 + 1) * CH]
                            ev = p.op("vector",
                                      lambda eng, dst=dst, pk4=pk4, pb_ap=pb_ap, qd_ap=qd_ap:
                                      eng.scalar_tensor_tensor(dst, pk4, pb_ap, qd_ap,
                                                               ALU.add, ALU.mult),
                                      deps=[gt] if tci4 == 0 else ())
                    elif kind == "k":
                        dst = reg(2, f, 0, 512)
                        ev = p.op("vector", lambda eng, dst=dst, pk=pk, pb_ap=pb_ap:
                                  eng.tensor_scalar_add(dst, pk, pb_ap), deps=[gt])
                    else:
                        dst = reg(4, f, 0, 512)
                        ev = p.op("scalar", lambda eng, dst=dst, pk=pk, pb_ap=pb_ap:
                                  eng.activation(dst, pk, AF.Silu, bias=pb_ap), deps=[gt])
                    bank_read(ctx, bank, ev)
        for half in range(2):
            kslot, ksl, ktok = wst.load(pw, 0, ND, 1024 + half * 512, 512)
            vslot, vsl, vtok = wst.load(pw, 0, ND, 2048 + half * 512, 512)
            for tci in range(4):
                tc = sc * 4 + tci
                krt = sb.krt[tci]
                vrt = sb.vrt[tci]
                bank = ctx.ring()
                mms = [(sb.xb.sl(c, tc * 128, (tc + 1) * 128), ksl(c, 0, 512))
                       for c in range(ND)]
                mms.append((onesr[0:1, :], pbr[0:1, half * 512:(half + 1) * 512]))
                gt = acc_group(ctx, bank, mms, deps=[ktok, sb.krt_busy[tci]])
                wst.done(kslot, gt)
                dst = krt[:, half * 512:(half + 1) * 512]
                kd_ap = sb.kd_b[:, half * 512:(half + 1) * 512]
                pk = ctx.bank(bank)
                ev = p.op("vector", lambda eng, dst=dst, pk=pk, kd_ap=kd_ap:
                          eng.tensor_tensor(dst, pk, kd_ap, ALU.mult), deps=[gt])
                bank_read(ctx, bank, ev)
                bank = ctx.ring()
                mms = [(sb.xb.sl(c, tc * 128, (tc + 1) * 128), vsl(c, 0, 512))
                       for c in range(ND)]
                mms.append((onesr[0:1, :], pbr[0:1, 1024 + half * 512:1024 + (half + 1) * 512]))
                gt = acc_group(ctx, bank, mms, deps=[vtok, sb.vrt_busy[tci]])
                wst.done(vslot, gt)
                dst = vrt[:, half * 512:(half + 1) * 512]
                pk = ctx.bank(bank)
                ev = p.op("scalar", lambda eng, dst=dst, pk=pk:
                          eng.activation(dst, pk, AF.Copy), deps=[gt])
                bank_read(ctx, bank, ev)
        p.join()
        ctx.ring_banks = [0, 1, 2, 3, 4]
        ret2_evs = []
        for tci in range(4):
            tc = sc * 4 + tci
            cofs = tci * 128
            krt = sb.krt[tci]
            vrt = sb.vrt[tci]
            ret_banks = [6, 7]
            ret_last = {}
            for hh in range(NH):
                hp = (hh % 2) * 64
                fc = hh // 2
                k_ap = rr[hp:hp + 64, 1 * 4096 + fc * 512 + cofs: 1 * 4096 + fc * 512 + cofs + 128]
                q_ap = rr[hp:hp + 64, 0 * 4096 + fc * 512 + cofs: 0 * 4096 + fc * 512 + cofs + 128]
                bank = ctx.ring()
                at_out = ctx.bank(bank, 0, 128, CH)
                d0 = merge([ctx.bank_free[bank], sb.at_busy[hh % 4]])
                at_mm = p.op("tensor", lambda eng, at_out=at_out, k_ap=k_ap, q_ap=q_ap:
                             eng.matmul(at_out, k_ap, q_ap, start=True, stop=True),
                             deps=d0)
                ctx.bank_ready[bank] = at_mm
                ctx.bank_free[bank] = None
                atb = sb.at_bf[hh % 4]
                dt_ap = sb.dt_b[:, hh * CH:(hh + 1) * CH]
                at_ev = p.op("vector", lambda eng, atb=atb, at_out=at_out, dt_ap=dt_ap:
                             eng.tensor_tensor(atb[:, :], at_out, dt_ap, ALU.mult),
                             deps=[at_mm])
                bank_read(ctx, bank, at_ev)
                sb.at_busy[hh % 4] = at_ev
                rbank = ret_banks[hh // 8]
                cblk = ((hh // 2) % 4) * 128
                r_out = ctx.ps[hp:hp + 64, rbank * TS + cblk: rbank * TS + cblk + 128]
                v_ap = vrt[:, hh * 64:(hh + 1) * 64]
                s_ap = S_bf[hp:hp + 64, fc * 64:fc * 64 + 64]
                rdep = merge([at_ev, sb.S_tok[layer]] +
                             ([ctx.bank_free[rbank]] if hh % 8 == 0 else []))
                p.op("tensor", lambda eng, r_out=r_out, v_ap=v_ap, atb=atb:
                     eng.matmul(r_out, v_ap, atb[:, :], start=True, stop=False),
                     deps=rdep, inc=0)
                rt = p.op("tensor", lambda eng, r_out=r_out, s_ap=s_ap, q_ap=q_ap:
                          eng.matmul(r_out, s_ap, q_ap, start=False, stop=True))
                ret_last[hh] = rt
                if hh % 8 == 7:
                    ctx.bank_ready[rbank] = rt
                    ctx.bank_free[rbank] = None
            for fc in range(ND):
                rbank = ret_banks[fc // 4]
                cblk = (fc % 4) * 128
                dst = reg(6, fc, cofs, cofs + 128)
                ps_ap = ctx.ps[0:128, rbank * TS + cblk: rbank * TS + cblk + 128]
                xo_ap = rr[:, 2 * 4096 + fc * 512 + cofs: 2 * 4096 + fc * 512 + cofs + 128]
                ev = p.op("vector", lambda eng, dst=dst, ps_ap=ps_ap, xo_ap=xo_ap:
                          eng.tensor_tensor(dst, ps_ap, xo_ap, ALU.mult),
                          deps=[ret_last[fc * 2 + 1]])
                bank_read(ctx, rbank, ev)
                ret2_evs.append(ev)
            ibank = 5
            ilast = None
            for hh in range(NH):
                hp = (hh % 2) * 64
                c0 = (hh // 2) * 64
                i_out = ctx.ps[hp:hp + 64, ibank * TS + c0: ibank * TS + c0 + 64]
                kk = krt[:, hh * 64:(hh + 1) * 64]
                vv = vrt[:, hh * 64:(hh + 1) * 64]
                d0 = merge([ctx.bank_free[ibank]]) if hh == 0 else []
                ilast = p.op("tensor", lambda eng, i_out=i_out, kk=kk, vv=vv:
                             eng.matmul(i_out, kk, vv, start=True, stop=True),
                             deps=d0, inc=(1 if hh == NH - 1 else 0))
            ctx.bank_ready[ibank] = ilast
            ctx.bank_free[ibank] = None
            sd = merge([ilast, sb.S_tok[layer]] + [ret_last[hh] for hh in range(NH)])
            p.op("vector", lambda eng: eng.tensor_tensor(S[:, :], S[:, :], sb.gc_f[:, :],
                                                         ALU.mult), deps=sd, inc=0)
            s2 = p.op("vector", lambda eng, ibank=ibank: eng.tensor_tensor(
                S[:, :], S[:, :], ctx.bank(ibank), ALU.add))
            bank_read(ctx, ibank, s2)
            sbf = p.op("scalar", lambda eng: eng.activation(S_bf[:, :], S[:, :], AF.Copy),
                       deps=[s2])
            sb.S_tok[layer] = sbf
            sb.krt_busy[tci] = ilast
            sb.vrt_busy[tci] = ilast
        ctx.ring_banks = None
        for fb in (0, 4):
            slot, wsl, wtok = wst.load(rw, 0, ND, fb * 128, 512)
            for mf in range(4):
                f = fb + mf
                bank = ctx.ring()
                gt = acc_group(ctx, bank,
                               [(wsl(c, mf * 128, (mf + 1) * 128),
                                 reg(6, c, 0, 512)) for c in range(ND)],
                               deps=merge([wtok], ret2_evs))
                wst.done(slot, gt)
                x_ap = sb.x.sl(f, tok0, tok0 + 512)
                pk = ctx.bank(bank)
                rb_ap = rbc[:, f:f + 1]
                ev = p.op("vector", lambda eng, x_ap=x_ap, pk=pk, rb_ap=rb_ap:
                          eng.scalar_tensor_tensor(x_ap, pk, rb_ap, x_ap,
                                                   ALU.add, ALU.add), deps=[gt])
                bank_read(ctx, bank, ev)
    p.join()


def emit_plain_ffn(ctx, sb, *, xb, w1, w2, b1_col, b2_col, HID, wst, act=AF.Gelu):
    p = ctx.p
    T = ctx.T
    NSl = ctx.NS
    NHC = HID // 128
    h = sb.h
    for g0 in range(0, NHC, 8):
        ng = min(8, NHC - g0)
        gl = {}
        m_tok = None
        for gi in range(ng):
            half = gi // 4
            if half not in gl:
                nw = min(4, ng - half * 4) * 128
                gl[half] = wst.load(w1, 0, ND, (g0 + half * 4) * 128, nw)
            slot, wsl, wtok = gl[half]
            gcol = (gi % 4) * 128
            bg = 4 + (gi % 2)
            for t in range(NSl):
                gt = acc_group(ctx, bg, [(wsl(c, gcol, gcol + 128),
                                          xb.sl(c, t * TS, (t + 1) * TS)) for c in range(ND)],
                               deps=[wtok])
                wst.done(slot, gt)
                m_tok = p.op("scalar", lambda eng, gi=gi, t=t, bg=bg, g0=g0: eng.activation(
                    h.sl(gi, t * TS, (t + 1) * TS), ctx.bank(bg), act,
                    bias=b1_col[:, g0 + gi:g0 + gi + 1]), deps=[gt])
                bank_read(ctx, bg, m_tok)
        dl = []
        for half in range(0, ng, 4):
            nked = min(4, ng - half)
            dl.append(wst.load(w2, (g0 + half) * 128, nked, 0, 1024))
        for f2 in range(ND):
            for t in range(NSl):
                bank = t if (f2 % 2 == 0) else 4 + t
                mms = []
                dep0 = [dl[-1][2], m_tok]
                for gi in range(ng):
                    dslot, dsl, dtok = dl[gi // 4]
                    mms.append((dsl(gi % 4, f2 * 128, (f2 + 1) * 128),
                                h.sl(gi, t * TS, (t + 1) * TS)))
                gt = acc_group(ctx, bank, mms, deps=dep0)
                for dslot, dsl, dtok in dl:
                    wst.done(dslot, gt)
                if g0 == 0:
                    ev = p.op("vector", lambda eng, f2=f2, t=t, bank=bank:
                              eng.scalar_tensor_tensor(
                                  sb.x.sl(f2, t * TS, (t + 1) * TS), ctx.bank(bank),
                                  b2_col[:, f2:f2 + 1],
                                  sb.x.sl(f2, t * TS, (t + 1) * TS), ALU.add, ALU.add),
                              deps=[gt])
                else:
                    ev = p.op("vector", lambda eng, f2=f2, t=t, bank=bank:
                              eng.tensor_tensor(
                                  sb.x.sl(f2, t * TS, (t + 1) * TS),
                                  sb.x.sl(f2, t * TS, (t + 1) * TS),
                                  ctx.bank(bank), ALU.add), deps=[gt])
                bank_read(ctx, bank, ev)
    p.join()


def emit_l0attn(ctx, sb, dram, wst, xin_dram, tile):
    p = ctx.p
    T = ctx.T
    t0 = tile * T
    dbg = getattr(ctx, "l0_dbg", None)
    for layer in range(2):
        emit_ln(ctx, sb, g=sb.vec[f"ln1g{layer}"], b=sb.vec[f"ln1b{layer}"],
                eps=EPS_TOP, out_bf=sb.xb, bf_only=True)
        if dbg is not None and layer == 0 and tile == 0 and "lnx" in dbg:
            p.join()
            for c in range(ND):
                p.dma("sync", dbg["lnx"][c * 128:(c + 1) * 128, 0:T], sb.xb.chunk(c))
            p.join()
        emit_retention_mixer(ctx, sb, dram, wst, layer)
        if dbg is not None and layer == 0 and tile == 0 and "mix" in dbg:
            p.join()
            for c in range(ND):
                p.dma("sync", dbg["mix"][c * 128:(c + 1) * 128, 0:T], sb.x.chunk(c))
            p.join()
        emit_ln(ctx, sb, g=sb.vec[f"ln2g{layer}"], b=sb.vec[f"ln2b{layer}"],
                eps=EPS_TOP, out_bf=sb.xb, bf_only=True)
        emit_plain_ffn(ctx, sb, xb=sb.xb, w1=dram[f"f1w{layer}"], w2=dram[f"f2w{layer}"],
                       b1_col=sb.vec[f"f1b{layer}"], b2_col=sb.vec[f"f2b{layer}"],
                       HID=4096, wst=wst, act=AF.Gelu)
        if dbg is not None and layer == 0 and tile == 0 and "ffn" in dbg:
            p.join()
            for c in range(ND):
                p.dma("sync", dbg["ffn"][c * 128:(c + 1) * 128, 0:T], sb.x.chunk(c))
            p.join()
    def add_f32(c, t, xin=xin_dram, t0=t0):
        dep = [("Sdv", p.cnt["Sdv"])] if p.cnt["Sdv"] else []
        tok = p.dma("sync", sb.xs2[:, c * TS:(c + 1) * TS],
                    xin[c * 128:(c + 1) * 128, t0 + t * TS: t0 + (t + 1) * TS],
                    deps=dep, strict=True)
        return sb.xs2[:, c * TS:(c + 1) * TS], tok
    emit_ln(ctx, sb, g=sb.vec["lnfg"], b=sb.vec["lnfb"], eps=EPS_TOP, add_f32=add_f32)
    emit_ln(ctx, sb, eps=EPS_TOP)


def build_l0attn_program(T, debug=()):
    nc = bass.Bass("TRN2", target_bir_lowering=False, debug=False)
    p = Prog(nc)
    ctx = Ctx(nc, p, T)
    consts = host_consts()
    sb = SB(ctx, T, consts)
    xin = nc.dram_tensor("s_in", [D, SEQ], F32, kind="ExternalInput").ap()
    xout = nc.dram_tensor("s_out", [D, SEQ], F32, kind="ExternalOutput").ap()
    if debug:
        ctx.l0_dbg = {}
        for nm in debug:
            dt_ = BF16 if nm == "lnx" else F32
            ctx.l0_dbg[nm] = nc.dram_tensor("dbg_" + nm, [D, T], dt_, kind="ExternalOutput").ap()
    dram = {}
    for i in range(2):
        dram[f"pw{i}"] = nc.dram_tensor(f"pw{i}", [D, 4 * D], F32, kind="ExternalInput").ap()
        dram[f"rw{i}"] = nc.dram_tensor(f"rw{i}", [D, D], F32, kind="ExternalInput").ap()
        dram[f"f1w{i}"] = nc.dram_tensor(f"f1w{i}", [D, 4 * D], F32, kind="ExternalInput").ap()
        dram[f"f2w{i}"] = nc.dram_tensor(f"f2w{i}", [4 * D, D], F32, kind="ExternalInput").ap()
    vecs = {"lnfg": (128, 8), "lnfb": (128, 8)}
    rows = []
    for i in range(2):
        for nm in (f"rb{i}", f"f2b{i}", f"ln1g{i}", f"ln1b{i}",
                   f"ln2g{i}", f"ln2b{i}"):
            vecs[nm] = (128, 8)
        vecs[f"pb{i}_col"] = (128, 32)
        vecs[f"f1b{i}"] = (128, 32)
        dram[f"pbrow{i}"] = nc.dram_tensor(f"pb{i}_row", [1, 2048], F32,
                                           kind="ExternalInput").ap()
    load_vec_inputs(ctx, sb, vecs)
    setup_l0(ctx, sb, consts)
    sb.xs2 = sb.rr[:, 0:2 * ND * TS].bitcast(F32)
    wst = WStream(ctx, "w", nbuf=2)
    prev_store = None
    for tile in range(SEQ // T):
        t0 = tile * T
        for c in range(ND):
            p.dma("sync", sb.x.chunk(c), xin[c * 128:(c + 1) * 128, t0:t0 + T],
                  deps=[prev_store] if (prev_store and c == 0) else (), strict=True)
        p.join()
        emit_l0attn(ctx, sb, dram, wst, xin, tile)
        for c in range(ND):
            prev_store = p.dma("sync", xout[c * 128:(c + 1) * 128, t0:t0 + T], sb.x.chunk(c))
    p._wait("gpsimd", "Ssy", p.cnt["Ssy"])
    p.build()
    return nc



def build_l0ffn_program(T):
    nc = bass.Bass("TRN2", target_bir_lowering=False, debug=False)
    p = Prog(nc)
    ctx = Ctx(nc, p, T)
    consts = host_consts()
    sb = SB(ctx, T, consts)
    xin = nc.dram_tensor("s_in", [D, SEQ], F32, kind="ExternalInput").ap()
    xout = nc.dram_tensor("s_out", [D, SEQ], F32, kind="ExternalOutput").ap()
    dram = {
        "wgT": nc.dram_tensor("wgT", [D, 3072], I32, kind="ExternalInput").ap(),
        "wuT": nc.dram_tensor("wuT", [D, 3072], I32, kind="ExternalInput").ap(),
        "wdT": nc.dram_tensor("wdT", [3072, D], I32, kind="ExternalInput").ap(),
    }
    wst = WStream(ctx, "w", nbuf=4)
    prev_store = None
    for tile in range(SEQ // T):
        t0 = tile * T
        for c in range(ND):
            p.dma("sync", sb.x.chunk(c), xin[c * 128:(c + 1) * 128, t0:t0 + T],
                  deps=[prev_store] if (prev_store and c == 0) else (), strict=True)
        p.join()
        emit_l0ffn(ctx, sb, dram, wst)
        emit_ln(ctx, sb, eps=EPS_TOP)
        for c in range(ND):
            prev_store = p.dma("sync", xout[c * 128:(c + 1) * 128, t0:t0 + T], sb.x.chunk(c))
    p._wait("gpsimd", "Ssy", p.cnt["Ssy"])
    p.build()
    return nc




def _load_router_weights(ctx, sb, wrT_dram, rb_dram):
    nc, p = ctx.nc, ctx.p
    sb.wr_b = nc.alloc_sbuf_tensor("wr_b", [128, 64], BF16)
    tok = None
    for c in range(ND):
        tok = p.dma("gpsimd", sb.wr_b[:, c * 8:c * 8 + 7],
                    wrT_dram[c * 128:(c + 1) * 128, 0:7])
    sb.rb_row_bf = nc.alloc_sbuf_tensor("rb_row_bf", [1, 8], BF16)
    tok = p.dma("gpsimd", sb.rb_row_bf[0:1, 0:7], rb_dram[0:1, 0:7])
    sb.wr_tok = tok


def build_moe_program(T, debug_w=False, n_routed=7):
    nc = bass.Bass("TRN2", target_bir_lowering=False, debug=False)
    p = Prog(nc)
    ctx = Ctx(nc, p, T)
    consts = host_consts()
    sb = SB(ctx, T, consts, need=("SEL7", "ident"), moe=True)
    sb.w_rows = nc.alloc_sbuf_tensor("w_rows", [7, T], BF16)
    sb.wbb = nc.alloc_sbuf_tensor("wbb", [128, T], F32)
    sb.rtr = nc.alloc_sbuf_tensor("rtr", [128, 64], F32)
    xin = nc.dram_tensor("s_in", [D, SEQ], F32, kind="ExternalInput").ap()
    xout = nc.dram_tensor("s_out", [D, SEQ], F32, kind="ExternalOutput").ap()
    dram = {}
    for nm, shape in (("sgT", [D, 2816]), ("suT", [D, 2816]), ("sdT", [2816, D]),
                      ("rgT", [7 * D, 2816]), ("ruT", [7 * D, 2816]),
                      ("rdT", [7 * 2816, D]), ("wrT", [D, 7]), ("rb_row", [1, 7])):
        dram[nm] = nc.dram_tensor(nm, shape, F32, kind="ExternalInput").ap()
    _load_router_weights(ctx, sb, dram["wrT"], dram["rb_row"])
    dbg_w = nc.dram_tensor("dbg_w", [7, SEQ], F32, kind="ExternalOutput").ap() if debug_w else None
    dbg_z = nc.dram_tensor("dbg_z", [D, SEQ], F32, kind="ExternalOutput").ap() if debug_w else None
    if debug_w:
        ctx.dbg_h = nc.dram_tensor("dbg_h", [1024, T], BF16, kind="ExternalOutput").ap()
        ctx.dbg_h_armed = True
    else:
        ctx.dbg_h = None
    wst = WStream(ctx, "w", nbuf=3)
    prev_store = None
    for tile in range(SEQ // T):
        t0 = tile * T
        for c in range(ND):
            p.dma("sync", sb.x.chunk(c), xin[c * 128:(c + 1) * 128, t0:t0 + T],
                  deps=[prev_store] if (prev_store and c == 0) else (), strict=True)
        p.join()
        emit_cast(ctx, sb.x, sb.xb)
        p.join()
        emit_router(ctx, sb, xb=sb.xb, wr_b=sb.wr_b, rb_row_bf=sb.rb_row_bf,
                    ones_row_bf=sb.ones_row_bf, w_rows=sb.w_rows)
        if debug_w:
            p.dma("sync", dbg_w[0:7, t0:t0 + T], sb.w_rows[0:7, :])
        emit_moe_experts(ctx, sb, dram, wst, n_routed=n_routed)
        if debug_w:
            for c in range(ND):
                p.dma("sync", dbg_z[c * 128:(c + 1) * 128, t0:t0 + T], sb.x.chunk(c))
        emit_ln(ctx, sb, eps=EPS_TOP)
        for c in range(ND):
            prev_store = p.dma("sync", xout[c * 128:(c + 1) * 128, t0:t0 + T], sb.x.chunk(c))
    p._wait("gpsimd", "Ssy", p.cnt["Ssy"])
    p.build()
    return nc





def build_fused_program(T=1024):
    nc = bass.Bass("TRN2", target_bir_lowering=False, debug=False)
    p = Prog(nc)
    ctx = Ctx(nc, p, T)
    consts = host_consts()
    sb = SB(ctx, T, consts, need=("SEL7", "SEL16", "ident"), moe=True)
    xin = nc.dram_tensor("xT", [D, SEQ], F32, kind="ExternalInput").ap()
    xout = nc.dram_tensor("s_out", [D, SEQ], F32, kind="ExternalOutput").ap()
    s2 = nc.dram_tensor("s2_scr", [D, SEQ], F32, kind="Internal").ap()
    pq = nc.dram_tensor("pq_scr", [D, SEQ], BF16, kind="Internal").ap()

    dram = {}
    for i in range(2):
        dram[f"pw{i}"] = nc.dram_tensor(f"pw{i}", [D, 4 * D], F32, kind="ExternalInput").ap()
        dram[f"rw{i}"] = nc.dram_tensor(f"rw{i}", [D, D], F32, kind="ExternalInput").ap()
        dram[f"f1w{i}"] = nc.dram_tensor(f"f1w{i}", [D, 4 * D], F32, kind="ExternalInput").ap()
        dram[f"f2w{i}"] = nc.dram_tensor(f"f2w{i}", [4 * D, D], F32, kind="ExternalInput").ap()
    dram["wgT"] = nc.dram_tensor("wgT", [D, 3072], I32, kind="ExternalInput").ap()
    dram["wuT"] = nc.dram_tensor("wuT", [D, 3072], I32, kind="ExternalInput").ap()
    dram["wdT"] = nc.dram_tensor("wdT", [3072, D], I32, kind="ExternalInput").ap()
    for nm, dn in (("wqT", "l1_wq"), ("wkT", "l1_wk"), ("wvT", "l1_wv"), ("wdT_l1", "l1_wd")):
        dram[nm] = nc.dram_tensor(dn, [D, D], F32, kind="ExternalInput").ap()
    for nm, shape in (("sgT", [D, 2816]), ("suT", [D, 2816]), ("sdT", [2816, D]),
                      ("rgT", [7 * D, 2816]), ("ruT", [7 * D, 2816]),
                      ("rdT", [7 * 2816, D]), ("wrT", [D, 7]), ("rb_row", [1, 7])):
        dram[nm] = nc.dram_tensor(nm, shape, F32, kind="ExternalInput").ap()

    vecs = {"lnfg": (128, 8), "lnfb": (128, 8),
            "bq_col": (128, 8), "bd_col": (128, 8),
            "l1g": (128, 8), "l1b": (128, 8),
            "bk_row": (1, 1024), "bv_row": (1, 1024)}
    rows_only = []
    for i in range(2):
        for nm in (f"rb{i}", f"f2b{i}", f"ln1g{i}", f"ln1b{i}", f"ln2g{i}", f"ln2b{i}"):
            vecs[nm] = (128, 8)
        vecs[f"pb{i}_col"] = (128, 32)
        vecs[f"f1b{i}"] = (128, 32)
        dram[f"pbrow{i}"] = nc.dram_tensor(f"pb{i}_row", [1, 2048], F32,
                                           kind="ExternalInput").ap()
    load_vec_inputs(ctx, sb, vecs, bf_rows=("bk_row", "bv_row"))
    setup_l0(ctx, sb, consts)

    class _KCB:
        def __init__(self, ap):
            self.h = ap

        def chunk(self, c):
            return self.h[:, c * 16:(c + 1) * 16]
    sb.ks_cb = _KCB(sb.rr[:, 8192 + 3 * T + 128:8192 + 3 * T + 256])
    setup_l1(ctx, sb, {"wkT": dram["wkT"], "wvT": dram["wvT"]})
    _load_router_weights(ctx, sb, dram["wrT"], dram["rb_row"])
    sb.xs2 = sb.rr[:, 0:2 * ND * TS].bitcast(F32)
    sb.wbb = sb.rr[:, 8192:8192 + 2 * T].bitcast(F32)
    sb.w_rows = sb.rr[0:7, 8192 + 2 * T:8192 + 3 * T]
    sb.rtr = sb.rr[:, 8192 + 3 * T:8192 + 3 * T + 128].bitcast(F32)
    wst = WStream(ctx, "w", nbuf=3)
    l1dram = {"wqT": dram["wqT"], "wkT": dram["wkT"], "wvT": dram["wvT"],
              "wdT": dram["wdT_l1"]}

    NT = SEQ // T
    for tile in range(NT):
        t0 = tile * T
        qdep = [("Ssy", p.cnt["Ssy"])] if p.cnt["Ssy"] else []
        for c in range(ND):
            p.dma("sync", sb.x.chunk(c), xin[c * 128:(c + 1) * 128, t0:t0 + T],
                  deps=qdep if c == 0 else (), strict=True)
        p.join()
        p.tag = "l0attn"
        emit_l0attn(ctx, sb, dram, wst, xin, tile)
        p.tag = "l0ffn"
        emit_l0ffn(ctx, sb, dram, wst)
        p.tag = "ln2top"
        emit_ln(ctx, sb, eps=EPS_TOP)
        for c in range(ND):
            p.dma("sync", s2[c * 128:(c + 1) * 128, t0:t0 + T], sb.x.chunk(c))
        p.tag = "l1a"
        emit_l1_phase_a(ctx, sb, l1dram, wst, pq, tile)
    p.tag = "ks"
    emit_ks_build(ctx, sb)
    for tile in range(NT):
        t0 = tile * T
        p.tag = "l1b"
        emit_l1_phase_b(ctx, sb, l1dram, wst, pq, s2, tile)
        p.tag = "moe"
        emit_cast(ctx, sb.x, sb.xb)
        p.join()
        emit_router(ctx, sb, xb=sb.xb, rb_row_bf=sb.rb_row_bf, wr_b=sb.wr_b,
                    ones_row_bf=sb.ones_row_bf, w_rows=sb.w_rows)
        emit_moe_experts(ctx, sb, dram, wst)
        p.tag = "ln4top"
        emit_ln(ctx, sb, eps=EPS_TOP)
        for c in range(ND):
            p.dma("sync", xout[c * 128:(c + 1) * 128, t0:t0 + T], sb.x.chunk(c))
        p.join()
    p._wait("gpsimd", "Ssy", p.cnt["Ssy"])
    p.build()
    return nc


def kernel(**inputs):
    if "f" not in _PROGS:
        _PROGS["f"] = build_fused_program(1024)
    nc = _PROGS["f"]
    n = 8
    packs = [host_pack(inputs, core) for core in range(n)]
    names = ["pw0", "pw1", "rw0", "rw1", "f1w0", "f1w1", "f2w0", "f2w1",
             "lnfg", "lnfb", "wgT", "wuT", "wdT",
             "l1_wq", "l1_wk", "l1_wv", "l1_wd",
             "bq_col", "bd_col", "l1g", "l1b", "bk_row", "bv_row",
             "sgT", "suT", "sdT", "rgT", "ruT", "rdT", "wrT", "rb_row"]
    for i in range(2):
        names += [f"pb{i}_col", f"pb{i}_row", f"rb{i}", f"f1b{i}", f"f2b{i}",
                  f"ln1g{i}", f"ln1b{i}", f"ln2g{i}", f"ln2b{i}"]
    in_maps = [dict(_sub(p_, names), xT=p_["xT"]) for p_ in packs]
    res = run_bass_kernel_spmd(nc, in_maps, core_ids=list(range(n)))
    outs = [np.ascontiguousarray(np.asarray(r["s_out"]).T) for r in res.results]
    return np.stack(outs, axis=0)



_PROGS = {}


def _get_progs():
    if "a" not in _PROGS:
        _PROGS["a"] = build_l0attn_program(2048)
        _PROGS["b"] = build_l0ffn_program(2048)
        _PROGS["c"] = build_l1_program(2048)
        _PROGS["d"] = build_moe_program(2048)
    return _PROGS


def _sub(m, names):
    return {k: m[k] for k in names}


def kernel_unfused(**inputs):
    progs = _get_progs()
    n = 8
    packs = [host_pack(inputs, core) for core in range(n)]
    cores = list(range(n))

    a_names = ["pw0", "pw1", "rw0", "rw1", "f1w0", "f1w1", "f2w0", "f2w1",
               "lnfg", "lnfb"]
    for i in range(2):
        a_names += [f"pb{i}_col", f"pb{i}_row", f"rb{i}", f"f1b{i}", f"f2b{i}",
                    f"ln1g{i}", f"ln1b{i}", f"ln2g{i}", f"ln2b{i}"]
    in_maps = [dict(_sub(p_, a_names), s_in=p_["xT"]) for p_ in packs]
    res = run_bass_kernel_spmd(progs["a"], in_maps, core_ids=cores)
    s1 = [np.asarray(r["s_out"]) for r in res.results]

    b_names = ["wgT", "wuT", "wdT"]
    in_maps = [dict(_sub(p_, b_names), s_in=s) for p_, s in zip(packs, s1)]
    res = run_bass_kernel_spmd(progs["b"], in_maps, core_ids=cores)
    s2 = [np.asarray(r["s_out"]) for r in res.results]

    c_names = ["l1_wq", "l1_wk", "l1_wv", "l1_wd", "bq_col", "bd_col",
               "l1g", "l1b", "bk_row", "bv_row"]
    in_maps = [dict(_sub(p_, c_names), s_in=s) for p_, s in zip(packs, s2)]
    res = run_bass_kernel_spmd(progs["c"], in_maps, core_ids=cores)
    s3 = [np.asarray(r["s_out"]) for r in res.results]

    d_names = ["sgT", "suT", "sdT", "rgT", "ruT", "rdT", "wrT", "rb_row"]
    in_maps = [dict(_sub(p_, d_names), s_in=s) for p_, s in zip(packs, s3)]
    res = run_bass_kernel_spmd(progs["d"], in_maps, core_ids=cores)
    outs = [np.ascontiguousarray(np.asarray(r["s_out"]).T) for r in res.results]
    return np.stack(outs, axis=0)


# version 05
# speedup vs baseline: 1.2094x; 1.0277x over previous
import numpy as np
import ml_dtypes
from contextlib import ExitStack

import concourse.bass as bass
import concourse.mybir as mybir
from concourse.bass_utils import run_bass_kernel_spmd

F32 = mybir.dt.float32
BF16 = mybir.dt.bfloat16
I32 = mybir.dt.int32
AF = mybir.ActivationFunctionType
ALU = mybir.AluOpType
AX = mybir.AxisListType

D = 1024
SEQ = 4096
NH = 16
HD = 64
CH = 128
ND = D // 128
TS = 512
EPS_TOP = 1e-5
EPS_L1LN = 1e-12
EPS_ATTN = 1e-6
Q127 = 1.0 / 127.0

ENGINES = ("sync", "scalar", "tensor", "vector", "gpsimd")
ESEM = {"sync": "Ssy", "scalar": "Sac", "tensor": "Spe", "vector": "Sdv", "gpsimd": "Sgp"}



class Prog:
    def __init__(self, nc):
        self.nc = nc
        self.ops = {e: [] for e in ENGINES}
        self.cnt = {s: 0 for s in ESEM.values()}
        self.sem_handles = {}
        self.last_wait = {e: {} for e in ENGINES}
        self.tag = ""

    def _wait(self, engine, s, v):
        lw = self.last_wait[engine]
        if lw.get(s, -1) >= v:
            return
        lw[s] = v

        def w(eng, s=s, v=v):
            eng.wait_ge(self.sem_handles[s], v)
        self.ops[engine].append(w)

    def op(self, engine, fn, deps=(), inc=1, strict=False):
        best = {}
        for t in deps:
            if t is None:
                continue
            s, v = t
            if best.get(s, -1) < v:
                best[s] = v
        own = ESEM[engine]
        for s, v in sorted(best.items()):
            if s != own or strict:
                self._wait(engine, s, v)
        tag = self.tag
        if inc:
            s = own
            self.cnt[s] += inc
            tok = (s, self.cnt[s])

            def wrapped(eng, fn=fn, s=s, inc=inc, tag=tag):
                inst = fn(eng)
                if tag:
                    inst.annotate(tag)
                inst.then_inc(self.sem_handles[s], inc)
            self.ops[engine].append(wrapped)
            return tok

        def plain(eng, fn=fn, tag=tag):
            inst = fn(eng)
            if tag:
                inst.annotate(tag)
        self.ops[engine].append(plain)
        return None

    def sop(self, engine, fn, deps=()):
        tok = self.op(engine, fn, deps=deps, inc=1)
        self._wait(engine, tok[0], tok[1])
        return tok

    def dma(self, engine, out, in_, deps=(), strict=False):
        return self.op(engine, lambda eng, o=out, i=in_: eng.dma_start(out=o, in_=i),
                       deps=deps, inc=16, strict=strict)

    def now(self):
        return [(s, self.cnt[s]) for s in self.cnt if self.cnt[s] > 0]

    def join(self, engines=ENGINES):
        toks = self.now()
        for e in engines:
            for s, v in toks:
                if s != ESEM[e]:
                    self._wait(e, s, v)

    def build(self):
        st = ExitStack()
        for s in ESEM.values():
            self.sem_handles[s] = st.enter_context(self.nc.semaphore(s))
        block = st.enter_context(self.nc.Block())
        for e in ENGINES:
            lst = self.ops[e]
            if not lst:
                continue

            def run(eng, lst=lst):
                for fn in lst:
                    fn(eng)
            getattr(block, e)(run)
        st.close()


def merge(*toks):
    out = []
    for t in toks:
        if t is None:
            continue
        if isinstance(t, (list, tuple)) and (len(t) == 0 or not isinstance(t[0], str)):
            out.extend(x for x in t if x is not None)
        else:
            out.append(t)
    return out


class CBuf:
    def __init__(self, nc, name, n, w, dtype):
        self.h = nc.alloc_sbuf_tensor(name, [128, n * w], dtype)
        self.n = n
        self.w = w

    def sl(self, c, a, b):
        return self.h[:, c * self.w + a: c * self.w + b]

    def chunk(self, c):
        return self.h[:, c * self.w:(c + 1) * self.w]


class Ctx:
    def __init__(self, nc, p, T):
        self.nc = nc
        self.p = p
        self.T = T
        self.NS = T // TS
        self.ps = nc.alloc_psum_tensor("ps", [128, 4096], F32)
        self.bank_ready = [None] * 8
        self.bank_free = [None] * 8
        self.ring_i = 0

    def bank(self, b, p0=0, p1=128, w=TS):
        return self.ps[p0:p1, b * TS: b * TS + w]

    def ring(self):
        banks = getattr(self, "ring_banks", None) or list(range(8))
        b = banks[self.ring_i % len(banks)]
        self.ring_i += 1
        return b


def acc_group(ctx, bank, mms, deps=(), p0=0, p1=128, w=TS, col0=0):
    p = ctx.p
    out_ap = ctx.ps[p0:p1, bank * TS + col0: bank * TS + col0 + w]
    n = len(mms)
    tok = None
    for i, (lhsT, rhs) in enumerate(mms):
        d = list(deps) if i == 0 else []
        if i == 0 and ctx.bank_free[bank] is not None:
            d.append(ctx.bank_free[bank])

        def fn(eng, lhsT=lhsT, rhs=rhs, i=i, out_ap=out_ap):
            return eng.matmul(out_ap, lhsT, rhs, start=(i == 0), stop=(i == n - 1))
        tok = p.op("tensor", fn, deps=d, inc=(1 if i == n - 1 else 0))
    ctx.bank_ready[bank] = tok
    ctx.bank_free[bank] = None
    return tok


def bank_read(ctx, bank, tok):
    ctx.bank_free[bank] = tok


class WStream:

    def __init__(self, ctx, name, nbuf=4, bufs=None):
        self.ctx = ctx
        if bufs is None:
            self.bufs = [CBuf(ctx.nc, f"{name}{i}", 8, 512, BF16).h[:, :]
                         for i in range(nbuf)]
        else:
            self.bufs = bufs
        self.nbuf = len(self.bufs)
        self.busy = [None] * self.nbuf
        self.i = 0

    def load(self, dram, r0, nk, c0, w):
        assert nk * w <= 4096
        p = self.ctx.p
        idx = self.i % self.nbuf
        self.i += 1
        buf = self.bufs[idx]
        dep = [self.busy[idx]] if self.busy[idx] is not None else []
        src3 = dram[r0:r0 + nk * 128, c0:c0 + w].rearrange("(c p) w -> p c w", p=128)
        dst3 = buf[:, 0:nk * w].rearrange("p (c w) -> p c w", c=nk)
        tok = p.dma("gpsimd", dst3, src3, deps=dep)

        def sl(c, a, b, buf=buf, w=w):
            return buf[:, c * w + a: c * w + b]
        return idx, sl, tok

    def done(self, idx, tok):
        prev = self.busy[idx]
        if prev is None or tok[1] > prev[1]:
            self.busy[idx] = tok



def _col_pack(v):
    v = np.asarray(v, dtype=np.float32)
    n = v.shape[0] // 128
    return np.ascontiguousarray(v.reshape(n, 128).T)


def _t(w):
    return np.ascontiguousarray(np.asarray(w).T)


def _row(v):
    return np.ascontiguousarray(np.asarray(v, np.float32).reshape(1, -1))


def host_pack(inputs, core):
    m = {}
    x = np.asarray(inputs["L0_attn_x"][core])
    m["xT"] = _t(x)
    for i in range(2):
        m[f"pw{i}"] = np.ascontiguousarray(np.asarray(inputs["L0_attn_proj_w"][i]))
        m[f"pb{i}_col"] = _col_pack(inputs["L0_attn_proj_b"][i])
        m[f"pb{i}_row"] = np.ascontiguousarray(
            np.asarray(inputs["L0_attn_proj_b"][i], np.float32)[1024:3072].reshape(1, -1))
        m[f"rw{i}"] = np.ascontiguousarray(np.asarray(inputs["L0_attn_reto_w"][i]))
        m[f"rb{i}"] = _col_pack(inputs["L0_attn_reto_b"][i])
        m[f"f1w{i}"] = np.ascontiguousarray(np.asarray(inputs["L0_attn_ffn1_w"][i]))
        m[f"f1b{i}"] = _col_pack(inputs["L0_attn_ffn1_b"][i])
        m[f"f2w{i}"] = np.ascontiguousarray(np.asarray(inputs["L0_attn_ffn2_w"][i]))
        m[f"f2b{i}"] = _col_pack(inputs["L0_attn_ffn2_b"][i])
        m[f"ln1g{i}"] = _col_pack(inputs["L0_attn_ln1_g"][i])
        m[f"ln1b{i}"] = _col_pack(inputs["L0_attn_ln1_b"][i])
        m[f"ln2g{i}"] = _col_pack(inputs["L0_attn_ln2_g"][i])
        m[f"ln2b{i}"] = _col_pack(inputs["L0_attn_ln2_b"][i])
    m["lnfg"] = _col_pack(inputs["L0_attn_lnf_g"])
    m["lnfb"] = _col_pack(inputs["L0_attn_lnf_b"])
    m["wgT"] = _t(inputs["L0_ffn_w_gate"])
    m["wuT"] = _t(inputs["L0_ffn_w_up"])
    m["wdT"] = _t(inputs["L0_ffn_w_down"])
    for nm, key in (("l1_wq", "L1_attn_Wq"), ("l1_wk", "L1_attn_Wk"),
                    ("l1_wv", "L1_attn_Wv"), ("l1_wd", "L1_attn_Wd")):
        m[nm] = _t(inputs[key])
    m["bq_col"] = _col_pack(inputs["L1_attn_bq"])
    m["bk_row"] = _row(inputs["L1_attn_bk"])
    m["bv_row"] = _row(inputs["L1_attn_bv"])
    m["bd_col"] = _col_pack(inputs["L1_attn_bd"])
    m["l1g"] = _col_pack(inputs["L1_attn_gamma"])
    m["l1b"] = _col_pack(inputs["L1_attn_beta"])
    m["sgT"] = _t(inputs["L1_ffn_sg"])
    m["suT"] = _t(inputs["L1_ffn_su"])
    m["sdT"] = _t(inputs["L1_ffn_sd"])
    m["rgT"] = np.ascontiguousarray(np.asarray(inputs["L1_ffn_rg"]).transpose(0, 2, 1).reshape(7 * D, 2816))
    m["ruT"] = np.ascontiguousarray(np.asarray(inputs["L1_ffn_ru"]).transpose(0, 2, 1).reshape(7 * D, 2816))
    m["rdT"] = np.ascontiguousarray(np.asarray(inputs["L1_ffn_rd"]).transpose(0, 2, 1).reshape(7 * 2816, D))
    m["wrT"] = _t(inputs["L1_ffn_wr"])
    m["rb_row"] = _row(inputs["L1_ffn_rb"])
    return m


def host_consts():
    c = {}
    gamma = (1.0 - 2.0 ** (-5.0 - np.arange(NH, dtype=np.float64)))
    j = np.arange(CH, dtype=np.float64)
    dts = np.zeros((NH, CH, CH), np.float32)
    for h in range(NH):
        mask = (j[None, :] - j[:, None]) >= 0
        vals = (gamma[h] ** (-(j + 1.0)))[:, None]
        dts[h] = np.where(mask, vals, 0.0).astype(np.float32)
    c["DT"] = dts.reshape(NH * CH, CH)
    qd = np.zeros((ND, 128, CH), np.float32)
    for cc in range(ND):
        for half in range(2):
            h = 2 * cc + half
            qd[cc, half * 64:(half + 1) * 64, :] = (gamma[h] ** (j + 1.0)).astype(np.float32)[None, :]
    c["QD"] = qd.reshape(ND * 128, CH)
    kd = np.zeros((2, CH, 512), np.float32)
    for s in range(2):
        for hh in range(8):
            h = s * 8 + hh
            colv = (gamma[h] ** (CH - 1.0 - j)).astype(np.float32)
            kd[s, :, hh * 64:(hh + 1) * 64] = colv[:, None]
    c["KD"] = kd.reshape(2 * CH, 512)
    gc = np.zeros((128, 512), np.float32)
    for h in range(NH):
        gc[(h % 2) * 64:(h % 2) * 64 + 64, (h // 2) * 64:(h // 2) * 64 + 64] = np.float32(gamma[h] ** CH)
    c["GC"] = gc
    c["ones1"] = np.ones((1, 128), np.float32)
    c["onesc"] = np.ones((128, 1), np.float32)
    c["ones_d"] = np.full((1, 128), 1.0 / D, np.float32)
    c["ones_sd"] = np.full((1, 128), float(np.sqrt(D)), np.float32)
    sel7 = np.zeros((7, 7 * 128), np.float32)
    for e in range(7):
        sel7[e, e * 128:(e + 1) * 128] = 1.0
    c["SEL7"] = sel7
    sel16 = np.zeros((16, 8 * 128), np.float32)
    for cc in range(8):
        sel16[2 * cc, cc * 128: cc * 128 + 64] = 1.0
        sel16[2 * cc + 1, cc * 128 + 64: cc * 128 + 128] = 1.0
    c["SEL16"] = sel16
    c["SEL2"] = np.zeros((2, 128), np.float32)
    c["SEL2"][0, 0:64] = 1.0
    c["SEL2"][1, 64:128] = 1.0
    c["ident"] = np.eye(128, dtype=np.float32)
    return c



class SB:
    def __init__(self, ctx, T, consts, need=(), moe=False):
        nc, p = ctx.nc, ctx.p
        self.T = T
        self.x = CBuf(nc, "x", ND, T, F32)
        self.xb = CBuf(nc, "xb", ND, T, BF16)
        self.h = CBuf(nc, "h", ND, T, BF16)
        self.rows = nc.alloc_sbuf_tensor("rows", [1, 2 * T], F32)
        self.t1 = [nc.alloc_sbuf_tensor(f"t1_{i}", [128, TS], F32) for i in range(2)]
        self.t1_busy = [None, None]
        self.const = {}
        bf_consts = getattr(self, "_bf_consts", ("SEL7",))
        for name in set(("ones_d", "ones_sd") + tuple(need)):
            arr = consts[name]
            dr = nc.inline_tensor(np.ascontiguousarray(arr), "ic_" + name)
            if name in bf_consts:
                hh = nc.alloc_sbuf_tensor("cst_" + name, list(arr.shape), BF16)
                p.dma("gpsimd", hh[:, :], dr.ap())
            else:
                hh = nc.alloc_sbuf_tensor("cst_" + name, list(arr.shape), F32)
                p.dma("sync", hh[:, :], dr.ap())
            self.const[name] = hh
        onesb = nc.inline_tensor(np.ones((128, 1), np.float32), "ic_ones_b")
        self.ones_bf = nc.alloc_sbuf_tensor("cst_ones_bf", [128, 1], BF16)
        p.dma("gpsimd", self.ones_bf[:, :], onesb.ap())
        onesr = nc.inline_tensor(np.ones((1, 128), np.float32), "ic_ones_r")
        self.ones_row_bf = nc.alloc_sbuf_tensor("cst_ones_r_bf", [1, 128], BF16)
        p.dma("gpsimd", self.ones_row_bf[:, :], onesr.ap())
        self.vec = {}
        self.vecrow = {}
        if moe:
            self.rtr = None
            self.rtr_busy = [None, None]
            self.w_rows = None
            self.w_rows_tok = None
            self.wbb = None
            self.wr_b = None
            self.wr_tok = None
            self.rb_row_bf = None

    def h_f32(self, i):
        return self.h.h[:, 2 * i * self.T:(2 * i + 2) * self.T].bitcast(F32)


def load_vec_inputs(ctx, sb, names_shapes, bf_rows=(), bf_only_rows=()):
    nc, p = ctx.nc, ctx.p
    for name, shape in names_shapes.items():
        d = nc.dram_tensor(name, list(shape), F32, kind="ExternalInput").ap()
        if name not in bf_only_rows:
            h = nc.alloc_sbuf_tensor("v_" + name, list(shape), F32)
            p.dma("sync", h[:, :], d[:, :])
            sb.vec[name] = h
        if name in bf_rows or name in bf_only_rows:
            hb = nc.alloc_sbuf_tensor("vb_" + name, list(shape), BF16)
            p.dma("gpsimd", hb[:, :], d[:, :])
            sb.vecrow[name] = hb



def emit_ln(ctx, sb, *, g=None, b=None, eps, out_bf=None, add_f32=None,
            bf_only=False, name=""):
    p = ctx.p
    T = ctx.T
    NSl = ctx.NS
    x = sb.x
    scr = [(sb.h.chunk(2 * i), sb.h.chunk(2 * i + 1)) for i in range(4)]
    onesb = sb.ones_bf

    prev_mm = {}
    for c in range(ND):
        xbs, xsq = scr[c % 4]
        dep = merge([prev_mm.get(c - 4)])
        p.op("scalar", lambda eng, xbs=xbs, c=c: eng.activation(xbs, x.chunk(c), AF.Copy),
             deps=dep, inc=0)
        ct = p.op("scalar", lambda eng, xsq=xsq, xbs=xbs: eng.activation(xsq, xbs, AF.Square))
        last = None
        for t in range(NSl):
            for kind in range(2):
                bank = kind * NSl + t
                src = (xbs if kind == 0 else xsq)[:, t * TS:(t + 1) * TS]
                d = [ct]
                if c == 0 and ctx.bank_free[bank] is not None:
                    d.append(ctx.bank_free[bank])
                last = p.op("tensor", lambda eng, bank=bank, src=src, c=c:
                            eng.matmul(ctx.bank(bank, 0, 1), onesb[:, 0:1], src,
                                       start=(c == 0), stop=(c == ND - 1)),
                            deps=d, inc=(1 if (kind == 1 and t == NSl - 1) else 0))
        prev_mm[c] = last
    mm_done = prev_mm[ND - 1]

    mu = sb.rows[0:1, 0:T]
    rstd = sb.rows[0:1, T:2 * T]
    ps_sum = ctx.ps[0:1, 0:NSl * TS]
    ps_sq = ctx.ps[0:1, NSl * TS:2 * NSl * TS]
    p.op("vector", lambda eng: eng.tensor_copy(mu, ps_sum), deps=[mm_done], inc=0)
    p.op("vector", lambda eng: eng.tensor_copy(rstd, ps_sq), inc=0)
    p.op("vector", lambda eng: eng.tensor_tensor(ps_sum, mu, mu, ALU.mult), inc=0)
    var_tok = p.op("vector", lambda eng: eng.scalar_tensor_tensor(
        rstd, ps_sum, -1.0 / D, rstd, ALU.mult, ALU.add))
    p.op("vector", lambda eng: eng.tensor_scalar_add(rstd, rstd, float(D * eps)), inc=0)
    sq_tok = p.op("scalar", lambda eng: eng.activation(rstd, rstd, AF.Sqrt),
                  deps=[var_tok])
    rstd_tok = p.op("vector", lambda eng: eng.reciprocal(rstd, rstd), deps=[sq_tok])
    for i in range(2 * NSl):
        bank_read(ctx, i, rstd_tok)

    ones_d = sb.const["ones_d"]
    ones_sd = sb.const["ones_sd"]
    for t in range(NSl):
        bmu, brs = 0, 1
        p.op("tensor", lambda eng, t=t: eng.matmul(
            ctx.bank(bmu), ones_d[0:1, :], mu[:, t * TS:(t + 1) * TS], start=True, stop=True),
            deps=merge([rstd_tok, ctx.bank_free[bmu]]), inc=0)
        bc_tok = p.op("tensor", lambda eng, t=t: eng.matmul(
            ctx.bank(brs), ones_sd[0:1, :], rstd[:, t * TS:(t + 1) * TS], start=True, stop=True),
            deps=merge([ctx.bank_free[brs]]))
        last = bc_tok
        for c in range(ND):
            tt = sb.t1[c % 2]
            p.op("vector", lambda eng, c=c, t=t, tt=tt: eng.tensor_tensor(
                tt[:], x.sl(c, t * TS, (t + 1) * TS), ctx.bank(bmu), ALU.subtract),
                deps=merge([bc_tok, sb.t1_busy[c % 2]]), inc=0)
            dve_tok = p.op("vector", lambda eng, tt=tt: eng.tensor_tensor(
                tt[:], tt[:], ctx.bank(brs), ALU.mult))
            dst = x.sl(c, t * TS, (t + 1) * TS)
            if bf_only:
                a_tok = p.op("vector", lambda eng, c=c, tt=tt: eng.tensor_scalar(
                    tt[:], tt[:], g[:, c:c + 1], b[:, c:c + 1], ALU.mult, ALU.add),
                    deps=[dve_tok])
                w_tok = p.op("scalar", lambda eng, c=c, t=t, tt=tt: eng.activation(
                    out_bf.sl(c, t * TS, (t + 1) * TS), tt[:], AF.Copy), deps=[a_tok])
                sb.t1_busy[c % 2] = w_tok
                last = dve_tok
                continue
            if g is not None:
                if add_f32 is not None:
                    add_ap, add_tok = add_f32(c, t)
                    a_tok = p.op("vector", lambda eng, c=c, tt=tt: eng.tensor_scalar(
                        tt[:], tt[:], g[:, c:c + 1], b[:, c:c + 1], ALU.mult, ALU.add),
                        deps=[dve_tok])
                    w_tok = p.op("vector", lambda eng, tt=tt, dst=dst, a=add_ap:
                                 eng.tensor_tensor(dst, tt[:], a, ALU.add),
                                 deps=merge([a_tok, add_tok]))
                else:
                    w_tok = p.op("vector", lambda eng, c=c, tt=tt, dst=dst: eng.tensor_scalar(
                        dst, tt[:], g[:, c:c + 1], b[:, c:c + 1], ALU.mult, ALU.add),
                        deps=[dve_tok])
            else:
                assert add_f32 is None
                w_tok = p.op("scalar", lambda eng, tt=tt, dst=dst: eng.activation(
                    dst, tt[:], AF.Copy), deps=[dve_tok])
            sb.t1_busy[c % 2] = w_tok
            if out_bf is not None:
                p.op("scalar", lambda eng, c=c, t=t, dst=dst: eng.activation(
                    out_bf.sl(c, t * TS, (t + 1) * TS), dst, AF.Copy), deps=[w_tok])
            last = dve_tok
        bank_read(ctx, bmu, last)
        bank_read(ctx, brs, last)
    p.join()


def emit_cast(ctx, src_cb, dst_cb, scale=None):
    p = ctx.p
    for c in range(ND):
        if scale is None:
            p.op("scalar", lambda eng, c=c: eng.activation(
                dst_cb.chunk(c), src_cb.chunk(c), AF.Copy))
        else:
            p.op("scalar", lambda eng, c=c: eng.activation(
                dst_cb.chunk(c), src_cb.chunk(c), AF.Copy, scale=scale))



def emit_gated_ffn(ctx, sb, *, xb, wg, wu, wd, HID, wst, act=AF.Silu,
                   up_scale=None, wb_ap=None, first_bias=None, wst_down=None):
    p = ctx.p
    T = ctx.T
    NSl = ctx.NS
    NHC = HID // 128
    h = sb.h
    upscr = [ctx.nc, None]
    for g0 in range(0, NHC, 8):
        ng = min(8, NHC - g0)
        gl = {}
        ul = {}
        for gi in range(ng):
            hc = g0 + gi
            half = gi // 4
            if half not in gl:
                nw = min(4, ng - half * 4) * 128
                gl[half] = wst.load(wg, 0, ND, (g0 + half * 4) * 128, nw)
                ul[half] = wst.load(wu, 0, ND, (g0 + half * 4) * 128, nw)
            slot, wsl, wtok = gl[half]
            uslot, usl, utok = ul[half]
            gcol = (gi % 4) * 128
            bg = 4 + (gi % 2)
            bu = 6 + (gi % 2)

            u_toks = []
            for t in range(NSl):
                gt = acc_group(ctx, bg, [(wsl(c, gcol, gcol + 128),
                                          xb.sl(c, t * TS, (t + 1) * TS)) for c in range(ND)],
                               deps=[wtok], col0=0)
                e1 = p.op("scalar", lambda eng, gi=gi, t=t, bg=bg: eng.activation(
                    h.sl(gi, t * TS, (t + 1) * TS), ctx.bank(bg), act), deps=[gt])
                bank_read(ctx, bg, e1)
                wst.done(slot, gt)
                ut = acc_group(ctx, bu, [(usl(c, gcol, gcol + 128),
                                          xb.sl(c, t * TS, (t + 1) * TS)) for c in range(ND)],
                               deps=[utok], col0=0)
                wst.done(uslot, ut)
                tt = sb.t1[t % 2]
                if wb_ap is not None:
                    e2 = p.op("vector", lambda eng, t=t, bu=bu, tt=tt: eng.tensor_tensor(
                        tt[:], ctx.bank(bu), wb_ap[:, t * TS:(t + 1) * TS], ALU.mult),
                        deps=merge([ut, sb.t1_busy[t % 2]]))
                else:
                    sc = {} if up_scale is None else {"scale": up_scale}
                    e2 = p.op("scalar", lambda eng, t=t, bu=bu, tt=tt, sc=sc: eng.activation(
                        tt[:], ctx.bank(bu), AF.Copy, **sc),
                        deps=merge([ut, sb.t1_busy[t % 2]]))
                bank_read(ctx, bu, e2)
                m_tok = p.op("vector", lambda eng, gi=gi, t=t, tt=tt: eng.tensor_tensor(
                    h.sl(gi, t * TS, (t + 1) * TS), h.sl(gi, t * TS, (t + 1) * TS),
                    tt[:], ALU.mult), deps=merge([e1, e2]))
                sb.t1_busy[t % 2] = m_tok
        if getattr(ctx, "dbg_h", None) is not None and g0 == 0 and ctx.dbg_h_armed:
            p.join()
            for gi in range(ng):
                p.dma("sync", ctx.dbg_h[gi * 128:(gi + 1) * 128, 0:T], h.chunk(gi))
            p.join()
            ctx.dbg_h_armed = False
        wd_st = wst_down if wst_down is not None else wst
        dl = []
        for half in range(0, ng, 4):
            nked = min(4, ng - half)
            dl.append(wd_st.load(wd, (g0 + half) * 128, nked, 0, 1024))
        for f2 in range(ND):
            for t in range(NSl):
                bank = t if (f2 % 2 == 0) else 4 + t
                mms = []
                dep0 = [dl[-1][2], m_tok]
                for gi in range(ng):
                    dslot, dsl, dtok = dl[gi // 4]
                    mms.append((dsl(gi % 4, f2 * 128, (f2 + 1) * 128),
                                h.sl(gi, t * TS, (t + 1) * TS)))
                gt = acc_group(ctx, bank, mms, deps=dep0)
                for dslot, dsl, dtok in dl:
                    wd_st.done(dslot, gt)
                if first_bias is not None and g0 == 0:
                    ev = p.op("vector", lambda eng, f2=f2, t=t, bank=bank:
                              eng.scalar_tensor_tensor(
                                  sb.x.sl(f2, t * TS, (t + 1) * TS), ctx.bank(bank),
                                  first_bias[:, f2:f2 + 1],
                                  sb.x.sl(f2, t * TS, (t + 1) * TS), ALU.add, ALU.add),
                              deps=[gt])
                else:
                    ev = p.op("vector", lambda eng, f2=f2, t=t, bank=bank:
                              eng.tensor_tensor(
                                  sb.x.sl(f2, t * TS, (t + 1) * TS),
                                  sb.x.sl(f2, t * TS, (t + 1) * TS),
                                  ctx.bank(bank), ALU.add),
                              deps=[gt])
                bank_read(ctx, bank, ev)
    p.join()



def proj_fm(ctx, *, wdram, wr0, nk, xb, fch_list, wst, evac, extra=None,
            t_list=None, ldeps=()):
    p = ctx.p
    NSl = ctx.NS if t_list is None else None
    ts = range(ctx.NS) if t_list is None else t_list
    i = 0
    for fb0 in range(0, len(fch_list), 4):
        fgrp = fch_list[fb0:fb0 + 4]
        slot, wsl, wtok = wst.load(wdram, wr0, nk, fgrp[0] * 128, len(fgrp) * 128)
        for mi, f in enumerate(fgrp):
            for t in ts:
                bank = ctx.ring()
                mms = [(wsl(c, mi * 128, (mi + 1) * 128),
                        xb.sl(c, t * TS, (t + 1) * TS)) for c in range(nk)]
                if extra is not None:
                    mms = mms + extra(f, t)
                gt = acc_group(ctx, bank, mms, deps=merge([wtok], list(ldeps)))
                wst.done(slot, gt)
                evac(f, t, bank, gt)
                i += 1



def emit_l0ffn(ctx, sb, dram, wst):
    p = ctx.p
    emit_cast(ctx, sb.x, sb.xb, scale=Q127)
    p.join()
    emit_gated_ffn(ctx, sb, xb=sb.xb, wg=dram["wgT"], wu=dram["wuT"],
                   wd=dram["wdT"], HID=3072, wst=wst, act=AF.Silu,
                   up_scale=Q127, wst_down=getattr(ctx, "wst_down", None))





def emit_router(ctx, sb, *, xb, wr_b, rb_row_bf, ones_row_bf, w_rows):
    p = ctx.p
    T = ctx.T
    NTC = T // 128
    nc = ctx.nc
    rtr = sb.rtr
    rbusy = sb.rtr_busy
    identf = sb.const["ident"]
    for tc in range(NTC):
        sl = (tc % 2) * 32
        bank = ctx.ring()
        mms = [(xb.sl(c, tc * 128, (tc + 1) * 128), wr_b[:, c * 8:c * 8 + 7])
               for c in range(ND)]
        mms.append((ones_row_bf[0:1, :], rb_row_bf[0:1, 0:7]))
        gt = acc_group(ctx, bank, mms, deps=merge([sb.wr_tok, rbusy[tc % 2]]), w=7)
        pb = ctx.ps[0:128, bank * TS: bank * TS + 7]
        pm = rtr[:, sl + 24:sl + 25]
        pmn = rtr[:, sl + 25:sl + 26]
        ssum = rtr[:, sl + 26:sl + 27]
        rec = rtr[:, sl + 27:sl + 28]
        m1 = rtr[:, sl + 28:sl + 29]
        m2 = rtr[:, sl + 29:sl + 30]
        s12 = rtr[:, sl + 30:sl + 31]
        rec2 = rtr[:, sl + 31:sl + 32]
        pe_ = rtr[:, sl + 0:sl + 7]
        msk = rtr[:, sl + 8:sl + 15]
        tmp7 = rtr[:, sl + 16:sl + 23]
        p.sop("vector", lambda eng, pb=pb, pm=pm: eng.reduce_max(pm, pb, AX.X),
              deps=[gt])
        v1 = p.sop("vector", lambda eng, pm=pm, pmn=pmn: eng.tensor_scalar_mul(pmn, pm, -1.0))
        et = p.op("scalar", lambda eng, pb=pb, pe_=pe_, pmn=pmn: eng.activation(
            pe_, pb, AF.Exp, bias=pmn), deps=[v1])
        bank_read(ctx, bank, et)
        p.sop("vector", lambda eng, pe_=pe_, ssum=ssum: eng.reduce_sum(ssum, pe_, AX.X),
              deps=[et])
        p.sop("vector", lambda eng, ssum=ssum, rec=rec: eng.reciprocal(rec, ssum))
        p.sop("vector", lambda eng, pe_=pe_, rec=rec: eng.tensor_scalar_mul(pe_, pe_, rec))
        p.sop("vector", lambda eng, pe_=pe_, m1=m1: eng.reduce_max(m1, pe_, AX.X))
        p.sop("vector", lambda eng, pe_=pe_, m1=m1, tmp7=tmp7: eng.tensor_single_scalar(
            tmp7, pe_, m1, ALU.is_ge))
        p.sop("vector", lambda eng, tmp7=tmp7: eng.tensor_scalar_mul(tmp7, tmp7, 2.0))
        p.sop("vector", lambda eng, pe_=pe_, tmp7=tmp7, msk=msk: eng.tensor_tensor(
            msk, pe_, tmp7, ALU.subtract))
        p.sop("vector", lambda eng, msk=msk, m2=m2: eng.reduce_max(m2, msk, AX.X))
        p.sop("vector", lambda eng, m1=m1, m2=m2, s12=s12: eng.tensor_tensor(
            s12, m1, m2, ALU.add))
        p.sop("vector", lambda eng, s12=s12, rec2=rec2: eng.reciprocal(rec2, s12))
        p.sop("vector", lambda eng, pe_=pe_, m2=m2, tmp7=tmp7: eng.tensor_single_scalar(
            tmp7, pe_, m2, ALU.is_ge))
        p.sop("vector", lambda eng, pe_=pe_, tmp7=tmp7: eng.tensor_tensor(
            pe_, pe_, tmp7, ALU.mult))
        wt = p.sop("vector", lambda eng, pe_=pe_, rec2=rec2: eng.tensor_scalar_mul(
            pe_, pe_, rec2))
        bank2 = ctx.ring()
        pt = p.op("tensor", lambda eng, pe_=pe_, bank2=bank2: eng.transpose(
            ctx.ps[0:7, bank2 * TS: bank2 * TS + 128], pe_, identf[:, :]),
            deps=merge([wt, ctx.bank_free[bank2]]))
        ctx.bank_ready[bank2] = pt
        ctx.bank_free[bank2] = None
        cb = p.op("vector", lambda eng, tc=tc, bank2=bank2: eng.tensor_copy(
            w_rows[0:7, tc * 128:(tc + 1) * 128],
            ctx.ps[0:7, bank2 * TS: bank2 * TS + 128]), deps=[pt])
        bank_read(ctx, bank2, cb)
        rbusy[tc % 2] = cb
    p.join()


def emit_moe_experts(ctx, sb, dram, wst, n_routed=7):
    p = ctx.p
    emit_gated_ffn(ctx, sb, xb=sb.xb, wg=dram["sgT"], wu=dram["suT"],
                   wd=dram["sdT"], HID=2816, wst=wst, act=AF.Silu,
                   wst_down=getattr(ctx, "wst_down", None))
    sel7 = sb.const["SEL7"]
    for e in range(n_routed):
        for t in range(ctx.NS):
            bank = ctx.ring()
            bt = acc_group(ctx, bank, [(sel7[0:7, e * 128:(e + 1) * 128],
                                        sb.w_rows[0:7, t * TS:(t + 1) * TS])],
                           deps=[sb.w_rows_tok] if sb.w_rows_tok else ())
            ct = p.op("vector", lambda eng, t=t, bank=bank: eng.tensor_copy(
                sb.wbb[:, t * TS:(t + 1) * TS], ctx.bank(bank)), deps=[bt])
            bank_read(ctx, bank, ct)
        p.join()
        emit_gated_ffn(ctx, sb, xb=sb.xb,
                       wg=dram["rgT"][e * D:(e + 1) * D, :],
                       wu=dram["ruT"][e * D:(e + 1) * D, :],
                       wd=dram["rdT"][e * 2816:(e + 1) * 2816, :],
                       HID=2816, wst=wst, act=AF.Silu, wb_ap=sb.wbb[:, :],
                       wst_down=getattr(ctx, "wst_down", None))
    p.join()




def emit_phi_evac(ctx, sb, dst_ap, bank, gtok, bias_ap, neg_bias_ap):
    p = ctx.p
    i = sb.phi_i % 2
    sb.phi_i += 1
    sA = sb.phi_s[3 * i]
    sB = sb.phi_s[3 * i + 1]
    sC = sb.phi_s[3 * i + 2]
    kw1 = {} if bias_ap is None else {"bias": bias_ap}
    kw2 = {} if neg_bias_ap is None else {"bias": neg_bias_ap}
    r1 = p.op("scalar", lambda eng, kw1=kw1: eng.activation(sA, ctx.bank(bank), AF.Relu, **kw1),
              deps=merge([gtok, sb.phi_busy[i]]))
    r2 = p.op("scalar", lambda eng, kw2=kw2: eng.activation(sB, ctx.bank(bank), AF.Relu,
                                                            scale=-1.0, **kw2))
    e2 = p.op("scalar", lambda eng: eng.activation(sC, sB, AF.Exp, scale=-1.0),
              deps=[r2], strict=True)
    w = p.op("vector", lambda eng: eng.tensor_tensor(sB, sA, sC, ALU.add),
             deps=[e2])
    f1 = p.op("scalar", lambda eng, dst=dst_ap: eng.activation(dst, sB, AF.Copy),
              deps=[w])
    sb.phi_busy[i] = f1
    return f1


def emit_l1_phase_a(ctx, sb, dram, wst, pq_dram, tile):
    p = ctx.p
    nc = ctx.nc
    T = ctx.T
    t0 = tile * T
    NSl = ctx.NS
    emit_cast(ctx, sb.x, sb.xb)
    p.join()
    bq = sb.vec["bq_col"]
    nbq = sb.vec_neg["bq_col"]
    for fb in range(0, ND, 4):
        slot, wsl, wtok = wst.load(dram["wqT"], 0, ND, fb * 128, 512)
        for mf in range(4):
            f = fb + mf
            toks = []
            for t in range(NSl):
                bank = ctx.ring()
                gt = acc_group(ctx, bank, [(wsl(c, mf * 128, (mf + 1) * 128),
                                            sb.xb.sl(c, t * TS, (t + 1) * TS))
                                           for c in range(ND)], deps=[wtok])
                wst.done(slot, gt)
                if getattr(ctx, "dbg_relu_only", False):
                    w = p.op("scalar", lambda eng, f=f, t=t, bank=bank: eng.activation(
                        sb.h.sl(f, t * TS, (t + 1) * TS), ctx.bank(bank), AF.Relu,
                        bias=bq[:, f:f + 1]), deps=[gt])
                else:
                    w = emit_phi_evac(ctx, sb, sb.h.sl(f, t * TS, (t + 1) * TS), bank, gt,
                                      bq[:, f:f + 1], nbq[:, f:f + 1])
                bank_read(ctx, bank, w)
                toks.append(w)
            p.dma("sync", pq_dram[f * 128:(f + 1) * 128, t0:t0 + T], sb.h.chunk(f),
                  deps=toks)
    NTC = T // 128
    bkb = sb.vecrow["bk_row"]
    bvb = sb.vecrow["bv_row"]
    onesr = sb.ones_row_bf
    onesc = sb.ones_bf
    for half in range(2):
        kslot, ksl, ktok = wst.load(dram["wkT"], 0, ND, half * 512, 512)
        vslot, vsl, vtok = wst.load(dram["wvT"], 0, ND, half * 512, 512)
        for tc in range(NTC):
            ktm = sb.ktm[tc % 2]
            vtm = sb.vtm[tc % 2]
            kv_toks = []
            bank = ctx.ring()
            mms = [(sb.xb.sl(c, tc * 128, (tc + 1) * 128),
                    ksl(c, 0, 512)) for c in range(ND)]
            mms.append((onesr[0:1, :], bkb[0:1, half * 512:(half + 1) * 512]))
            gt = acc_group(ctx, bank, mms, deps=[ktok, sb.ktm_busy[tc % 2]])
            wst.done(kslot, gt)
            w = emit_phi_evac_tm(ctx, sb, ktm[:, 0:512], bank, gt)
            bank_read(ctx, bank, w)
            kv_toks.append(w)
            bank = ctx.ring()
            mms = [(sb.xb.sl(c, tc * 128, (tc + 1) * 128),
                    vsl(c, 0, 512)) for c in range(ND)]
            mms.append((onesr[0:1, :], bvb[0:1, half * 512:(half + 1) * 512]))
            gt = acc_group(ctx, bank, mms, deps=[vtok, sb.vtm_busy[tc % 2]])
            wst.done(vslot, gt)
            w = p.op("scalar", lambda eng, vtm=vtm, bank=bank: eng.activation(
                vtm[:, 0:512], ctx.bank(bank), AF.Copy), deps=[gt])
            bank_read(ctx, bank, w)
            kv_toks.append(w)
            bank = ctx.ring()
            fdep = merge(kv_toks, [ctx.bank_free[bank]])
            last = None
            for hh in range(8):
                h = half * 8 + hh
                p0 = (h % 2) * 64
                c0 = (h // 2) * 64
                last = p.op("tensor",
                            lambda eng, hh=hh, p0=p0, c0=c0, bank=bank, ktm=ktm, vtm=vtm:
                            eng.matmul(ctx.ps[p0:p0 + 64, bank * TS + c0: bank * TS + c0 + 64],
                                       ktm[:, hh * 64:(hh + 1) * 64],
                                       vtm[:, hh * 64:(hh + 1) * 64],
                                       start=True, stop=True),
                            deps=fdep if hh == 0 else (), inc=(1 if hh == 7 else 0))
            ctx.bank_ready[bank] = last
            ctx.bank_free[bank] = None
            cola = (half * 8 // 2) * 64
            at = p.op("vector", lambda eng, bank=bank, cola=cola: eng.tensor_tensor(
                sb.kvt[:, cola:cola + 256], sb.kvt[:, cola:cola + 256],
                ctx.ps[0:128, bank * TS + cola: bank * TS + cola + 256], ALU.add),
                deps=[last])
            bank_read(ctx, bank, at)
            bank = ctx.ring()
            fdep = merge(kv_toks, [ctx.bank_free[bank]])
            last = None
            for db in range(4):
                last = p.op("tensor", lambda eng, db=db, bank=bank, ktm=ktm:
                            eng.matmul(ctx.ps[0:128, bank * TS + db: bank * TS + db + 1],
                                       ktm[:, db * 128:(db + 1) * 128], onesc[:, 0:1],
                                       start=True, stop=True),
                            deps=fdep if db == 0 else (), inc=(1 if db == 3 else 0))
            ctx.bank_ready[bank] = last
            ctx.bank_free[bank] = None
            at2 = p.sop("vector", lambda eng, bank=bank, half=half: eng.tensor_tensor(
                sb.ksum_col[:, half * 4:(half + 1) * 4],
                sb.ksum_col[:, half * 4:(half + 1) * 4],
                ctx.ps[0:128, bank * TS: bank * TS + 4], ALU.add), deps=[at, last])
            bank_read(ctx, bank, at2)
            sb.ktm_busy[tc % 2] = last
            sb.vtm_busy[tc % 2] = last
    p.join()


def emit_phi_evac_tm(ctx, sb, dst_ap, bank, gtok):
    return emit_phi_evac(ctx, sb, dst_ap, bank, gtok, None, None)


def emit_l1_phase_b(ctx, sb, dram, wst, pq_dram, xin_dram, tile, out_bf=None):
    p = ctx.p
    nc = ctx.nc
    T = ctx.T
    t0 = tile * T
    NSl = ctx.NS
    qdep = [("Ssy", p.cnt["Ssy"])] if p.cnt["Ssy"] else []
    for c in range(ND):
        p.dma("sync", sb.x.chunk(c), xin_dram[c * 128:(c + 1) * 128, t0:t0 + T],
              deps=qdep if c == 0 else (), strict=True)
        p.dma("sync", sb.h.chunk(c), pq_dram[c * 128:(c + 1) * 128, t0:t0 + T])
    p.join()
    for t in range(NSl):
        bank = ctx.ring()
        gt = acc_group(ctx, bank, [(sb.ks_cb.chunk(c), sb.h.sl(c, t * TS, (t + 1) * TS))
                                   for c in range(ND)], deps=[sb.ks_tok], p0=0, p1=16)
        zt = p.op("vector", lambda eng, t=t, bank=bank: eng.reciprocal(
            sb.zrow[0:16, t * TS:(t + 1) * TS], ctx.bank(bank, 0, 16)), deps=[gt])
        bank_read(ctx, bank, zt)
    p.join()
    sel16 = sb.const["SEL16"]
    pq_toks = {}
    for c in range(ND):
        for t in range(NSl):
            bank = ctx.ring()
            bt = acc_group(ctx, bank, [(sel16[0:16, c * 128:(c + 1) * 128],
                                        sb.zrow[0:16, t * TS:(t + 1) * TS])])
            w = p.op("vector", lambda eng, c=c, t=t, bank=bank: eng.tensor_tensor(
                sb.xb.sl(c, t * TS, (t + 1) * TS), sb.h.sl(c, t * TS, (t + 1) * TS),
                ctx.bank(bank), ALU.mult), deps=[bt])
            bank_read(ctx, bank, w)
            pq_toks[(c, t)] = w
    kb = p.op("scalar", lambda eng: eng.activation(sb.kvt_bf[:, :], sb.kvt[:, :], AF.Copy))
    for fc in range(ND):
        for t in range(NSl):
            bank = ctx.ring()
            dep0 = merge([kb, pq_toks[(fc, t)], ctx.bank_free[bank]])
            last = None
            for half in range(2):
                h = 2 * fc + half
                p0 = half * 64
                c0 = (h // 2) * 64
                last = p.op("tensor", lambda eng, p0=p0, c0=c0, bank=bank, fc=fc, t=t:
                            eng.matmul(ctx.ps[p0:p0 + 64, bank * TS: bank * TS + TS],
                                       sb.kvt_bf[p0:p0 + 64, c0:c0 + 64],
                                       sb.xb.h[p0:p0 + 64,
                                               fc * T + t * TS: fc * T + (t + 1) * TS],
                                       start=True, stop=True),
                            deps=dep0 if half == 0 else (), inc=(1 if half == 1 else 0))
            ctx.bank_ready[bank] = last
            ctx.bank_free[bank] = None
            ev = p.op("scalar", lambda eng, fc=fc, t=t, bank=bank: eng.activation(
                sb.h.sl(fc, t * TS, (t + 1) * TS), ctx.bank(bank), AF.Copy), deps=[last])
            bank_read(ctx, bank, ev)
    p.join()
    bd = sb.vec["bd_col"]
    for fb in range(0, ND, 4):
        slot, wsl, wtok = wst.load(dram["wdT"], 0, ND, fb * 128, 512)
        for mf in range(4):
            f = fb + mf
            for t in range(NSl):
                bank = ctx.ring()
                gt = acc_group(ctx, bank, [(wsl(c, mf * 128, (mf + 1) * 128),
                                            sb.h.sl(c, t * TS, (t + 1) * TS))
                                           for c in range(ND)], deps=[wtok])
                wst.done(slot, gt)
                ev = p.op("vector", lambda eng, f=f, t=t, bank=bank:
                          eng.scalar_tensor_tensor(
                              sb.x.sl(f, t * TS, (t + 1) * TS), ctx.bank(bank),
                              bd[:, f:f + 1], sb.x.sl(f, t * TS, (t + 1) * TS),
                              ALU.add, ALU.add), deps=[gt])
                bank_read(ctx, bank, ev)
    p.join()
    xs2 = sb.xs2

    def add_f32(c, t, xin=xin_dram, t0=t0):
        dep = [("Sdv", ctx.p.cnt["Sdv"])] if ctx.p.cnt["Sdv"] else []
        tok = ctx.p.dma("sync", xs2[:, c * TS:(c + 1) * TS],
                        xin[c * 128:(c + 1) * 128, t0 + t * TS: t0 + (t + 1) * TS],
                        deps=dep, strict=True)
        return xs2[:, c * TS:(c + 1) * TS], tok
    emit_ln(ctx, sb, g=sb.vec["l1g"], b=sb.vec["l1b"], eps=EPS_L1LN, add_f32=add_f32)
    emit_ln(ctx, sb, eps=EPS_TOP, out_bf=out_bf)




def setup_l1(ctx, sb, dram):
    nc, p = ctx.nc, ctx.p
    T = ctx.T
    sb.ktm = [nc.alloc_sbuf_tensor(f"ktm{i}", [128, 512], BF16) for i in range(2)]
    sb.vtm = [nc.alloc_sbuf_tensor(f"vtm{i}", [128, 512], BF16) for i in range(2)]
    sb.ktm_busy = [None, None]
    sb.vtm_busy = [None, None]
    phi_h = nc.alloc_sbuf_tensor("phi_h", [128, 6 * TS], F32)
    sb.phi_s = [phi_h[:, i * TS:(i + 1) * TS] for i in range(6)]
    sb.phi_busy = [None, None]
    sb.phi_i = 0
    sb.kvt = nc.alloc_sbuf_tensor("kvt", [128, 512], F32)
    sb.kvt_bf = sb.ktm[0][:, :]
    sb.ksum_col = nc.alloc_sbuf_tensor("ksum_col", [128, 8], F32)
    sb.zrow = phi_h[0:16, 0:T]
    if getattr(sb, "ks_cb", None) is None:
        sb.ks_cb = CBuf(nc, "ks_cb", ND, 16, BF16)
    sb.ks_tok = None
    sb.xs2 = sb.h.h[:, 4 * T:8 * T].bitcast(F32)
    p.op("gpsimd", lambda eng: eng.memset(sb.kvt[:, :], 0.0), inc=0)
    mt = p.op("gpsimd", lambda eng: eng.memset(sb.ksum_col[:, :], 0.0))
    sb.kv_zero_tok = mt
    p.join()
    sb.vec_neg = {}
    for nm in ("bq_col",):
        h = nc.alloc_sbuf_tensor("vn_" + nm, [128, 8], F32)
        p.op("vector", lambda eng, nm=nm, h=h: eng.tensor_scalar_mul(
            h[:, :], sb.vec[nm][:, :], -1.0))
        sb.vec_neg[nm] = h
    p.join()


def emit_ks_build(ctx, sb):
    p = ctx.p
    nc = ctx.nc
    mz = p.op("gpsimd", lambda eng: eng.memset(sb.ks_cb.h[:, :], 0.0))
    ke = p.op("vector", lambda eng: eng.tensor_scalar_add(
        sb.ksum_col[:, :], sb.ksum_col[:, :], float(EPS_ATTN)))
    last = None
    for c in range(ND):
        p.op("scalar", lambda eng, c=c: eng.copy(
            sb.ks_cb.h[0:64, c * 16 + 2 * c: c * 16 + 2 * c + 1],
            sb.ksum_col[0:64, c:c + 1]), deps=[ke, mz], inc=0)
        last = p.op("scalar", lambda eng, c=c: eng.copy(
            sb.ks_cb.h[64:128, c * 16 + 2 * c + 1: c * 16 + 2 * c + 2],
            sb.ksum_col[64:128, c:c + 1]))
    sb.ks_tok = last
    p.join()


def build_l1_program(T, debug=None):
    nc = bass.Bass("TRN2", target_bir_lowering=False, debug=False)
    p = Prog(nc)
    ctx = Ctx(nc, p, T)
    consts = host_consts()
    sb = SB(ctx, T, consts, need=("SEL16",))
    if debug == "relu":
        ctx.dbg_relu_only = True
        debug = "a"
    xin = nc.dram_tensor("s_in", [D, SEQ], F32, kind="ExternalInput").ap()
    xout = nc.dram_tensor("s_out", [D, SEQ], F32, kind="ExternalOutput").ap()
    pq = nc.dram_tensor("pq_scr", [D, SEQ], BF16,
                        kind=("ExternalOutput" if debug == "a" else "Internal")).ap()
    if debug == "a":
        dbg_kvt = nc.dram_tensor("dbg_kvt", [128, 512], F32, kind="ExternalOutput").ap()
        dbg_ks = nc.dram_tensor("dbg_ks", [128, 8], F32, kind="ExternalOutput").ap()
    dram = {}
    for nm, dn in (("wqT", "l1_wq"), ("wkT", "l1_wk"), ("wvT", "l1_wv"), ("wdT", "l1_wd")):
        dram[nm] = nc.dram_tensor(dn, [D, D], F32, kind="ExternalInput").ap()
    load_vec_inputs(ctx, sb, {"bq_col": (128, 8), "bd_col": (128, 8),
                              "l1g": (128, 8), "l1b": (128, 8),
                              "bk_row": (1, 1024), "bv_row": (1, 1024)},
                    bf_rows=("bk_row", "bv_row"))
    setup_l1(ctx, sb, dram)
    wst = WStream(ctx, "w", nbuf=2)
    for tile in range(SEQ // T):
        t0 = tile * T
        for c in range(ND):
            p.dma("sync", sb.x.chunk(c), xin[c * 128:(c + 1) * 128, t0:t0 + T],
                  strict=True)
        p.join()
        emit_l1_phase_a(ctx, sb, dram, wst, pq, tile)
    if debug == "a":
        p.dma("sync", dbg_kvt[:, :], sb.kvt[:, :])
        p.dma("sync", dbg_ks[:, :], sb.ksum_col[:, :])
        p.dma("sync", xout[0:128, 0:T], sb.x.chunk(0))
        p._wait("gpsimd", "Ssy", p.cnt["Ssy"])
        p.build()
        return nc
    emit_ks_build(ctx, sb)
    prev_store = None
    for tile in range(SEQ // T):
        t0 = tile * T
        emit_l1_phase_b(ctx, sb, dram, wst, pq, xin, tile)
        for c in range(ND):
            prev_store = p.dma("sync", xout[c * 128:(c + 1) * 128, t0:t0 + T],
                               sb.x.chunk(c))
        p.join()
    p._wait("gpsimd", "Ssy", p.cnt["Ssy"])
    p.build()
    return nc





def setup_l0(ctx, sb, consts):
    nc, p = ctx.nc, ctx.p
    sb.dt_b = nc.alloc_sbuf_tensor("dt_b", [128, NH * CH], BF16)
    sb.qd_b = nc.alloc_sbuf_tensor("qd_b", [128, ND * CH], BF16)
    sb.kd_b = nc.alloc_sbuf_tensor("kd_b", [128, 2 * 512], BF16)
    sb.gc_f = nc.alloc_sbuf_tensor("gc_f", [128, 512], F32)
    for name, dst, w in (("DT", sb.dt_b, CH), ("QD", sb.qd_b, CH), ("KD", sb.kd_b, 512)):
        arr = consts[name]
        dr = nc.inline_tensor(np.ascontiguousarray(arr), "ic_" + name)
        n = arr.shape[0] // 128
        for i in range(n):
            p.dma("gpsimd", dst[:, i * w:(i + 1) * w], dr.ap()[i * 128:(i + 1) * 128, :])
    drg = nc.inline_tensor(np.ascontiguousarray(consts["GC"]), "ic_GC")
    p.dma("sync", sb.gc_f[:, :], drg.ap())
    sb.S = [nc.alloc_sbuf_tensor(f"S{i}", [128, 512], F32) for i in range(2)]
    sb.S_bf = [nc.alloc_sbuf_tensor(f"Sbf{i}", [128, 512], BF16) for i in range(2)]
    sb.S_tok = [None, None]
    sb.at_bf = [nc.alloc_sbuf_tensor(f"atbf{i}", [128, CH], BF16) for i in range(4)]
    sb.at_busy = [None] * 4
    sb.rr = nc.alloc_sbuf_tensor("rr", [128, 4 * 4096], BF16)
    sb.pbrow = nc.alloc_sbuf_tensor("pbrow", [1, 2048], BF16)
    sb.krtvrt = nc.alloc_sbuf_tensor("krtvrt", [128, 8192], BF16)
    sb.krt = [sb.krtvrt[:, i * 1024:(i + 1) * 1024] for i in range(4)]
    sb.vrt = [sb.krtvrt[:, 4096 + i * 1024:4096 + (i + 1) * 1024] for i in range(4)]
    sb.krt_busy = [None] * 4
    sb.vrt_busy = [None] * 4
    for i in range(2):
        p.op("gpsimd", lambda eng, i=i: eng.memset(sb.S[i][:, :], 0.0), inc=0)
        p.op("gpsimd", lambda eng, i=i: eng.memset(sb.S_bf[i][:, :], 0.0))
    p.join()


def emit_retention_mixer(ctx, sb, dram, wst, layer):
    p = ctx.p
    T = ctx.T
    pw = dram[f"pw{layer}"]
    rw = dram[f"rw{layer}"]
    pbc = sb.vec[f"pb{layer}_col"]
    pbr = sb.pbrow
    p.dma("gpsimd", pbr[0:1, :], dram[f"pbrow{layer}"][0:1, :], deps=p.now())
    rbc = sb.vec[f"rb{layer}"]
    onesr = sb.ones_row_bf
    S = sb.S[layer]
    S_bf = sb.S_bf[layer]
    h = sb.h
    rr = sb.rr

    def reg(base, fc, a, b):
        r = base // 2
        return rr[:, r * 4096 + fc * 512 + a: r * 4096 + fc * 512 + b]

    NSC = T // 512
    for sc in range(NSC):
        tok0 = sc * 512
        for colbase, kind in ((0, "q"), (1024, "k"), (3072, "xo")):
            for fb in (0, 4):
                slot, wsl, wtok = wst.load(pw, 0, ND, colbase + fb * 128, 512)
                for mf in range(4):
                    f = fb + mf
                    fg = (colbase // 128) + f
                    bank = ctx.ring()
                    gt = acc_group(ctx, bank,
                                   [(wsl(c, mf * 128, (mf + 1) * 128),
                                     sb.xb.sl(c, tok0, tok0 + 512)) for c in range(ND)],
                                   deps=[wtok])
                    wst.done(slot, gt)
                    pb_ap = pbc[:, fg:fg + 1]
                    pk = ctx.bank(bank)
                    if kind == "q":
                        qd_ap = sb.qd_b[:, f * CH:(f + 1) * CH]
                        for tci4 in range(4):
                            dst = reg(0, f, tci4 * CH, (tci4 + 1) * CH)
                            pk4 = ctx.ps[0:128, bank * TS + tci4 * CH: bank * TS + (tci4 + 1) * CH]
                            ev = p.op("vector",
                                      lambda eng, dst=dst, pk4=pk4, pb_ap=pb_ap, qd_ap=qd_ap:
                                      eng.scalar_tensor_tensor(dst, pk4, pb_ap, qd_ap,
                                                               ALU.add, ALU.mult),
                                      deps=[gt] if tci4 == 0 else ())
                    elif kind == "k":
                        dst = reg(2, f, 0, 512)
                        ev = p.op("vector", lambda eng, dst=dst, pk=pk, pb_ap=pb_ap:
                                  eng.tensor_scalar_add(dst, pk, pb_ap), deps=[gt])
                    else:
                        dst = reg(4, f, 0, 512)
                        ev = p.op("scalar", lambda eng, dst=dst, pk=pk, pb_ap=pb_ap:
                                  eng.activation(dst, pk, AF.Silu, bias=pb_ap), deps=[gt])
                    bank_read(ctx, bank, ev)
        for half in range(2):
            kslot, ksl, ktok = wst.load(pw, 0, ND, 1024 + half * 512, 512)
            vslot, vsl, vtok = wst.load(pw, 0, ND, 2048 + half * 512, 512)
            for tci in range(4):
                tc = sc * 4 + tci
                krt = sb.krt[tci]
                vrt = sb.vrt[tci]
                bank = ctx.ring()
                mms = [(sb.xb.sl(c, tc * 128, (tc + 1) * 128), ksl(c, 0, 512))
                       for c in range(ND)]
                mms.append((onesr[0:1, :], pbr[0:1, half * 512:(half + 1) * 512]))
                gt = acc_group(ctx, bank, mms, deps=[ktok, sb.krt_busy[tci]])
                wst.done(kslot, gt)
                dst = krt[:, half * 512:(half + 1) * 512]
                kd_ap = sb.kd_b[:, half * 512:(half + 1) * 512]
                pk = ctx.bank(bank)
                ev = p.op("vector", lambda eng, dst=dst, pk=pk, kd_ap=kd_ap:
                          eng.tensor_tensor(dst, pk, kd_ap, ALU.mult), deps=[gt])
                bank_read(ctx, bank, ev)
                bank = ctx.ring()
                mms = [(sb.xb.sl(c, tc * 128, (tc + 1) * 128), vsl(c, 0, 512))
                       for c in range(ND)]
                mms.append((onesr[0:1, :], pbr[0:1, 1024 + half * 512:1024 + (half + 1) * 512]))
                gt = acc_group(ctx, bank, mms, deps=[vtok, sb.vrt_busy[tci]])
                wst.done(vslot, gt)
                dst = vrt[:, half * 512:(half + 1) * 512]
                pk = ctx.bank(bank)
                ev = p.op("scalar", lambda eng, dst=dst, pk=pk:
                          eng.activation(dst, pk, AF.Copy), deps=[gt])
                bank_read(ctx, bank, ev)
        p.join()
        ctx.ring_banks = [0, 1, 2, 3, 4]
        ret2_evs = []
        for tci in range(4):
            tc = sc * 4 + tci
            cofs = tci * 128
            krt = sb.krt[tci]
            vrt = sb.vrt[tci]
            ret_banks = [6, 7]
            ret_last = {}
            for hh in range(NH):
                hp = (hh % 2) * 64
                fc = hh // 2
                k_ap = rr[hp:hp + 64, 1 * 4096 + fc * 512 + cofs: 1 * 4096 + fc * 512 + cofs + 128]
                q_ap = rr[hp:hp + 64, 0 * 4096 + fc * 512 + cofs: 0 * 4096 + fc * 512 + cofs + 128]
                bank = ctx.ring()
                at_out = ctx.bank(bank, 0, 128, CH)
                d0 = merge([ctx.bank_free[bank], sb.at_busy[hh % 4]])
                at_mm = p.op("tensor", lambda eng, at_out=at_out, k_ap=k_ap, q_ap=q_ap:
                             eng.matmul(at_out, k_ap, q_ap, start=True, stop=True),
                             deps=d0)
                ctx.bank_ready[bank] = at_mm
                ctx.bank_free[bank] = None
                atb = sb.at_bf[hh % 4]
                dt_ap = sb.dt_b[:, hh * CH:(hh + 1) * CH]
                at_ev = p.op("vector", lambda eng, atb=atb, at_out=at_out, dt_ap=dt_ap:
                             eng.tensor_tensor(atb[:, :], at_out, dt_ap, ALU.mult),
                             deps=[at_mm])
                bank_read(ctx, bank, at_ev)
                sb.at_busy[hh % 4] = at_ev
                rbank = ret_banks[hh // 8]
                cblk = ((hh // 2) % 4) * 128
                r_out = ctx.ps[hp:hp + 64, rbank * TS + cblk: rbank * TS + cblk + 128]
                v_ap = vrt[:, hh * 64:(hh + 1) * 64]
                s_ap = S_bf[hp:hp + 64, fc * 64:fc * 64 + 64]
                rdep = merge([at_ev, sb.S_tok[layer]] +
                             ([ctx.bank_free[rbank]] if hh % 8 == 0 else []))
                p.op("tensor", lambda eng, r_out=r_out, v_ap=v_ap, atb=atb:
                     eng.matmul(r_out, v_ap, atb[:, :], start=True, stop=False),
                     deps=rdep, inc=0)
                rt = p.op("tensor", lambda eng, r_out=r_out, s_ap=s_ap, q_ap=q_ap:
                          eng.matmul(r_out, s_ap, q_ap, start=False, stop=True))
                ret_last[hh] = rt
                if hh % 8 == 7:
                    ctx.bank_ready[rbank] = rt
                    ctx.bank_free[rbank] = None
            for fc in range(ND):
                rbank = ret_banks[fc // 4]
                cblk = (fc % 4) * 128
                dst = reg(6, fc, cofs, cofs + 128)
                ps_ap = ctx.ps[0:128, rbank * TS + cblk: rbank * TS + cblk + 128]
                xo_ap = rr[:, 2 * 4096 + fc * 512 + cofs: 2 * 4096 + fc * 512 + cofs + 128]
                ev = p.op("vector", lambda eng, dst=dst, ps_ap=ps_ap, xo_ap=xo_ap:
                          eng.tensor_tensor(dst, ps_ap, xo_ap, ALU.mult),
                          deps=[ret_last[fc * 2 + 1]])
                bank_read(ctx, rbank, ev)
                ret2_evs.append(ev)
            ibank = 5
            ilast = None
            for hh in range(NH):
                hp = (hh % 2) * 64
                c0 = (hh // 2) * 64
                i_out = ctx.ps[hp:hp + 64, ibank * TS + c0: ibank * TS + c0 + 64]
                kk = krt[:, hh * 64:(hh + 1) * 64]
                vv = vrt[:, hh * 64:(hh + 1) * 64]
                d0 = merge([ctx.bank_free[ibank]]) if hh == 0 else []
                ilast = p.op("tensor", lambda eng, i_out=i_out, kk=kk, vv=vv:
                             eng.matmul(i_out, kk, vv, start=True, stop=True),
                             deps=d0, inc=(1 if hh == NH - 1 else 0))
            ctx.bank_ready[ibank] = ilast
            ctx.bank_free[ibank] = None
            sd = merge([ilast, sb.S_tok[layer]] + [ret_last[hh] for hh in range(NH)])
            p.op("vector", lambda eng: eng.tensor_tensor(S[:, :], S[:, :], sb.gc_f[:, :],
                                                         ALU.mult), deps=sd, inc=0)
            s2 = p.op("vector", lambda eng, ibank=ibank: eng.tensor_tensor(
                S[:, :], S[:, :], ctx.bank(ibank), ALU.add))
            bank_read(ctx, ibank, s2)
            sbf = p.op("scalar", lambda eng: eng.activation(S_bf[:, :], S[:, :], AF.Copy),
                       deps=[s2])
            sb.S_tok[layer] = sbf
            sb.krt_busy[tci] = ilast
            sb.vrt_busy[tci] = ilast
        ctx.ring_banks = None
        for fb in (0, 4):
            slot, wsl, wtok = wst.load(rw, 0, ND, fb * 128, 512)
            for mf in range(4):
                f = fb + mf
                bank = ctx.ring()
                gt = acc_group(ctx, bank,
                               [(wsl(c, mf * 128, (mf + 1) * 128),
                                 reg(6, c, 0, 512)) for c in range(ND)],
                               deps=merge([wtok], ret2_evs))
                wst.done(slot, gt)
                x_ap = sb.x.sl(f, tok0, tok0 + 512)
                pk = ctx.bank(bank)
                rb_ap = rbc[:, f:f + 1]
                ev = p.op("vector", lambda eng, x_ap=x_ap, pk=pk, rb_ap=rb_ap:
                          eng.scalar_tensor_tensor(x_ap, pk, rb_ap, x_ap,
                                                   ALU.add, ALU.add), deps=[gt])
                bank_read(ctx, bank, ev)
    p.join()


def emit_plain_ffn(ctx, sb, *, xb, w1, w2, b1_col, b2_col, HID, wst, act=AF.Gelu,
                   wst_down=None):
    p = ctx.p
    T = ctx.T
    NSl = ctx.NS
    NHC = HID // 128
    h = sb.h
    for g0 in range(0, NHC, 8):
        ng = min(8, NHC - g0)
        gl = {}
        m_tok = None
        for gi in range(ng):
            half = gi // 4
            if half not in gl:
                nw = min(4, ng - half * 4) * 128
                gl[half] = wst.load(w1, 0, ND, (g0 + half * 4) * 128, nw)
            slot, wsl, wtok = gl[half]
            gcol = (gi % 4) * 128
            bg = 4 + (gi % 2)
            for t in range(NSl):
                gt = acc_group(ctx, bg, [(wsl(c, gcol, gcol + 128),
                                          xb.sl(c, t * TS, (t + 1) * TS)) for c in range(ND)],
                               deps=[wtok])
                wst.done(slot, gt)
                m_tok = p.op("scalar", lambda eng, gi=gi, t=t, bg=bg, g0=g0: eng.activation(
                    h.sl(gi, t * TS, (t + 1) * TS), ctx.bank(bg), act,
                    bias=b1_col[:, g0 + gi:g0 + gi + 1]), deps=[gt])
                bank_read(ctx, bg, m_tok)
        wd_st = wst_down if wst_down is not None else wst
        dl = []
        for half in range(0, ng, 4):
            nked = min(4, ng - half)
            dl.append(wd_st.load(w2, (g0 + half) * 128, nked, 0, 1024))
        for f2 in range(ND):
            for t in range(NSl):
                bank = t if (f2 % 2 == 0) else 4 + t
                mms = []
                dep0 = [dl[-1][2], m_tok]
                for gi in range(ng):
                    dslot, dsl, dtok = dl[gi // 4]
                    mms.append((dsl(gi % 4, f2 * 128, (f2 + 1) * 128),
                                h.sl(gi, t * TS, (t + 1) * TS)))
                gt = acc_group(ctx, bank, mms, deps=dep0)
                for dslot, dsl, dtok in dl:
                    wd_st.done(dslot, gt)
                if g0 == 0:
                    ev = p.op("vector", lambda eng, f2=f2, t=t, bank=bank:
                              eng.scalar_tensor_tensor(
                                  sb.x.sl(f2, t * TS, (t + 1) * TS), ctx.bank(bank),
                                  b2_col[:, f2:f2 + 1],
                                  sb.x.sl(f2, t * TS, (t + 1) * TS), ALU.add, ALU.add),
                              deps=[gt])
                else:
                    ev = p.op("vector", lambda eng, f2=f2, t=t, bank=bank:
                              eng.tensor_tensor(
                                  sb.x.sl(f2, t * TS, (t + 1) * TS),
                                  sb.x.sl(f2, t * TS, (t + 1) * TS),
                                  ctx.bank(bank), ALU.add), deps=[gt])
                bank_read(ctx, bank, ev)
    p.join()


def emit_l0attn(ctx, sb, dram, wst, xin_dram, tile):
    p = ctx.p
    T = ctx.T
    t0 = tile * T
    dbg = getattr(ctx, "l0_dbg", None)
    for layer in range(2):
        emit_ln(ctx, sb, g=sb.vec[f"ln1g{layer}"], b=sb.vec[f"ln1b{layer}"],
                eps=EPS_TOP, out_bf=sb.xb, bf_only=True)
        if dbg is not None and layer == 0 and tile == 0 and "lnx" in dbg:
            p.join()
            for c in range(ND):
                p.dma("sync", dbg["lnx"][c * 128:(c + 1) * 128, 0:T], sb.xb.chunk(c))
            p.join()
        emit_retention_mixer(ctx, sb, dram, wst, layer)
        if dbg is not None and layer == 0 and tile == 0 and "mix" in dbg:
            p.join()
            for c in range(ND):
                p.dma("sync", dbg["mix"][c * 128:(c + 1) * 128, 0:T], sb.x.chunk(c))
            p.join()
        emit_ln(ctx, sb, g=sb.vec[f"ln2g{layer}"], b=sb.vec[f"ln2b{layer}"],
                eps=EPS_TOP, out_bf=sb.xb, bf_only=True)
        emit_plain_ffn(ctx, sb, xb=sb.xb, w1=dram[f"f1w{layer}"], w2=dram[f"f2w{layer}"],
                       b1_col=sb.vec[f"f1b{layer}"], b2_col=sb.vec[f"f2b{layer}"],
                       HID=4096, wst=wst, act=AF.Gelu,
                       wst_down=getattr(ctx, "wst_down", None))
        if dbg is not None and layer == 0 and tile == 0 and "ffn" in dbg:
            p.join()
            for c in range(ND):
                p.dma("sync", dbg["ffn"][c * 128:(c + 1) * 128, 0:T], sb.x.chunk(c))
            p.join()
    def add_f32(c, t, xin=xin_dram, t0=t0):
        dep = [("Sdv", p.cnt["Sdv"])] if p.cnt["Sdv"] else []
        tok = p.dma("sync", sb.xs2[:, c * TS:(c + 1) * TS],
                    xin[c * 128:(c + 1) * 128, t0 + t * TS: t0 + (t + 1) * TS],
                    deps=dep, strict=True)
        return sb.xs2[:, c * TS:(c + 1) * TS], tok
    emit_ln(ctx, sb, g=sb.vec["lnfg"], b=sb.vec["lnfb"], eps=EPS_TOP, add_f32=add_f32)
    emit_ln(ctx, sb, eps=EPS_TOP)


def build_l0attn_program(T, debug=()):
    nc = bass.Bass("TRN2", target_bir_lowering=False, debug=False)
    p = Prog(nc)
    ctx = Ctx(nc, p, T)
    consts = host_consts()
    sb = SB(ctx, T, consts)
    xin = nc.dram_tensor("s_in", [D, SEQ], F32, kind="ExternalInput").ap()
    xout = nc.dram_tensor("s_out", [D, SEQ], F32, kind="ExternalOutput").ap()
    if debug:
        ctx.l0_dbg = {}
        for nm in debug:
            dt_ = BF16 if nm == "lnx" else F32
            ctx.l0_dbg[nm] = nc.dram_tensor("dbg_" + nm, [D, T], dt_, kind="ExternalOutput").ap()
    dram = {}
    for i in range(2):
        dram[f"pw{i}"] = nc.dram_tensor(f"pw{i}", [D, 4 * D], F32, kind="ExternalInput").ap()
        dram[f"rw{i}"] = nc.dram_tensor(f"rw{i}", [D, D], F32, kind="ExternalInput").ap()
        dram[f"f1w{i}"] = nc.dram_tensor(f"f1w{i}", [D, 4 * D], F32, kind="ExternalInput").ap()
        dram[f"f2w{i}"] = nc.dram_tensor(f"f2w{i}", [4 * D, D], F32, kind="ExternalInput").ap()
    vecs = {"lnfg": (128, 8), "lnfb": (128, 8)}
    rows = []
    for i in range(2):
        for nm in (f"rb{i}", f"f2b{i}", f"ln1g{i}", f"ln1b{i}",
                   f"ln2g{i}", f"ln2b{i}"):
            vecs[nm] = (128, 8)
        vecs[f"pb{i}_col"] = (128, 32)
        vecs[f"f1b{i}"] = (128, 32)
        dram[f"pbrow{i}"] = nc.dram_tensor(f"pb{i}_row", [1, 2048], F32,
                                           kind="ExternalInput").ap()
    load_vec_inputs(ctx, sb, vecs)
    setup_l0(ctx, sb, consts)
    sb.xs2 = sb.rr[:, 0:2 * ND * TS].bitcast(F32)
    wst = WStream(ctx, "w", nbuf=2)
    prev_store = None
    for tile in range(SEQ // T):
        t0 = tile * T
        for c in range(ND):
            p.dma("sync", sb.x.chunk(c), xin[c * 128:(c + 1) * 128, t0:t0 + T],
                  deps=[prev_store] if (prev_store and c == 0) else (), strict=True)
        p.join()
        emit_l0attn(ctx, sb, dram, wst, xin, tile)
        for c in range(ND):
            prev_store = p.dma("sync", xout[c * 128:(c + 1) * 128, t0:t0 + T], sb.x.chunk(c))
    p._wait("gpsimd", "Ssy", p.cnt["Ssy"])
    p.build()
    return nc



def build_l0ffn_program(T):
    nc = bass.Bass("TRN2", target_bir_lowering=False, debug=False)
    p = Prog(nc)
    ctx = Ctx(nc, p, T)
    consts = host_consts()
    sb = SB(ctx, T, consts)
    xin = nc.dram_tensor("s_in", [D, SEQ], F32, kind="ExternalInput").ap()
    xout = nc.dram_tensor("s_out", [D, SEQ], F32, kind="ExternalOutput").ap()
    dram = {
        "wgT": nc.dram_tensor("wgT", [D, 3072], I32, kind="ExternalInput").ap(),
        "wuT": nc.dram_tensor("wuT", [D, 3072], I32, kind="ExternalInput").ap(),
        "wdT": nc.dram_tensor("wdT", [3072, D], I32, kind="ExternalInput").ap(),
    }
    wst = WStream(ctx, "w", nbuf=4)
    prev_store = None
    for tile in range(SEQ // T):
        t0 = tile * T
        for c in range(ND):
            p.dma("sync", sb.x.chunk(c), xin[c * 128:(c + 1) * 128, t0:t0 + T],
                  deps=[prev_store] if (prev_store and c == 0) else (), strict=True)
        p.join()
        emit_l0ffn(ctx, sb, dram, wst)
        emit_ln(ctx, sb, eps=EPS_TOP)
        for c in range(ND):
            prev_store = p.dma("sync", xout[c * 128:(c + 1) * 128, t0:t0 + T], sb.x.chunk(c))
    p._wait("gpsimd", "Ssy", p.cnt["Ssy"])
    p.build()
    return nc




def _load_router_weights(ctx, sb, wrT_dram, rb_dram):
    nc, p = ctx.nc, ctx.p
    sb.wr_b = nc.alloc_sbuf_tensor("wr_b", [128, 64], BF16)
    tok = None
    for c in range(ND):
        tok = p.dma("gpsimd", sb.wr_b[:, c * 8:c * 8 + 7],
                    wrT_dram[c * 128:(c + 1) * 128, 0:7])
    sb.rb_row_bf = nc.alloc_sbuf_tensor("rb_row_bf", [1, 8], BF16)
    tok = p.dma("gpsimd", sb.rb_row_bf[0:1, 0:7], rb_dram[0:1, 0:7])
    sb.wr_tok = tok


def build_moe_program(T, debug_w=False, n_routed=7):
    nc = bass.Bass("TRN2", target_bir_lowering=False, debug=False)
    p = Prog(nc)
    ctx = Ctx(nc, p, T)
    consts = host_consts()
    sb = SB(ctx, T, consts, need=("SEL7", "ident"), moe=True)
    sb.w_rows = nc.alloc_sbuf_tensor("w_rows", [7, T], BF16)
    sb.wbb = nc.alloc_sbuf_tensor("wbb", [128, T], F32)
    sb.rtr = nc.alloc_sbuf_tensor("rtr", [128, 64], F32)
    xin = nc.dram_tensor("s_in", [D, SEQ], F32, kind="ExternalInput").ap()
    xout = nc.dram_tensor("s_out", [D, SEQ], F32, kind="ExternalOutput").ap()
    dram = {}
    for nm, shape in (("sgT", [D, 2816]), ("suT", [D, 2816]), ("sdT", [2816, D]),
                      ("rgT", [7 * D, 2816]), ("ruT", [7 * D, 2816]),
                      ("rdT", [7 * 2816, D]), ("wrT", [D, 7]), ("rb_row", [1, 7])):
        dram[nm] = nc.dram_tensor(nm, shape, F32, kind="ExternalInput").ap()
    _load_router_weights(ctx, sb, dram["wrT"], dram["rb_row"])
    dbg_w = nc.dram_tensor("dbg_w", [7, SEQ], F32, kind="ExternalOutput").ap() if debug_w else None
    dbg_z = nc.dram_tensor("dbg_z", [D, SEQ], F32, kind="ExternalOutput").ap() if debug_w else None
    if debug_w:
        ctx.dbg_h = nc.dram_tensor("dbg_h", [1024, T], BF16, kind="ExternalOutput").ap()
        ctx.dbg_h_armed = True
    else:
        ctx.dbg_h = None
    wst = WStream(ctx, "w", nbuf=3)
    prev_store = None
    for tile in range(SEQ // T):
        t0 = tile * T
        for c in range(ND):
            p.dma("sync", sb.x.chunk(c), xin[c * 128:(c + 1) * 128, t0:t0 + T],
                  deps=[prev_store] if (prev_store and c == 0) else (), strict=True)
        p.join()
        emit_cast(ctx, sb.x, sb.xb)
        p.join()
        emit_router(ctx, sb, xb=sb.xb, wr_b=sb.wr_b, rb_row_bf=sb.rb_row_bf,
                    ones_row_bf=sb.ones_row_bf, w_rows=sb.w_rows)
        if debug_w:
            p.dma("sync", dbg_w[0:7, t0:t0 + T], sb.w_rows[0:7, :])
        emit_moe_experts(ctx, sb, dram, wst, n_routed=n_routed)
        if debug_w:
            for c in range(ND):
                p.dma("sync", dbg_z[c * 128:(c + 1) * 128, t0:t0 + T], sb.x.chunk(c))
        emit_ln(ctx, sb, eps=EPS_TOP)
        for c in range(ND):
            prev_store = p.dma("sync", xout[c * 128:(c + 1) * 128, t0:t0 + T], sb.x.chunk(c))
    p._wait("gpsimd", "Ssy", p.cnt["Ssy"])
    p.build()
    return nc





def build_fused_program(T=1024):
    nc = bass.Bass("TRN2", target_bir_lowering=False, debug=False)
    p = Prog(nc)
    ctx = Ctx(nc, p, T)
    consts = host_consts()
    sb = SB(ctx, T, consts, need=("SEL7", "SEL16", "ident"), moe=True)
    xin = nc.dram_tensor("xT", [D, SEQ], F32, kind="ExternalInput").ap()
    xout = nc.dram_tensor("s_out", [D, SEQ], F32, kind="ExternalOutput").ap()
    s2 = nc.dram_tensor("s2_scr", [D, SEQ], F32, kind="Internal").ap()
    pq = nc.dram_tensor("pq_scr", [D, SEQ], BF16, kind="Internal").ap()

    dram = {}
    for i in range(2):
        dram[f"pw{i}"] = nc.dram_tensor(f"pw{i}", [D, 4 * D], F32, kind="ExternalInput").ap()
        dram[f"rw{i}"] = nc.dram_tensor(f"rw{i}", [D, D], F32, kind="ExternalInput").ap()
        dram[f"f1w{i}"] = nc.dram_tensor(f"f1w{i}", [D, 4 * D], F32, kind="ExternalInput").ap()
        dram[f"f2w{i}"] = nc.dram_tensor(f"f2w{i}", [4 * D, D], F32, kind="ExternalInput").ap()
    dram["wgT"] = nc.dram_tensor("wgT", [D, 3072], I32, kind="ExternalInput").ap()
    dram["wuT"] = nc.dram_tensor("wuT", [D, 3072], I32, kind="ExternalInput").ap()
    dram["wdT"] = nc.dram_tensor("wdT", [3072, D], I32, kind="ExternalInput").ap()
    for nm, dn in (("wqT", "l1_wq"), ("wkT", "l1_wk"), ("wvT", "l1_wv"), ("wdT_l1", "l1_wd")):
        dram[nm] = nc.dram_tensor(dn, [D, D], F32, kind="ExternalInput").ap()
    for nm, shape in (("sgT", [D, 2816]), ("suT", [D, 2816]), ("sdT", [2816, D]),
                      ("rgT", [7 * D, 2816]), ("ruT", [7 * D, 2816]),
                      ("rdT", [7 * 2816, D]), ("wrT", [D, 7]), ("rb_row", [1, 7])):
        dram[nm] = nc.dram_tensor(nm, shape, F32, kind="ExternalInput").ap()

    vecs = {"lnfg": (128, 8), "lnfb": (128, 8),
            "bq_col": (128, 8), "bd_col": (128, 8),
            "l1g": (128, 8), "l1b": (128, 8),
            "bk_row": (1, 1024), "bv_row": (1, 1024)}
    rows_only = []
    for i in range(2):
        for nm in (f"rb{i}", f"f2b{i}", f"ln1g{i}", f"ln1b{i}", f"ln2g{i}", f"ln2b{i}"):
            vecs[nm] = (128, 8)
        vecs[f"pb{i}_col"] = (128, 32)
        vecs[f"f1b{i}"] = (128, 32)
        dram[f"pbrow{i}"] = nc.dram_tensor(f"pb{i}_row", [1, 2048], F32,
                                           kind="ExternalInput").ap()
    load_vec_inputs(ctx, sb, vecs, bf_rows=("bk_row", "bv_row"))
    setup_l0(ctx, sb, consts)

    class _KCB:
        def __init__(self, ap):
            self.h = ap

        def chunk(self, c):
            return self.h[:, c * 16:(c + 1) * 16]
    sb.ks_cb = _KCB(sb.rr[:, 8192 + 3 * T + 128:8192 + 3 * T + 256])
    setup_l1(ctx, sb, {"wkT": dram["wkT"], "wvT": dram["wvT"]})
    _load_router_weights(ctx, sb, dram["wrT"], dram["rb_row"])
    sb.xs2 = sb.rr[:, 0:2 * ND * TS].bitcast(F32)
    sb.wbb = sb.rr[:, 8192:8192 + 2 * T].bitcast(F32)
    sb.w_rows = sb.rr[0:7, 8192 + 2 * T:8192 + 3 * T]
    sb.rtr = sb.rr[:, 8192 + 3 * T:8192 + 3 * T + 128].bitcast(F32)
    wst = WStream(ctx, "w", nbuf=3)
    ctx.wst_down = WStream(ctx, "wdn", bufs=[sb.krtvrt[:, 0:4096],
                                             sb.krtvrt[:, 4096:8192]])
    l1dram = {"wqT": dram["wqT"], "wkT": dram["wkT"], "wvT": dram["wvT"],
              "wdT": dram["wdT_l1"]}

    NT = SEQ // T
    for tile in range(NT):
        t0 = tile * T
        qdep = [("Ssy", p.cnt["Ssy"])] if p.cnt["Ssy"] else []
        for c in range(ND):
            p.dma("sync", sb.x.chunk(c), xin[c * 128:(c + 1) * 128, t0:t0 + T],
                  deps=qdep if c == 0 else (), strict=True)
        p.join()
        p.tag = "l0attn"
        emit_l0attn(ctx, sb, dram, wst, xin, tile)
        p.tag = "l0ffn"
        emit_l0ffn(ctx, sb, dram, wst)
        p.tag = "ln2top"
        emit_ln(ctx, sb, eps=EPS_TOP)
        for c in range(ND):
            p.dma("sync", s2[c * 128:(c + 1) * 128, t0:t0 + T], sb.x.chunk(c))
        p.tag = "l1a"
        emit_l1_phase_a(ctx, sb, l1dram, wst, pq, tile)
    p.tag = "ks"
    emit_ks_build(ctx, sb)
    for tile in range(NT):
        t0 = tile * T
        p.tag = "l1b"
        emit_l1_phase_b(ctx, sb, l1dram, wst, pq, s2, tile)
        p.tag = "moe"
        emit_cast(ctx, sb.x, sb.xb)
        p.join()
        emit_router(ctx, sb, xb=sb.xb, rb_row_bf=sb.rb_row_bf, wr_b=sb.wr_b,
                    ones_row_bf=sb.ones_row_bf, w_rows=sb.w_rows)
        emit_moe_experts(ctx, sb, dram, wst)
        p.tag = "ln4top"
        emit_ln(ctx, sb, eps=EPS_TOP)
        for c in range(ND):
            p.dma("sync", xout[c * 128:(c + 1) * 128, t0:t0 + T], sb.x.chunk(c))
        p.join()
    p._wait("gpsimd", "Ssy", p.cnt["Ssy"])
    p.build()
    return nc


def kernel(**inputs):
    if "f" not in _PROGS:
        _PROGS["f"] = build_fused_program(1024)
    nc = _PROGS["f"]
    n = 8
    packs = [host_pack(inputs, core) for core in range(n)]
    names = ["pw0", "pw1", "rw0", "rw1", "f1w0", "f1w1", "f2w0", "f2w1",
             "lnfg", "lnfb", "wgT", "wuT", "wdT",
             "l1_wq", "l1_wk", "l1_wv", "l1_wd",
             "bq_col", "bd_col", "l1g", "l1b", "bk_row", "bv_row",
             "sgT", "suT", "sdT", "rgT", "ruT", "rdT", "wrT", "rb_row"]
    for i in range(2):
        names += [f"pb{i}_col", f"pb{i}_row", f"rb{i}", f"f1b{i}", f"f2b{i}",
                  f"ln1g{i}", f"ln1b{i}", f"ln2g{i}", f"ln2b{i}"]
    in_maps = [dict(_sub(p_, names), xT=p_["xT"]) for p_ in packs]
    res = run_bass_kernel_spmd(nc, in_maps, core_ids=list(range(n)))
    outs = [np.ascontiguousarray(np.asarray(r["s_out"]).T) for r in res.results]
    return np.stack(outs, axis=0)



_PROGS = {}


def _get_progs():
    if "a" not in _PROGS:
        _PROGS["a"] = build_l0attn_program(2048)
        _PROGS["b"] = build_l0ffn_program(2048)
        _PROGS["c"] = build_l1_program(2048)
        _PROGS["d"] = build_moe_program(2048)
    return _PROGS


def _sub(m, names):
    return {k: m[k] for k in names}


def kernel_unfused(**inputs):
    progs = _get_progs()
    n = 8
    packs = [host_pack(inputs, core) for core in range(n)]
    cores = list(range(n))

    a_names = ["pw0", "pw1", "rw0", "rw1", "f1w0", "f1w1", "f2w0", "f2w1",
               "lnfg", "lnfb"]
    for i in range(2):
        a_names += [f"pb{i}_col", f"pb{i}_row", f"rb{i}", f"f1b{i}", f"f2b{i}",
                    f"ln1g{i}", f"ln1b{i}", f"ln2g{i}", f"ln2b{i}"]
    in_maps = [dict(_sub(p_, a_names), s_in=p_["xT"]) for p_ in packs]
    res = run_bass_kernel_spmd(progs["a"], in_maps, core_ids=cores)
    s1 = [np.asarray(r["s_out"]) for r in res.results]

    b_names = ["wgT", "wuT", "wdT"]
    in_maps = [dict(_sub(p_, b_names), s_in=s) for p_, s in zip(packs, s1)]
    res = run_bass_kernel_spmd(progs["b"], in_maps, core_ids=cores)
    s2 = [np.asarray(r["s_out"]) for r in res.results]

    c_names = ["l1_wq", "l1_wk", "l1_wv", "l1_wd", "bq_col", "bd_col",
               "l1g", "l1b", "bk_row", "bv_row"]
    in_maps = [dict(_sub(p_, c_names), s_in=s) for p_, s in zip(packs, s2)]
    res = run_bass_kernel_spmd(progs["c"], in_maps, core_ids=cores)
    s3 = [np.asarray(r["s_out"]) for r in res.results]

    d_names = ["sgT", "suT", "sdT", "rgT", "ruT", "rdT", "wrT", "rb_row"]
    in_maps = [dict(_sub(p_, d_names), s_in=s) for p_, s in zip(packs, s3)]
    res = run_bass_kernel_spmd(progs["d"], in_maps, core_ids=cores)
    outs = [np.ascontiguousarray(np.asarray(r["s_out"]).T) for r in res.results]
    return np.stack(outs, axis=0)


# version 06
# speedup vs baseline: 1.2598x; 1.0417x over previous
import numpy as np
import ml_dtypes
from contextlib import ExitStack

import concourse.bass as bass
import concourse.mybir as mybir
from concourse.bass_utils import run_bass_kernel_spmd

F32 = mybir.dt.float32
BF16 = mybir.dt.bfloat16
I32 = mybir.dt.int32
AF = mybir.ActivationFunctionType
ALU = mybir.AluOpType
AX = mybir.AxisListType

D = 1024
SEQ = 4096
NH = 16
HD = 64
CH = 128
ND = D // 128
TS = 512
EPS_TOP = 1e-5
EPS_L1LN = 1e-12
EPS_ATTN = 1e-6
Q127 = 1.0 / 127.0

ENGINES = ("sync", "scalar", "tensor", "vector", "gpsimd")
ESEM = {"sync": "Ssy", "scalar": "Sac", "tensor": "Spe", "vector": "Sdv", "gpsimd": "Sgp"}



class Prog:
    def __init__(self, nc):
        self.nc = nc
        self.ops = {e: [] for e in ENGINES}
        self.cnt = {s: 0 for s in ESEM.values()}
        self.sem_handles = {}
        self.last_wait = {e: {} for e in ENGINES}
        self.tag = ""

    def _wait(self, engine, s, v):
        lw = self.last_wait[engine]
        if lw.get(s, -1) >= v:
            return
        lw[s] = v

        def w(eng, s=s, v=v):
            eng.wait_ge(self.sem_handles[s], v)
        self.ops[engine].append(w)

    def op(self, engine, fn, deps=(), inc=1, strict=False):
        best = {}
        for t in deps:
            if t is None:
                continue
            s, v = t
            if best.get(s, -1) < v:
                best[s] = v
        own = ESEM[engine]
        for s, v in sorted(best.items()):
            if s != own or strict:
                self._wait(engine, s, v)
        tag = self.tag
        if inc:
            s = own
            self.cnt[s] += inc
            tok = (s, self.cnt[s])

            def wrapped(eng, fn=fn, s=s, inc=inc, tag=tag):
                inst = fn(eng)
                if tag:
                    inst.annotate(tag)
                inst.then_inc(self.sem_handles[s], inc)
            self.ops[engine].append(wrapped)
            return tok

        def plain(eng, fn=fn, tag=tag):
            inst = fn(eng)
            if tag:
                inst.annotate(tag)
        self.ops[engine].append(plain)
        return None

    def sop(self, engine, fn, deps=()):
        tok = self.op(engine, fn, deps=deps, inc=1)
        self._wait(engine, tok[0], tok[1])
        return tok

    def dma(self, engine, out, in_, deps=(), strict=False):
        return self.op(engine, lambda eng, o=out, i=in_: eng.dma_start(out=o, in_=i),
                       deps=deps, inc=16, strict=strict)

    def now(self):
        return [(s, self.cnt[s]) for s in self.cnt if self.cnt[s] > 0]

    def join(self, engines=ENGINES):
        toks = self.now()
        for e in engines:
            for s, v in toks:
                if s != ESEM[e]:
                    self._wait(e, s, v)

    def build(self):
        st = ExitStack()
        for s in ESEM.values():
            self.sem_handles[s] = st.enter_context(self.nc.semaphore(s))
        block = st.enter_context(self.nc.Block())
        for e in ENGINES:
            lst = self.ops[e]
            if not lst:
                continue

            def run(eng, lst=lst):
                for fn in lst:
                    fn(eng)
            getattr(block, e)(run)
        st.close()


def merge(*toks):
    out = []
    for t in toks:
        if t is None:
            continue
        if isinstance(t, (list, tuple)) and (len(t) == 0 or not isinstance(t[0], str)):
            out.extend(x for x in t if x is not None)
        else:
            out.append(t)
    return out


class CBuf:
    def __init__(self, nc, name, n, w, dtype):
        self.h = nc.alloc_sbuf_tensor(name, [128, n * w], dtype)
        self.n = n
        self.w = w

    def sl(self, c, a, b):
        return self.h[:, c * self.w + a: c * self.w + b]

    def chunk(self, c):
        return self.h[:, c * self.w:(c + 1) * self.w]


class Ctx:
    def __init__(self, nc, p, T):
        self.nc = nc
        self.p = p
        self.T = T
        self.NS = T // TS
        self.ps = nc.alloc_psum_tensor("ps", [128, 4096], F32)
        self.bank_ready = [None] * 8
        self.bank_free = [None] * 8
        self.ring_i = 0

    def bank(self, b, p0=0, p1=128, w=TS):
        return self.ps[p0:p1, b * TS: b * TS + w]

    def ring(self):
        banks = getattr(self, "ring_banks", None) or list(range(8))
        b = banks[self.ring_i % len(banks)]
        self.ring_i += 1
        return b


def acc_group(ctx, bank, mms, deps=(), p0=0, p1=128, w=TS, col0=0):
    p = ctx.p
    out_ap = ctx.ps[p0:p1, bank * TS + col0: bank * TS + col0 + w]
    n = len(mms)
    tok = None
    for i, (lhsT, rhs) in enumerate(mms):
        d = list(deps) if i == 0 else []
        if i == 0 and ctx.bank_free[bank] is not None:
            d.append(ctx.bank_free[bank])

        def fn(eng, lhsT=lhsT, rhs=rhs, i=i, out_ap=out_ap):
            return eng.matmul(out_ap, lhsT, rhs, start=(i == 0), stop=(i == n - 1))
        tok = p.op("tensor", fn, deps=d, inc=(1 if i == n - 1 else 0))
    ctx.bank_ready[bank] = tok
    ctx.bank_free[bank] = None
    return tok


def bank_read(ctx, bank, tok):
    ctx.bank_free[bank] = tok


class WStream:

    def __init__(self, ctx, name, nbuf=4, bufs=None):
        self.ctx = ctx
        if bufs is None:
            self.bufs = [CBuf(ctx.nc, f"{name}{i}", 8, 512, BF16).h[:, :]
                         for i in range(nbuf)]
        else:
            self.bufs = bufs
        self.nbuf = len(self.bufs)
        self.busy = [None] * self.nbuf
        self.i = 0

    def load(self, dram, r0, nk, c0, w):
        assert nk * w <= 4096
        p = self.ctx.p
        idx = self.i % self.nbuf
        self.i += 1
        buf = self.bufs[idx]
        dep = [self.busy[idx]] if self.busy[idx] is not None else []
        src3 = dram[r0:r0 + nk * 128, c0:c0 + w].rearrange("(c p) w -> p c w", p=128)
        dst3 = buf[:, 0:nk * w].rearrange("p (c w) -> p c w", c=nk)
        tok = p.dma("gpsimd", dst3, src3, deps=dep)

        def sl(c, a, b, buf=buf, w=w):
            return buf[:, c * w + a: c * w + b]
        return idx, sl, tok

    def done(self, idx, tok):
        prev = self.busy[idx]
        if prev is None or tok[1] > prev[1]:
            self.busy[idx] = tok



def _col_pack(v):
    v = np.asarray(v, dtype=np.float32)
    n = v.shape[0] // 128
    return np.ascontiguousarray(v.reshape(n, 128).T)


def _t(w):
    return np.ascontiguousarray(np.asarray(w).T)


def _row(v):
    return np.ascontiguousarray(np.asarray(v, np.float32).reshape(1, -1))


def host_pack(inputs, core):
    m = {}
    x = np.asarray(inputs["L0_attn_x"][core])
    m["xT"] = _t(x)
    for i in range(2):
        m[f"pw{i}"] = np.ascontiguousarray(np.asarray(inputs["L0_attn_proj_w"][i]))
        m[f"pb{i}_col"] = _col_pack(inputs["L0_attn_proj_b"][i])
        m[f"pb{i}_row"] = np.ascontiguousarray(
            np.asarray(inputs["L0_attn_proj_b"][i], np.float32)[1024:3072].reshape(1, -1))
        m[f"rw{i}"] = np.ascontiguousarray(np.asarray(inputs["L0_attn_reto_w"][i]))
        m[f"rb{i}"] = _col_pack(inputs["L0_attn_reto_b"][i])
        m[f"f1w{i}"] = np.ascontiguousarray(np.asarray(inputs["L0_attn_ffn1_w"][i]))
        m[f"f1b{i}"] = _col_pack(inputs["L0_attn_ffn1_b"][i])
        m[f"f2w{i}"] = np.ascontiguousarray(np.asarray(inputs["L0_attn_ffn2_w"][i]))
        m[f"f2b{i}"] = _col_pack(inputs["L0_attn_ffn2_b"][i])
        m[f"ln1g{i}"] = _col_pack(inputs["L0_attn_ln1_g"][i])
        m[f"ln1b{i}"] = _col_pack(inputs["L0_attn_ln1_b"][i])
        m[f"ln2g{i}"] = _col_pack(inputs["L0_attn_ln2_g"][i])
        m[f"ln2b{i}"] = _col_pack(inputs["L0_attn_ln2_b"][i])
    m["lnfg"] = _col_pack(inputs["L0_attn_lnf_g"])
    m["lnfb"] = _col_pack(inputs["L0_attn_lnf_b"])
    m["wgT"] = _t(inputs["L0_ffn_w_gate"])
    m["wuT"] = _t(inputs["L0_ffn_w_up"])
    m["wdT"] = _t(inputs["L0_ffn_w_down"])
    for nm, key in (("l1_wq", "L1_attn_Wq"), ("l1_wk", "L1_attn_Wk"),
                    ("l1_wv", "L1_attn_Wv"), ("l1_wd", "L1_attn_Wd")):
        m[nm] = _t(inputs[key])
    m["bq_col"] = _col_pack(inputs["L1_attn_bq"])
    m["bk_row"] = _row(inputs["L1_attn_bk"])
    m["bv_row"] = _row(inputs["L1_attn_bv"])
    m["bd_col"] = _col_pack(inputs["L1_attn_bd"])
    m["l1g"] = _col_pack(inputs["L1_attn_gamma"])
    m["l1b"] = _col_pack(inputs["L1_attn_beta"])
    m["sgT"] = _t(inputs["L1_ffn_sg"])
    m["suT"] = _t(inputs["L1_ffn_su"])
    m["sdT"] = _t(inputs["L1_ffn_sd"])
    m["rgT"] = np.ascontiguousarray(np.asarray(inputs["L1_ffn_rg"]).transpose(0, 2, 1).reshape(7 * D, 2816))
    m["ruT"] = np.ascontiguousarray(np.asarray(inputs["L1_ffn_ru"]).transpose(0, 2, 1).reshape(7 * D, 2816))
    m["rdT"] = np.ascontiguousarray(np.asarray(inputs["L1_ffn_rd"]).transpose(0, 2, 1).reshape(7 * 2816, D))
    m["wrT"] = _t(inputs["L1_ffn_wr"])
    m["rb_row"] = _row(inputs["L1_ffn_rb"])
    return m


def host_consts():
    c = {}
    gamma = (1.0 - 2.0 ** (-5.0 - np.arange(NH, dtype=np.float64)))
    j = np.arange(CH, dtype=np.float64)
    dts = np.zeros((NH, CH, CH), np.float32)
    for h in range(NH):
        mask = (j[None, :] - j[:, None]) >= 0
        vals = (gamma[h] ** (-(j + 1.0)))[:, None]
        dts[h] = np.where(mask, vals, 0.0).astype(np.float32)
    c["DT"] = dts.reshape(NH * CH, CH)
    qd = np.zeros((ND, 128, CH), np.float32)
    for cc in range(ND):
        for half in range(2):
            h = 2 * cc + half
            qd[cc, half * 64:(half + 1) * 64, :] = (gamma[h] ** (j + 1.0)).astype(np.float32)[None, :]
    c["QD"] = qd.reshape(ND * 128, CH)
    kd = np.zeros((2, CH, 512), np.float32)
    for s in range(2):
        for hh in range(8):
            h = s * 8 + hh
            colv = (gamma[h] ** (CH - 1.0 - j)).astype(np.float32)
            kd[s, :, hh * 64:(hh + 1) * 64] = colv[:, None]
    c["KD"] = kd.reshape(2 * CH, 512)
    gc = np.zeros((128, 512), np.float32)
    for h in range(NH):
        gc[(h % 2) * 64:(h % 2) * 64 + 64, (h // 2) * 64:(h // 2) * 64 + 64] = np.float32(gamma[h] ** CH)
    c["GC"] = gc
    c["ones1"] = np.ones((1, 128), np.float32)
    c["onesc"] = np.ones((128, 1), np.float32)
    c["ones_d"] = np.full((1, 128), 1.0 / D, np.float32)
    c["ones_sd"] = np.full((1, 128), float(np.sqrt(D)), np.float32)
    sel7 = np.zeros((7, 7 * 128), np.float32)
    for e in range(7):
        sel7[e, e * 128:(e + 1) * 128] = 1.0
    c["SEL7"] = sel7
    sel16 = np.zeros((16, 8 * 128), np.float32)
    for cc in range(8):
        sel16[2 * cc, cc * 128: cc * 128 + 64] = 1.0
        sel16[2 * cc + 1, cc * 128 + 64: cc * 128 + 128] = 1.0
    c["SEL16"] = sel16
    c["SEL2"] = np.zeros((2, 128), np.float32)
    c["SEL2"][0, 0:64] = 1.0
    c["SEL2"][1, 64:128] = 1.0
    c["ident"] = np.eye(128, dtype=np.float32)
    return c



class SB:
    def __init__(self, ctx, T, consts, need=(), moe=False):
        nc, p = ctx.nc, ctx.p
        self.T = T
        self.x = CBuf(nc, "x", ND, T, F32)
        self.xb = CBuf(nc, "xb", ND, T, BF16)
        self.h = CBuf(nc, "h", ND, T, BF16)
        self.rows = nc.alloc_sbuf_tensor("rows", [1, 2 * T], F32)
        self.t1 = [nc.alloc_sbuf_tensor(f"t1_{i}", [128, TS], F32) for i in range(2)]
        self.t1_busy = [None, None]
        self.const = {}
        bf_consts = getattr(self, "_bf_consts", ("SEL7",))
        for name in set(("ones_d", "ones_sd") + tuple(need)):
            arr = consts[name]
            dr = nc.inline_tensor(np.ascontiguousarray(arr), "ic_" + name)
            if name in bf_consts:
                hh = nc.alloc_sbuf_tensor("cst_" + name, list(arr.shape), BF16)
                p.dma("gpsimd", hh[:, :], dr.ap())
            else:
                hh = nc.alloc_sbuf_tensor("cst_" + name, list(arr.shape), F32)
                p.dma("sync", hh[:, :], dr.ap())
            self.const[name] = hh
        onesb = nc.inline_tensor(np.ones((128, 1), np.float32), "ic_ones_b")
        self.ones_bf = nc.alloc_sbuf_tensor("cst_ones_bf", [128, 1], BF16)
        p.dma("gpsimd", self.ones_bf[:, :], onesb.ap())
        onesr = nc.inline_tensor(np.ones((1, 128), np.float32), "ic_ones_r")
        self.ones_row_bf = nc.alloc_sbuf_tensor("cst_ones_r_bf", [1, 128], BF16)
        p.dma("gpsimd", self.ones_row_bf[:, :], onesr.ap())
        self.vec = {}
        self.vecrow = {}
        if moe:
            self.rtr = None
            self.rtr_busy = [None, None]
            self.w_rows = None
            self.w_rows_tok = None
            self.wbb = None
            self.wr_b = None
            self.wr_tok = None
            self.rb_row_bf = None

    def h_f32(self, i):
        return self.h.h[:, 2 * i * self.T:(2 * i + 2) * self.T].bitcast(F32)


def load_vec_inputs(ctx, sb, names_shapes, bf_rows=(), bf_only_rows=()):
    nc, p = ctx.nc, ctx.p
    for name, shape in names_shapes.items():
        d = nc.dram_tensor(name, list(shape), F32, kind="ExternalInput").ap()
        if name not in bf_only_rows:
            h = nc.alloc_sbuf_tensor("v_" + name, list(shape), F32)
            p.dma("sync", h[:, :], d[:, :])
            sb.vec[name] = h
        if name in bf_rows or name in bf_only_rows:
            hb = nc.alloc_sbuf_tensor("vb_" + name, list(shape), BF16)
            p.dma("gpsimd", hb[:, :], d[:, :])
            sb.vecrow[name] = hb



def emit_ln(ctx, sb, *, g=None, b=None, eps, out_bf=None, add_f32=None,
            bf_only=False, x_ready=None, name=""):
    p = ctx.p
    T = ctx.T
    NSl = ctx.NS
    x = sb.x
    if x_ready is not None:
        scr = [(sb.rr[:, (2 * i) * T:(2 * i + 1) * T],
                sb.rr[:, (2 * i + 1) * T:(2 * i + 2) * T]) for i in range(4)]
    else:
        scr = [(sb.h.chunk(2 * i), sb.h.chunk(2 * i + 1)) for i in range(4)]
    onesb = sb.ones_bf

    prev_mm = {}
    for c in range(ND):
        xbs, xsq = scr[c % 4]
        dep = merge([prev_mm.get(c - 4)] +
                    (list(x_ready[c]) if x_ready is not None else []))
        p.op("scalar", lambda eng, xbs=xbs, c=c: eng.activation(xbs, x.chunk(c), AF.Copy),
             deps=dep, inc=0)
        ct = p.op("scalar", lambda eng, xsq=xsq, xbs=xbs: eng.activation(xsq, xbs, AF.Square))
        last = None
        for t in range(NSl):
            for kind in range(2):
                bank = kind * NSl + t
                src = (xbs if kind == 0 else xsq)[:, t * TS:(t + 1) * TS]
                d = [ct]
                if c == 0 and ctx.bank_free[bank] is not None:
                    d.append(ctx.bank_free[bank])
                last = p.op("tensor", lambda eng, bank=bank, src=src, c=c:
                            eng.matmul(ctx.bank(bank, 0, 1), onesb[:, 0:1], src,
                                       start=(c == 0), stop=(c == ND - 1)),
                            deps=d, inc=(1 if (kind == 1 and t == NSl - 1) else 0))
        prev_mm[c] = last
    mm_done = prev_mm[ND - 1]

    mu = sb.rows[0:1, 0:T]
    rstd = sb.rows[0:1, T:2 * T]
    ps_sum = ctx.ps[0:1, 0:NSl * TS]
    ps_sq = ctx.ps[0:1, NSl * TS:2 * NSl * TS]
    p.op("vector", lambda eng: eng.tensor_copy(mu, ps_sum), deps=[mm_done], inc=0)
    p.op("vector", lambda eng: eng.tensor_copy(rstd, ps_sq), inc=0)
    p.op("vector", lambda eng: eng.tensor_tensor(ps_sum, mu, mu, ALU.mult), inc=0)
    var_tok = p.op("vector", lambda eng: eng.scalar_tensor_tensor(
        rstd, ps_sum, -1.0 / D, rstd, ALU.mult, ALU.add))
    p.op("vector", lambda eng: eng.tensor_scalar_add(rstd, rstd, float(D * eps)), inc=0)
    sq_tok = p.op("scalar", lambda eng: eng.activation(rstd, rstd, AF.Sqrt),
                  deps=[var_tok])
    rstd_tok = p.op("vector", lambda eng: eng.reciprocal(rstd, rstd), deps=[sq_tok])
    for i in range(2 * NSl):
        bank_read(ctx, i, rstd_tok)

    ones_d = sb.const["ones_d"]
    ones_sd = sb.const["ones_sd"]
    for t in range(NSl):
        bmu, brs = 0, 1
        p.op("tensor", lambda eng, t=t: eng.matmul(
            ctx.bank(bmu), ones_d[0:1, :], mu[:, t * TS:(t + 1) * TS], start=True, stop=True),
            deps=merge([rstd_tok, ctx.bank_free[bmu]]), inc=0)
        bc_tok = p.op("tensor", lambda eng, t=t: eng.matmul(
            ctx.bank(brs), ones_sd[0:1, :], rstd[:, t * TS:(t + 1) * TS], start=True, stop=True),
            deps=merge([ctx.bank_free[brs]]))
        last = bc_tok
        for c in range(ND):
            tt = sb.t1[c % 2]
            p.op("vector", lambda eng, c=c, t=t, tt=tt: eng.tensor_tensor(
                tt[:], x.sl(c, t * TS, (t + 1) * TS), ctx.bank(bmu), ALU.subtract),
                deps=merge([bc_tok, sb.t1_busy[c % 2]]), inc=0)
            dve_tok = p.op("vector", lambda eng, tt=tt: eng.tensor_tensor(
                tt[:], tt[:], ctx.bank(brs), ALU.mult))
            dst = x.sl(c, t * TS, (t + 1) * TS)
            if bf_only:
                a_tok = p.op("vector", lambda eng, c=c, tt=tt: eng.tensor_scalar(
                    tt[:], tt[:], g[:, c:c + 1], b[:, c:c + 1], ALU.mult, ALU.add),
                    deps=[dve_tok])
                w_tok = p.op("scalar", lambda eng, c=c, t=t, tt=tt: eng.activation(
                    out_bf.sl(c, t * TS, (t + 1) * TS), tt[:], AF.Copy), deps=[a_tok])
                sb.t1_busy[c % 2] = w_tok
                last = dve_tok
                continue
            if g is not None:
                if add_f32 is not None:
                    add_ap, add_tok = add_f32(c, t)
                    a_tok = p.op("vector", lambda eng, c=c, tt=tt: eng.tensor_scalar(
                        tt[:], tt[:], g[:, c:c + 1], b[:, c:c + 1], ALU.mult, ALU.add),
                        deps=[dve_tok])
                    w_tok = p.op("vector", lambda eng, tt=tt, dst=dst, a=add_ap:
                                 eng.tensor_tensor(dst, tt[:], a, ALU.add),
                                 deps=merge([a_tok, add_tok]))
                else:
                    w_tok = p.op("vector", lambda eng, c=c, tt=tt, dst=dst: eng.tensor_scalar(
                        dst, tt[:], g[:, c:c + 1], b[:, c:c + 1], ALU.mult, ALU.add),
                        deps=[dve_tok])
            else:
                assert add_f32 is None
                w_tok = p.op("scalar", lambda eng, tt=tt, dst=dst: eng.activation(
                    dst, tt[:], AF.Copy), deps=[dve_tok])
            sb.t1_busy[c % 2] = w_tok
            if out_bf is not None:
                p.op("scalar", lambda eng, c=c, t=t, dst=dst: eng.activation(
                    out_bf.sl(c, t * TS, (t + 1) * TS), dst, AF.Copy), deps=[w_tok])
            last = dve_tok
        bank_read(ctx, bmu, last)
        bank_read(ctx, brs, last)
    p.join()


def emit_cast(ctx, src_cb, dst_cb, scale=None):
    p = ctx.p
    for c in range(ND):
        if scale is None:
            p.op("scalar", lambda eng, c=c: eng.activation(
                dst_cb.chunk(c), src_cb.chunk(c), AF.Copy))
        else:
            p.op("scalar", lambda eng, c=c: eng.activation(
                dst_cb.chunk(c), src_cb.chunk(c), AF.Copy, scale=scale))



def emit_gated_ffn(ctx, sb, *, xb, wg, wu, wd, HID, wst, act=AF.Silu,
                   up_scale=None, wb_ap=None, first_bias=None, wst_down=None):
    p = ctx.p
    T = ctx.T
    NSl = ctx.NS
    NHC = HID // 128
    h = sb.h
    fin = [[] for _ in range(ND)]
    for g0 in range(0, NHC, 8):
        ng = min(8, NHC - g0)
        gl = {}
        ul = {}
        for gi in range(ng):
            hc = g0 + gi
            half = gi // 4
            if half not in gl:
                nw = min(4, ng - half * 4) * 128
                gl[half] = wst.load(wg, 0, ND, (g0 + half * 4) * 128, nw)
                ul[half] = wst.load(wu, 0, ND, (g0 + half * 4) * 128, nw)
            slot, wsl, wtok = gl[half]
            uslot, usl, utok = ul[half]
            gcol = (gi % 4) * 128
            bg = 4 + (gi % 2)
            bu = 6 + (gi % 2)

            u_toks = []
            for t in range(NSl):
                gt = acc_group(ctx, bg, [(wsl(c, gcol, gcol + 128),
                                          xb.sl(c, t * TS, (t + 1) * TS)) for c in range(ND)],
                               deps=[wtok], col0=0)
                e1 = p.op("scalar", lambda eng, gi=gi, t=t, bg=bg: eng.activation(
                    h.sl(gi, t * TS, (t + 1) * TS), ctx.bank(bg), act), deps=[gt])
                bank_read(ctx, bg, e1)
                wst.done(slot, gt)
                ut = acc_group(ctx, bu, [(usl(c, gcol, gcol + 128),
                                          xb.sl(c, t * TS, (t + 1) * TS)) for c in range(ND)],
                               deps=[utok], col0=0)
                wst.done(uslot, ut)
                tt = sb.t1[t % 2]
                if wb_ap is not None:
                    e2 = p.op("vector", lambda eng, t=t, bu=bu, tt=tt: eng.tensor_tensor(
                        tt[:], ctx.bank(bu), wb_ap[:, t * TS:(t + 1) * TS], ALU.mult),
                        deps=merge([ut, sb.t1_busy[t % 2]]))
                else:
                    sc = {} if up_scale is None else {"scale": up_scale}
                    e2 = p.op("scalar", lambda eng, t=t, bu=bu, tt=tt, sc=sc: eng.activation(
                        tt[:], ctx.bank(bu), AF.Copy, **sc),
                        deps=merge([ut, sb.t1_busy[t % 2]]))
                bank_read(ctx, bu, e2)
                m_tok = p.op("vector", lambda eng, gi=gi, t=t, tt=tt: eng.tensor_tensor(
                    h.sl(gi, t * TS, (t + 1) * TS), h.sl(gi, t * TS, (t + 1) * TS),
                    tt[:], ALU.mult), deps=merge([e1, e2]))
                sb.t1_busy[t % 2] = m_tok
        if getattr(ctx, "dbg_h", None) is not None and g0 == 0 and ctx.dbg_h_armed:
            p.join()
            for gi in range(ng):
                p.dma("sync", ctx.dbg_h[gi * 128:(gi + 1) * 128, 0:T], h.chunk(gi))
            p.join()
            ctx.dbg_h_armed = False
        wd_st = wst_down if wst_down is not None else wst
        dl = []
        for half in range(0, ng, 4):
            nked = min(4, ng - half)
            dl.append(wd_st.load(wd, (g0 + half) * 128, nked, 0, 1024))
        for f2 in range(ND):
            for t in range(NSl):
                bank = t if (f2 % 2 == 0) else 4 + t
                mms = []
                dep0 = [dl[-1][2], m_tok]
                for gi in range(ng):
                    dslot, dsl, dtok = dl[gi // 4]
                    mms.append((dsl(gi % 4, f2 * 128, (f2 + 1) * 128),
                                h.sl(gi, t * TS, (t + 1) * TS)))
                gt = acc_group(ctx, bank, mms, deps=dep0)
                for dslot, dsl, dtok in dl:
                    wd_st.done(dslot, gt)
                if first_bias is not None and g0 == 0:
                    ev = p.op("vector", lambda eng, f2=f2, t=t, bank=bank:
                              eng.scalar_tensor_tensor(
                                  sb.x.sl(f2, t * TS, (t + 1) * TS), ctx.bank(bank),
                                  first_bias[:, f2:f2 + 1],
                                  sb.x.sl(f2, t * TS, (t + 1) * TS), ALU.add, ALU.add),
                              deps=[gt])
                else:
                    ev = p.op("vector", lambda eng, f2=f2, t=t, bank=bank:
                              eng.tensor_tensor(
                                  sb.x.sl(f2, t * TS, (t + 1) * TS),
                                  sb.x.sl(f2, t * TS, (t + 1) * TS),
                                  ctx.bank(bank), ALU.add),
                              deps=[gt])
                bank_read(ctx, bank, ev)
                fin[f2] = [ev]
    return fin



def proj_fm(ctx, *, wdram, wr0, nk, xb, fch_list, wst, evac, extra=None,
            t_list=None, ldeps=()):
    p = ctx.p
    NSl = ctx.NS if t_list is None else None
    ts = range(ctx.NS) if t_list is None else t_list
    i = 0
    for fb0 in range(0, len(fch_list), 4):
        fgrp = fch_list[fb0:fb0 + 4]
        slot, wsl, wtok = wst.load(wdram, wr0, nk, fgrp[0] * 128, len(fgrp) * 128)
        for mi, f in enumerate(fgrp):
            for t in ts:
                bank = ctx.ring()
                mms = [(wsl(c, mi * 128, (mi + 1) * 128),
                        xb.sl(c, t * TS, (t + 1) * TS)) for c in range(nk)]
                if extra is not None:
                    mms = mms + extra(f, t)
                gt = acc_group(ctx, bank, mms, deps=merge([wtok], list(ldeps)))
                wst.done(slot, gt)
                evac(f, t, bank, gt)
                i += 1



def emit_l0ffn(ctx, sb, dram, wst):
    p = ctx.p
    emit_cast(ctx, sb.x, sb.xb, scale=Q127)
    p.join()
    return emit_gated_ffn(ctx, sb, xb=sb.xb, wg=dram["wgT"], wu=dram["wuT"],
                          wd=dram["wdT"], HID=3072, wst=wst, act=AF.Silu,
                          up_scale=Q127, wst_down=getattr(ctx, "wst_down", None))





def emit_router(ctx, sb, *, xb, wr_b, rb_row_bf, ones_row_bf, w_rows):
    p = ctx.p
    T = ctx.T
    NTC = T // 128
    nc = ctx.nc
    rtr = sb.rtr
    rbusy = sb.rtr_busy
    identf = sb.const["ident"]
    for tc in range(NTC):
        sl = (tc % 2) * 32
        bank = ctx.ring()
        mms = [(xb.sl(c, tc * 128, (tc + 1) * 128), wr_b[:, c * 8:c * 8 + 7])
               for c in range(ND)]
        mms.append((ones_row_bf[0:1, :], rb_row_bf[0:1, 0:7]))
        gt = acc_group(ctx, bank, mms, deps=merge([sb.wr_tok, rbusy[tc % 2]]), w=7)
        pb = ctx.ps[0:128, bank * TS: bank * TS + 7]
        pm = rtr[:, sl + 24:sl + 25]
        pmn = rtr[:, sl + 25:sl + 26]
        ssum = rtr[:, sl + 26:sl + 27]
        rec = rtr[:, sl + 27:sl + 28]
        m1 = rtr[:, sl + 28:sl + 29]
        m2 = rtr[:, sl + 29:sl + 30]
        s12 = rtr[:, sl + 30:sl + 31]
        rec2 = rtr[:, sl + 31:sl + 32]
        pe_ = rtr[:, sl + 0:sl + 7]
        msk = rtr[:, sl + 8:sl + 15]
        tmp7 = rtr[:, sl + 16:sl + 23]
        p.sop("vector", lambda eng, pb=pb, pm=pm: eng.reduce_max(pm, pb, AX.X),
              deps=[gt])
        v1 = p.sop("vector", lambda eng, pm=pm, pmn=pmn: eng.tensor_scalar_mul(pmn, pm, -1.0))
        et = p.op("scalar", lambda eng, pb=pb, pe_=pe_, pmn=pmn: eng.activation(
            pe_, pb, AF.Exp, bias=pmn), deps=[v1])
        bank_read(ctx, bank, et)
        p.sop("vector", lambda eng, pe_=pe_, ssum=ssum: eng.reduce_sum(ssum, pe_, AX.X),
              deps=[et])
        p.sop("vector", lambda eng, ssum=ssum, rec=rec: eng.reciprocal(rec, ssum))
        p.sop("vector", lambda eng, pe_=pe_, rec=rec: eng.tensor_scalar_mul(pe_, pe_, rec))
        p.sop("vector", lambda eng, pe_=pe_, m1=m1: eng.reduce_max(m1, pe_, AX.X))
        p.sop("vector", lambda eng, pe_=pe_, m1=m1, tmp7=tmp7: eng.tensor_single_scalar(
            tmp7, pe_, m1, ALU.is_ge))
        p.sop("vector", lambda eng, tmp7=tmp7: eng.tensor_scalar_mul(tmp7, tmp7, 2.0))
        p.sop("vector", lambda eng, pe_=pe_, tmp7=tmp7, msk=msk: eng.tensor_tensor(
            msk, pe_, tmp7, ALU.subtract))
        p.sop("vector", lambda eng, msk=msk, m2=m2: eng.reduce_max(m2, msk, AX.X))
        p.sop("vector", lambda eng, m1=m1, m2=m2, s12=s12: eng.tensor_tensor(
            s12, m1, m2, ALU.add))
        p.sop("vector", lambda eng, s12=s12, rec2=rec2: eng.reciprocal(rec2, s12))
        p.sop("vector", lambda eng, pe_=pe_, m2=m2, tmp7=tmp7: eng.tensor_single_scalar(
            tmp7, pe_, m2, ALU.is_ge))
        p.sop("vector", lambda eng, pe_=pe_, tmp7=tmp7: eng.tensor_tensor(
            pe_, pe_, tmp7, ALU.mult))
        wt = p.sop("vector", lambda eng, pe_=pe_, rec2=rec2: eng.tensor_scalar_mul(
            pe_, pe_, rec2))
        bank2 = ctx.ring()
        pt = p.op("tensor", lambda eng, pe_=pe_, bank2=bank2: eng.transpose(
            ctx.ps[0:7, bank2 * TS: bank2 * TS + 128], pe_, identf[:, :]),
            deps=merge([wt, ctx.bank_free[bank2]]))
        ctx.bank_ready[bank2] = pt
        ctx.bank_free[bank2] = None
        cb = p.op("vector", lambda eng, tc=tc, bank2=bank2: eng.tensor_copy(
            w_rows[0:7, tc * 128:(tc + 1) * 128],
            ctx.ps[0:7, bank2 * TS: bank2 * TS + 128]), deps=[pt])
        bank_read(ctx, bank2, cb)
        rbusy[tc % 2] = cb
    p.join()


def emit_moe_experts(ctx, sb, dram, wst, n_routed=7):
    p = ctx.p
    emit_gated_ffn(ctx, sb, xb=sb.xb, wg=dram["sgT"], wu=dram["suT"],
                   wd=dram["sdT"], HID=2816, wst=wst, act=AF.Silu,
                   wst_down=getattr(ctx, "wst_down", None))
    sel7 = sb.const["SEL7"]
    fin = None
    for e in range(n_routed):
        for t in range(ctx.NS):
            bank = ctx.ring()
            bt = acc_group(ctx, bank, [(sel7[0:7, e * 128:(e + 1) * 128],
                                        sb.w_rows[0:7, t * TS:(t + 1) * TS])],
                           deps=[sb.w_rows_tok] if sb.w_rows_tok else ())
            ct = p.op("vector", lambda eng, t=t, bank=bank: eng.tensor_copy(
                sb.wbb[:, t * TS:(t + 1) * TS], ctx.bank(bank)), deps=[bt])
            bank_read(ctx, bank, ct)
        p.join()
        fin = emit_gated_ffn(ctx, sb, xb=sb.xb,
                             wg=dram["rgT"][e * D:(e + 1) * D, :],
                             wu=dram["ruT"][e * D:(e + 1) * D, :],
                             wd=dram["rdT"][e * 2816:(e + 1) * 2816, :],
                             HID=2816, wst=wst, act=AF.Silu, wb_ap=sb.wbb[:, :],
                             wst_down=getattr(ctx, "wst_down", None))
    if fin is None:
        p.join()
    return fin




def emit_phi_evac(ctx, sb, dst_ap, bank, gtok, bias_ap, neg_bias_ap):
    p = ctx.p
    i = sb.phi_i % 2
    sb.phi_i += 1
    sA = sb.phi_s[3 * i]
    sB = sb.phi_s[3 * i + 1]
    sC = sb.phi_s[3 * i + 2]
    kw1 = {} if bias_ap is None else {"bias": bias_ap}
    kw2 = {} if neg_bias_ap is None else {"bias": neg_bias_ap}
    r1 = p.op("scalar", lambda eng, kw1=kw1: eng.activation(sA, ctx.bank(bank), AF.Relu, **kw1),
              deps=merge([gtok, sb.phi_busy[i]]))
    r2 = p.op("scalar", lambda eng, kw2=kw2: eng.activation(sB, ctx.bank(bank), AF.Relu,
                                                            scale=-1.0, **kw2))
    e2 = p.op("scalar", lambda eng: eng.activation(sC, sB, AF.Exp, scale=-1.0),
              deps=[r2], strict=True)
    w = p.op("vector", lambda eng: eng.tensor_tensor(sB, sA, sC, ALU.add),
             deps=[e2])
    f1 = p.op("scalar", lambda eng, dst=dst_ap: eng.activation(dst, sB, AF.Copy),
              deps=[w])
    sb.phi_busy[i] = f1
    return f1


def emit_l1_phase_a(ctx, sb, dram, wst, pq_dram, tile):
    p = ctx.p
    nc = ctx.nc
    T = ctx.T
    t0 = tile * T
    NSl = ctx.NS
    emit_cast(ctx, sb.x, sb.xb)
    p.join()
    bq = sb.vec["bq_col"]
    nbq = sb.vec_neg["bq_col"]
    for fb in range(0, ND, 4):
        slot, wsl, wtok = wst.load(dram["wqT"], 0, ND, fb * 128, 512)
        for mf in range(4):
            f = fb + mf
            toks = []
            for t in range(NSl):
                bank = ctx.ring()
                gt = acc_group(ctx, bank, [(wsl(c, mf * 128, (mf + 1) * 128),
                                            sb.xb.sl(c, t * TS, (t + 1) * TS))
                                           for c in range(ND)], deps=[wtok])
                wst.done(slot, gt)
                if getattr(ctx, "dbg_relu_only", False):
                    w = p.op("scalar", lambda eng, f=f, t=t, bank=bank: eng.activation(
                        sb.h.sl(f, t * TS, (t + 1) * TS), ctx.bank(bank), AF.Relu,
                        bias=bq[:, f:f + 1]), deps=[gt])
                else:
                    w = emit_phi_evac(ctx, sb, sb.h.sl(f, t * TS, (t + 1) * TS), bank, gt,
                                      bq[:, f:f + 1], nbq[:, f:f + 1])
                bank_read(ctx, bank, w)
                toks.append(w)
            p.dma("sync", pq_dram[f * 128:(f + 1) * 128, t0:t0 + T], sb.h.chunk(f),
                  deps=toks)
    NTC = T // 128
    bkb = sb.vecrow["bk_row"]
    bvb = sb.vecrow["bv_row"]
    onesr = sb.ones_row_bf
    onesc = sb.ones_bf
    for half in range(2):
        kslot, ksl, ktok = wst.load(dram["wkT"], 0, ND, half * 512, 512)
        vslot, vsl, vtok = wst.load(dram["wvT"], 0, ND, half * 512, 512)
        for tc in range(NTC):
            ktm = sb.ktm[tc % 2]
            vtm = sb.vtm[tc % 2]
            kv_toks = []
            bank = ctx.ring()
            mms = [(sb.xb.sl(c, tc * 128, (tc + 1) * 128),
                    ksl(c, 0, 512)) for c in range(ND)]
            mms.append((onesr[0:1, :], bkb[0:1, half * 512:(half + 1) * 512]))
            gt = acc_group(ctx, bank, mms, deps=[ktok, sb.ktm_busy[tc % 2]])
            wst.done(kslot, gt)
            w = emit_phi_evac_tm(ctx, sb, ktm[:, 0:512], bank, gt)
            bank_read(ctx, bank, w)
            kv_toks.append(w)
            bank = ctx.ring()
            mms = [(sb.xb.sl(c, tc * 128, (tc + 1) * 128),
                    vsl(c, 0, 512)) for c in range(ND)]
            mms.append((onesr[0:1, :], bvb[0:1, half * 512:(half + 1) * 512]))
            gt = acc_group(ctx, bank, mms, deps=[vtok, sb.vtm_busy[tc % 2]])
            wst.done(vslot, gt)
            w = p.op("scalar", lambda eng, vtm=vtm, bank=bank: eng.activation(
                vtm[:, 0:512], ctx.bank(bank), AF.Copy), deps=[gt])
            bank_read(ctx, bank, w)
            kv_toks.append(w)
            bank = ctx.ring()
            fdep = merge(kv_toks, [ctx.bank_free[bank]])
            last = None
            for hh in range(8):
                h = half * 8 + hh
                p0 = (h % 2) * 64
                c0 = (h // 2) * 64
                last = p.op("tensor",
                            lambda eng, hh=hh, p0=p0, c0=c0, bank=bank, ktm=ktm, vtm=vtm:
                            eng.matmul(ctx.ps[p0:p0 + 64, bank * TS + c0: bank * TS + c0 + 64],
                                       ktm[:, hh * 64:(hh + 1) * 64],
                                       vtm[:, hh * 64:(hh + 1) * 64],
                                       start=True, stop=True),
                            deps=fdep if hh == 0 else (), inc=(1 if hh == 7 else 0))
            ctx.bank_ready[bank] = last
            ctx.bank_free[bank] = None
            cola = (half * 8 // 2) * 64
            at = p.op("vector", lambda eng, bank=bank, cola=cola: eng.tensor_tensor(
                sb.kvt[:, cola:cola + 256], sb.kvt[:, cola:cola + 256],
                ctx.ps[0:128, bank * TS + cola: bank * TS + cola + 256], ALU.add),
                deps=[last])
            bank_read(ctx, bank, at)
            bank = ctx.ring()
            fdep = merge(kv_toks, [ctx.bank_free[bank]])
            last = None
            for db in range(4):
                last = p.op("tensor", lambda eng, db=db, bank=bank, ktm=ktm:
                            eng.matmul(ctx.ps[0:128, bank * TS + db: bank * TS + db + 1],
                                       ktm[:, db * 128:(db + 1) * 128], onesc[:, 0:1],
                                       start=True, stop=True),
                            deps=fdep if db == 0 else (), inc=(1 if db == 3 else 0))
            ctx.bank_ready[bank] = last
            ctx.bank_free[bank] = None
            at2 = p.sop("vector", lambda eng, bank=bank, half=half: eng.tensor_tensor(
                sb.ksum_col[:, half * 4:(half + 1) * 4],
                sb.ksum_col[:, half * 4:(half + 1) * 4],
                ctx.ps[0:128, bank * TS: bank * TS + 4], ALU.add), deps=[at, last])
            bank_read(ctx, bank, at2)
            sb.ktm_busy[tc % 2] = last
            sb.vtm_busy[tc % 2] = last
    p.join()


def emit_phi_evac_tm(ctx, sb, dst_ap, bank, gtok):
    return emit_phi_evac(ctx, sb, dst_ap, bank, gtok, None, None)


def emit_l1_phase_b(ctx, sb, dram, wst, pq_dram, xin_dram, tile, out_bf=None):
    p = ctx.p
    nc = ctx.nc
    T = ctx.T
    t0 = tile * T
    NSl = ctx.NS
    qdep = [("Ssy", p.cnt["Ssy"])] if p.cnt["Ssy"] else []
    for c in range(ND):
        p.dma("sync", sb.x.chunk(c), xin_dram[c * 128:(c + 1) * 128, t0:t0 + T],
              deps=qdep if c == 0 else (), strict=True)
        p.dma("sync", sb.h.chunk(c), pq_dram[c * 128:(c + 1) * 128, t0:t0 + T])
    p.join()
    for t in range(NSl):
        bank = ctx.ring()
        gt = acc_group(ctx, bank, [(sb.ks_cb.chunk(c), sb.h.sl(c, t * TS, (t + 1) * TS))
                                   for c in range(ND)], deps=[sb.ks_tok], p0=0, p1=16)
        zt = p.op("vector", lambda eng, t=t, bank=bank: eng.reciprocal(
            sb.zrow[0:16, t * TS:(t + 1) * TS], ctx.bank(bank, 0, 16)), deps=[gt])
        bank_read(ctx, bank, zt)
    p.join()
    sel16 = sb.const["SEL16"]
    pq_toks = {}
    for c in range(ND):
        for t in range(NSl):
            bank = ctx.ring()
            bt = acc_group(ctx, bank, [(sel16[0:16, c * 128:(c + 1) * 128],
                                        sb.zrow[0:16, t * TS:(t + 1) * TS])])
            w = p.op("vector", lambda eng, c=c, t=t, bank=bank: eng.tensor_tensor(
                sb.xb.sl(c, t * TS, (t + 1) * TS), sb.h.sl(c, t * TS, (t + 1) * TS),
                ctx.bank(bank), ALU.mult), deps=[bt])
            bank_read(ctx, bank, w)
            pq_toks[(c, t)] = w
    kb = p.op("scalar", lambda eng: eng.activation(sb.kvt_bf[:, :], sb.kvt[:, :], AF.Copy))
    for fc in range(ND):
        for t in range(NSl):
            bank = ctx.ring()
            dep0 = merge([kb, pq_toks[(fc, t)], ctx.bank_free[bank]])
            last = None
            for half in range(2):
                h = 2 * fc + half
                p0 = half * 64
                c0 = (h // 2) * 64
                last = p.op("tensor", lambda eng, p0=p0, c0=c0, bank=bank, fc=fc, t=t:
                            eng.matmul(ctx.ps[p0:p0 + 64, bank * TS: bank * TS + TS],
                                       sb.kvt_bf[p0:p0 + 64, c0:c0 + 64],
                                       sb.xb.h[p0:p0 + 64,
                                               fc * T + t * TS: fc * T + (t + 1) * TS],
                                       start=True, stop=True),
                            deps=dep0 if half == 0 else (), inc=(1 if half == 1 else 0))
            ctx.bank_ready[bank] = last
            ctx.bank_free[bank] = None
            ev = p.op("scalar", lambda eng, fc=fc, t=t, bank=bank: eng.activation(
                sb.h.sl(fc, t * TS, (t + 1) * TS), ctx.bank(bank), AF.Copy), deps=[last])
            bank_read(ctx, bank, ev)
    p.join()
    bd = sb.vec["bd_col"]
    for fb in range(0, ND, 4):
        slot, wsl, wtok = wst.load(dram["wdT"], 0, ND, fb * 128, 512)
        for mf in range(4):
            f = fb + mf
            for t in range(NSl):
                bank = ctx.ring()
                gt = acc_group(ctx, bank, [(wsl(c, mf * 128, (mf + 1) * 128),
                                            sb.h.sl(c, t * TS, (t + 1) * TS))
                                           for c in range(ND)], deps=[wtok])
                wst.done(slot, gt)
                ev = p.op("vector", lambda eng, f=f, t=t, bank=bank:
                          eng.scalar_tensor_tensor(
                              sb.x.sl(f, t * TS, (t + 1) * TS), ctx.bank(bank),
                              bd[:, f:f + 1], sb.x.sl(f, t * TS, (t + 1) * TS),
                              ALU.add, ALU.add), deps=[gt])
                bank_read(ctx, bank, ev)
    p.join()
    xs2 = sb.xs2

    def add_f32(c, t, xin=xin_dram, t0=t0):
        dep = [("Sdv", ctx.p.cnt["Sdv"])] if ctx.p.cnt["Sdv"] else []
        tok = ctx.p.dma("sync", xs2[:, c * TS:(c + 1) * TS],
                        xin[c * 128:(c + 1) * 128, t0 + t * TS: t0 + (t + 1) * TS],
                        deps=dep, strict=True)
        return xs2[:, c * TS:(c + 1) * TS], tok
    emit_ln(ctx, sb, g=sb.vec["l1g"], b=sb.vec["l1b"], eps=EPS_L1LN, add_f32=add_f32)
    emit_ln(ctx, sb, eps=EPS_TOP, out_bf=out_bf)




def setup_l1(ctx, sb, dram):
    nc, p = ctx.nc, ctx.p
    T = ctx.T
    sb.ktm = [nc.alloc_sbuf_tensor(f"ktm{i}", [128, 512], BF16) for i in range(2)]
    sb.vtm = [nc.alloc_sbuf_tensor(f"vtm{i}", [128, 512], BF16) for i in range(2)]
    sb.ktm_busy = [None, None]
    sb.vtm_busy = [None, None]
    phi_h = nc.alloc_sbuf_tensor("phi_h", [128, 6 * TS], F32)
    sb.phi_s = [phi_h[:, i * TS:(i + 1) * TS] for i in range(6)]
    sb.phi_busy = [None, None]
    sb.phi_i = 0
    sb.kvt = nc.alloc_sbuf_tensor("kvt", [128, 512], F32)
    sb.kvt_bf = sb.ktm[0][:, :]
    sb.ksum_col = nc.alloc_sbuf_tensor("ksum_col", [128, 8], F32)
    sb.zrow = phi_h[0:16, 0:T]
    if getattr(sb, "ks_cb", None) is None:
        sb.ks_cb = CBuf(nc, "ks_cb", ND, 16, BF16)
    sb.ks_tok = None
    sb.xs2 = sb.h.h[:, 4 * T:8 * T].bitcast(F32)
    p.op("gpsimd", lambda eng: eng.memset(sb.kvt[:, :], 0.0), inc=0)
    mt = p.op("gpsimd", lambda eng: eng.memset(sb.ksum_col[:, :], 0.0))
    sb.kv_zero_tok = mt
    p.join()
    sb.vec_neg = {}
    for nm in ("bq_col",):
        h = nc.alloc_sbuf_tensor("vn_" + nm, [128, 8], F32)
        p.op("vector", lambda eng, nm=nm, h=h: eng.tensor_scalar_mul(
            h[:, :], sb.vec[nm][:, :], -1.0))
        sb.vec_neg[nm] = h
    p.join()


def emit_ks_build(ctx, sb):
    p = ctx.p
    nc = ctx.nc
    mz = p.op("gpsimd", lambda eng: eng.memset(sb.ks_cb.h[:, :], 0.0))
    ke = p.op("vector", lambda eng: eng.tensor_scalar_add(
        sb.ksum_col[:, :], sb.ksum_col[:, :], float(EPS_ATTN)))
    last = None
    for c in range(ND):
        p.op("scalar", lambda eng, c=c: eng.copy(
            sb.ks_cb.h[0:64, c * 16 + 2 * c: c * 16 + 2 * c + 1],
            sb.ksum_col[0:64, c:c + 1]), deps=[ke, mz], inc=0)
        last = p.op("scalar", lambda eng, c=c: eng.copy(
            sb.ks_cb.h[64:128, c * 16 + 2 * c + 1: c * 16 + 2 * c + 2],
            sb.ksum_col[64:128, c:c + 1]))
    sb.ks_tok = last
    p.join()


def build_l1_program(T, debug=None):
    nc = bass.Bass("TRN2", target_bir_lowering=False, debug=False)
    p = Prog(nc)
    ctx = Ctx(nc, p, T)
    consts = host_consts()
    sb = SB(ctx, T, consts, need=("SEL16",))
    if debug == "relu":
        ctx.dbg_relu_only = True
        debug = "a"
    xin = nc.dram_tensor("s_in", [D, SEQ], F32, kind="ExternalInput").ap()
    xout = nc.dram_tensor("s_out", [D, SEQ], F32, kind="ExternalOutput").ap()
    pq = nc.dram_tensor("pq_scr", [D, SEQ], BF16,
                        kind=("ExternalOutput" if debug == "a" else "Internal")).ap()
    if debug == "a":
        dbg_kvt = nc.dram_tensor("dbg_kvt", [128, 512], F32, kind="ExternalOutput").ap()
        dbg_ks = nc.dram_tensor("dbg_ks", [128, 8], F32, kind="ExternalOutput").ap()
    dram = {}
    for nm, dn in (("wqT", "l1_wq"), ("wkT", "l1_wk"), ("wvT", "l1_wv"), ("wdT", "l1_wd")):
        dram[nm] = nc.dram_tensor(dn, [D, D], F32, kind="ExternalInput").ap()
    load_vec_inputs(ctx, sb, {"bq_col": (128, 8), "bd_col": (128, 8),
                              "l1g": (128, 8), "l1b": (128, 8),
                              "bk_row": (1, 1024), "bv_row": (1, 1024)},
                    bf_rows=("bk_row", "bv_row"))
    setup_l1(ctx, sb, dram)
    wst = WStream(ctx, "w", nbuf=2)
    for tile in range(SEQ // T):
        t0 = tile * T
        for c in range(ND):
            p.dma("sync", sb.x.chunk(c), xin[c * 128:(c + 1) * 128, t0:t0 + T],
                  strict=True)
        p.join()
        emit_l1_phase_a(ctx, sb, dram, wst, pq, tile)
    if debug == "a":
        p.dma("sync", dbg_kvt[:, :], sb.kvt[:, :])
        p.dma("sync", dbg_ks[:, :], sb.ksum_col[:, :])
        p.dma("sync", xout[0:128, 0:T], sb.x.chunk(0))
        p._wait("gpsimd", "Ssy", p.cnt["Ssy"])
        p.build()
        return nc
    emit_ks_build(ctx, sb)
    prev_store = None
    for tile in range(SEQ // T):
        t0 = tile * T
        emit_l1_phase_b(ctx, sb, dram, wst, pq, xin, tile)
        for c in range(ND):
            prev_store = p.dma("sync", xout[c * 128:(c + 1) * 128, t0:t0 + T],
                               sb.x.chunk(c))
        p.join()
    p._wait("gpsimd", "Ssy", p.cnt["Ssy"])
    p.build()
    return nc





def setup_l0(ctx, sb, consts):
    nc, p = ctx.nc, ctx.p
    sb.dt_b = nc.alloc_sbuf_tensor("dt_b", [128, NH * CH], BF16)
    sb.qd_b = nc.alloc_sbuf_tensor("qd_b", [128, ND * CH], BF16)
    sb.kd_b = nc.alloc_sbuf_tensor("kd_b", [128, 2 * 512], BF16)
    sb.gc_f = nc.alloc_sbuf_tensor("gc_f", [128, 512], F32)
    for name, dst, w in (("DT", sb.dt_b, CH), ("QD", sb.qd_b, CH), ("KD", sb.kd_b, 512)):
        arr = consts[name]
        dr = nc.inline_tensor(np.ascontiguousarray(arr), "ic_" + name)
        n = arr.shape[0] // 128
        for i in range(n):
            p.dma("gpsimd", dst[:, i * w:(i + 1) * w], dr.ap()[i * 128:(i + 1) * 128, :])
    drg = nc.inline_tensor(np.ascontiguousarray(consts["GC"]), "ic_GC")
    p.dma("sync", sb.gc_f[:, :], drg.ap())
    sb.S = [nc.alloc_sbuf_tensor(f"S{i}", [128, 512], F32) for i in range(2)]
    sb.S_bf = [nc.alloc_sbuf_tensor(f"Sbf{i}", [128, 512], BF16) for i in range(2)]
    sb.S_tok = [None, None]
    sb.at_bf = [nc.alloc_sbuf_tensor(f"atbf{i}", [128, CH], BF16) for i in range(4)]
    sb.at_busy = [None] * 4
    sb.rr = nc.alloc_sbuf_tensor("rr", [128, 4 * 4096], BF16)
    sb.pbrow = nc.alloc_sbuf_tensor("pbrow", [1, 2048], BF16)
    sb.krtvrt = nc.alloc_sbuf_tensor("krtvrt", [128, 8192], BF16)
    sb.krt = [sb.krtvrt[:, i * 1024:(i + 1) * 1024] for i in range(4)]
    sb.vrt = [sb.krtvrt[:, 4096 + i * 1024:4096 + (i + 1) * 1024] for i in range(4)]
    sb.krt_busy = [None] * 4
    sb.vrt_busy = [None] * 4
    for i in range(2):
        p.op("gpsimd", lambda eng, i=i: eng.memset(sb.S[i][:, :], 0.0), inc=0)
        p.op("gpsimd", lambda eng, i=i: eng.memset(sb.S_bf[i][:, :], 0.0))
    p.join()


def emit_retention_mixer(ctx, sb, dram, wst, layer):
    p = ctx.p
    T = ctx.T
    pw = dram[f"pw{layer}"]
    rw = dram[f"rw{layer}"]
    pbc = sb.vec[f"pb{layer}_col"]
    pbr = sb.pbrow
    p.dma("gpsimd", pbr[0:1, :], dram[f"pbrow{layer}"][0:1, :], deps=p.now())
    rbc = sb.vec[f"rb{layer}"]
    onesr = sb.ones_row_bf
    S = sb.S[layer]
    S_bf = sb.S_bf[layer]
    h = sb.h
    rr = sb.rr

    def reg(base, fc, a, b):
        r = base // 2
        return rr[:, r * 4096 + fc * 512 + a: r * 4096 + fc * 512 + b]

    mix_fin = [[] for _ in range(ND)]
    NSC = T // 512
    for sc in range(NSC):
        tok0 = sc * 512
        for colbase, kind in ((0, "q"), (1024, "k"), (3072, "xo")):
            for fb in (0, 4):
                slot, wsl, wtok = wst.load(pw, 0, ND, colbase + fb * 128, 512)
                for mf in range(4):
                    f = fb + mf
                    fg = (colbase // 128) + f
                    bank = ctx.ring()
                    gt = acc_group(ctx, bank,
                                   [(wsl(c, mf * 128, (mf + 1) * 128),
                                     sb.xb.sl(c, tok0, tok0 + 512)) for c in range(ND)],
                                   deps=[wtok])
                    wst.done(slot, gt)
                    pb_ap = pbc[:, fg:fg + 1]
                    pk = ctx.bank(bank)
                    if kind == "q":
                        qd_ap = sb.qd_b[:, f * CH:(f + 1) * CH]
                        for tci4 in range(4):
                            dst = reg(0, f, tci4 * CH, (tci4 + 1) * CH)
                            pk4 = ctx.ps[0:128, bank * TS + tci4 * CH: bank * TS + (tci4 + 1) * CH]
                            ev = p.op("vector",
                                      lambda eng, dst=dst, pk4=pk4, pb_ap=pb_ap, qd_ap=qd_ap:
                                      eng.scalar_tensor_tensor(dst, pk4, pb_ap, qd_ap,
                                                               ALU.add, ALU.mult),
                                      deps=[gt] if tci4 == 0 else ())
                    elif kind == "k":
                        dst = reg(2, f, 0, 512)
                        ev = p.op("vector", lambda eng, dst=dst, pk=pk, pb_ap=pb_ap:
                                  eng.tensor_scalar_add(dst, pk, pb_ap), deps=[gt])
                    else:
                        dst = reg(4, f, 0, 512)
                        ev = p.op("scalar", lambda eng, dst=dst, pk=pk, pb_ap=pb_ap:
                                  eng.activation(dst, pk, AF.Silu, bias=pb_ap), deps=[gt])
                    bank_read(ctx, bank, ev)
        for half in range(2):
            kslot, ksl, ktok = wst.load(pw, 0, ND, 1024 + half * 512, 512)
            vslot, vsl, vtok = wst.load(pw, 0, ND, 2048 + half * 512, 512)
            for tci in range(4):
                tc = sc * 4 + tci
                krt = sb.krt[tci]
                vrt = sb.vrt[tci]
                bank = ctx.ring()
                mms = [(sb.xb.sl(c, tc * 128, (tc + 1) * 128), ksl(c, 0, 512))
                       for c in range(ND)]
                mms.append((onesr[0:1, :], pbr[0:1, half * 512:(half + 1) * 512]))
                gt = acc_group(ctx, bank, mms, deps=[ktok, sb.krt_busy[tci]])
                wst.done(kslot, gt)
                dst = krt[:, half * 512:(half + 1) * 512]
                kd_ap = sb.kd_b[:, half * 512:(half + 1) * 512]
                pk = ctx.bank(bank)
                ev = p.op("vector", lambda eng, dst=dst, pk=pk, kd_ap=kd_ap:
                          eng.tensor_tensor(dst, pk, kd_ap, ALU.mult), deps=[gt])
                bank_read(ctx, bank, ev)
                bank = ctx.ring()
                mms = [(sb.xb.sl(c, tc * 128, (tc + 1) * 128), vsl(c, 0, 512))
                       for c in range(ND)]
                mms.append((onesr[0:1, :], pbr[0:1, 1024 + half * 512:1024 + (half + 1) * 512]))
                gt = acc_group(ctx, bank, mms, deps=[vtok, sb.vrt_busy[tci]])
                wst.done(vslot, gt)
                dst = vrt[:, half * 512:(half + 1) * 512]
                pk = ctx.bank(bank)
                ev = p.op("scalar", lambda eng, dst=dst, pk=pk:
                          eng.activation(dst, pk, AF.Copy), deps=[gt])
                bank_read(ctx, bank, ev)
        p.join()
        ctx.ring_banks = [0, 1, 2, 3, 4]
        ret2_evs = []
        for tci in range(4):
            tc = sc * 4 + tci
            cofs = tci * 128
            krt = sb.krt[tci]
            vrt = sb.vrt[tci]
            ret_banks = [6, 7]
            ret_last = {}
            for hh in range(NH):
                hp = (hh % 2) * 64
                fc = hh // 2
                k_ap = rr[hp:hp + 64, 1 * 4096 + fc * 512 + cofs: 1 * 4096 + fc * 512 + cofs + 128]
                q_ap = rr[hp:hp + 64, 0 * 4096 + fc * 512 + cofs: 0 * 4096 + fc * 512 + cofs + 128]
                bank = ctx.ring()
                at_out = ctx.bank(bank, 0, 128, CH)
                d0 = merge([ctx.bank_free[bank], sb.at_busy[hh % 4]])
                at_mm = p.op("tensor", lambda eng, at_out=at_out, k_ap=k_ap, q_ap=q_ap:
                             eng.matmul(at_out, k_ap, q_ap, start=True, stop=True),
                             deps=d0)
                ctx.bank_ready[bank] = at_mm
                ctx.bank_free[bank] = None
                atb = sb.at_bf[hh % 4]
                dt_ap = sb.dt_b[:, hh * CH:(hh + 1) * CH]
                at_ev = p.op("vector", lambda eng, atb=atb, at_out=at_out, dt_ap=dt_ap:
                             eng.tensor_tensor(atb[:, :], at_out, dt_ap, ALU.mult),
                             deps=[at_mm])
                bank_read(ctx, bank, at_ev)
                sb.at_busy[hh % 4] = at_ev
                rbank = ret_banks[hh // 8]
                cblk = ((hh // 2) % 4) * 128
                r_out = ctx.ps[hp:hp + 64, rbank * TS + cblk: rbank * TS + cblk + 128]
                v_ap = vrt[:, hh * 64:(hh + 1) * 64]
                s_ap = S_bf[hp:hp + 64, fc * 64:fc * 64 + 64]
                rdep = merge([at_ev, sb.S_tok[layer]] +
                             ([ctx.bank_free[rbank]] if hh % 8 == 0 else []))
                p.op("tensor", lambda eng, r_out=r_out, v_ap=v_ap, atb=atb:
                     eng.matmul(r_out, v_ap, atb[:, :], start=True, stop=False),
                     deps=rdep, inc=0)
                rt = p.op("tensor", lambda eng, r_out=r_out, s_ap=s_ap, q_ap=q_ap:
                          eng.matmul(r_out, s_ap, q_ap, start=False, stop=True))
                ret_last[hh] = rt
                if hh % 8 == 7:
                    ctx.bank_ready[rbank] = rt
                    ctx.bank_free[rbank] = None
            for fc in range(ND):
                rbank = ret_banks[fc // 4]
                cblk = (fc % 4) * 128
                dst = reg(6, fc, cofs, cofs + 128)
                ps_ap = ctx.ps[0:128, rbank * TS + cblk: rbank * TS + cblk + 128]
                xo_ap = rr[:, 2 * 4096 + fc * 512 + cofs: 2 * 4096 + fc * 512 + cofs + 128]
                ev = p.op("vector", lambda eng, dst=dst, ps_ap=ps_ap, xo_ap=xo_ap:
                          eng.tensor_tensor(dst, ps_ap, xo_ap, ALU.mult),
                          deps=[ret_last[fc * 2 + 1]])
                bank_read(ctx, rbank, ev)
                ret2_evs.append(ev)
            ibank = 5
            ilast = None
            for hh in range(NH):
                hp = (hh % 2) * 64
                c0 = (hh // 2) * 64
                i_out = ctx.ps[hp:hp + 64, ibank * TS + c0: ibank * TS + c0 + 64]
                kk = krt[:, hh * 64:(hh + 1) * 64]
                vv = vrt[:, hh * 64:(hh + 1) * 64]
                d0 = merge([ctx.bank_free[ibank]]) if hh == 0 else []
                ilast = p.op("tensor", lambda eng, i_out=i_out, kk=kk, vv=vv:
                             eng.matmul(i_out, kk, vv, start=True, stop=True),
                             deps=d0, inc=(1 if hh == NH - 1 else 0))
            ctx.bank_ready[ibank] = ilast
            ctx.bank_free[ibank] = None
            sd = merge([ilast, sb.S_tok[layer]] + [ret_last[hh] for hh in range(NH)])
            p.op("vector", lambda eng: eng.tensor_tensor(S[:, :], S[:, :], sb.gc_f[:, :],
                                                         ALU.mult), deps=sd, inc=0)
            s2 = p.op("vector", lambda eng, ibank=ibank: eng.tensor_tensor(
                S[:, :], S[:, :], ctx.bank(ibank), ALU.add))
            bank_read(ctx, ibank, s2)
            sbf = p.op("scalar", lambda eng: eng.activation(S_bf[:, :], S[:, :], AF.Copy),
                       deps=[s2])
            sb.S_tok[layer] = sbf
            sb.krt_busy[tci] = ilast
            sb.vrt_busy[tci] = ilast
        ctx.ring_banks = None
        for fb in (0, 4):
            slot, wsl, wtok = wst.load(rw, 0, ND, fb * 128, 512)
            for mf in range(4):
                f = fb + mf
                bank = ctx.ring()
                gt = acc_group(ctx, bank,
                               [(wsl(c, mf * 128, (mf + 1) * 128),
                                 reg(6, c, 0, 512)) for c in range(ND)],
                               deps=merge([wtok], ret2_evs))
                wst.done(slot, gt)
                x_ap = sb.x.sl(f, tok0, tok0 + 512)
                pk = ctx.bank(bank)
                rb_ap = rbc[:, f:f + 1]
                ev = p.op("vector", lambda eng, x_ap=x_ap, pk=pk, rb_ap=rb_ap:
                          eng.scalar_tensor_tensor(x_ap, pk, rb_ap, x_ap,
                                                   ALU.add, ALU.add), deps=[gt])
                bank_read(ctx, bank, ev)
                mix_fin[f] = [ev]
    return mix_fin


def emit_plain_ffn(ctx, sb, *, xb, w1, w2, b1_col, b2_col, HID, wst, act=AF.Gelu,
                   wst_down=None):
    p = ctx.p
    T = ctx.T
    NSl = ctx.NS
    NHC = HID // 128
    h = sb.h
    fin = [[] for _ in range(ND)]
    for g0 in range(0, NHC, 8):
        ng = min(8, NHC - g0)
        gl = {}
        m_tok = None
        for gi in range(ng):
            half = gi // 4
            if half not in gl:
                nw = min(4, ng - half * 4) * 128
                gl[half] = wst.load(w1, 0, ND, (g0 + half * 4) * 128, nw)
            slot, wsl, wtok = gl[half]
            gcol = (gi % 4) * 128
            bg = 4 + (gi % 2)
            for t in range(NSl):
                gt = acc_group(ctx, bg, [(wsl(c, gcol, gcol + 128),
                                          xb.sl(c, t * TS, (t + 1) * TS)) for c in range(ND)],
                               deps=[wtok])
                wst.done(slot, gt)
                m_tok = p.op("scalar", lambda eng, gi=gi, t=t, bg=bg, g0=g0: eng.activation(
                    h.sl(gi, t * TS, (t + 1) * TS), ctx.bank(bg), act,
                    bias=b1_col[:, g0 + gi:g0 + gi + 1]), deps=[gt])
                bank_read(ctx, bg, m_tok)
        wd_st = wst_down if wst_down is not None else wst
        dl = []
        for half in range(0, ng, 4):
            nked = min(4, ng - half)
            dl.append(wd_st.load(w2, (g0 + half) * 128, nked, 0, 1024))
        for f2 in range(ND):
            for t in range(NSl):
                bank = t if (f2 % 2 == 0) else 4 + t
                mms = []
                dep0 = [dl[-1][2], m_tok]
                for gi in range(ng):
                    dslot, dsl, dtok = dl[gi // 4]
                    mms.append((dsl(gi % 4, f2 * 128, (f2 + 1) * 128),
                                h.sl(gi, t * TS, (t + 1) * TS)))
                gt = acc_group(ctx, bank, mms, deps=dep0)
                for dslot, dsl, dtok in dl:
                    wd_st.done(dslot, gt)
                if g0 == 0:
                    ev = p.op("vector", lambda eng, f2=f2, t=t, bank=bank:
                              eng.scalar_tensor_tensor(
                                  sb.x.sl(f2, t * TS, (t + 1) * TS), ctx.bank(bank),
                                  b2_col[:, f2:f2 + 1],
                                  sb.x.sl(f2, t * TS, (t + 1) * TS), ALU.add, ALU.add),
                              deps=[gt])
                else:
                    ev = p.op("vector", lambda eng, f2=f2, t=t, bank=bank:
                              eng.tensor_tensor(
                                  sb.x.sl(f2, t * TS, (t + 1) * TS),
                                  sb.x.sl(f2, t * TS, (t + 1) * TS),
                                  ctx.bank(bank), ALU.add), deps=[gt])
                bank_read(ctx, bank, ev)
                fin[f2] = [ev]
    return fin


def emit_l0attn(ctx, sb, dram, wst, xin_dram, tile):
    p = ctx.p
    T = ctx.T
    t0 = tile * T
    dbg = getattr(ctx, "l0_dbg", None)
    prev_fin = getattr(ctx, "l0_x_ready", None)
    for layer in range(2):
        emit_ln(ctx, sb, g=sb.vec[f"ln1g{layer}"], b=sb.vec[f"ln1b{layer}"],
                eps=EPS_TOP, out_bf=sb.xb, bf_only=True, x_ready=prev_fin)
        if dbg is not None and layer == 0 and tile == 0 and "lnx" in dbg:
            p.join()
            for c in range(ND):
                p.dma("sync", dbg["lnx"][c * 128:(c + 1) * 128, 0:T], sb.xb.chunk(c))
            p.join()
        mix_fin = emit_retention_mixer(ctx, sb, dram, wst, layer)
        if dbg is not None and layer == 0 and tile == 0 and "mix" in dbg:
            p.join()
            for c in range(ND):
                p.dma("sync", dbg["mix"][c * 128:(c + 1) * 128, 0:T], sb.x.chunk(c))
            p.join()
        emit_ln(ctx, sb, g=sb.vec[f"ln2g{layer}"], b=sb.vec[f"ln2b{layer}"],
                eps=EPS_TOP, out_bf=sb.xb, bf_only=True, x_ready=mix_fin)
        prev_fin = emit_plain_ffn(
            ctx, sb, xb=sb.xb, w1=dram[f"f1w{layer}"], w2=dram[f"f2w{layer}"],
            b1_col=sb.vec[f"f1b{layer}"], b2_col=sb.vec[f"f2b{layer}"],
            HID=4096, wst=wst, act=AF.Gelu,
            wst_down=getattr(ctx, "wst_down", None))
        if dbg is not None and layer == 0 and tile == 0 and "ffn" in dbg:
            p.join()
            for c in range(ND):
                p.dma("sync", dbg["ffn"][c * 128:(c + 1) * 128, 0:T], sb.x.chunk(c))
            p.join()
    p.join()
    def add_f32(c, t, xin=xin_dram, t0=t0):
        dep = [("Sdv", p.cnt["Sdv"])] if p.cnt["Sdv"] else []
        tok = p.dma("sync", sb.xs2[:, c * TS:(c + 1) * TS],
                    xin[c * 128:(c + 1) * 128, t0 + t * TS: t0 + (t + 1) * TS],
                    deps=dep, strict=True)
        return sb.xs2[:, c * TS:(c + 1) * TS], tok
    emit_ln(ctx, sb, g=sb.vec["lnfg"], b=sb.vec["lnfb"], eps=EPS_TOP, add_f32=add_f32)
    emit_ln(ctx, sb, eps=EPS_TOP)


def build_l0attn_program(T, debug=()):
    nc = bass.Bass("TRN2", target_bir_lowering=False, debug=False)
    p = Prog(nc)
    ctx = Ctx(nc, p, T)
    consts = host_consts()
    sb = SB(ctx, T, consts)
    xin = nc.dram_tensor("s_in", [D, SEQ], F32, kind="ExternalInput").ap()
    xout = nc.dram_tensor("s_out", [D, SEQ], F32, kind="ExternalOutput").ap()
    if debug:
        ctx.l0_dbg = {}
        for nm in debug:
            dt_ = BF16 if nm == "lnx" else F32
            ctx.l0_dbg[nm] = nc.dram_tensor("dbg_" + nm, [D, T], dt_, kind="ExternalOutput").ap()
    dram = {}
    for i in range(2):
        dram[f"pw{i}"] = nc.dram_tensor(f"pw{i}", [D, 4 * D], F32, kind="ExternalInput").ap()
        dram[f"rw{i}"] = nc.dram_tensor(f"rw{i}", [D, D], F32, kind="ExternalInput").ap()
        dram[f"f1w{i}"] = nc.dram_tensor(f"f1w{i}", [D, 4 * D], F32, kind="ExternalInput").ap()
        dram[f"f2w{i}"] = nc.dram_tensor(f"f2w{i}", [4 * D, D], F32, kind="ExternalInput").ap()
    vecs = {"lnfg": (128, 8), "lnfb": (128, 8)}
    rows = []
    for i in range(2):
        for nm in (f"rb{i}", f"f2b{i}", f"ln1g{i}", f"ln1b{i}",
                   f"ln2g{i}", f"ln2b{i}"):
            vecs[nm] = (128, 8)
        vecs[f"pb{i}_col"] = (128, 32)
        vecs[f"f1b{i}"] = (128, 32)
        dram[f"pbrow{i}"] = nc.dram_tensor(f"pb{i}_row", [1, 2048], F32,
                                           kind="ExternalInput").ap()
    load_vec_inputs(ctx, sb, vecs)
    setup_l0(ctx, sb, consts)
    sb.xs2 = sb.rr[:, 0:2 * ND * TS].bitcast(F32)
    wst = WStream(ctx, "w", nbuf=2)
    prev_store = None
    for tile in range(SEQ // T):
        t0 = tile * T
        for c in range(ND):
            p.dma("sync", sb.x.chunk(c), xin[c * 128:(c + 1) * 128, t0:t0 + T],
                  deps=[prev_store] if (prev_store and c == 0) else (), strict=True)
        p.join()
        emit_l0attn(ctx, sb, dram, wst, xin, tile)
        for c in range(ND):
            prev_store = p.dma("sync", xout[c * 128:(c + 1) * 128, t0:t0 + T], sb.x.chunk(c))
    p._wait("gpsimd", "Ssy", p.cnt["Ssy"])
    p.build()
    return nc



def build_l0ffn_program(T):
    nc = bass.Bass("TRN2", target_bir_lowering=False, debug=False)
    p = Prog(nc)
    ctx = Ctx(nc, p, T)
    consts = host_consts()
    sb = SB(ctx, T, consts)
    xin = nc.dram_tensor("s_in", [D, SEQ], F32, kind="ExternalInput").ap()
    xout = nc.dram_tensor("s_out", [D, SEQ], F32, kind="ExternalOutput").ap()
    dram = {
        "wgT": nc.dram_tensor("wgT", [D, 3072], I32, kind="ExternalInput").ap(),
        "wuT": nc.dram_tensor("wuT", [D, 3072], I32, kind="ExternalInput").ap(),
        "wdT": nc.dram_tensor("wdT", [3072, D], I32, kind="ExternalInput").ap(),
    }
    wst = WStream(ctx, "w", nbuf=4)
    prev_store = None
    for tile in range(SEQ // T):
        t0 = tile * T
        for c in range(ND):
            p.dma("sync", sb.x.chunk(c), xin[c * 128:(c + 1) * 128, t0:t0 + T],
                  deps=[prev_store] if (prev_store and c == 0) else (), strict=True)
        p.join()
        emit_l0ffn(ctx, sb, dram, wst)
        emit_ln(ctx, sb, eps=EPS_TOP)
        for c in range(ND):
            prev_store = p.dma("sync", xout[c * 128:(c + 1) * 128, t0:t0 + T], sb.x.chunk(c))
    p._wait("gpsimd", "Ssy", p.cnt["Ssy"])
    p.build()
    return nc




def _load_router_weights(ctx, sb, wrT_dram, rb_dram):
    nc, p = ctx.nc, ctx.p
    sb.wr_b = nc.alloc_sbuf_tensor("wr_b", [128, 64], BF16)
    tok = None
    for c in range(ND):
        tok = p.dma("gpsimd", sb.wr_b[:, c * 8:c * 8 + 7],
                    wrT_dram[c * 128:(c + 1) * 128, 0:7])
    sb.rb_row_bf = nc.alloc_sbuf_tensor("rb_row_bf", [1, 8], BF16)
    tok = p.dma("gpsimd", sb.rb_row_bf[0:1, 0:7], rb_dram[0:1, 0:7])
    sb.wr_tok = tok


def build_moe_program(T, debug_w=False, n_routed=7):
    nc = bass.Bass("TRN2", target_bir_lowering=False, debug=False)
    p = Prog(nc)
    ctx = Ctx(nc, p, T)
    consts = host_consts()
    sb = SB(ctx, T, consts, need=("SEL7", "ident"), moe=True)
    sb.w_rows = nc.alloc_sbuf_tensor("w_rows", [7, T], BF16)
    sb.wbb = nc.alloc_sbuf_tensor("wbb", [128, T], F32)
    sb.rtr = nc.alloc_sbuf_tensor("rtr", [128, 64], F32)
    xin = nc.dram_tensor("s_in", [D, SEQ], F32, kind="ExternalInput").ap()
    xout = nc.dram_tensor("s_out", [D, SEQ], F32, kind="ExternalOutput").ap()
    dram = {}
    for nm, shape in (("sgT", [D, 2816]), ("suT", [D, 2816]), ("sdT", [2816, D]),
                      ("rgT", [7 * D, 2816]), ("ruT", [7 * D, 2816]),
                      ("rdT", [7 * 2816, D]), ("wrT", [D, 7]), ("rb_row", [1, 7])):
        dram[nm] = nc.dram_tensor(nm, shape, F32, kind="ExternalInput").ap()
    _load_router_weights(ctx, sb, dram["wrT"], dram["rb_row"])
    dbg_w = nc.dram_tensor("dbg_w", [7, SEQ], F32, kind="ExternalOutput").ap() if debug_w else None
    dbg_z = nc.dram_tensor("dbg_z", [D, SEQ], F32, kind="ExternalOutput").ap() if debug_w else None
    if debug_w:
        ctx.dbg_h = nc.dram_tensor("dbg_h", [1024, T], BF16, kind="ExternalOutput").ap()
        ctx.dbg_h_armed = True
    else:
        ctx.dbg_h = None
    wst = WStream(ctx, "w", nbuf=3)
    prev_store = None
    for tile in range(SEQ // T):
        t0 = tile * T
        for c in range(ND):
            p.dma("sync", sb.x.chunk(c), xin[c * 128:(c + 1) * 128, t0:t0 + T],
                  deps=[prev_store] if (prev_store and c == 0) else (), strict=True)
        p.join()
        emit_cast(ctx, sb.x, sb.xb)
        p.join()
        emit_router(ctx, sb, xb=sb.xb, wr_b=sb.wr_b, rb_row_bf=sb.rb_row_bf,
                    ones_row_bf=sb.ones_row_bf, w_rows=sb.w_rows)
        if debug_w:
            p.dma("sync", dbg_w[0:7, t0:t0 + T], sb.w_rows[0:7, :])
        emit_moe_experts(ctx, sb, dram, wst, n_routed=n_routed)
        if debug_w:
            for c in range(ND):
                p.dma("sync", dbg_z[c * 128:(c + 1) * 128, t0:t0 + T], sb.x.chunk(c))
        emit_ln(ctx, sb, eps=EPS_TOP)
        for c in range(ND):
            prev_store = p.dma("sync", xout[c * 128:(c + 1) * 128, t0:t0 + T], sb.x.chunk(c))
    p._wait("gpsimd", "Ssy", p.cnt["Ssy"])
    p.build()
    return nc





def build_fused_program(T=1024):
    nc = bass.Bass("TRN2", target_bir_lowering=False, debug=False)
    p = Prog(nc)
    ctx = Ctx(nc, p, T)
    consts = host_consts()
    sb = SB(ctx, T, consts, need=("SEL7", "SEL16", "ident"), moe=True)
    xin = nc.dram_tensor("xT", [D, SEQ], F32, kind="ExternalInput").ap()
    xout = nc.dram_tensor("s_out", [D, SEQ], F32, kind="ExternalOutput").ap()
    s2 = nc.dram_tensor("s2_scr", [D, SEQ], F32, kind="Internal").ap()
    pq = nc.dram_tensor("pq_scr", [D, SEQ], BF16, kind="Internal").ap()

    dram = {}
    for i in range(2):
        dram[f"pw{i}"] = nc.dram_tensor(f"pw{i}", [D, 4 * D], F32, kind="ExternalInput").ap()
        dram[f"rw{i}"] = nc.dram_tensor(f"rw{i}", [D, D], F32, kind="ExternalInput").ap()
        dram[f"f1w{i}"] = nc.dram_tensor(f"f1w{i}", [D, 4 * D], F32, kind="ExternalInput").ap()
        dram[f"f2w{i}"] = nc.dram_tensor(f"f2w{i}", [4 * D, D], F32, kind="ExternalInput").ap()
    dram["wgT"] = nc.dram_tensor("wgT", [D, 3072], I32, kind="ExternalInput").ap()
    dram["wuT"] = nc.dram_tensor("wuT", [D, 3072], I32, kind="ExternalInput").ap()
    dram["wdT"] = nc.dram_tensor("wdT", [3072, D], I32, kind="ExternalInput").ap()
    for nm, dn in (("wqT", "l1_wq"), ("wkT", "l1_wk"), ("wvT", "l1_wv"), ("wdT_l1", "l1_wd")):
        dram[nm] = nc.dram_tensor(dn, [D, D], F32, kind="ExternalInput").ap()
    for nm, shape in (("sgT", [D, 2816]), ("suT", [D, 2816]), ("sdT", [2816, D]),
                      ("rgT", [7 * D, 2816]), ("ruT", [7 * D, 2816]),
                      ("rdT", [7 * 2816, D]), ("wrT", [D, 7]), ("rb_row", [1, 7])):
        dram[nm] = nc.dram_tensor(nm, shape, F32, kind="ExternalInput").ap()

    vecs = {"lnfg": (128, 8), "lnfb": (128, 8),
            "bq_col": (128, 8), "bd_col": (128, 8),
            "l1g": (128, 8), "l1b": (128, 8),
            "bk_row": (1, 1024), "bv_row": (1, 1024)}
    rows_only = []
    for i in range(2):
        for nm in (f"rb{i}", f"f2b{i}", f"ln1g{i}", f"ln1b{i}", f"ln2g{i}", f"ln2b{i}"):
            vecs[nm] = (128, 8)
        vecs[f"pb{i}_col"] = (128, 32)
        vecs[f"f1b{i}"] = (128, 32)
        dram[f"pbrow{i}"] = nc.dram_tensor(f"pb{i}_row", [1, 2048], F32,
                                           kind="ExternalInput").ap()
    load_vec_inputs(ctx, sb, vecs, bf_rows=("bk_row", "bv_row"))
    setup_l0(ctx, sb, consts)

    class _KCB:
        def __init__(self, ap):
            self.h = ap

        def chunk(self, c):
            return self.h[:, c * 16:(c + 1) * 16]
    sb.ks_cb = _KCB(sb.rr[:, 8192 + 3 * T + 128:8192 + 3 * T + 256])
    setup_l1(ctx, sb, {"wkT": dram["wkT"], "wvT": dram["wvT"]})
    _load_router_weights(ctx, sb, dram["wrT"], dram["rb_row"])
    sb.xs2 = sb.rr[:, 0:2 * ND * TS].bitcast(F32)
    sb.wbb = sb.rr[:, 8192:8192 + 2 * T].bitcast(F32)
    sb.w_rows = sb.rr[0:7, 8192 + 2 * T:8192 + 3 * T]
    sb.rtr = sb.rr[:, 8192 + 3 * T:8192 + 3 * T + 128].bitcast(F32)
    wst = WStream(ctx, "w", nbuf=3)
    ctx.wst_down = WStream(ctx, "wdn", bufs=[sb.krtvrt[:, 0:4096],
                                             sb.krtvrt[:, 4096:8192]])
    l1dram = {"wqT": dram["wqT"], "wkT": dram["wkT"], "wvT": dram["wvT"],
              "wdT": dram["wdT_l1"]}

    NT = SEQ // T
    for tile in range(NT):
        t0 = tile * T
        qdep = [("Ssy", p.cnt["Ssy"])] if p.cnt["Ssy"] else []
        ld_toks = []
        for c in range(ND):
            ld_toks.append(p.dma("sync", sb.x.chunk(c),
                                 xin[c * 128:(c + 1) * 128, t0:t0 + T],
                                 deps=qdep if c == 0 else (), strict=True))
        p.tag = "l0attn"
        ctx.l0_x_ready = [[t_] for t_ in ld_toks]
        emit_l0attn(ctx, sb, dram, wst, xin, tile)
        p.tag = "l0ffn"
        ffn_fin = emit_l0ffn(ctx, sb, dram, wst)
        p.tag = "ln2top"
        emit_ln(ctx, sb, eps=EPS_TOP, x_ready=ffn_fin)
        for c in range(ND):
            p.dma("sync", s2[c * 128:(c + 1) * 128, t0:t0 + T], sb.x.chunk(c))
        p.tag = "l1a"
        emit_l1_phase_a(ctx, sb, l1dram, wst, pq, tile)
    p.tag = "ks"
    emit_ks_build(ctx, sb)
    for tile in range(NT):
        t0 = tile * T
        p.tag = "l1b"
        emit_l1_phase_b(ctx, sb, l1dram, wst, pq, s2, tile)
        p.tag = "moe"
        emit_cast(ctx, sb.x, sb.xb)
        p.join()
        emit_router(ctx, sb, xb=sb.xb, rb_row_bf=sb.rb_row_bf, wr_b=sb.wr_b,
                    ones_row_bf=sb.ones_row_bf, w_rows=sb.w_rows)
        moe_fin = emit_moe_experts(ctx, sb, dram, wst)
        p.tag = "ln4top"
        emit_ln(ctx, sb, eps=EPS_TOP, x_ready=moe_fin)
        for c in range(ND):
            p.dma("sync", xout[c * 128:(c + 1) * 128, t0:t0 + T], sb.x.chunk(c))
        p.join()
    p._wait("gpsimd", "Ssy", p.cnt["Ssy"])
    p.build()
    return nc


def kernel(**inputs):
    if "f" not in _PROGS:
        _PROGS["f"] = build_fused_program(1024)
    nc = _PROGS["f"]
    n = 8
    packs = [host_pack(inputs, core) for core in range(n)]
    names = ["pw0", "pw1", "rw0", "rw1", "f1w0", "f1w1", "f2w0", "f2w1",
             "lnfg", "lnfb", "wgT", "wuT", "wdT",
             "l1_wq", "l1_wk", "l1_wv", "l1_wd",
             "bq_col", "bd_col", "l1g", "l1b", "bk_row", "bv_row",
             "sgT", "suT", "sdT", "rgT", "ruT", "rdT", "wrT", "rb_row"]
    for i in range(2):
        names += [f"pb{i}_col", f"pb{i}_row", f"rb{i}", f"f1b{i}", f"f2b{i}",
                  f"ln1g{i}", f"ln1b{i}", f"ln2g{i}", f"ln2b{i}"]
    in_maps = [dict(_sub(p_, names), xT=p_["xT"]) for p_ in packs]
    res = run_bass_kernel_spmd(nc, in_maps, core_ids=list(range(n)))
    outs = [np.ascontiguousarray(np.asarray(r["s_out"]).T) for r in res.results]
    return np.stack(outs, axis=0)



_PROGS = {}


def _get_progs():
    if "a" not in _PROGS:
        _PROGS["a"] = build_l0attn_program(2048)
        _PROGS["b"] = build_l0ffn_program(2048)
        _PROGS["c"] = build_l1_program(2048)
        _PROGS["d"] = build_moe_program(2048)
    return _PROGS


def _sub(m, names):
    return {k: m[k] for k in names}


def kernel_unfused(**inputs):
    progs = _get_progs()
    n = 8
    packs = [host_pack(inputs, core) for core in range(n)]
    cores = list(range(n))

    a_names = ["pw0", "pw1", "rw0", "rw1", "f1w0", "f1w1", "f2w0", "f2w1",
               "lnfg", "lnfb"]
    for i in range(2):
        a_names += [f"pb{i}_col", f"pb{i}_row", f"rb{i}", f"f1b{i}", f"f2b{i}",
                    f"ln1g{i}", f"ln1b{i}", f"ln2g{i}", f"ln2b{i}"]
    in_maps = [dict(_sub(p_, a_names), s_in=p_["xT"]) for p_ in packs]
    res = run_bass_kernel_spmd(progs["a"], in_maps, core_ids=cores)
    s1 = [np.asarray(r["s_out"]) for r in res.results]

    b_names = ["wgT", "wuT", "wdT"]
    in_maps = [dict(_sub(p_, b_names), s_in=s) for p_, s in zip(packs, s1)]
    res = run_bass_kernel_spmd(progs["b"], in_maps, core_ids=cores)
    s2 = [np.asarray(r["s_out"]) for r in res.results]

    c_names = ["l1_wq", "l1_wk", "l1_wv", "l1_wd", "bq_col", "bd_col",
               "l1g", "l1b", "bk_row", "bv_row"]
    in_maps = [dict(_sub(p_, c_names), s_in=s) for p_, s in zip(packs, s2)]
    res = run_bass_kernel_spmd(progs["c"], in_maps, core_ids=cores)
    s3 = [np.asarray(r["s_out"]) for r in res.results]

    d_names = ["sgT", "suT", "sdT", "rgT", "ruT", "rdT", "wrT", "rb_row"]
    in_maps = [dict(_sub(p_, d_names), s_in=s) for p_, s in zip(packs, s3)]
    res = run_bass_kernel_spmd(progs["d"], in_maps, core_ids=cores)
    outs = [np.ascontiguousarray(np.asarray(r["s_out"]).T) for r in res.results]
    return np.stack(outs, axis=0)
